# version 02
# baseline (speedup 1.0000x reference)
import numpy as np
import ml_dtypes
import concourse.bass as bass
import concourse.mybir as mybir
from concourse.bass import AP
from concourse.bass_utils import run_bass_kernel_spmd

F32 = mybir.dt.float32
BF16 = mybir.dt.bfloat16
I32 = mybir.dt.int32
AF = mybir.ActivationFunctionType
ALU = mybir.AluOpType

NCORES = 8
D = 1024
DC = D // 128
CH = 64
NEG = -1e30
EPS5 = 1e-5
EPS6 = 1e-6
RT127 = 1.0 / 127.0

ENGS = ("te", "ve", "se", "ge", "sy")
ESEM = {"te": "sTE", "ve": "sV", "se": "sS", "ge": "sG", "sy": "sSY"}


class Res:
    __slots__ = ("name", "last_w", "readers")

    def __init__(self, name):
        self.name = name
        self.last_w = None
        self.readers = []


class Op:
    __slots__ = ("eng", "fn", "deps", "dma_sem", "seq", "pub", "waits", "tag")

    def __init__(self, eng, fn, deps, dma_sem, tag):
        self.eng = eng
        self.fn = fn
        self.deps = deps
        self.dma_sem = dma_sem
        self.seq = None
        self.pub = None
        self.waits = []
        self.tag = tag


class Sched:

    def __init__(self, nc):
        self.nc = nc
        self.ops = []
        self.per_eng = {e: [] for e in ENGS}
        self.dma_seq = {}
        self.pub_cnt = {e: 0 for e in ENGS}

    def res(self, name):
        return Res(name)

    def add(self, eng, fn, reads=(), writes=(), dma=None, tag=""):
        deps = []
        for r in reads:
            if r.last_w is not None:
                deps.append(r.last_w)
        for w in writes:
            if w.last_w is not None:
                deps.append(w.last_w)
            deps.extend(w.readers)
        deps = list({id(d): d for d in deps}.values())
        o = Op(eng, fn, deps, dma, tag)
        o.seq = len(self.per_eng[eng])
        self.per_eng[eng].append(o)
        self.ops.append(o)
        for r in reads:
            r.readers.append(o)
        for w in writes:
            w.last_w = o
            w.readers = []
        return o

    DMA_POOL = {"sW": 2, "sWb": 2, "sIO": 4, "sSp": 4, "sGd": 4}

    def finalize(self):
        rrobin = {}
        dma_counts = {}
        prev_on_sem = {}
        for o in self.ops:
            if o.dma_sem is not None:
                ch = o.dma_sem
                n = self.DMA_POOL[ch]
                idx = rrobin.get(ch, 0)
                rrobin[ch] = (idx + 1) % n
                ps = f"{ch}{idx}"
                dma_counts[ps] = dma_counts.get(ps, 0) + 1
                o.pub = (ps, dma_counts[ps] * 16)
                if ps in prev_on_sem:
                    o.deps.append(prev_on_sem[ps])
                prev_on_sem[ps] = o
        need = set()
        for o in self.ops:
            for d in o.deps:
                if d.dma_sem is None:
                    need.add(id(d))
        pub_idx = {e: 0 for e in ENGS}
        for o in self.ops:
            if o.dma_sem is None and id(o) in need:
                pub_idx[o.eng] += 1
                o.pub = (ESEM[o.eng], pub_idx[o.eng])
        hw = {e: {} for e in ENGS}
        for o in self.ops:
            want = {}
            for d in o.deps:
                if d.dma_sem is not None:
                    s, v = d.pub
                    want[s] = max(want.get(s, 0), v)
                elif d.eng != o.eng:
                    if d.pub is None:
                        raise RuntimeError(f"dep of {o.tag} not published: {d.tag}")
                    s, v = d.pub
                    want[s] = max(want.get(s, 0), v)
                elif d.pub is not None:
                    s, v = d.pub
                    want[s] = max(want.get(s, 0), v)
            o.waits = []
            for s, v in want.items():
                if hw[o.eng].get(s, 0) < v:
                    o.waits.append((s, v))
                    hw[o.eng][s] = v

    def emit(self, c):
        self.finalize()
        nc = self.nc
        per_eng = self.per_eng

        def body(eng_name):
            def run(h):
                for o in per_eng[eng_name]:
                    for s, v in o.waits:
                        h.wait_ge(c.sem[s], v)
                    ins = o.fn(h)
                    if o.pub is not None:
                        if ins is None:
                            raise RuntimeError(f"op {o.tag} published but emitted None")
                        ins.then_inc(c.sem[o.pub[0]], 16 if o.dma_sem else 1)
            return run

        with nc.Block() as blk:
            blk.tensor(body("te"))
            blk.vector(body("ve"))
            blk.scalar(body("se"))
            blk.gpsimd(body("ge"))
            blk.sync(body("sy"))


class Ctx:
    def __init__(self, nc):
        self.nc = nc
        self.sem = {}

    def new_sem(self, name):
        self.sem[name] = self.nc.alloc_semaphore(name=name)



def build_program(T, stages=(0, 1, 2, 3, 4, 5), debug_taps=()):
    TT = min(512, T)
    NTT = T // TT
    ETT = min(1024, T)
    NCH = T // CH

    nc = bass.Bass("TRN2", target_bir_lowering=False)
    c = Ctx(nc)
    semlist = ["sTE", "sV", "sS", "sG", "sSY", "sCC"]
    for base, n in [("sW", 2), ("sWb", 2), ("sIO", 4), ("sSp", 4), ("sGd", 4)]:
        for i in range(n):
            semlist.append(f"{base}{i}")
    for s in semlist:
        c.new_sem(s)

    din = {}

    def inp(name, shape, dtype=F32):
        din[name] = nc.dram_tensor(name, list(shape), dtype, kind="ExternalInput")
        return din[name]

    inp("x_in", [T, D])
    out_t = nc.dram_tensor("out", [T, D], F32, kind="ExternalOutput")

    inp("wg_t", [D, 3072], I32)
    inp("wu_t", [D, 3072], I32)
    inp("wd_t", [3072, D], I32)

    inp("l0_ln_w", [128, DC])
    inp("l0_ln_b", [128, DC])
    inp("l0_W_in", [D, 4192])
    inp("l0_dt_bias", [32, 1])
    inp("l0_A", [32, 1])
    inp("l0_D", [128, 16])
    inp("l0_rms_w", [128, 16])
    inp("l0_W_out", [2048, D])
    inp("rope_cos", [8, T])
    inp("rope_sin", [8, T])

    inp("l1_W_in", [D, 4096])
    inp("l1_conv_w", [128, 64])
    inp("l1_conv_b", [128, 16])
    inp("l1_W_xproj", [2048, 33])
    inp("l1_D", [128, 16])
    inp("l1_W_out", [2048, D])

    inp("e_W1", [9, D, 4096])
    inp("e_W2", [9, 4096, D])
    inp("e_b1", [9, 128, 32])
    inp("e_g1", [9, 128, 32])
    inp("e_be1", [9, 128, 32])
    inp("e_b2", [9, 128, 8])
    inp("e_g2", [9, 128, 8])
    inp("e_be2", [9, 128, 8])
    inp("e_pb", [9, 128, 8])
    inp("e_ctx", [9, 128, 8])
    inp("e_th1", [9, 4])
    inp("e_th2", [9, 4])
    inp("e_wpath", [128, 8])
    inp("e_wteam", [128, 8])
    inp("e_bp", [9, 1])
    inp("e_ovp", [9, 1])
    inp("e_bt", [3, 1])
    inp("e_ovt", [3, 1])

    inp("ones_col", [128, 1])
    inp("ones_col_bf", [128, 1], BF16)
    inp("ones_row", [1, 128])
    inp("ones_row_bf", [1, 128], BF16)
    inp("ident_bf", [128, 128], BF16)
    inp("ident_f", [128, 128])
    inp("mask_d", [64, 512])
    inp("e2a", [2, 128], BF16)
    inp("e2x", [4, 8])
    inp("altsign", [8, 1])
    inp("pswap", [8, 128])
    inp("ipad", [128, 128], BF16)
    inp("ones_T", [1, 4096])
    inp("negid", [32, 2048])
    inp("tril8", [8, 8])
    inp("lt_col", [8, 1])
    inp("a2b_bf", [32, 128], BF16)
    inp("indj2", [32, 16])
    inp("inds_c", [32, 512])
    inp("ones32", [32, 4096])
    inp("zeros32", [32, 2048])
    inp("a2b_f", [32, 128])
    inp("mask_d2", [64, 512])
    inp("ident64f", [64, 128])
    inp("ident64b", [64, 64], BF16)
    inp("prevsel", [128, 8])
    inp("indj", [32, 16])
    inp("core_lt", [32, 8])
    inp("prev_oh", [1, 8])
    inp("eps_tab", [128, 3])

    dram = {}
    for nm, shape, dt in [
        ("sp_h0", [DC * 128, T], BF16),
        ("sp_yl", [16 * 128, T], BF16),
        ("sp_lnx4", [DC * 128, T], BF16),
        ("sp_xconv", [16 * 128, T], BF16),
        ("sp_tf", [3, DC * 128, TT], BF16),
        ("agA_in", [1, 2048 * 32 + 32], F32),
        ("agA_out", [NCORES, 2048 * 32 + 32], F32),
        ("agB_in", [1, 2048 * 3], F32),
        ("agB_out", [NCORES, 2048 * 3], F32),
        ("agC_in", [1, 2048 * 16], F32),
        ("agC_out", [NCORES, 2048 * 16], F32),
    ]:
        dram[nm] = nc.dram_tensor(nm, shape, dt)

    taps = {}
    for k in debug_taps:
        taps[k] = nc.dram_tensor(f"tap{k}", [T, D], F32, kind="ExternalOutput")

    t_ = nc.alloc_sbuf_tensor
    X = [t_(f"x{i}", [128, T], F32) for i in range(DC)]
    WS = t_("WS", [128, 8192], BF16)
    HB = t_("HB", [128, 32768], BF16)
    FA = t_("FA", [128, 4096], F32)
    FB = t_("FB", [128, 2048], F32)
    SM = t_("SM", [128, 2048], F32)
    SMb = t_("SMb", [128, 6656], BF16)

    CTb = t_("CTb", [128, 1024], BF16)
    CTf = t_("CTf", [128, 2176], F32)
    CT2 = t_("CT2", [128, 1536], F32)
    PA = nc.alloc_psum_tensor("pA", [128, 1024], F32)
    PB = nc.alloc_psum_tensor("pB", [128, 1024], F32)
    P47 = [nc.alloc_psum_tensor(f"pb{i}", [128, 512], F32) for i in range(4)]
    P = [PA[:, 0:512], PA[:, 512:1024], PB[:, 0:512], PB[:, 512:1024],
         P47[0][:, :], P47[1][:, :], P47[2][:, :], P47[3][:, :]]

    nc.const_aps.register(F32, EPS5, SM[:, 2040:2041].opt())
    nc.const_aps.register(F32, EPS6, SM[:, 2041:2042].opt())
    nc.const_aps.register(F32, float(np.pi / 2), SM[:, 2042:2043].opt())

    meta = dict(din=din, dram=dram, X=X, WS=WS, HB=HB, FA=FA, FB=FB, SM=SM,
                SMb=SMb, P=P, PA=PA, PB=PB, CTb=CTb, CTf=CTf, CT2=CT2, taps=taps, T=T, TT=TT, NTT=NTT, ETT=ETT,
                NCH=NCH, out=out_t, stages=stages)

    cst = {
        "ident": CTb[:, 0:128],
        "e2a": CTb[0:2, 128:256],
        "ipad": CTb[:, 512:640],
        "ones_col_bf": CTb[:, 256:257],
        "ones_row_bf": CTb[0:1, 384:512],
        "mask_d": CTf[0:64, 0:512],
        "ones_col": CTf[:, 512:513],
        "ones_row": CTf[0:1, 513 + 127:513 + 255][:, 0:128] if False else CTf[0:1, 640:768],
        "indj": CTf[0:32, 768:784],
        "core_lt": CTf[0:32, 784:792],
        "prev_oh": CTf[0:1, 792:800],
        "l0_ln_w": CTf[:, 800:808],
        "l0_ln_b": CTf[:, 808:816],

        "l0_D": CTf[:, 832:848],
        "l0_rms_w": CTf[:, 848:864],
        "l1_conv_w": CTf[:, 864:928],
        "l1_conv_b": CTf[:, 928:944],
        "l1_D": CTf[:, 944:960],
        "l1_W_xproj_sb": None,
        "e_b1": CT2[:, 0:288], "e_g1": CT2[:, 288:576], "e_be1": CT2[:, 576:864],
        "e_b2": CT2[:, 864:936], "e_g2": CT2[:, 936:1008], "e_be2": CT2[:, 1008:1080],
        "e_pb": CT2[:, 1080:1152], "e_ctx": CT2[:, 1152:1224],
        "e_wpath": CT2[:, 1224:1232], "e_wteam": CT2[:, 1232:1240],
        "e_bp": CT2[0:9, 1240:1241], "e_ovp": CT2[0:9, 1241:1242],
        "e_bt": CT2[0:3, 1242:1243], "e_ovt": CT2[0:3, 1243:1244],
        "e_th1": CT2[0:9, 1248:1252], "e_th2": CT2[0:9, 1252:1256],
        "e_cs1": CT2[0:9, 1256:1264],
        "e_cs2": CT2[0:9, 1264:1272],
        "ident_f": CTf[:, 1024:1152],
        "e2x": CTf[0:4, 1152:1160],
        "altsign": CTf[0:8, 1160:1161],
        "pswap": CTf[0:8, 1280:1408],
        "tril8": CTf[0:8, 960:968],
        "lt_col": CTf[0:8, 968:969],
        "indj64": CTf[64:96, 969:985],
        "dtb64": CTf[64:96, 816:817],
        "A64": CTf[64:96, 817:818],
        "a2b0": CTb[0:32, 640:768],
        "a2b64": CTb[64:96, 768:896],
        "inds": CTf[0:32, 1664:2176],
        "mask_d64": CTf[64:128, 0:512],
        "a2bf64": CTf[64:96, 1408:1536],
        "ident_f64": CTf[64:128, 1536:1664],
        "ident_b64": CTb[64:128, 896:960],
        "prevsel": CTf[:, 985:993],
    }
    meta_cst = cst
    sch = Sched(nc)
    meta["cst"] = cst
    rs = build_resources(sch, meta)
    emit_init(sch, rs, meta)
    for si, st in enumerate(stages):
        if si > 0:
            stage_fence(rs)
        if st in (0, 2):
            emit_ffn0(sch, rs, meta, st)
        elif st == 1:
            emit_l0_attn(sch, rs, meta)
        elif st == 3 or st == 5:
            emit_eidos(sch, rs, meta, st)
        elif st == 4:
            emit_l1_attn(sch, rs, meta)
        if st in meta["taps"]:
            emit_tap(sch, rs, meta, st)
    emit_out(sch, rs, meta)
    sch.emit(c)
    return nc, meta


def stage_fence(rs):
    groups = (("WS0", "WS1", "WS2a", "WS2b"),
              ("HBa", "HBb", "HBc") + tuple(f"HR{k}" for k in range(32)),
              ("SM", "RV0", "RV1"), ("SMb", "SQ1"))
    for grp in groups:
        pool = []
        for n in grp:
            r = rs[n]
            pool.extend(r.readers)
            if r.last_w is not None:
                pool.append(r.last_w)
        for n in grp:
            r = rs[n]
            seen = {id(o) for o in r.readers}
            for o in pool:
                if id(o) not in seen:
                    r.readers.append(o)
                    seen.add(id(o))


def build_resources(sch, meta):
    rs = {}
    for i in range(DC):
        rs[f"X{i}"] = sch.res(f"X{i}")
    for i in range(8):
        rs[f"P{i}"] = sch.res(f"P{i}")
    rs["Pt"] = sch.res("Pt")
    for nm in ["WS0", "WS1", "WS2a", "WS2b", "HBa", "HBb", "HBc", "FA", "FB",
               "SM", "SMb", "dram", "state", "G", "BC", "consts",
               "RV0", "RV1", "SQ1"]:
        rs[nm] = sch.res(nm)
    for k in range(32):
        rs[f"HR{k}"] = sch.res(f"HR{k}")
    rs["HB"] = rs["HBb"]
    rs["rows"] = rs["FB"]
    return rs



def emit_init(sch, rs, meta):
    din, X, P, FA, T = meta["din"], meta["X"], meta["P"], meta["FA"], meta["T"]
    nt = T // 128
    SM = meta["SM"]
    cst = meta["cst"]
    sch.add("sy", lambda h: h.dma_start(out=SM[:, 2040:2043], in_=din["eps_tab"].ap()),
            reads=(), writes=(rs["consts"],), dma="sIO", tag="ldeps")
    cload = [("ident", "ident_bf"), ("e2a", "e2a"), ("ones_col_bf", "ones_col_bf"),
             ("ones_row_bf", "ones_row_bf"), ("mask_d", "mask_d"),
             ("ones_col", "ones_col"), ("ones_row", "ones_row"), ("indj", "indj"),
             ("core_lt", "core_lt"), ("prev_oh", "prev_oh"),
             ("l0_ln_w", "l0_ln_w"), ("l0_ln_b", "l0_ln_b"),
             ("dtb64", "l0_dt_bias"), ("A64", "l0_A"), ("l0_D", "l0_D"),
             ("l0_rms_w", "l0_rms_w"), ("l1_conv_w", "l1_conv_w"),
             ("l1_conv_b", "l1_conv_b"), ("l1_D", "l1_D"),
             ("e_b1", "e_b1"), ("e_g1", "e_g1"), ("e_be1", "e_be1"),
             ("e_b2", "e_b2"), ("e_g2", "e_g2"), ("e_be2", "e_be2"),
             ("e_pb", "e_pb"), ("e_ctx", "e_ctx"),
             ("e_wpath", "e_wpath"), ("e_wteam", "e_wteam"),
             ("e_bp", "e_bp"), ("e_ovp", "e_ovp"), ("e_bt", "e_bt"),
             ("e_ovt", "e_ovt"), ("e_th1", "e_th1"), ("e_th2", "e_th2"),
             ("ident_f", "ident_f"), ("e2x", "e2x"), ("altsign", "altsign"),
             ("pswap", "pswap"), ("ipad", "ipad"), ("tril8", "tril8"),
             ("lt_col", "lt_col"), ("indj64", "indj2"), ("a2b0", "a2b_bf"),
             ("a2b64", "a2b_bf"), ("inds", "inds_c"), ("mask_d64", "mask_d2"),
             ("a2bf64", "a2b_f"), ("ident_f64", "ident64f"),
             ("ident_b64", "ident64b"), ("prevsel", "prevsel")]
    for cname, iname in cload:
        dst = cst[cname]
        ish = din[iname].shape
        if len(ish) == 3:
            sch.add("sy", lambda h, dst=dst, iname=iname, ish=ish: h.dma_start(
                out=dst.rearrange("p (a b) -> p a b", a=ish[0]),
                in_=din[iname].ap().rearrange("a p b -> p a b")),
                reads=(), writes=(rs["consts"],), dma="sIO", tag=f"ldc_{cname}")
        else:
            sch.add("sy", lambda h, dst=dst, iname=iname: h.dma_start(
                out=dst, in_=din[iname].ap()),
                reads=(), writes=(rs["consts"],), dma="sIO", tag=f"ldc_{cname}")
    for e in ENGS:
        sch.add(e, lambda h: h.nop(), reads=(rs["consts"],), writes=(),
                tag=f"cfence_{e}")
    stage = rs["FA"]
    for it in range(nt):
        sl = (it % 4) * 1024
        ld = sch.add("sy", lambda h, it=it, sl=sl: h.dma_start(
            out=FA[:, sl:sl + 1024], in_=din["x_in"][it * 128:(it + 1) * 128, :]),
            reads=(), writes=(stage,), dma="sIO", tag=f"ldx{it}")
        for d in range(DC):
            pb = rs[f"P{d % 2}"]
            tr = sch.add("te", lambda h, it=it, sl=sl, d=d: h.transpose(
                P[d % 2][:, (it % 2) * 128:(it % 2) * 128 + 128].bitcast(F32),
                FA[:, sl + d * 128:sl + (d + 1) * 128],
                meta["cst"]["ident_f"]),
                reads=(stage,), writes=(pb,), tag=f"trx{it}_{d}")
            sch.add("ve", lambda h, it=it, d=d: h.tensor_copy(
                X[d][:, it * 128:(it + 1) * 128],
                P[d % 2][:, (it % 2) * 128:(it % 2) * 128 + 128].bitcast(F32)),
                reads=(pb,), writes=(rs[f"X{d}"],), tag=f"evx{it}_{d}")


def emit_tap(sch, rs, meta, st):
    din, X, FA, T = meta["din"], meta["X"], meta["FA"], meta["T"]
    P, tap = meta["P"], meta["taps"][st]
    nt = T // 128
    for it in range(nt):
        sl = (it % 4) * 1024
        stg = rs["FA"]
        for d in range(DC):
            pb = rs[f"P{d % 2}"]
            sch.add("te", lambda h, it=it, d=d: h.transpose(
                P[d % 2][:, (it % 2) * 128:(it % 2) * 128 + 128].bitcast(F32),
                X[d][:, it * 128:(it + 1) * 128], meta["cst"]["ident_f"]),
                reads=(rs[f"X{d}"],), writes=(pb,), tag=f"ttr{st}_{it}_{d}")
            sch.add("ve", lambda h, it=it, sl=sl, d=d: h.tensor_copy(
                FA[:, sl + d * 128:sl + (d + 1) * 128],
                P[d % 2][:, (it % 2) * 128:(it % 2) * 128 + 128].bitcast(F32)),
                reads=(pb,), writes=(stg,), tag=f"tev{st}_{it}_{d}")
        sch.add("sy", lambda h, it=it, sl=sl: h.dma_start(
            out=tap[it * 128:(it + 1) * 128, :], in_=FA[:, sl:sl + 1024]),
            reads=(stg,), writes=(), dma="sIO", tag=f"tdm{st}_{it}")


def emit_out(sch, rs, meta):
    din, X, FA, T, P = meta["din"], meta["X"], meta["FA"], meta["T"], meta["P"]
    nt = T // 128
    last = None
    for it in range(nt):
        sl = (it % 4) * 1024
        stg = rs["FA"]
        for d in range(DC):
            pb = rs[f"P{d % 2}"]
            sch.add("te", lambda h, it=it, d=d: h.transpose(
                P[d % 2][:, (it % 2) * 128:(it % 2) * 128 + 128].bitcast(F32),
                X[d][:, it * 128:(it + 1) * 128], meta["cst"]["ident_f"]),
                reads=(rs[f"X{d}"],), writes=(pb,), tag=f"otr{it}_{d}")
            sch.add("ve", lambda h, it=it, sl=sl, d=d: h.tensor_copy(
                FA[:, sl + d * 128:sl + (d + 1) * 128],
                P[d % 2][:, (it % 2) * 128:(it % 2) * 128 + 128].bitcast(F32)),
                reads=(pb,), writes=(stg,), tag=f"oev{it}_{d}")
        last = sch.add("sy", lambda h, it=it, sl=sl: h.dma_start(
            out=meta["out"][it * 128:(it + 1) * 128, :], in_=FA[:, sl:sl + 1024]),
            reads=(stg,), writes=(), dma="sIO", tag=f"odm{it}")
    fin = sch.res("fin")
    sch.add("sy", lambda h: h.nop(), reads=(), writes=(fin,), tag="outwait")
    sch.per_eng["sy"][-1].deps.append(last)




def hap(handle, offset, dims):
    return AP(handle, offset, [list(d) for d in dims])


def emit_ln(sch, rs, meta, t0, TT, dst, eps, tag="ln",
            sq_off=24576, sq_res="HBc", dst_res="HBa", f32_scr=0,
            resid_x=False, src=None, src_res=None, affine=None,
            sq_res_fn=None):
    X, P, FB, HB, FA, din = (meta["X"], meta["P"], meta["FB"], meta["HB"],
                             meta["FA"], meta["din"])
    cst = meta["cst"]
    p6, p7, rrows = rs["P6"], rs["P7"], rs["rows"]
    if sq_res_fn is None:
        sq_res_fn = lambda d: rs[sq_res]
    dstr = rs[dst_res]
    if src is None:
        xa = lambda d: X[d][:, t0:t0 + TT]
        srcres = lambda d: rs[f"X{d}"]
    else:
        xa = src
        srcres = lambda d: rs[src_res]

    def sq(d):
        return HB[:, sq_off + d * TT: sq_off + (d + 1) * TT]

    for d in range(DC):
        sch.add("ve", lambda h, d=d: h.tensor_tensor(
            sq(d), xa(d), xa(d), ALU.mult),
            reads=(srcres(d),), writes=(sq_res_fn(d),), tag=f"{tag}sq{d}")
    onesv = cst["ones_col"] if src is None else cst["ones_col_bf"]
    for d in range(DC):
        sch.add("te", lambda h, d=d: h.matmul(
            P[6][0:1, 0:TT], onesv, xa(d),
            start=(d == 0), stop=(d == DC - 1)),
            reads=(srcres(d), rs["consts"]), writes=(p6,), tag=f"{tag}mmx{d}")
    for d in range(DC):
        sch.add("te", lambda h, d=d: h.matmul(
            P[7][0:1, 0:TT], cst["ones_col_bf"], sq(d),
            start=(d == 0), stop=(d == DC - 1)),
            reads=(sq_res_fn(d), rs["consts"]), writes=(p7,), tag=f"{tag}mmq{d}")
    mu = FB[0:1, 0:TT]
    m2 = FB[0:1, 512:512 + TT]
    var = FB[0:1, 1024:1024 + TT]
    nmr = FB[0:1, 1536:1536 + TT]
    sch.add("ve", lambda h: h.tensor_scalar(mu, P[6][0:1, 0:TT], 1.0 / D, None, ALU.mult),
            reads=(p6,), writes=(rrows,), tag=f"{tag}mu")
    sch.add("ve", lambda h: h.tensor_scalar(m2, P[7][0:1, 0:TT], 1.0 / D, None, ALU.mult),
            reads=(p7,), writes=(rrows,), tag=f"{tag}m2")
    sch.add("ve", lambda h: h.tensor_tensor(var, mu, mu, ALU.mult),
            reads=(rrows,), writes=(rrows,), tag=f"{tag}musq")
    sch.add("ve", lambda h: h.tensor_sub(m2, m2, var),
            reads=(rrows,), writes=(rrows,), tag=f"{tag}var")
    sch.add("se", lambda h: h.activation(var, m2, AF.Ln, bias=eps, scale=1.0),
            reads=(rrows, rs["consts"]), writes=(rrows,), tag=f"{tag}lnv")
    sch.add("se", lambda h: h.activation(m2, var, AF.Exp, bias=0.0, scale=-0.5),
            reads=(rrows,), writes=(rrows,), tag=f"{tag}rinv")
    rinv = m2
    sch.add("ve", lambda h: h.scalar_tensor_tensor(nmr, mu, -1.0, rinv,
                                                   ALU.mult, ALU.mult),
            reads=(rrows,), writes=(rrows,), tag=f"{tag}nmr")
    sch.add("te", lambda h: h.matmul(P[6][:, 0:TT], cst["ones_row"], rinv,
                                     start=True, stop=True),
            reads=(rrows, rs["consts"]), writes=(p6,), tag=f"{tag}bc1")
    sch.add("te", lambda h: h.matmul(P[7][:, 0:TT], cst["ones_row"], nmr,
                                     start=True, stop=True),
            reads=(rrows, rs["consts"]), writes=(p7,), tag=f"{tag}bc2")
    for d in range(DC):
        tsl = FA[:, f32_scr + (d % 2) * TT:f32_scr + (d % 2) * TT + TT]
        lsl = FA[:, f32_scr + (2 + d % 2) * TT:f32_scr + (2 + d % 2) * TT + TT] \
            if (resid_x or affine is None) and False else None
        sch.add("ve", lambda h, d=d, tsl=tsl: h.scalar_tensor_tensor(
            tsl, xa(d), 1.0, P[6][:, 0:TT], ALU.mult, ALU.mult),
            reads=(srcres(d), p6), writes=(rs["FA"],), tag=f"{tag}ap1_{d}")
        if affine is None and not resid_x:
            sch.add("ve", lambda h, d=d, tsl=tsl: h.tensor_tensor(
                dst(d), tsl, P[7][:, 0:TT], ALU.add),
                reads=(rs["FA"], p7), writes=(dstr,), tag=f"{tag}ap2_{d}")
        else:
            osl = FA[:, f32_scr + 2 * TT + (d % 2) * TT:
                     f32_scr + 2 * TT + (d % 2) * TT + TT]
            sch.add("ve", lambda h, d=d, tsl=tsl, osl=osl: h.tensor_tensor(
                osl, tsl, P[7][:, 0:TT], ALU.add),
                reads=(rs["FA"], p7), writes=(rs["FA"],), tag=f"{tag}ap2_{d}")
            if resid_x:
                sch.add("ve", lambda h, d=d, osl=osl: h.tensor_add(
                    X[d][:, t0:t0 + TT], X[d][:, t0:t0 + TT], osl),
                    reads=(rs["FA"],), writes=(rs[f"X{d}"],), tag=f"{tag}rx{d}")
            if affine is not None:
                ga, be = affine
                sch.add("se", lambda h, d=d, osl=osl, ga=ga, be=be: h.activation(
                    dst(d), osl, AF.Identity, bias=be[:, d:d + 1],
                    scale=ga[:, d:d + 1]),
                    reads=(rs["FA"], rs["consts"]), writes=(dstr,),
                    tag=f"{tag}af{d}")
            else:
                sch.add("ve", lambda h, d=d, osl=osl: h.tensor_copy(dst(d), osl),
                        reads=(rs["FA"],), writes=(dstr,), tag=f"{tag}cb{d}")



def emit_ffn0(sch, rs, meta, st):
    nc = sch.nc
    X, P, HB, WS, SMb, din = (meta["X"], meta["P"], meta["HB"], meta["WS"],
                              meta["SMb"], meta["din"])
    T, TT, NTT = meta["T"], meta["TT"], meta["NTT"]
    H3 = 3072

    def lnb(d):
        return HB[:, d * TT:(d + 1) * TT]

    def hb(k):
        return HB[:, 4096 + k * TT:4096 + (k + 1) * TT]

    wsq = [rs["WS0"], rs["WS1"]]
    hbr = rs["HBb"]
    smr = rs["SMb"]

    for tt in range(NTT):
        t0 = tt * TT
        emit_ln(sch, rs, meta, t0, TT, lnb, EPS5, tag=f"f{st}ln{tt}")
        lnr = rs["HBa"]
        for hbk in range(12):
            par = hbk % 2
            c0 = hbk * 256
            wof = par * 4096
            ld_g = sch.add("ge", lambda h, c0=c0, wof=wof: h.dma_start(
                out=WS[:, wof:wof + 2048].rearrange("p (k j) -> p k j", k=8),
                in_=hap(din["wg_t"], c0, [[H3, 128], [128 * H3, 8], [1, 256]])),
                reads=(), writes=(wsq[par],), dma=("sW" if par == 0 else "sWb"),
                tag=f"f{st}wg{tt}_{hbk}")
            ld_u = sch.add("ge", lambda h, c0=c0, wof=wof: h.dma_start(
                out=WS[:, wof + 2048:wof + 4096].rearrange("p (k j) -> p k j", k=8),
                in_=hap(din["wu_t"], c0, [[H3, 128], [128 * H3, 8], [1, 256]])),
                reads=(), writes=(wsq[par],), dma=("sW" if par == 0 else "sWb"),
                tag=f"f{st}wu{tt}_{hbk}")
            for hc in range(2):
                pg, pu = rs[f"P{2 * par}"], rs[f"P{2 * par + 1}"]
                for k in range(DC):
                    sch.add("te", lambda h, wof=wof, k=k, hc=hc, par=par: h.matmul(
                        P[2 * par][:, 0:TT],
                        WS[:, wof + k * 256 + hc * 128: wof + k * 256 + hc * 128 + 128],
                        lnb(k), start=(k == 0), stop=(k == DC - 1)),
                        reads=(wsq[par], lnr), writes=(pg,), tag=f"f{st}g{tt}_{hbk}_{hc}_{k}")
                for k in range(DC):
                    sch.add("te", lambda h, wof=wof, k=k, hc=hc, par=par: h.matmul(
                        P[2 * par + 1][:, 0:TT],
                        WS[:, wof + 2048 + k * 256 + hc * 128: wof + 2048 + k * 256 + hc * 128 + 128],
                        lnb(k), start=(k == 0), stop=(k == DC - 1)),
                        reads=(wsq[par], lnr), writes=(pu,), tag=f"f{st}u{tt}_{hbk}_{hc}_{k}")
                hidx = hbk * 2 + hc
                sch.add("se", lambda h, par=par: h.activation(
                    SMb[:, 0:TT], P[2 * par][:, 0:TT], AF.Silu, bias=0.0, scale=RT127),
                    reads=(pg,), writes=(smr,), tag=f"f{st}sil{tt}_{hidx}")
                sch.add("ve", lambda h, par=par, hidx=hidx: h.scalar_tensor_tensor(
                    hb(hidx), P[2 * par + 1][:, 0:TT], RT127, SMb[:, 0:TT],
                    ALU.mult, ALU.mult),
                    reads=(pu, smr), writes=(hbr,), tag=f"f{st}h{tt}_{hidx}")
        for k in range(24):
            par = k % 2
            wof = par * 1024
            sch.add("ge", lambda h, k=k, wof=wof: h.dma_start(
                out=WS[:, wof:wof + 1024],
                in_=din["wd_t"][k * 128:(k + 1) * 128, :]),
                reads=(), writes=(wsq[0], wsq[1]),
                dma=("sW" if par == 0 else "sWb"),
                tag=f"f{st}wd{tt}_{k}")
            for do in range(8):
                sch.add("te", lambda h, k=k, do=do, wof=wof: h.matmul(
                    P[do][:, 0:TT], WS[:, wof + do * 128:wof + (do + 1) * 128],
                    hb(k), start=(k == 0), stop=(k == 23)),
                    reads=(wsq[0], wsq[1], hbr), writes=(rs[f"P{do}"],),
                    tag=f"f{st}d{tt}_{k}_{do}")
        for do in range(8):
            sch.add("ve", lambda h, do=do, t0=t0: h.scalar_tensor_tensor(
                X[do][:, t0:t0 + TT], P[do][:, 0:TT], 0.5 * RT127,
                X[do][:, t0:t0 + TT], ALU.mult, ALU.add),
                reads=(rs[f"P{do}"],), writes=(rs[f"X{do}"],),
                tag=f"f{st}res{tt}_{do}")


def emit_l0_attn(sch, rs, meta):
    nc = sch.nc
    din, X, P, HB, WS, SM, SMb, FA, FB, cst = (
        meta["din"], meta["X"], meta["P"], meta["HB"], meta["WS"], meta["SM"],
        meta["SMb"], meta["FA"], meta["FB"], meta["cst"])
    PA, PB = meta["PA"], meta["PB"]
    T, TT, NTT = meta["T"], meta["TT"], meta["NTT"]
    CPT = TT // CH
    NCH = meta["NCH"]
    dram = meta["dram"]
    csts = rs["consts"]
    wsq = [rs["WS0"], rs["WS1"]]
    hba, hbb, hbc = rs["HBa"], rs["HBb"], rs["HBc"]
    far, fbr, smr, smbr = rs["FA"], rs["FB"], rs["SM"], rs["SMb"]
    star, gr = rs["state"], rs["G"]
    pan, pbn = sch.res("l0pa"), sch.res("l0pb")
    FApitch = 4096
    FBpitch = 2048
    SMbpitch = 6656

    def hT(d):
        return HB[:, d * TT:(d + 1) * TT]

    def xsT(j):
        return HB[:, 4096 + j * TT:4096 + (j + 1) * TT]

    def xcT(i):
        return HB[:, 12288 + i * 2048:12288 + (i + 1) * 2048]

    def ylT(j):
        return HB[:, 20480 + j * TT:20480 + (j + 1) * TT]

    MTreg = lambda c0, n: HB[:, 28672 + c0:28672 + c0 + n]
    SPT = HB[:, 30720:32768]
    Cbf = lambda c0, n: SMb[64:96, 2176 + c0:2176 + c0 + n]
    Bbf = lambda c0, n: SMb[64:96, 2176 + T + c0:2176 + T + c0 + n]
    dtrow = SM[64:96, 0:TT]
    stbuf = [SM[:, 512:1024], SM[:, 1082:1594]]
    dtTs = SM[:, 1024:1056]
    dtotc = SM[:, 1056:1072]
    carry = SM[64:96, 1080:1081]
    Acol = SM[64:96, 1081:1082]
    gprev = SM[64:96, 1596:1597]
    dttc = SM[64:96, 1597:1598]
    arow = FA[64:96, 2048:2048 + TT]
    eAr = SM[64:96, 1598:1662]
    dcyr = SM[64:96, 1662:1726]

    def Gsl(c0, n):
        return FA[64:96, c0:c0 + n]

    sch.add("sy", lambda h: h.dma_start(out=FA[96:128, 0:T],
                                        in_=din["ones32"].ap()[:, 0:T]),
            reads=(), writes=(far,), dma="sIO", tag="l0ones")
    sch.add("sy", lambda h: h.dma_start(out=FB[64:96, 0:2048],
                                        in_=din["negid"].ap()),
            reads=(), writes=(fbr,), dma="sIO", tag="l0negid")
    sch.add("sy", lambda h: h.dma_start(out=FB[96:128, 0:2048],
                                        in_=din["zeros32"].ap()),
            reads=(), writes=(fbr,), dma="sIO", tag="l0z32")
    sch.add("se", lambda h: h.activation(Acol, cst["A64"], AF.Exp,
                                         bias=0.0, scale=1.0),
            reads=(csts,), writes=(smr,), tag="l0A1")
    sch.add("ve", lambda h: h.tensor_scalar(Acol, Acol, -1.0, None, ALU.mult),
            reads=(smr,), writes=(smr,), tag="l0A2")
    sch.add("ve", lambda h: h.memset(stbuf[0], 0.0), reads=(), writes=(star,),
            tag="l0st0")
    sch.add("ve", lambda h: h.memset(carry, 0.0), reads=(), writes=(smr,),
            tag="l0cr0")
    sch.add("ve", lambda h: h.memset(gprev, 0.0), reads=(), writes=(smr,),
            tag="l0gp0")

    gc = 0
    for tt in range(NTT):
        t0 = tt * TT
        sch.add("sy", lambda h, t0=t0: h.dma_start(
            out=FB[0:8, 0:TT], in_=din["rope_cos"].ap()[:, t0:t0 + TT]),
            reads=(), writes=(fbr,), dma="sIO", tag=f"l0cos{tt}")
        sch.add("sy", lambda h, t0=t0: h.dma_start(
            out=FB[0:8, TT:2 * TT], in_=din["rope_sin"].ap()[:, t0:t0 + TT]),
            reads=(), writes=(fbr,), dma="sIO", tag=f"l0sin{tt}")
        emit_ln(sch, rs, meta, t0, TT, hT, EPS5, tag=f"l0A_{tt}",
                sq_off=24576, sq_res="HBc", dst_res="HBa", f32_scr=2048,
                resid_x=True)
        emit_ln(sch, rs, meta, t0, TT, hT, EPS6, tag=f"l0B_{tt}",
                sq_off=24576, sq_res="HBc", dst_res="HBa", f32_scr=2048,
                src=hT, src_res="HBa",
                affine=(cst["l0_ln_w"], cst["l0_ln_b"]))
        sch.add("sy", lambda h, t0=t0: h.dma_start(
            out=dram["sp_h0"][:, t0:t0 + TT].rearrange("(d q) t -> q d t", d=DC),
            in_=HB[:, 0:DC * TT].rearrange("q (d t) -> q d t", d=DC)),
            reads=(hba,), writes=(rs["dram"],), dma="sSp", tag=f"l0hsp{tt}")
        for xh in range(2):
            for k in range(DC):
                par = k % 2
                wof = par * 1024
                sch.add("ge", lambda h, k=k, xh=xh, wof=wof: h.dma_start(
                    out=WS[:, wof:wof + 1024],
                    in_=hap(din["l0_W_in"], k * 128 * 4192 + 2048 + xh * 1024,
                            [[4192, 128], [1, 1024]])),
                    reads=(), writes=(wsq[par],),
                    dma=("sW" if par == 0 else "sWb"), tag=f"l0wx{tt}_{xh}_{k}")
                for f in range(8):
                    sch.add("te", lambda h, k=k, f=f, wof=wof, par=par: h.matmul(
                        P[f][:, 0:TT], WS[:, wof + f * 128:wof + (f + 1) * 128],
                        hT(k), start=(k == 0), stop=(k == DC - 1)),
                        reads=(wsq[par], hba), writes=(rs[f"P{f}"],),
                        tag=f"l0mx{tt}_{xh}_{k}_{f}")
            for f in range(8):
                j = xh * 8 + f
                sch.add("se", lambda h, f=f, j=j: h.activation(
                    xsT(j), P[f][:, 0:TT], AF.Silu, bias=0.0, scale=1.0),
                    reads=(rs[f"P{f}"],), writes=(hbb,), tag=f"l0xs{tt}_{j}")
        grps = [(0, 8, 0, 0), (8, 8, 1, 0), (16, 16, 2, 32),
                (32, 8, 3, 0), (40, 8, 5, 0), (48, 16, 6, 32),
                (64, 32, 4, 64)]
        for k in range(DC):
            par = k % 2
            wof = 2048 + par * 96
            sch.add("ge", lambda h, k=k, wof=wof: h.dma_start(
                out=WS[:, wof:wof + 96],
                in_=hap(din["l0_W_in"], k * 128 * 4192 + 4096,
                        [[4192, 128], [1, 96]])),
                reads=(), writes=(wsq[par],),
                dma=("sW" if par == 0 else "sWb"), tag=f"l0wb{tt}_{k}")
            for (c0, nn, pi, pof) in grps:
                sch.add("te", lambda h, k=k, wof=wof, c0=c0, nn=nn, pi=pi,
                        pof=pof, par=par: h.matmul(
                    P[pi][pof:pof + nn, 0:TT], WS[:, wof + c0:wof + c0 + nn],
                    hT(k), start=(k == 0), stop=(k == DC - 1)),
                    reads=(wsq[par], hba), writes=(rs[f"P{pi}"],),
                    tag=f"l0mb{tt}_{k}_{c0}")
        sBv1 = FA[0:8, 2048:2048 + TT]
        sBv2 = FA[0:8, 2560:2560 + TT]
        sCv1 = FA[0:8, 3072:3072 + TT]
        sCv2 = FA[0:8, 3584:3584 + TT]
        sBr = FA[32:48, 2048:2048 + TT]
        sCr = FA[32:48, 2560:2560 + TT]
        for (tg, pi, pof, dst, nn) in [("bv1", 0, 0, sBv1, 8),
                                       ("bv2", 1, 0, sBv2, 8),
                                       ("br", 2, 32, sBr, 16),
                                       ("cv1", 3, 0, sCv1, 8),
                                       ("cv2", 5, 0, sCv2, 8),
                                       ("cr", 6, 32, sCr, 16)]:
            sch.add("se", lambda h, pi=pi, pof=pof, dst=dst, nn=nn: h.activation(
                dst, P[pi][pof:pof + nn, 0:TT], AF.Silu, bias=0.0, scale=1.0),
                reads=(rs[f"P{pi}"],), writes=(far,), tag=f"l0{tg}_{tt}")
        sch.add("se", lambda h: h.activation(
            dtrow, P[4][64:96, 0:TT], AF.Exp, bias=cst["dtb64"], scale=1.0),
            reads=(rs["P4"], csts), writes=(smr,), tag=f"l0dte{tt}")
        sch.add("ve", lambda h: h.tensor_scalar(
            arow, dtrow, 1.0, None, ALU.add), reads=(smr,), writes=(far,),
            tag=f"l0dt1{tt}")
        sch.add("se", lambda h: h.activation(
            dtrow, arow, AF.Ln, bias=0.0, scale=1.0),
            reads=(far,), writes=(smr,), tag=f"l0dtl{tt}")
        cosr = FB[0:8, 0:TT]
        sinr = FB[0:8, TT:2 * TT]
        scC = FB[0:8, 1024:1024 + TT]
        scD = FB[0:8, 1536:1536 + TT]
        scE = SM[0:8, 0:TT]
        for (va, vb, tg) in [(sBv1, sBv2, "B"), (sCv1, sCv2, "C")]:
            sch.add("ve", lambda h, va=va: h.tensor_mul(scC, va, cosr),
                    reads=(far, fbr), writes=(fbr,), tag=f"l0rp1{tg}{tt}")
            sch.add("ve", lambda h, va=va: h.tensor_mul(scD, va, sinr),
                    reads=(far, fbr), writes=(fbr,), tag=f"l0rp2{tg}{tt}")
            sch.add("ve", lambda h, vb=vb: h.tensor_mul(scE, vb, sinr),
                    reads=(far, fbr), writes=(smr,), tag=f"l0rp3{tg}{tt}")
            sch.add("ve", lambda h, va=va, vb=vb: h.tensor_mul(va, vb, cosr),
                    reads=(far, fbr), writes=(far,), tag=f"l0rp4{tg}{tt}")
            sch.add("ve", lambda h, vb=vb: h.tensor_sub(vb, scC, scE),
                    reads=(fbr, smr), writes=(far,), tag=f"l0rp5{tg}{tt}")
            sch.add("ve", lambda h, va=va: h.tensor_add(scC, scD, va),
                    reads=(fbr, far), writes=(fbr,), tag=f"l0rp6{tg}{tt}")
            sch.add("ve", lambda h, va=va: h.tensor_copy(va, scC),
                    reads=(fbr,), writes=(far,), tag=f"l0rp7{tg}{tt}")
        asm = [(sBv2, Bbf(t0, TT), 0, 8), (sBv1, Bbf(t0, TT), 8, 8),
               (sBr, Bbf(t0, TT), 16, 16),
               (sCv2, Cbf(t0, TT), 0, 8), (sCv1, Cbf(t0, TT), 8, 8),
               (sCr, Cbf(t0, TT), 16, 16)]
        for ai, (sap, dreg, drow, nn) in enumerate(asm):
            sch.add("ge", lambda h, sap=sap, dreg=dreg, drow=drow, nn=nn:
                    h.dma_start(out=dreg[drow:drow + nn, :], in_=sap),
                    reads=(far,), writes=(smbr,), dma="sGd",
                    tag=f"l0asm{tt}_{ai}")
        sch.add("ve", lambda h: h.tensor_scalar(
            arow, dtrow, Acol, None, ALU.mult),
            reads=(smr,), writes=(far,), tag=f"l0a{tt}")
        sch.add("ve", lambda h, t0=t0: h.tensor_tensor_scan(
            Gsl(t0, TT), arow, arow, carry, ALU.add, ALU.bypass),
            reads=(far, smr), writes=(gr,), tag=f"l0G{tt}")
        sch.add("ve", lambda h, t0=t0: h.tensor_copy(
            carry, Gsl(t0 + TT - 1, 1)),
            reads=(gr,), writes=(smr,), tag=f"l0Gc{tt}")
        for i in range(TT // 128):
            sch.add("te", lambda h, i=i: h.transpose(
                P[3][:, 0:32], dtrow[:, i * 128:(i + 1) * 128],
                cst["ident_f64"][0:32, 0:32]),
                reads=(smr, csts), writes=(rs["P3"],), tag=f"l0dT{tt}_{i}")
            sch.add("ve", lambda h: h.tensor_copy(dtTs, P[3][:, 0:32]),
                    reads=(rs["P3"],), writes=(smr,), tag=f"l0dTe{tt}_{i}")
            for j in range(16):
                sch.add("te", lambda h, i=i, j=j: h.transpose(
                    P[3][:, 128 + (j % 2) * 64:128 + (j % 2) * 64 + 64].bitcast(BF16),
                    xsT(j)[:, i * 128:(i + 1) * 128], cst["ident"]),
                    reads=(hbb, csts), writes=(rs["P3"],), tag=f"l0xT{tt}_{i}_{j}")
                dtap = hap(meta["SM"], 1024 + 2 * j, [[2048, 128], [1, 2], [0, 64]])
                sch.add("ve", lambda h, i=i, j=j, dtap=dtap: h.tensor_tensor(
                    xcT(i)[:, j * 128:(j + 1) * 128].rearrange(
                        "q (a b) -> q a b", a=2),
                    P[3][:, 128 + (j % 2) * 64:128 + (j % 2) * 64 + 64].bitcast(
                        BF16).rearrange("q (a b) -> q a b", a=2),
                    dtap, ALU.mult),
                    reads=(rs["P3"], smr), writes=(hbb,), tag=f"l0xcE{tt}_{i}_{j}")
        for ci in range(CPT):
            b = (ci % 2) * 64
            cs0 = t0 + ci * CH
            i128 = (ci * CH) // 128
            sch.add("ge", lambda h, cs0=cs0: h.dma_start(
                out=hap(meta["FB"], 96 * 2048, [[2048, 1], [1, 2048]]),
                in_=hap(meta["FA"], 64 * 4096 + cs0, [[4096, 32], [1, 64]])),
                reads=(gr,), writes=(fbr,), dma="sGd", tag=f"l0dg{tt}_{ci}")
            sch.add("te", lambda h, cs0=cs0, b=b: h.matmul(
                P[6][b:b + 64, 0:64], Bbf(cs0, 64), Cbf(cs0, 64),
                start=True, stop=True),
                reads=(smbr,), writes=(rs["P6"],), tag=f"l0sc{tt}_{ci}")
            sch.add("ve", lambda h, b=b: h.tensor_copy(
                SMb[b:b + 64, 6272:6336], P[6][b:b + 64, 0:64]),
                reads=(rs["P6"],), writes=(smbr,), tag=f"l0sce{tt}_{ci}")
            for hb in range(4):
                sch.add("te", lambda h, cs0=cs0, b=b, hb=hb: h.matmul(
                    P[5][b:b + 64, 0:512], FA[64:128, cs0:cs0 + 64],
                    FB[64:128, hb * 512:(hb + 1) * 512],
                    start=True, stop=True),
                    reads=(gr, far, fbr), writes=(rs["P5"],),
                    tag=f"l0dm{tt}_{ci}_{hb}")
                msk = cst["mask_d"] if b == 0 else cst["mask_d64"]
                sch.add("ve", lambda h, b=b, msk=msk: h.scalar_tensor_tensor(
                    P[6][b:b + 64, 0:512], P[5][b:b + 64, 0:512], 1.0,
                    msk, ALU.mult, ALU.add),
                    reads=(rs["P5"], csts), writes=(rs["P6"],),
                    tag=f"l0dk{tt}_{ci}_{hb}")
                sch.add("se", lambda h, b=b: h.activation(
                    P[5][b:b + 64, 0:512], P[6][b:b + 64, 0:512], AF.Exp,
                    bias=0.0, scale=1.0),
                    reads=(rs["P6"],), writes=(rs["P5"],),
                    tag=f"l0de{tt}_{ci}_{hb}")
                scap = hap(meta["SMb"], b * 6656 + 6272,
                           [[6656, 64], [0, 8], [1, 64]])
                sch.add("ve", lambda h, b=b, hb=hb, scap=scap: h.tensor_tensor(
                    MTreg(hb * 512, 512)[b:b + 64, :].rearrange(
                        "q (a l) -> q a l", a=8),
                    P[5][b:b + 64, 0:512].rearrange("q (a l) -> q a l", a=8),
                    scap, ALU.mult),
                    reads=(rs["P5"], smbr), writes=(hbc,),
                    tag=f"l0mt{tt}_{ci}_{hb}")
            sch.add("ve", lambda h, cs0=cs0: h.tensor_scalar(
                dcyr, Gsl(cs0, CH), Gsl(cs0 + CH - 1, 1), None, ALU.subtract),
                reads=(gr,), writes=(smr,), tag=f"l0dc1{tt}_{ci}")
            sch.add("se", lambda h: h.activation(
                dcyr, dcyr, AF.Exp, bias=0.0, scale=-1.0),
                reads=(smr,), writes=(smr,), tag=f"l0dc2{tt}_{ci}")
            sch.add("ve", lambda h, cs0=cs0: h.tensor_scalar(
                eAr, Gsl(cs0, CH), gprev, None, ALU.subtract),
                reads=(gr, smr), writes=(smr,), tag=f"l0ea1{tt}_{ci}")
            sch.add("se", lambda h: h.activation(
                eAr, eAr, AF.Exp, bias=0.0, scale=1.0),
                reads=(smr,), writes=(smr,), tag=f"l0ea2{tt}_{ci}")
            sch.add("ve", lambda h, cs0=cs0: h.tensor_sub(
                dttc, Gsl(cs0 + CH - 1, 1), gprev),
                reads=(gr, smr), writes=(smr,), tag=f"l0dt2{tt}_{ci}")
            sch.add("se", lambda h: h.activation(
                dttc, dttc, AF.Exp, bias=0.0, scale=1.0),
                reads=(smr,), writes=(smr,), tag=f"l0dt3{tt}_{ci}")
            sch.add("ve", lambda h, cs0=cs0: h.tensor_copy(
                gprev, Gsl(cs0 + CH - 1, 1)),
                reads=(gr,), writes=(smr,), tag=f"l0gp{tt}_{ci}")
            sch.add("ve", lambda h: h.tensor_scalar(
                SM[64:96, 1727:1743], cst["indj64"], dttc, None, ALU.mult),
                reads=(smr, csts), writes=(smr,), tag=f"l0dti{tt}_{ci}")
            sch.add("te", lambda h: h.matmul(
                P[6][:, 64:80], cst["a2bf64"], SM[64:96, 1727:1743],
                start=True, stop=True),
                reads=(smr, csts), writes=(rs["P6"],), tag=f"l0dtm{tt}_{ci}")
            sch.add("ve", lambda h: h.tensor_copy(dtotc, P[6][:, 64:80]),
                    reads=(rs["P6"],), writes=(smr,), tag=f"l0dte2{tt}_{ci}")
            sch.add("ve", lambda h, cs0=cs0: h.tensor_tensor(
                SMb[64:96, 0:64], Bbf(cs0, 64), dcyr, ALU.mult),
                reads=(smbr, smr), writes=(smbr,), tag=f"l0bd{tt}_{ci}")
            sch.add("te", lambda h: h.transpose(
                P[7][0:64, 96:128].bitcast(BF16)[:, 0:32], SMb[64:96, 0:64],
                cst["ident_b64"][0:32, 0:32]),
                reads=(smbr, csts), writes=(rs["P7"],), tag=f"l0bdt{tt}_{ci}")
            sch.add("ve", lambda h, b=b: h.tensor_copy(
                SMb[b:b + 64, 6336:6368], P[7][0:64, 96:128].bitcast(BF16)[:, 0:32]),
                reads=(rs["P7"],), writes=(smbr,), tag=f"l0bde{tt}_{ci}")
            for j in range(16):
                sch.add("te", lambda h, i128=i128, ci=ci, j=j, b=b: h.matmul(
                    P[4][:, j * 32:(j + 1) * 32],
                    xcT(i128)[b:b + 64, j * 128:(j + 1) * 128],
                    SMb[b:b + 64, 6336:6368], start=True, stop=True),
                    reads=(hbb, smbr), writes=(rs["P4"],),
                    tag=f"l0stm{tt}_{ci}_{j}")
            sb = stbuf[gc % 2]
            nb = stbuf[(gc + 1) % 2]
            for jq in range(4):
                for j4 in range(4):
                    j = jq * 4 + j4
                    sch.add("te", lambda h, j=j, j4=j4, sb=sb: h.transpose(
                        P[7][0:32, j4 * 128:(j4 + 1) * 128],
                        sb[:, j * 32:(j + 1) * 32],
                        cst["ident_f"]),
                        reads=(star,), writes=(rs["P7"],),
                        tag=f"l0spt{tt}_{ci}_{j}")
                sch.add("ve", lambda h, jq=jq: h.tensor_copy(
                    SPT[64:96, jq * 512:(jq + 1) * 512],
                    P[7][0:32, 0:512]),
                    reads=(rs["P7"],), writes=(hbc,), tag=f"l0spe{tt}_{ci}_{jq}")
            for j in range(16):
                sch.add("ve", lambda h, j=j, sb=sb, nb=nb: h.scalar_tensor_tensor(
                    nb[:, j * 32:(j + 1) * 32], sb[:, j * 32:(j + 1) * 32],
                    dtotc[:, j:j + 1], P[4][:, j * 32:(j + 1) * 32],
                    ALU.mult, ALU.add),
                    reads=(star, smr, rs["P4"]), writes=(star,),
                    tag=f"l0stu{tt}_{ci}_{j}")
            for j in range(16):
                jh = j % 8
                for hh in range(2):
                    sch.add("te", lambda h, i128=i128, ci=ci, j=j, hh=hh, b=b:
                            h.matmul(
                        PA[hh * 64:(hh + 1) * 64, j * 64:(j + 1) * 64],
                        xcT(i128)[b:b + 64, j * 128 + hh * 64:
                                  j * 128 + hh * 64 + 64],
                        MTreg((2 * j + hh) * 64, 64)[b:b + 64, :],
                        start=True, stop=True),
                        reads=(hbb, hbc), writes=(rs["P0"], rs["P1"]),
                        tag=f"l0yd{tt}_{ci}_{j}_{hh}")
                sch.add("te", lambda h, cs0=cs0, j=j: h.matmul(
                    PB[:, j * 64:(j + 1) * 64], SPT[64:96, j * 128:(j + 1) * 128],
                    Cbf(cs0, 64), start=True, stop=True),
                    reads=(hbc, smbr), writes=(rs["P2"], rs["P3"]),
                    tag=f"l0yo{tt}_{ci}_{j}")
                sch.add("ve", lambda h, j=j: h.tensor_scalar(
                    SMb[64:96, 96:160], eAr, cst["indj64"][:, j:j + 1],
                    None, ALU.mult),
                    reads=(smr, csts), writes=(smbr,), tag=f"l0eg1{tt}_{ci}_{j}")
                sch.add("te", lambda h, j=j, jh=jh: h.matmul(
                    P[6][:, jh * 64:(jh + 1) * 64], cst["a2b64"],
                    SMb[64:96, 96:160], start=True, stop=True),
                    reads=(smbr, csts), writes=(rs["P6"],),
                    tag=f"l0eg2{tt}_{ci}_{j}")
                sch.add("ve", lambda h, jh=jh: h.tensor_copy(
                    SMb[:, 6336 + 0:6336 + 64] if False else
                    SMb[:, 6400:6464], P[6][:, jh * 64:(jh + 1) * 64]),
                    reads=(rs["P6"],), writes=(smbr,), tag=f"l0eg3{tt}_{ci}_{j}")
                sch.add("ve", lambda h, j=j: h.tensor_tensor(
                    FA[:, 2560:2624], PB[:, j * 64:(j + 1) * 64],
                    SMb[:, 6400:6464], ALU.mult),
                    reads=(rs["P2"], rs["P3"], smbr), writes=(far,),
                    tag=f"l0ym1{tt}_{ci}_{j}")
                sch.add("ve", lambda h, j=j: h.tensor_tensor(
                    FA[:, 2624:2688], FA[:, 2560:2624],
                    PA[:, j * 64:(j + 1) * 64], ALU.add),
                    reads=(far, rs["P0"], rs["P1"]), writes=(far,),
                    tag=f"l0ym2{tt}_{ci}_{j}")
                sch.add("ve", lambda h, j=j, ci=ci: h.scalar_tensor_tensor(
                    ylT(j)[:, ci * CH:(ci + 1) * CH],
                    xsT(j)[:, ci * CH:(ci + 1) * CH], cst["l0_D"][:, j:j + 1],
                    FA[:, 2624:2688], ALU.mult, ALU.add),
                    reads=(hbb, csts, far), writes=(hbc,),
                    tag=f"l0ym3{tt}_{ci}_{j}")
            gc += 1
        sch.add("sy", lambda h, t0=t0: h.dma_start(
            out=dram["sp_yl"][:, t0:t0 + TT].rearrange("(j q) t -> q j t", j=16),
            in_=HB[:, 20480:20480 + 16 * TT].rearrange("q (j t) -> q j t", j=16)),
            reads=(hbc,), writes=(rs["dram"],), dma="sSp", tag=f"l0ysp{tt}")

    fstate = stbuf[NCH % 2]
    sch.add("sy", lambda h, fstate=fstate: h.dma_start(
        out=hap(dram["agA_in"], 0, [[512, 128], [1, 512]]), in_=fstate),
        reads=(star,), writes=(rs["dram"],), dma="sSp", tag="l0agw1")
    sch.add("sy", lambda h: h.dma_start(
        out=hap(dram["agA_in"], 65536, [[1, 32], [1, 1]]), in_=carry),
        reads=(smr,), writes=(rs["dram"],), dma="sSp", tag="l0agw2")
    cc = sch.add("ge", lambda h: h.collective_compute(
        "AllGather", ALU.bypass, replica_groups=[list(range(NCORES))],
        ins=[dram["agA_in"].ap().opt()], outs=[dram["agA_out"].ap().opt()]),
        reads=(rs["dram"],), writes=(rs["dram"],), tag="l0agA")
    sch.add("sy", lambda h: h.dma_start(
        out=SM[0:8, 1024:1056], in_=hap(dram["agA_out"], 65536,
                                        [[65568, 8], [1, 32]])),
        reads=(rs["dram"],), writes=(smr,), dma="sSp", tag="l0suml")
    sch.add("te", lambda h: h.matmul(
        P[2][0:32, 0:8], SM[0:8, 1024:1056], cst["tril8"],
        start=True, stop=True),
        reads=(smr, csts), writes=(rs["P2"],), tag="l0Q")
    sch.add("te", lambda h: h.matmul(
        P[3][0:32, 0:1], SM[0:8, 1024:1056], cst["lt_col"],
        start=True, stop=True),
        reads=(smr, csts), writes=(rs["P3"],), tag="l0P")
    sch.add("ve", lambda h: h.tensor_copy(SM[0:32, 1659:1660], P[3][0:32, 0:1]),
            reads=(rs["P3"],), writes=(smr,), tag="l0Pe")
    sch.add("ve", lambda h: h.tensor_scalar(
        SM[0:32, 1660:1668], P[2][0:32, 0:8], -1.0, SM[0:32, 1659:1660],
        ALU.mult, ALU.add),
        reads=(rs["P2"], smr), writes=(smr,), tag="l0cf1")
    sch.add("ve", lambda h: h.tensor_scalar(
        SM[0:32, 1660:1668], SM[0:32, 1660:1668], 0.0, None, ALU.min),
        reads=(smr,), writes=(smr,), tag="l0cfc")
    sch.add("se", lambda h: h.activation(
        SM[0:32, 1668:1676], SM[0:32, 1660:1668], AF.Exp, bias=0.0, scale=1.0),
        reads=(smr,), writes=(smr,), tag="l0cf2")
    sch.add("ve", lambda h: h.tensor_tensor(
        SM[0:32, 1660:1668], SM[0:32, 1668:1676], cst["core_lt"], ALU.mult),
        reads=(smr, csts), writes=(smr,), tag="l0cf3")
    for j in range(NCORES):
        sch.add("sy", lambda h, j=j: h.dma_start(
            out=HB[:, 4096 + j * 1024:4096 + (j + 1) * 1024].bitcast(F32),
            in_=hap(dram["agA_out"], j * 65568, [[512, 128], [1, 512]])),
            reads=(rs["dram"],), writes=(hbb,), dma="sSp", tag=f"l0stl{j}")
        sch.add("ve", lambda h, j=j: h.tensor_scalar(
            SMb[0:32, 0:512].bitcast(F32)[:, 0:0] if False else
            SMb[0:32, 0:512], cst["inds"], SM[0:32, 1660 + j:1661 + j],
            None, ALU.mult),
            reads=(csts, smr), writes=(smbr,), tag=f"l0ej1{j}")
        sch.add("te", lambda h: h.matmul(
            P[4][:, 0:512], cst["a2b0"], SMb[0:32, 0:512],
            start=True, stop=True),
            reads=(smbr, csts), writes=(rs["P4"],), tag=f"l0ej2{j}")
        dstb = stbuf[j % 2]
        if j == 0:
            sch.add("ve", lambda h, j=j: h.tensor_tensor(
                stbuf[0], P[4][:, 0:512],
                HB[:, 4096 + j * 1024:4096 + (j + 1) * 1024].bitcast(F32),
                ALU.mult),
                reads=(rs["P4"], hbb), writes=(star,), tag=f"l0si{j}")
        else:
            sch.add("ve", lambda h, j=j: h.tensor_tensor(
                FA[:, 2048:2560], P[4][:, 0:512],
                HB[:, 4096 + j * 1024:4096 + (j + 1) * 1024].bitcast(F32),
                ALU.mult),
                reads=(rs["P4"], hbb), writes=(far,), tag=f"l0si{j}")
            sch.add("ve", lambda h, j=j: h.tensor_add(
                stbuf[j % 2], stbuf[(j + 1) % 2], FA[:, 2048:2560]),
                reads=(far, star), writes=(star,), tag=f"l0sa{j}")
    sin_f = stbuf[(NCORES - 1) % 2]
    for jq in range(4):
        for j4 in range(4):
            j = jq * 4 + j4
            sch.add("te", lambda h, j=j, j4=j4, sin_f=sin_f: h.transpose(
                P[3][0:32, j4 * 128:(j4 + 1) * 128],
                sin_f[:, j * 32:(j + 1) * 32], cst["ident_f"]),
                reads=(star,), writes=(rs["P3"],), tag=f"l0sit{j}")
        sch.add("ve", lambda h, jq=jq: h.tensor_copy(
            SPT[64:96, jq * 512:(jq + 1) * 512], P[3][0:32, 0:512]),
            reads=(rs["P3"],), writes=(hbc,), tag=f"l0site{jq}")

    for tt in range(NTT):
        t0 = tt * TT
        sch.add("sy", lambda h, t0=t0: h.dma_start(
            out=HB[:, 0:DC * TT].rearrange("q (d t) -> q d t", d=DC),
            in_=dram["sp_h0"][:, t0:t0 + TT].rearrange("(d q) t -> q d t", d=DC)),
            reads=(rs["dram"],), writes=(hba,), dma="sSp", tag=f"l0hrl{tt}")
        sch.add("sy", lambda h, t0=t0: h.dma_start(
            out=HB[:, 20480:20480 + 16 * TT].rearrange("q (j t) -> q j t", j=16),
            in_=dram["sp_yl"][:, t0:t0 + TT].rearrange("(j q) t -> q j t", j=16)),
            reads=(rs["dram"],), writes=(hbc,), dma="sSp", tag=f"l0yrl{tt}")
        for zh in range(2):
            for k in range(DC):
                par = k % 2
                wof = par * 1024
                sch.add("ge", lambda h, k=k, zh=zh, wof=wof: h.dma_start(
                    out=WS[:, wof:wof + 1024],
                    in_=hap(din["l0_W_in"], k * 128 * 4192 + zh * 1024,
                            [[4192, 128], [1, 1024]])),
                    reads=(), writes=(wsq[par],),
                    dma=("sW" if par == 0 else "sWb"), tag=f"l0wz{tt}_{zh}_{k}")
                for f in range(8):
                    sch.add("te", lambda h, k=k, f=f, wof=wof, par=par: h.matmul(
                        P[f][:, 0:TT], WS[:, wof + f * 128:wof + (f + 1) * 128],
                        hT(k), start=(k == 0), stop=(k == DC - 1)),
                        reads=(wsq[par], hba), writes=(rs[f"P{f}"],),
                        tag=f"l0mz{tt}_{zh}_{k}_{f}")
            for f in range(8):
                j = zh * 8 + f
                sch.add("se", lambda h, f=f, j=j: h.activation(
                    xsT(j), P[f][:, 0:TT], AF.Silu, bias=0.0, scale=1.0),
                    reads=(rs[f"P{f}"],), writes=(hbb,), tag=f"l0zs{tt}_{j}")
        sch.add("se", lambda h, t0=t0: h.activation(
            FA[64:96, 2048:2048 + TT], Gsl(t0, TT), AF.Exp, bias=0.0, scale=1.0),
            reads=(gr,), writes=(far,), tag=f"l0xg{tt}")
        for j in range(16):
            sch.add("te", lambda h, j=j, t0=t0: h.matmul(
                P[2][:, 0:TT], SPT[64:96, j * 128:(j + 1) * 128],
                Cbf(t0, TT), start=True, stop=True),
                reads=(hbc, smbr), writes=(rs["P2"],), tag=f"l0cor{tt}_{j}")
            sch.add("ve", lambda h, j=j: h.tensor_scalar(
                SMb[64:96, 96:96 + TT], FA[64:96, 2048:2048 + TT],
                cst["indj64"][:, j:j + 1], None, ALU.mult),
                reads=(far, csts), writes=(smbr,), tag=f"l0gg1{tt}_{j}")
            sch.add("te", lambda h, j=j: h.matmul(
                P[3][:, 0:TT], cst["a2b64"], SMb[64:96, 96:96 + TT],
                start=True, stop=True),
                reads=(smbr, csts), writes=(rs["P3"],), tag=f"l0gg2{tt}_{j}")
            sch.add("ve", lambda h, j=j: h.tensor_copy(
                MTreg(0, TT), P[3][:, 0:TT]),
                reads=(rs["P3"],), writes=(hbc,), tag=f"l0gg3{tt}_{j}")
            sch.add("ve", lambda h, j=j: h.tensor_tensor(
                FA[:, 2560:2560 + TT], P[2][:, 0:TT], MTreg(0, TT), ALU.mult),
                reads=(rs["P2"], hbc), writes=(far,), tag=f"l0mg1{tt}_{j}")
            sch.add("ve", lambda h, j=j: h.tensor_tensor(
                FA[:, 3072:3072 + TT], FA[:, 2560:2560 + TT], ylT(j), ALU.add),
                reads=(far, hbc), writes=(far,), tag=f"l0mg2{tt}_{j}")
            sch.add("ve", lambda h, j=j: h.scalar_tensor_tensor(
                ylT(j), xsT(j), 1.0, FA[:, 3072:3072 + TT],
                ALU.mult, ALU.mult),
                reads=(hbb, far), writes=(hbc,), tag=f"l0mg3{tt}_{j}")
        for j in range(16):
            sch.add("ve", lambda h, j=j: h.tensor_tensor(
                MTreg(512, TT), ylT(j), ylT(j), ALU.mult),
                reads=(hbc,), writes=(hbc,), tag=f"l0rs{tt}_{j}")
            sch.add("te", lambda h, j=j: h.matmul(
                P[6][0:1, 0:TT], cst["ones_col_bf"], MTreg(512, TT),
                start=(j == 0), stop=(j == 15)),
                reads=(hbc, csts), writes=(rs["P6"],), tag=f"l0rm{tt}_{j}")
        mrow = FB[0:1, 0:TT]
        sch.add("ve", lambda h: h.tensor_scalar(
            mrow, P[6][0:1, 0:TT], 1.0 / 2048.0, None, ALU.mult),
            reads=(rs["P6"],), writes=(rs["rows"],), tag=f"l0rr1{tt}")
        sch.add("se", lambda h: h.activation(mrow, mrow, AF.Ln, bias=EPS6, scale=1.0),
                reads=(rs["rows"], csts), writes=(rs["rows"],), tag=f"l0rr2{tt}")
        sch.add("se", lambda h: h.activation(mrow, mrow, AF.Exp, bias=0.0, scale=-0.5),
                reads=(rs["rows"],), writes=(rs["rows"],), tag=f"l0rr3{tt}")
        sch.add("te", lambda h: h.matmul(
            P[6][:, 0:TT], cst["ones_row"], mrow, start=True, stop=True),
            reads=(rs["rows"], csts), writes=(rs["P6"],), tag=f"l0rb{tt}")
        for j in range(16):
            sch.add("ve", lambda h, j=j: h.scalar_tensor_tensor(
                xsT(j), ylT(j), cst["l0_rms_w"][:, j:j + 1], P[6][:, 0:TT],
                ALU.mult, ALU.mult),
                reads=(hbc, csts, rs["P6"]), writes=(hbb,), tag=f"l0yb{tt}_{j}")
        for k in range(16):
            par = k % 2
            wof = par * 1024
            sch.add("ge", lambda h, k=k, wof=wof: h.dma_start(
                out=WS[:, wof:wof + 1024],
                in_=din["l0_W_out"][k * 128:(k + 1) * 128, :]),
                reads=(), writes=(wsq[par],),
                dma=("sW" if par == 0 else "sWb"), tag=f"l0wo{tt}_{k}")
            for do in range(8):
                sch.add("te", lambda h, k=k, do=do, wof=wof, par=par: h.matmul(
                    P[do][:, 0:TT], WS[:, wof + do * 128:wof + (do + 1) * 128],
                    xsT(k), start=(k == 0), stop=(k == 15)),
                    reads=(wsq[par], hbb), writes=(rs[f"P{do}"],),
                    tag=f"l0wm{tt}_{k}_{do}")
        for do in range(8):
            sch.add("ve", lambda h, do=do, t0=t0: h.tensor_tensor(
                X[do][:, t0:t0 + TT], X[do][:, t0:t0 + TT], P[do][:, 0:TT],
                ALU.add),
                reads=(rs[f"P{do}"],), writes=(rs[f"X{do}"],),
                tag=f"l0res{tt}_{do}")


def eidos_init(sch, rs, meta):
    nc = sch.nc
    SM, SMb, FB, P, cst = meta["SM"], meta["SMb"], meta["FB"], meta["P"], meta["cst"]
    csts = rs["consts"]
    p5 = rs["P5"]
    for nm, dst in [("e_th1", "e_cs1"), ("e_th2", "e_cs2")]:
        th = cst[nm]
        cs = cst[dst]
        sch.add("se", lambda h, th=th, cs=cs: h.activation(
            cs[:, 0:4], th, AF.Sin, bias=float(np.pi / 2), scale=1.0),
            reads=(csts,), writes=(csts,), tag=f"ei_cos_{nm}")
        sch.add("se", lambda h, th=th, cs=cs: h.activation(
            cs[:, 4:8], th, AF.Sin, bias=0.0, scale=1.0),
            reads=(csts,), writes=(csts,), tag=f"ei_sin_{nm}")
    for j, coff, soff, csname in [(0, 0, 16, "e_cs1"), (1, 32, 48, "e_cs2")]:
        sch.add("te", lambda h, csname=csname: h.transpose(
            P[5][0:4, 0:9].bitcast(F32), cst[csname][:, 0:4],
            cst["ident_f"][0:9, 0:9]),
            reads=(csts,), writes=(p5,), tag=f"ei_cTc{j}")
        sch.add("ve", lambda h, coff=coff: h.tensor_copy(
            FB[0:4, coff:coff + 9], P[5][0:4, 0:9].bitcast(F32)),
            reads=(p5,), writes=(rs["FB"],), tag=f"ei_cTce{j}")
        sch.add("te", lambda h, csname=csname: h.transpose(
            P[5][0:4, 32:41].bitcast(F32), cst[csname][:, 4:8],
            cst["ident_f"][0:9, 0:9]),
            reads=(csts,), writes=(p5,), tag=f"ei_cTs{j}")
        sch.add("ve", lambda h, soff=soff: h.tensor_copy(
            FB[0:4, soff:soff + 9], P[5][0:4, 32:41].bitcast(F32)),
            reads=(p5,), writes=(rs["FB"],), tag=f"ei_cTse{j}")
    for j, dst0, coff, soff in [(0, 0, 0, 16), (1, 72, 32, 48)]:
        for p in range(9):
            sch.add("te", lambda h, p=p, coff=coff: h.transpose(
                P[5][0:1, 0:4].bitcast(F32), FB[0:4, coff + p:coff + p + 1],
                cst["ident_f"][0:4, 0:4]),
                reads=(rs["FB"], csts), writes=(p5,), tag=f"ei_cr{j}_{p}")
            sch.add("te", lambda h, p=p, soff=soff: h.transpose(
                P[5][0:1, 4:8].bitcast(F32), FB[0:4, soff + p:soff + p + 1],
                cst["ident_f"][0:4, 0:4]),
                reads=(rs["FB"], csts), writes=(p5,), tag=f"ei_sr{j}_{p}")
            sch.add("ve", lambda h: h.tensor_copy(FB[0:1, 64:72],
                                                  P[5][0:1, 0:8].bitcast(F32)),
                    reads=(p5,), writes=(rs["FB"],), tag=f"ei_cse{j}_{p}")
            sch.add("te", lambda h: h.matmul(
                P[5][:, 0:8], cst["ones_row"], FB[0:1, 64:72],
                start=True, stop=True),
                reads=(rs["FB"], csts), writes=(p5,), tag=f"ei_csb{j}_{p}")
            sch.add("ve", lambda h, p=p, dst0=dst0: h.tensor_copy(
                SM[:, dst0 + p * 8:dst0 + p * 8 + 8], P[5][:, 0:8]),
                reads=(p5,), writes=(csts,), tag=f"ei_csbe{j}_{p}")
    for j, r0, coff, soff in [(0, 4096, 0, 16), (1, 5248, 32, 48)]:
        for p in range(9):
            R = SMb[:, r0 + p * 128:r0 + (p + 1) * 128]
            sch.add("ve", lambda h, R=R: h.tensor_copy(R, cst["ipad"]),
                    reads=(csts,), writes=(csts,), tag=f"ei_Rc{j}_{p}")
            sch.add("te", lambda h, p=p, coff=coff: h.matmul(
                P[5][0:8, 16:17].bitcast(F32), cst["e2x"],
                FB[0:4, coff + p:coff + p + 1], start=True, stop=True),
                reads=(rs["FB"], csts), writes=(p5,), tag=f"ei_Rcc{j}_{p}")
            sch.add("ve", lambda h: h.tensor_copy(FB[0:8, 80:81],
                                                  P[5][0:8, 16:17].bitcast(F32)),
                    reads=(p5,), writes=(rs["FB"],), tag=f"ei_Rcce{j}_{p}")
            sch.add("te", lambda h, p=p, soff=soff: h.matmul(
                P[5][0:8, 16:17].bitcast(F32), cst["e2x"],
                FB[0:4, soff + p:soff + p + 1], start=True, stop=True),
                reads=(rs["FB"], csts), writes=(p5,), tag=f"ei_Rsc{j}_{p}")
            sch.add("ve", lambda h: h.tensor_mul(
                FB[0:8, 81:82], P[5][0:8, 16:17].bitcast(F32), cst["altsign"]),
                reads=(p5, csts), writes=(rs["FB"],), tag=f"ei_Rsse{j}_{p}")
            sch.add("ve", lambda h: h.tensor_mul(
                FB[0:8, 96:104], FB[0:8, 80:81].broadcast_to([8, 8]),
                cst["ident_f"][0:8, 0:8]),
                reads=(rs["FB"], csts), writes=(rs["FB"],), tag=f"ei_Rcd{j}_{p}")
            sch.add("ve", lambda h: h.tensor_mul(
                FB[0:8, 104:112], FB[0:8, 81:82].broadcast_to([8, 8]),
                cst["ident_f"][0:8, 0:8]),
                reads=(rs["FB"], csts), writes=(rs["FB"],), tag=f"ei_Rsd{j}_{p}")
            sch.add("te", lambda h: h.matmul(
                P[5][0:8, 0:128], FB[0:8, 96:104], cst["ident_f"][0:8, 0:128],
                start=True, stop=False),
                reads=(rs["FB"], csts), writes=(p5,), tag=f"ei_Rm1{j}_{p}")
            sch.add("te", lambda h: h.matmul(
                P[5][0:8, 0:128], FB[0:8, 104:112], cst["pswap"],
                start=False, stop=True),
                reads=(rs["FB"], csts), writes=(p5,), tag=f"ei_Rm2{j}_{p}")
            sch.add("ve", lambda h, R=R: h.tensor_copy(R[0:8, :], P[5][0:8, 0:128]),
                    reads=(p5,), writes=(csts,), tag=f"ei_Re{j}_{p}")
    sch.add("ve", lambda h: h.tensor_copy(SMb[:, 6416:6424], cst["e_wpath"]),
            reads=(csts,), writes=(csts,), tag="ei_wpb")
    sch.add("ve", lambda h: h.tensor_copy(SMb[:, 6424:6432], cst["e_wteam"]),
            reads=(csts,), writes=(csts,), tag="ei_wtb")
    sch.add("ve", lambda h: h.scalar_tensor_tensor(
        SM[:, 144:216], cst["e_pb"] if False else cst["e_be2"], 1.0,
        cst["e_be2"] if False else cst["e_pb"], ALU.mult, ALU.add),
        reads=(csts,), writes=(csts,), tag="ei_b2p")
    sch.add("ve", lambda h: h.tensor_scalar(
        SM[:, 216:288], cst["e_ctx"], 0.1, None, ALU.mult),
        reads=(csts,), writes=(csts,), tag="ei_ctx01")
    sch.add("ve", lambda h: h.tensor_scalar(
        SM[:, 288:360], cst["e_g2"], 7.0, None, ALU.mult),
        reads=(csts,), writes=(csts,), tag="ei_g27")
    sch.add("ve", lambda h: h.tensor_scalar(
        SM[:, 360:648], cst["e_g1"], 7.0, None, ALU.mult),
        reads=(csts,), writes=(csts,), tag="ei_g17")
    sch.add("ve", lambda h: h.scalar_tensor_tensor(
        SM[0:9, 2044:2045], cst["e_ovp"], -1.0, cst["e_bp"], ALU.mult, ALU.add),
        reads=(csts,), writes=(csts,), tag="ei_bmp")
    sch.add("ve", lambda h: h.scalar_tensor_tensor(
        SM[0:3, 2045:2046], cst["e_ovt"], -1.0, cst["e_bt"], ALU.mult, ALU.add),
        reads=(csts,), writes=(csts,), tag="ei_bmt")
    sch.add("te", lambda h: h.transpose(
        P[5][0:1, 0:9].bitcast(F32), SM[0:9, 2044:2045],
        cst["ident_f"][0:9, 0:9]),
        reads=(csts,), writes=(p5,), tag="ei_bmpT")
    sch.add("ve", lambda h: h.tensor_copy(SM[0:1, 2020:2029],
                                          P[5][0:1, 0:9].bitcast(F32)),
            reads=(p5,), writes=(csts,), tag="ei_bmpE")
    sch.add("te", lambda h: h.transpose(
        P[5][0:1, 16:19].bitcast(F32), SM[0:3, 2045:2046],
        cst["ident_f"][0:3, 0:3]),
        reads=(csts,), writes=(p5,), tag="ei_bmtT")
    sch.add("ve", lambda h: h.tensor_copy(SM[0:1, 2036:2039],
                                          P[5][0:1, 16:19].bitcast(F32)),
            reads=(p5,), writes=(csts,), tag="ei_bmtE")
    sch.add("te", lambda h: h.matmul(
        P[5][:, 32:35], cst["ones_row"], SM[0:1, 2036:2039],
        start=True, stop=True),
        reads=(csts,), writes=(p5,), tag="ei_bmtB")
    sch.add("ve", lambda h: h.tensor_copy(SM[:, 2030:2033], P[5][:, 32:35]),
            reads=(p5,), writes=(csts,), tag="ei_bmtBE")


def emit_eidos(sch, rs, meta, st):
    HB, WS, SM, SMb, FA, FB, X, P = (meta["HB"], meta["WS"], meta["SM"],
                                     meta["SMb"], meta["FA"], meta["FB"],
                                     meta["X"], meta["P"])
    TT, NTT, T = meta["TT"], meta["NTT"], meta["T"]
    cst = meta["cst"]
    din = meta["din"]
    dram = meta["dram"]
    if not meta.get("_eidos_init_done"):
        eidos_init(sch, rs, meta)
        meta["_eidos_init_done"] = True

    csts = rs["consts"]
    wsq = [rs["WS0"], rs["WS1"]]
    w2q = [rs["WS2a"], rs["WS2b"]]
    hba, hbc, smbr = rs["HBa"], rs["HBc"], rs["SMb"]
    smr, far, fbr = rs["SM"], rs["FA"], rs["FB"]
    HR = [rs[f"HR{k}"] for k in range(32)]
    RV = [rs["RV0"], rs["RV1"]]
    sq1r = rs["SQ1"]
    rinv1s = [SM[:, 800:800 + TT], SM[:, 1312:1312 + TT]]
    rinv2s = FA[:, 1024:1024 + TT]
    sq1scr = SMb[:, 2176:2176 + TT]

    def lnxt(k):
        return HB[:, k * TT:(k + 1) * TT]

    def hraw(k):
        return HB[:, 4096 + k * TT:4096 + (k + 1) * TT]

    def feat(s, dc):
        o = 20480 + s * 8 * TT + dc * TT
        return HB[:, o:o + TT]

    def o2(dc):
        return SMb[:, dc * TT:(dc + 1) * TT]

    bcol = lambda tab, idx: tab[:, idx:idx + 1]

    def a_thunks(p):
        rv = rinv1s[p % 2]
        rvr = RV[p % 2]

        def w1dma(fq):
            par = fq % 2
            wof = par * 2048
            sch.add("ge", lambda h, p=p, fq=fq, wof=wof: h.dma_start(
                out=WS[:, wof:wof + 2048].rearrange("q (k j) -> q k j", k=8),
                in_=hap(din["e_W1"], p * D * 4096 + fq * 256,
                        [[4096, 128], [128 * 4096, 8], [1, 256]])),
                reads=(), writes=(wsq[par],),
                dma=("sW" if par == 0 else "sWb"), tag=f"e{st}W1_{p}_{fq}")

        def aux(fc):
            bank = fc % 4
            pb = rs[f"P{bank}"]
            sch.add("se", lambda h, fc=fc, bank=bank, p=p: h.activation(
                hraw(fc), P[bank][:, 0:TT], AF.Identity,
                bias=bcol(cst["e_b1"], p * 32 + fc), scale=1.0),
                reads=(pb, csts), writes=(HR[fc],), tag=f"e{st}ev1_{p}_{fc}")
            if fc == 0:
                sch.add("te", lambda h, p=p: h.matmul(
                    P[5][:, 0:TT], SMb[:, 4096 + p * 128:4096 + (p + 1) * 128],
                    hraw(0), start=True, stop=True),
                    reads=(HR[0], csts), writes=(rs["P5"],),
                    tag=f"e{st}rot1_{p}")
                sch.add("se", lambda h: h.activation(
                    hraw(0), P[5][:, 0:TT], AF.Identity, bias=0.0, scale=1.0),
                    reads=(rs["P5"],), writes=(HR[0],), tag=f"e{st}rev1_{p}")
            sch.add("se", lambda h, fc=fc: h.activation(
                sq1scr, hraw(fc), AF.Square, bias=0.0, scale=1.0),
                reads=(HR[fc],), writes=(sq1r,), tag=f"e{st}sq1_{p}_{fc}")
            sch.add("te", lambda h, fc=fc: h.matmul(
                P[6][0:1, 0:TT], cst["ones_col_bf"], sq1scr,
                start=(fc == 0), stop=(fc == 31)),
                reads=(sq1r, csts), writes=(rs["P6"],), tag=f"e{st}st1_{p}_{fc}")

        ths = []
        for fc in range(32):
            def grp(fc=fc):
                if fc == 0:
                    w1dma(0)
                    w1dma(1)
                elif fc % 2 == 0 and fc // 2 + 1 < 16:
                    w1dma(fc // 2 + 1)
                par = (fc // 2) % 2
                wof = par * 2048
                f2 = fc % 2
                pb = rs[f"P{fc % 4}"]
                for k in range(DC):
                    sch.add("te", lambda h, wof=wof, k=k, f2=f2, fc=fc: h.matmul(
                        P[fc % 4][:, 0:TT],
                        WS[:, wof + k * 256 + f2 * 128:
                           wof + k * 256 + (f2 + 1) * 128],
                        lnxt(k), start=(k == 0), stop=(k == DC - 1)),
                        reads=(wsq[par], hba), writes=(pb,),
                        tag=f"e{st}p1_{p}_{fc}_{k}")
                if fc >= 2:
                    aux(fc - 2)
            ths.append(grp)

        def tail():
            aux(30)
            aux(31)
            sch.add("ve", lambda h: h.tensor_scalar(
                rv[0:1, :], P[6][0:1, 0:TT], 1.0 / 4096.0, None, ALU.mult),
                reads=(rs["P6"],), writes=(rvr,), tag=f"e{st}m1_{p}")
            sch.add("se", lambda h: h.activation(
                rv[0:1, :], rv[0:1, :], AF.Ln, bias=EPS6, scale=1.0),
                reads=(rvr, csts), writes=(rvr,), tag=f"e{st}l1_{p}")
            sch.add("se", lambda h: h.activation(
                rv[0:1, :], rv[0:1, :], AF.Exp, bias=0.0, scale=-0.5),
                reads=(rvr,), writes=(rvr,), tag=f"e{st}r1_{p}")
            sch.add("te", lambda h: h.matmul(
                P[6][:, 0:TT], cst["ones_row"], rv[0:1, :],
                start=True, stop=True),
                reads=(rvr, csts), writes=(rs["P6"],), tag=f"e{st}bc1_{p}")
            sch.add("ve", lambda h: h.tensor_copy(rv, P[6][:, 0:TT]),
                    reads=(rs["P6"],), writes=(rvr,), tag=f"e{st}rvc_{p}")
        ths.append(tail)
        return ths

    def kloop(p):
        rv = rinv1s[p % 2]
        rvr = RV[p % 2]
        for kx in range(34):
            if kx < 32:
                par = kx % 2
                w2of = 4096 + par * 1024
                sch.add("ge", lambda h, p=p, kx=kx, w2of=w2of: h.dma_start(
                    out=WS[:, w2of:w2of + 1024],
                    in_=hap(din["e_W2"], p * 4096 * D + kx * 128 * D,
                            [[D, 128], [1, 1024]])),
                    reads=(), writes=(w2q[par],),
                    dma=("sW" if par == 0 else "sWb"), tag=f"e{st}W2_{p}_{kx}")
                g1c = bcol(cst["e_g1"], p * 32 + kx)
                g17b = SM[:, 360 + p * 32 + kx:
                          361 + p * 32 + kx].broadcast_to([128, TT])
                be1c = bcol(cst["e_be1"], p * 32 + kx)
                sch.add("ve", lambda h, kx=kx, g17b=g17b: h.scalar_tensor_tensor(
                    FB[:, 1536:1536 + TT], hraw(kx), 0.0, g17b,
                    ALU.is_lt, ALU.mult),
                    reads=(HR[kx], csts, smr), writes=(fbr,),
                    tag=f"e{st}t1_{p}_{kx}")
                sch.add("ve", lambda h, kx=kx, g1c=g1c, rv=rv:
                        h.scalar_tensor_tensor(
                    FA[:, 0:TT], hraw(kx), g1c, rv, ALU.mult, ALU.mult),
                    reads=(HR[kx], csts, rvr), writes=(far,),
                    tag=f"e{st}u_{p}_{kx}")
                sch.add("ve", lambda h: h.tensor_add(
                    FA[:, 512:512 + TT], FA[:, 0:TT], FB[:, 1536:1536 + TT]),
                    reads=(far, fbr), writes=(far,), tag=f"e{st}v_{p}_{kx}")
                sch.add("se", lambda h, kx=kx, be1c=be1c: h.activation(
                    hraw(kx), FA[:, 512:512 + TT], AF.Identity,
                    bias=be1c, scale=1.0),
                    reads=(far, csts), writes=(HR[kx],),
                    tag=f"e{st}mp1_{p}_{kx}")
            if kx >= 2:
                k = kx - 2
                par = k % 2
                w2of = 4096 + par * 1024
                for do in range(8):
                    pb = rs[f"P{do}"]
                    sch.add("te", lambda h, k=k, do=do, w2of=w2of: h.matmul(
                        P[do][:, 0:TT],
                        WS[:, w2of + do * 128:w2of + (do + 1) * 128],
                        hraw(k), start=(k == 0), stop=(k == 31)),
                        reads=(w2q[par], HR[k]), writes=(pb,),
                        tag=f"e{st}p2_{p}_{k}_{do}")

    def evac(p):
        s = p % 3
        for dc in range(8):
            pb = rs[f"P{dc}"]
            sch.add("se", lambda h, dc=dc, p=p: h.activation(
                o2(dc), P[dc][:, 0:TT], AF.Identity,
                bias=bcol(cst["e_b2"], p * 8 + dc), scale=1.0),
                reads=(pb, csts), writes=(smbr,), tag=f"e{st}ev2_{p}_{dc}")
            if dc == 0:
                sch.add("te", lambda h, p=p: h.matmul(
                    P[5][:, 0:TT], SMb[:, 5248 + p * 128:5248 + (p + 1) * 128],
                    o2(0), start=True, stop=True),
                    reads=(smbr, csts), writes=(rs["P5"],),
                    tag=f"e{st}rot2_{p}")
                sch.add("se", lambda h: h.activation(
                    o2(0), P[5][:, 0:TT], AF.Identity, bias=0.0, scale=1.0),
                    reads=(rs["P5"],), writes=(smbr,), tag=f"e{st}rev2_{p}")
        for dc in range(8):
            sch.add("se", lambda h, dc=dc, s=s: h.activation(
                feat(s, dc), o2(dc), AF.Square, bias=0.0, scale=1.0),
                reads=(smbr,), writes=(hbc,), tag=f"e{st}sq2_{p}_{dc}")
            sch.add("te", lambda h, dc=dc, s=s: h.matmul(
                P[7][0:1, 0:TT], cst["ones_col_bf"], feat(s, dc),
                start=(dc == 0), stop=(dc == 7)),
                reads=(hbc, csts), writes=(rs["P7"],),
                tag=f"e{st}st2_{p}_{dc}")
        m2row = FB[0:1, 512:512 + TT]
        sch.add("ve", lambda h: h.tensor_scalar(
            m2row, P[7][0:1, 0:TT], 1.0 / 1024.0, None, ALU.mult),
            reads=(rs["P7"],), writes=(fbr,), tag=f"e{st}m2_{p}")
        sch.add("se", lambda h: h.activation(
            m2row, m2row, AF.Ln, bias=EPS6, scale=1.0),
            reads=(fbr, csts), writes=(fbr,), tag=f"e{st}l2_{p}")
        sch.add("se", lambda h: h.activation(
            m2row, m2row, AF.Exp, bias=0.0, scale=-0.5),
            reads=(fbr,), writes=(fbr,), tag=f"e{st}r2_{p}")
        sch.add("te", lambda h: h.matmul(
            P[7][:, 0:TT], cst["ones_row"], m2row, start=True, stop=True),
            reads=(fbr, csts), writes=(rs["P7"],), tag=f"e{st}bc2_{p}")
        sch.add("ve", lambda h: h.tensor_copy(rinv2s, P[7][:, 0:TT]),
                reads=(rs["P7"],), writes=(far,), tag=f"e{st}rv2c_{p}")

    def c_thunks(p, tt):
        s = p % 3
        team = p // 3
        ths = []
        for dc in range(8):
            def mp2(dc=dc):
                g2c = bcol(cst["e_g2"], p * 8 + dc)
                g27b = SM[:, 288 + p * 8 + dc:
                          289 + p * 8 + dc].broadcast_to([128, TT])
                b2pc = SM[:, 144 + p * 8 + dc:145 + p * 8 + dc]
                ctxc = SM[:, 216 + p * 8 + dc:217 + p * 8 + dc]
                sch.add("ve", lambda h, dc=dc, g27b=g27b: h.scalar_tensor_tensor(
                    FB[:, 1536:1536 + TT], o2(dc), 0.0, g27b,
                    ALU.is_lt, ALU.mult),
                    reads=(smbr, smr), writes=(fbr,),
                    tag=f"e{st}t2_{p}_{dc}")
                sch.add("ve", lambda h, dc=dc, g2c=g2c: h.scalar_tensor_tensor(
                    FA[:, 0:TT], o2(dc), g2c, rinv2s, ALU.mult, ALU.mult),
                    reads=(smbr, csts, far), writes=(far,),
                    tag=f"e{st}u2_{p}_{dc}")
                sch.add("ve", lambda h: h.tensor_add(
                    FA[:, 512:512 + TT], FA[:, 0:TT], FB[:, 1536:1536 + TT]),
                    reads=(far, fbr), writes=(far,), tag=f"e{st}v2_{p}_{dc}")
                sch.add("se", lambda h, b2pc=b2pc: h.activation(
                    FA[:, 1536:1536 + TT], FA[:, 512:512 + TT], AF.Identity,
                    bias=b2pc, scale=1.0),
                    reads=(far, smr), writes=(far,), tag=f"e{st}w2_{p}_{dc}")
                sch.add("se", lambda h: h.activation(
                    FB[:, 1536:1536 + TT], FA[:, 1536:1536 + TT], AF.Tanh,
                    bias=0.0, scale=1.0),
                    reads=(far,), writes=(fbr,), tag=f"e{st}th_{p}_{dc}")
                sch.add("ve", lambda h, s=s, dc=dc, ctxc=ctxc:
                        h.scalar_tensor_tensor(
                    feat(s, dc), FB[:, 1536:1536 + TT], ctxc,
                    FA[:, 1536:1536 + TT], ALU.mult, ALU.add),
                    reads=(fbr, far, smr), writes=(hbc,),
                    tag=f"e{st}ft_{p}_{dc}")
                wp = hap(meta["SMb"], 6416 + dc, [[6656, 128], [1, 1]])
                wt = hap(meta["SMb"], 6424 + dc, [[6656, 128], [1, 1]])
                sch.add("te", lambda h, s=s, dc=dc, wp=wp: h.matmul(
                    P[7][0:1, 0:TT], wp, feat(s, dc),
                    start=(dc == 0), stop=(dc == 7)),
                    reads=(hbc, csts), writes=(rs["P7"],),
                    tag=f"e{st}pq_{p}_{dc}")
                sch.add("te", lambda h, s=s, dc=dc, wt=wt: h.matmul(
                    P[4][0:1, 0:TT], wt, feat(s, dc),
                    start=(dc == 0), stop=(dc == 7)),
                    reads=(hbc, csts), writes=(rs["P4"],),
                    tag=f"e{st}qq_{p}_{dc}")
            ths.append(mp2)

        def rows():
            bmv = SM[0:1, 2020 + p:2021 + p]
            sch.add("ve", lambda h, bmv=bmv: h.tensor_scalar(
                FB[0:1, 1536:1536 + TT], P[7][0:1, 0:TT], bmv, None, ALU.add),
                reads=(rs["P7"], csts), writes=(fbr,), tag=f"e{st}psr_{p}")
            sch.add("ve", lambda h: h.tensor_copy(
                FB[0:1, 1024:1024 + TT], P[4][0:1, 0:TT]),
                reads=(rs["P4"],), writes=(fbr,), tag=f"e{st}qr_{p}")
        ths.append(rows)

        for i in range(TT // 128):
            def ptr(i=i):
                sch.add("te", lambda h, i=i: h.transpose(
                    P[4][:, 64 + i:65 + i].bitcast(F32),
                    FB[0:1, 1536 + i * 128:1536 + (i + 1) * 128],
                    cst["ident_f"][0:1, 0:1]),
                    reads=(fbr, csts), writes=(rs["P4"],),
                    tag=f"e{st}psT{p}_{i}")
                sch.add("ve", lambda h, i=i, p=p: h.tensor_copy(
                    SM[:, 648 + i * 32 + p:649 + i * 32 + p],
                    P[4][:, 64 + i:65 + i].bitcast(F32)),
                    reads=(rs["P4"],), writes=(smr,), tag=f"e{st}psE{p}_{i}")
                sch.add("te", lambda h, i=i: h.transpose(
                    P[4][:, 72 + i:73 + i].bitcast(F32),
                    FB[0:1, 1024 + i * 128:1024 + (i + 1) * 128],
                    cst["ident_f"][0:1, 0:1]),
                    reads=(fbr, csts), writes=(rs["P4"],),
                    tag=f"e{st}qT{p}_{i}")
                sch.add("ve", lambda h, i=i, p=p: h.tensor_copy(
                    SM[:, 648 + i * 32 + 16 + p:649 + i * 32 + 16 + p],
                    P[4][:, 72 + i:73 + i].bitcast(F32)),
                    reads=(rs["P4"],), writes=(smr,), tag=f"e{st}qE{p}_{i}")
            ths.append(ptr)

        if s == 2:
            ths.append(lambda: _eidos_team(sch, rs, meta, st, tt, team))
        return ths

    def ln_tile(tt2):
        emit_ln(sch, rs, meta, tt2 * TT, TT, lnxt, EPS5, tag=f"e{st}ln{tt2}",
                sq_off=4096, dst_res="HBa",
                sq_res_fn=lambda d: HR[d])

    for tt in range(NTT):
        t0 = tt * TT
        if tt == 0:
            ln_tile(0)
            for th in a_thunks(0):
                th()
        for p in range(9):
            kloop(p)
            evac(p)
            cth = c_thunks(p, tt)
            if p < 8:
                ath = a_thunks(p + 1)
                j = 0
                ncp = len(cth)
                na = len(ath)
                for i, a in enumerate(ath):
                    a()
                    while j < ncp and j / ncp <= (i + 1) / na:
                        cth[j]()
                        j += 1
                while j < ncp:
                    cth[j]()
                    j += 1
            else:
                for c in cth:
                    c()
                if tt < NTT - 1:
                    ln_tile(tt + 1)
                _eidos_final(sch, rs, meta, st, tt, t0)
        if tt < NTT - 1:
            for th in a_thunks(0):
                th()



def _softmax3_cols(sch, rs, meta, col_fn, out_rows_off, tag):
    SM, FB, P, cst = meta["SM"], meta["FB"], meta["P"], meta["cst"]
    TT = meta["TT"]
    p5, fbr, smr, csts = rs["P5"], rs["FB"], rs["SM"], rs["consts"]
    nch = TT // 128
    for i in range(nch):
        cols = col_fn(i)
        sch.add("ve", lambda h, cols=cols: h.reduce_max(
            SM[:, 795:796], cols, mybir.AxisListType.X),
            reads=(smr,), writes=(smr,), tag=f"{tag}mx{i}")
        sch.add("ve", lambda h: h.tensor_scalar(
            SM[:, 796:797], SM[:, 795:796], -1.0, None, ALU.mult),
            reads=(smr,), writes=(smr,), tag=f"{tag}ng{i}")
        sch.add("se", lambda h, cols=cols: h.activation(
            cols, cols, AF.Exp, bias=SM[:, 796:797], scale=1.0),
            reads=(smr,), writes=(smr,), tag=f"{tag}ex{i}")
        sch.add("ve", lambda h, cols=cols: h.reduce_sum(
            SM[:, 797:798], cols, mybir.AxisListType.X),
            reads=(smr,), writes=(smr,), tag=f"{tag}sm{i}")
        sch.add("ve", lambda h: h.reciprocal(
            SM[:, 798:799], SM[:, 797:798]),
            reads=(smr,), writes=(smr,), tag=f"{tag}rc{i}")
        sch.add("ve", lambda h, cols=cols: h.tensor_scalar(
            cols, cols, SM[:, 798:799], None, ALU.mult),
            reads=(smr,), writes=(smr,), tag=f"{tag}wn{i}")
        for s in range(3):
            sch.add("te", lambda h, cols=cols, s=s: h.transpose(
                P[5][0:1, s * 128:(s + 1) * 128], cols[:, s:s + 1],
                cst["ident_f"]),
                reads=(smr, csts), writes=(p5,), tag=f"{tag}tr{i}_{s}")
            sch.add("ve", lambda h, i=i, s=s: h.tensor_copy(
                FB[0:1, out_rows_off + s * TT + i * 128:
                   out_rows_off + s * TT + (i + 1) * 128],
                P[5][0:1, s * 128:(s + 1) * 128]),
                reads=(p5,), writes=(fbr,), tag=f"{tag}te{i}_{s}")


def _eidos_team(sch, rs, meta, st, tt, team):
    SM, FB, FA, HB, P, cst = (meta["SM"], meta["FB"], meta["FA"], meta["HB"],
                              meta["P"], meta["cst"])
    TT = meta["TT"]
    dram = meta["dram"]
    p5, fbr, smr, far, hbc, csts = (rs["P5"], rs["FB"], rs["SM"], rs["FA"],
                                    rs["HBc"], rs["consts"])
    nch = TT // 128

    def feat(s, dc):
        o = 20480 + s * 8 * TT + dc * TT
        return HB[:, o:o + TT]

    _softmax3_cols(sch, rs, meta,
                   lambda i: SM[:, 648 + i * 32 + 3 * team:648 + i * 32 + 3 * team + 3],
                   0, f"e{st}wi{tt}_{team}_")
    for i in range(nch):
        wi = SM[:, 648 + i * 32 + 3 * team:648 + i * 32 + 3 * team + 3]
        qq = SM[:, 648 + i * 32 + 16 + 3 * team:648 + i * 32 + 16 + 3 * team + 3]
        sch.add("ve", lambda h, wi=wi, qq=qq: h.tensor_mul(
            SM[:, 792:795], wi, qq), reads=(smr,), writes=(smr,),
            tag=f"e{st}tsm{tt}_{team}_{i}")
        sch.add("ve", lambda h: h.reduce_sum(
            SM[:, 795:796], SM[:, 792:795], mybir.AxisListType.X),
            reads=(smr,), writes=(smr,), tag=f"e{st}tss{tt}_{team}_{i}")
        sch.add("ve", lambda h, i=i, team=team: h.tensor_scalar(
            SM[:, 776 + i * 4 + team:777 + i * 4 + team], SM[:, 795:796],
            SM[:, 2030 + team:2031 + team], None, ALU.add),
            reads=(smr, csts), writes=(smr,), tag=f"e{st}tsb{tt}_{team}_{i}")
    for s3 in range(3):
        sch.add("te", lambda h, s3=s3: h.matmul(
            P[5][:, 0:TT], cst["ones_row"], FB[0:1, s3 * TT:(s3 + 1) * TT],
            start=True, stop=True),
            reads=(fbr, csts), writes=(p5,), tag=f"e{st}wib_{tt}_{team}_{s3}")
        for dc in range(8):
            if s3 == 0:
                sch.add("ve", lambda h, s3=s3, dc=dc: h.scalar_tensor_tensor(
                    FA[:, dc * TT:(dc + 1) * TT], feat(s3, dc), 1.0,
                    P[5][:, 0:TT], ALU.mult, ALU.mult),
                    reads=(hbc, p5), writes=(far,), tag=f"e{st}tf0_{tt}_{team}_{dc}")
            else:
                sch.add("ve", lambda h, s3=s3, dc=dc: h.scalar_tensor_tensor(
                    FB[:, 1536:1536 + TT], feat(s3, dc), 1.0, P[5][:, 0:TT],
                    ALU.mult, ALU.mult),
                    reads=(hbc, p5), writes=(fbr,), tag=f"e{st}tfm{s3}_{tt}_{team}_{dc}")
                sch.add("ve", lambda h, dc=dc: h.tensor_add(
                    FA[:, dc * TT:(dc + 1) * TT], FA[:, dc * TT:(dc + 1) * TT],
                    FB[:, 1536:1536 + TT]),
                    reads=(fbr, far), writes=(far,), tag=f"e{st}tfa{s3}_{tt}_{team}_{dc}")
    sch.add("ge", lambda h, team=team: h.dma_start(
        out=dram["sp_tf"][team].rearrange("(dc q) t -> q dc t", dc=8),
        in_=FA[:, 0:8 * TT].rearrange("q (dc t) -> q dc t", dc=8)),
        reads=(far,), writes=(rs["dram"],), dma="sGd", tag=f"e{st}tfsp_{tt}_{team}")


def _eidos_final(sch, rs, meta, st, tt, t0):
    SM, FB, FA, HB, P, X, cst = (meta["SM"], meta["FB"], meta["FA"], meta["HB"],
                                 meta["P"], meta["X"], meta["cst"])
    TT = meta["TT"]
    dram = meta["dram"]
    p5, fbr, smr, far, hbc, csts = (rs["P5"], rs["FB"], rs["SM"], rs["FA"],
                                    rs["HBc"], rs["consts"])
    _softmax3_cols(sch, rs, meta,
                   lambda i: SM[:, 776 + i * 4:776 + i * 4 + 3],
                   0, f"e{st}wt{tt}_")
    for t3 in range(3):
        sch.add("ge", lambda h, t3=t3: h.dma_start(
            out=HB[:, 20480 + t3 * 8 * TT:20480 + (t3 + 1) * 8 * TT].rearrange(
                "q (dc t) -> q dc t", dc=8),
            in_=dram["sp_tf"][t3].rearrange("(dc q) t -> q dc t", dc=8)),
            reads=(rs["dram"],), writes=(rs["HBc"],), dma="sGd",
            tag=f"e{st}tfrl_{tt}_{t3}")
        sch.add("te", lambda h, t3=t3: h.matmul(
            P[5][:, 0:TT], cst["ones_row"], FB[0:1, t3 * TT:(t3 + 1) * TT],
            start=True, stop=True),
            reads=(fbr, csts), writes=(p5,), tag=f"e{st}wtb_{tt}_{t3}")
        for dc in range(8):
            tfb = HB[:, 20480 + t3 * 8 * TT + dc * TT:20480 + t3 * 8 * TT + (dc + 1) * TT]
            if t3 == 0:
                sch.add("ve", lambda h, tfb=tfb, dc=dc: h.scalar_tensor_tensor(
                    FA[:, dc * TT:(dc + 1) * TT], tfb, 1.0, P[5][:, 0:TT],
                    ALU.mult, ALU.mult),
                    reads=(rs["HBc"], p5), writes=(far,), tag=f"e{st}o0_{tt}_{dc}")
            else:
                sch.add("ve", lambda h, tfb=tfb, dc=dc: h.scalar_tensor_tensor(
                    FB[:, 1536:1536 + TT], tfb, 1.0, P[5][:, 0:TT],
                    ALU.mult, ALU.mult),
                    reads=(rs["HBc"], p5), writes=(fbr,), tag=f"e{st}om{t3}_{tt}_{dc}")
                sch.add("ve", lambda h, dc=dc: h.tensor_add(
                    FA[:, dc * TT:(dc + 1) * TT], FA[:, dc * TT:(dc + 1) * TT],
                    FB[:, 1536:1536 + TT]),
                    reads=(fbr, far), writes=(far,), tag=f"e{st}oa{t3}_{tt}_{dc}")
    for dc in range(8):
        sch.add("ve", lambda h, dc=dc, t0=t0: h.scalar_tensor_tensor(
            X[dc][:, t0:t0 + TT], FA[:, dc * TT:(dc + 1) * TT], 0.5,
            X[dc][:, t0:t0 + TT], ALU.mult, ALU.add),
            reads=(far,), writes=(rs[f"X{dc}"],), tag=f"e{st}res_{tt}_{dc}")


def emit_l1_attn(sch, rs, meta):
    nc = sch.nc
    din, X, P, HB, WS, SM, SMb, FA, FB, cst = (
        meta["din"], meta["X"], meta["P"], meta["HB"], meta["WS"], meta["SM"],
        meta["SMb"], meta["FA"], meta["FB"], meta["cst"])
    T, TT, NTT = meta["T"], meta["TT"], meta["NTT"]
    dram = meta["dram"]
    csts = rs["consts"]
    wsq = [rs["WS0"], rs["WS1"]]
    hba, hbb, hbc = rs["HBa"], rs["HBb"], rs["HBc"]
    far, fbr, smr, smbr = rs["FA"], rs["FB"], rs["SM"], rs["SMb"]

    def lnxT(d):
        return HB[:, d * TT:(d + 1) * TT]

    def xcv(ch):
        return HB[:, 4096 + ch * TT:4096 + (ch + 1) * TT]

    def zsT(ch):
        return HB[:, 12288 + ch * TT:12288 + (ch + 1) * TT]

    def ygT(ch):
        return HB[:, 20480 + ch * TT:20480 + (ch + 1) * TT]

    Cmb = lambda c0, n: SMb[64:80, c0:c0 + n]
    hmT = lambda c0, n: SMb[64:80, 2048 + c0:2048 + c0 + n]
    carries = SM[:, 512:560]
    hpbuf = [SM[:, 560:816], SM[:, 816:1072]]
    last3 = SM[:, 1072:1120]
    dBr = FB[:, 640:640 + TT]
    dltr = FB[0:1, 1152:1152 + TT]

    sch.add("ge", lambda h: h.dma_start(
        out=WS[:, 4096:4624].rearrange("q (k c) -> q k c", k=16),
        in_=din["l1_W_xproj"].ap().rearrange("(k q) c -> q k c", k=16)),
        reads=(), writes=(rs["WS2a"],), dma="sW", tag="l1wxp")

    for tt in range(NTT):
        t0 = tt * TT
        emit_ln(sch, rs, meta, t0, TT, lnxT, EPS5, tag=f"l1ln{tt}",
                sq_off=24576, sq_res="HBc", dst_res="HBa", f32_scr=2048)
        sch.add("sy", lambda h, t0=t0: h.dma_start(
            out=dram["sp_lnx4"][:, t0:t0 + TT].rearrange("(d q) t -> q d t", d=DC),
            in_=HB[:, 0:DC * TT].rearrange("q (d t) -> q d t", d=DC)),
            reads=(hba,), writes=(rs["dram"],), dma="sSp", tag=f"l1lsp{tt}")
        for xh in range(2):
            for k in range(DC):
                par = k % 2
                wof = par * 1024
                sch.add("ge", lambda h, k=k, xh=xh, wof=wof: h.dma_start(
                    out=WS[:, wof:wof + 1024],
                    in_=hap(din["l1_W_in"], k * 128 * 4096 + xh * 1024,
                            [[4096, 128], [1, 1024]])),
                    reads=(), writes=(wsq[par],),
                    dma=("sW" if par == 0 else "sWb"), tag=f"l1wp{tt}_{xh}_{k}")
                for f in range(8):
                    sch.add("te", lambda h, k=k, f=f, wof=wof, par=par: h.matmul(
                        P[f][:, 0:TT], WS[:, wof + f * 128:wof + (f + 1) * 128],
                        lnxT(k), start=(k == 0), stop=(k == DC - 1)),
                        reads=(wsq[par], hba), writes=(rs[f"P{f}"],),
                        tag=f"l1mp{tt}_{xh}_{k}_{f}")
            for f in range(8):
                ch = xh * 8 + f
                sch.add("se", lambda h, f=f, ch=ch: h.activation(
                    xcv(ch), P[f][:, 0:TT], AF.Identity, bias=0.0, scale=1.0),
                    reads=(rs[f"P{f}"],), writes=(hbb,), tag=f"l1xpe{tt}_{ch}")
                if tt == NTT - 1:
                    sch.add("ve", lambda h, ch=ch: h.tensor_copy(
                        last3[:, ch * 3:(ch + 1) * 3], xcv(ch)[:, TT - 3:TT]),
                        reads=(hbb,), writes=(smr,), tag=f"l1l3{tt}_{ch}")
        sch.add("sy", lambda h, t0=t0: h.dma_start(
            out=dram["sp_xconv"][:, t0:t0 + TT].rearrange(
                "(c q) t -> q c t", c=16),
            in_=HB[:, 4096:4096 + 16 * TT].rearrange("q (c t) -> q c t", c=16)),
            reads=(hbb,), writes=(rs["dram"],), dma="sSp", tag=f"l1psp{tt}")
    sch.add("sy", lambda h: h.dma_start(
        out=hap(dram["agB_in"], 0, [[48, 128], [1, 48]]), in_=last3),
        reads=(smr,), writes=(rs["dram"],), dma="sSp", tag="l1agBw")
    sch.add("ge", lambda h: h.collective_compute(
        "AllGather", ALU.bypass, replica_groups=[list(range(NCORES))],
        ins=[dram["agB_in"].ap().opt()], outs=[dram["agB_out"].ap().opt()]),
        reads=(rs["dram"],), writes=(rs["dram"],), tag="l1agB")
    for j in range(NCORES):
        sch.add("sy", lambda h, j=j: h.dma_start(
            out=FA[:, 2048 + (j % 2) * 48:2048 + (j % 2) * 48 + 48],
            in_=hap(dram["agB_out"], j * 6144, [[48, 128], [1, 48]])),
            reads=(rs["dram"],), writes=(far,), dma="sSp", tag=f"l1hl{j}")
        if j == 0:
            sch.add("ve", lambda h, j=j: h.tensor_scalar(
                carries, FA[:, 2048:2096], cst["prevsel"][:, j:j + 1], None,
                ALU.mult),
                reads=(far, csts), writes=(smr,), tag=f"l1hm{j}")
        else:
            sch.add("ve", lambda h, j=j: h.scalar_tensor_tensor(
                carries, FA[:, 2048 + (j % 2) * 48:2048 + (j % 2) * 48 + 48],
                cst["prevsel"][:, j:j + 1], carries, ALU.mult, ALU.add),
                reads=(far, csts, smr), writes=(smr,), tag=f"l1hm{j}")
    for tt in range(NTT):
        t0 = tt * TT
        sch.add("sy", lambda h, t0=t0: h.dma_start(
            out=HB[:, 4096:4096 + 16 * TT].rearrange("q (c t) -> q c t", c=16),
            in_=dram["sp_xconv"][:, t0:t0 + TT].rearrange(
                "(c q) t -> q c t", c=16)),
            reads=(rs["dram"],), writes=(hbb,), dma="sSp", tag=f"l1prl{tt}")
        for ch in range(16):
            ext = FB[:, 0:TT + 3]
            sch.add("ve", lambda h, ch=ch: h.tensor_copy(
                ext[:, 0:3], carries[:, ch * 3:(ch + 1) * 3]),
                reads=(smr,), writes=(fbr,), tag=f"l1cx{tt}_{ch}")
            sch.add("ve", lambda h, ch=ch: h.tensor_copy(
                ext[:, 3:TT + 3], xcv(ch)),
                reads=(hbb,), writes=(fbr,), tag=f"l1cy{tt}_{ch}")
            sch.add("ve", lambda h, ch=ch: h.tensor_copy(
                carries[:, ch * 3:(ch + 1) * 3], ext[:, TT:TT + 3]),
                reads=(fbr,), writes=(smr,), tag=f"l1cc{tt}_{ch}")
            w = lambda tap, ch=ch: cst["l1_conv_w"][:, ch * 4 + tap:ch * 4 + tap + 1]
            a1 = FB[:, 640:640 + TT]
            a2 = FB[:, 1152:1152 + TT]
            sch.add("ve", lambda h, ch=ch, w=w: h.tensor_scalar(
                a1, ext[:, 0:TT], w(0), None, ALU.mult),
                reads=(fbr, csts), writes=(fbr,), tag=f"l1c0{tt}_{ch}")
            sch.add("ve", lambda h, ch=ch, w=w: h.scalar_tensor_tensor(
                a2, ext[:, 1:TT + 1], w(1), a1, ALU.mult, ALU.add),
                reads=(fbr, csts), writes=(fbr,), tag=f"l1c1{tt}_{ch}")
            sch.add("ve", lambda h, ch=ch, w=w: h.scalar_tensor_tensor(
                a1, ext[:, 2:TT + 2], w(2), a2, ALU.mult, ALU.add),
                reads=(fbr, csts), writes=(fbr,), tag=f"l1c2{tt}_{ch}")
            sch.add("ve", lambda h, ch=ch, w=w: h.scalar_tensor_tensor(
                a2, ext[:, 3:TT + 3], w(3), a1, ALU.mult, ALU.add),
                reads=(fbr, csts), writes=(fbr,), tag=f"l1c3{tt}_{ch}")
            sch.add("se", lambda h, ch=ch: h.activation(
                xcv(ch), a2, AF.Silu, bias=cst["l1_conv_b"][:, ch:ch + 1],
                scale=1.0),
                reads=(fbr, csts), writes=(hbb,), tag=f"l1cs{tt}_{ch}")
        for k in range(16):
            sch.add("te", lambda h, k=k: h.matmul(
                P[7][0:1, 0:TT], WS[:, 4096 + k * 33:4096 + k * 33 + 1],
                xcv(k), start=(k == 0), stop=(k == 15)),
                reads=(rs["WS2a"], hbb), writes=(rs["P7"],), tag=f"l1sd{tt}_{k}")
            sch.add("te", lambda h, k=k: h.matmul(
                P[5][0:16, 0:TT], WS[:, 4096 + k * 33 + 1:4096 + k * 33 + 17],
                xcv(k), start=(k == 0), stop=(k == 15)),
                reads=(rs["WS2a"], hbb), writes=(rs["P5"],), tag=f"l1sb{tt}_{k}")
            sch.add("te", lambda h, k=k: h.matmul(
                P[3][64:80, 0:TT], WS[:, 4096 + k * 33 + 17:4096 + k * 33 + 33],
                xcv(k), start=(k == 0), stop=(k == 15)),
                reads=(rs["WS2a"], hbb), writes=(rs["P3"],), tag=f"l1sc{tt}_{k}")
        sch.add("se", lambda h: h.activation(
            dltr, P[7][0:1, 0:TT], AF.Exp, bias=0.0, scale=1.0),
            reads=(rs["P7"],), writes=(fbr,), tag=f"l1de{tt}")
        sch.add("ve", lambda h: h.tensor_scalar(
            dltr, dltr, 1.0, None, ALU.add), reads=(fbr,), writes=(fbr,),
            tag=f"l1d1{tt}")
        sch.add("se", lambda h: h.activation(
            dltr, dltr, AF.Ln, bias=0.0, scale=1.0),
            reads=(fbr,), writes=(fbr,), tag=f"l1d2{tt}")
        sch.add("ve", lambda h, t0=t0: h.tensor_copy(
            Cmb(t0, TT), P[3][64:80, 0:TT]),
            reads=(rs["P3"],), writes=(smbr,), tag=f"l1cme{tt}")
        sch.add("te", lambda h: h.matmul(
            P[4][0:16, 0:TT], cst["ones_row"][:, 0:16], dltr,
            start=True, stop=True),
            reads=(fbr, csts), writes=(rs["P4"],), tag=f"l1db{tt}")
        sch.add("ve", lambda h: h.tensor_tensor(
            dBr[0:16, :], P[5][0:16, 0:TT], P[4][0:16, 0:TT], ALU.mult)
            if False else h.tensor_copy(FB[0:16, 1664:1664 + 256]
                                        if False else dBr[0:16, :],
                                        P[5][0:16, 0:TT]),
            reads=(rs["P5"],), writes=(fbr,), tag=f"l1bme{tt}")
        sch.add("ve", lambda h: h.tensor_tensor(
            dBr[0:16, :], dBr[0:16, :], P[4][0:16, 0:TT], ALU.mult),
            reads=(fbr, rs["P4"]), writes=(fbr,), tag=f"l1dB{tt}")
        for i in range(TT // 128):
            sch.add("te", lambda h, i=i: h.transpose(
                P[2][:, 0:16], dBr[0:16, i * 128:(i + 1) * 128],
                cst["ident_f"][0:16, 0:16]),
                reads=(fbr, csts), writes=(rs["P2"],), tag=f"l1dT{tt}_{i}")
            sch.add("ve", lambda h, i=i: h.tensor_copy(
                FB[:, 1664:1680], P[2][:, 0:16]),
                reads=(rs["P2"],), writes=(fbr,), tag=f"l1dTe{tt}_{i}")
            sch.add("ge", lambda h: h.dma_start(
                out=FB[:, 1680:1696].bitcast(BF16)[:, 0:16],
                in_=FB[:, 1664:1680]),
                reads=(fbr,), writes=(fbr,), dma="sGd", tag=f"l1dTb{tt}_{i}")
            for ch in range(16):
                sch.add("te", lambda h, i=i, ch=ch: h.transpose(
                    P[2][:, 128 + (ch % 2) * 64:128 + (ch % 2) * 64 + 64].bitcast(BF16),
                    xcv(ch)[:, i * 128:(i + 1) * 128], cst["ident"]),
                    reads=(hbb, csts), writes=(rs["P2"],), tag=f"l1xT{tt}_{i}_{ch}")
                sch.add("ve", lambda h, i=i, ch=ch: h.tensor_copy(
                    SMb[:, 4224 + (ch % 2) * 128:4224 + (ch % 2) * 128 + 128],
                    P[2][:, 128 + (ch % 2) * 64:128 + (ch % 2) * 64 + 64].bitcast(BF16)),
                    reads=(rs["P2"],), writes=(smbr,), tag=f"l1xTe{tt}_{i}_{ch}")
                sch.add("te", lambda h, i=i, ch=ch, tt=tt: h.matmul(
                    P[6][:, ch * 16:(ch + 1) * 16],
                    SMb[:, 4224 + (ch % 2) * 128:4224 + (ch % 2) * 128 + 128],
                    FB[:, 1680:1696].bitcast(BF16)[:, 0:16],
                    start=True, stop=True),
                    reads=(smbr, fbr), writes=(rs["P6"],),
                    tag=f"l1hp{tt}_{i}_{ch}")
            if tt == 0 and i == 0:
                sch.add("ve", lambda h: h.tensor_copy(hpbuf[0], P[6][:, 0:256]),
                        reads=(rs["P6"],), writes=(smr,), tag=f"l1hpa{tt}_{i}")
            else:
                sch.add("ve", lambda h: h.tensor_add(
                    hpbuf[0], hpbuf[0], P[6][:, 0:256]),
                    reads=(rs["P6"], smr), writes=(smr,), tag=f"l1hpa{tt}_{i}")
    sch.add("sy", lambda h: h.dma_start(
        out=hap(dram["agC_in"], 0, [[256, 128], [1, 256]]), in_=hpbuf[0]),
        reads=(smr,), writes=(rs["dram"],), dma="sSp", tag="l1agCw")
    sch.add("ge", lambda h: h.collective_compute(
        "AllGather", ALU.bypass, replica_groups=[list(range(NCORES))],
        ins=[dram["agC_in"].ap().opt()], outs=[dram["agC_out"].ap().opt()]),
        reads=(rs["dram"],), writes=(rs["dram"],), tag="l1agC")
    for j in range(NCORES):
        sch.add("sy", lambda h, j=j: h.dma_start(
            out=FA[:, 2048 + (j % 2) * 256:2048 + (j % 2) * 256 + 256],
            in_=hap(dram["agC_out"], j * 32768, [[256, 128], [1, 256]])),
            reads=(rs["dram"],), writes=(far,), dma="sSp", tag=f"l1hc{j}")
        if j == 0:
            sch.add("ve", lambda h: h.tensor_copy(hpbuf[0], FA[:, 2048:2304]),
                    reads=(far,), writes=(smr,), tag="l1hs0")
        else:
            sch.add("ve", lambda h, j=j: h.tensor_add(
                hpbuf[j % 2], hpbuf[(j + 1) % 2],
                FA[:, 2048 + (j % 2) * 256:2048 + (j % 2) * 256 + 256]),
                reads=(far, smr), writes=(smr,), tag=f"l1hs{j}")
    hm = hpbuf[(NCORES - 1) % 2]
    sch.add("ve", lambda h: h.tensor_scalar(
        hm, hm, 1.0 / (NCORES * T), None, ALU.mult),
        reads=(smr,), writes=(smr,), tag="l1hdiv")
    for ch in range(16):
        sch.add("te", lambda h, ch=ch: h.transpose(
            P[2][0:16, (ch % 4) * 128:(ch % 4) * 128 + 128],
            hm[:, ch * 16:(ch + 1) * 16], cst["ident_f"]),
            reads=(smr, csts), writes=(rs["P2"],), tag=f"l1hT{ch}")
        sch.add("ve", lambda h, ch=ch: h.tensor_copy(
            hmT(ch * 128, 128), P[2][0:16, (ch % 4) * 128:(ch % 4) * 128 + 128]),
            reads=(rs["P2"],), writes=(smbr,), tag=f"l1hTe{ch}")
    for tt in range(NTT):
        t0 = tt * TT
        sch.add("sy", lambda h, t0=t0: h.dma_start(
            out=HB[:, 0:DC * TT].rearrange("q (d t) -> q d t", d=DC),
            in_=dram["sp_lnx4"][:, t0:t0 + TT].rearrange("(d q) t -> q d t", d=DC)),
            reads=(rs["dram"],), writes=(hba,), dma="sSp", tag=f"l1lrl{tt}")
        sch.add("sy", lambda h, t0=t0: h.dma_start(
            out=HB[:, 4096:4096 + 16 * TT].rearrange("q (c t) -> q c t", c=16),
            in_=dram["sp_xconv"][:, t0:t0 + TT].rearrange(
                "(c q) t -> q c t", c=16)),
            reads=(rs["dram"],), writes=(hbb,), dma="sSp", tag=f"l1prl2{tt}")
        for zh in range(2):
            for k in range(DC):
                par = k % 2
                wof = par * 1024
                sch.add("ge", lambda h, k=k, zh=zh, wof=wof: h.dma_start(
                    out=WS[:, wof:wof + 1024],
                    in_=hap(din["l1_W_in"], k * 128 * 4096 + 2048 + zh * 1024,
                            [[4096, 128], [1, 1024]])),
                    reads=(), writes=(wsq[par],),
                    dma=("sW" if par == 0 else "sWb"), tag=f"l1wz{tt}_{zh}_{k}")
                for f in range(8):
                    sch.add("te", lambda h, k=k, f=f, wof=wof, par=par: h.matmul(
                        P[f][:, 0:TT], WS[:, wof + f * 128:wof + (f + 1) * 128],
                        lnxT(k), start=(k == 0), stop=(k == DC - 1)),
                        reads=(wsq[par], hba), writes=(rs[f"P{f}"],),
                        tag=f"l1mz{tt}_{zh}_{k}_{f}")
            for f in range(8):
                ch = zh * 8 + f
                sch.add("se", lambda h, f=f, ch=ch: h.activation(
                    zsT(ch), P[f][:, 0:TT], AF.Silu, bias=0.0, scale=1.0),
                    reads=(rs[f"P{f}"],), writes=(hbb,), tag=f"l1zs{tt}_{ch}")
        for ch in range(16):
            pi = ch % 4
            sch.add("te", lambda h, ch=ch, t0=t0, pi=pi: h.matmul(
                P[pi][:, 0:TT], hmT(ch * 128, 128), Cmb(t0, TT),
                start=True, stop=True),
                reads=(smbr,), writes=(rs[f"P{pi}"],), tag=f"l1ym{tt}_{ch}")
            sch.add("ve", lambda h, ch=ch, pi=pi: h.scalar_tensor_tensor(
                FA[:, 2048:2048 + TT], xcv(ch), cst["l1_D"][:, ch:ch + 1],
                P[pi][:, 0:TT], ALU.mult, ALU.add),
                reads=(hbb, csts, rs[f"P{pi}"]), writes=(far,),
                tag=f"l1yd{tt}_{ch}")
            sch.add("ve", lambda h, ch=ch: h.tensor_tensor(
                ygT(ch), FA[:, 2048:2048 + TT], zsT(ch), ALU.mult),
                reads=(far, hbb), writes=(hbc,), tag=f"l1yg{tt}_{ch}")
        for k in range(16):
            par = k % 2
            wof = par * 1024
            sch.add("ge", lambda h, k=k, wof=wof: h.dma_start(
                out=WS[:, wof:wof + 1024],
                in_=din["l1_W_out"][k * 128:(k + 1) * 128, :]),
                reads=(), writes=(wsq[par],),
                dma=("sW" if par == 0 else "sWb"), tag=f"l1wo{tt}_{k}")
            for do in range(8):
                sch.add("te", lambda h, k=k, do=do, wof=wof, par=par: h.matmul(
                    P[do][:, 0:TT], WS[:, wof + do * 128:wof + (do + 1) * 128],
                    ygT(k), start=(k == 0), stop=(k == 15)),
                    reads=(wsq[par], hbc), writes=(rs[f"P{do}"],),
                    tag=f"l1wm{tt}_{k}_{do}")
        for do in range(8):
            sch.add("ve", lambda h, do=do, t0=t0: h.tensor_tensor(
                X[do][:, t0:t0 + TT], X[do][:, t0:t0 + TT], P[do][:, 0:TT],
                ALU.add),
                reads=(rs[f"P{do}"],), writes=(rs[f"X{do}"],),
                tag=f"l1res{tt}_{do}")



def _bcdt_perm():
    b = np.arange(32)
    bp = np.concatenate([b[0:16:2], b[1:16:2], b[16:]])
    return np.concatenate([bp, 32 + bp, 64 + np.arange(32)])


def prepare_inputs(inputs, T_core, ncores=NCORES):
    f32 = np.float32
    x = np.asarray(inputs["L0_attn_x"], f32).reshape(-1, D)

    g = {}
    g["wg_t"] = np.ascontiguousarray(np.asarray(inputs["L0_ffn_w_gate"], np.int32).T)
    g["wu_t"] = np.ascontiguousarray(np.asarray(inputs["L0_ffn_w_up"], np.int32).T)
    g["wd_t"] = np.ascontiguousarray(np.asarray(inputs["L0_ffn_w_down"], np.int32).T)

    g["l0_ln_w"] = np.ascontiguousarray(np.asarray(inputs["L0_attn_ln_w"], f32).reshape(DC, 128).T)
    g["l0_ln_b"] = np.ascontiguousarray(np.asarray(inputs["L0_attn_ln_b"], f32).reshape(DC, 128).T)
    W_in = np.asarray(inputs["L0_attn_W_in"], f32)
    perm = np.concatenate([np.arange(4096), 4096 + _bcdt_perm()])
    g["l0_W_in"] = np.ascontiguousarray(W_in[:, perm])
    g["l0_dt_bias"] = np.asarray(inputs["L0_attn_dt_bias"], f32).reshape(32, 1)
    g["l0_A"] = np.asarray(inputs["L0_attn_A_log"], f32).reshape(32, 1)
    g["l0_D"] = np.ascontiguousarray(
        np.repeat(np.asarray(inputs["L0_attn_D"], f32), 64).reshape(16, 128).T)
    g["l0_rms_w"] = np.ascontiguousarray(
        np.asarray(inputs["L0_attn_rms_w"], f32).reshape(16, 128).T)
    g["l0_W_out"] = np.ascontiguousarray(np.asarray(inputs["L0_attn_W_out"], f32))

    g["l1_W_in"] = np.ascontiguousarray(np.asarray(inputs["L1_attn_W_in"], f32))
    cw = np.asarray(inputs["L1_attn_conv_w"], f32).reshape(2048, 4)
    g["l1_conv_w"] = np.ascontiguousarray(
        cw.reshape(16, 128, 4).transpose(1, 0, 2).reshape(128, 64))
    g["l1_conv_b"] = np.ascontiguousarray(
        np.asarray(inputs["L1_attn_conv_b"], f32).reshape(16, 128).T)
    g["l1_W_xproj"] = np.ascontiguousarray(np.asarray(inputs["L1_attn_W_xproj"], f32))
    g["l1_D"] = np.ascontiguousarray(np.asarray(inputs["L1_attn_D"], f32).reshape(16, 128).T)
    g["l1_W_out"] = np.ascontiguousarray(np.asarray(inputs["L1_attn_W_out"], f32))

    g["e_W1"] = np.ascontiguousarray(np.asarray(inputs["L1_ffn_W1"], f32))
    g["e_W2"] = np.ascontiguousarray(np.asarray(inputs["L1_ffn_W2"], f32))
    for nm, src, nch in [("e_b1", "L1_ffn_b1", 32), ("e_g1", "L1_ffn_g1", 32),
                         ("e_be1", "L1_ffn_be1", 32), ("e_b2", "L1_ffn_b2", 8),
                         ("e_g2", "L1_ffn_g2", 8), ("e_be2", "L1_ffn_be2", 8),
                         ("e_pb", "L1_ffn_path_bias", 8), ("e_ctx", "L1_ffn_context", 8)]:
        a = np.asarray(inputs[src], f32)
        g[nm] = np.ascontiguousarray(a.reshape(9, nch, 128).transpose(0, 2, 1))
    g["e_th1"] = np.asarray(inputs["L1_ffn_th1"], f32).copy()
    g["e_th2"] = np.asarray(inputs["L1_ffn_th2"], f32).copy()
    g["e_wpath"] = np.ascontiguousarray(np.asarray(inputs["L1_ffn_w_path"], f32).reshape(8, 128).T)
    g["e_wteam"] = np.ascontiguousarray(np.asarray(inputs["L1_ffn_w_team"], f32).reshape(8, 128).T)
    g["e_bp"] = np.broadcast_to(np.asarray(inputs["L1_ffn_b_path"], f32).reshape(1, 1), (9, 1)).copy()
    g["e_ovp"] = np.asarray(inputs["L1_ffn_ov_path"], f32).reshape(9, 1).copy()
    g["e_bt"] = np.broadcast_to(np.asarray(inputs["L1_ffn_b_team"], f32).reshape(1, 1), (3, 1)).copy()
    g["e_ovt"] = np.asarray(inputs["L1_ffn_ov_team"], f32).reshape(3, 1).copy()

    g["ones_col"] = np.ones((128, 1), f32)
    g["ones_col_bf"] = np.ones((128, 1), ml_dtypes.bfloat16)
    g["ones_row"] = np.ones((1, 128), f32)
    g["ones_row_bf"] = np.ones((1, 128), ml_dtypes.bfloat16)
    g["ident_bf"] = np.eye(128, dtype=ml_dtypes.bfloat16)
    g["ident_f"] = np.eye(128, dtype=f32)
    g["e2x"] = ((np.arange(8)[None, :] >> 1) == np.arange(4)[:, None]).astype(f32)
    g["altsign"] = (1.0 - 2.0 * (np.arange(8) % 2)).astype(f32).reshape(8, 1)
    g["pswap"] = (np.arange(128)[None, :] == (np.arange(8)[:, None] ^ 1)).astype(f32)
    ip = np.eye(128, dtype=np.float32); ip[:8, :8] = 0
    g["ipad"] = ip.astype(ml_dtypes.bfloat16)
    g["ones_T"] = np.ones((1, 4096), f32)
    nid = np.zeros((32, 2048), f32)
    for hh in range(32):
        nid[hh, hh * 64:(hh + 1) * 64] = -1.0
    g["negid"] = nid
    g["tril8"] = np.tril(np.ones((8, 8), f32))
    g["a2b_bf"] = (np.arange(32)[:, None] % 2 == (np.arange(128)[None, :] >> 6)
                   ).astype(ml_dtypes.bfloat16)
    g["indj2"] = ((np.arange(32)[:, None] >> 1) == np.arange(16)[None, :]).astype(f32)
    g["ones32"] = np.ones((32, 4096), f32)
    g["zeros32"] = np.zeros((32, 2048), f32)
    g["a2b_f"] = (np.arange(32)[:, None] % 2 == (np.arange(128)[None, :] >> 6)).astype(f32)
    m64b = np.where(np.arange(64)[:, None] <= np.arange(64)[None, :], 0.0, NEG).astype(f32)
    g["mask_d2"] = np.ascontiguousarray(np.tile(m64b, (1, 8)))
    i64 = np.zeros((64, 128), f32); i64[:, :64] = np.eye(64)
    g["ident64f"] = i64
    g["ident64b"] = np.eye(64, dtype=ml_dtypes.bfloat16)
    g["inds_c"] = np.repeat(
        ((np.arange(32)[:, None] >> 1) == np.arange(16)[None, :]).astype(f32),
        32, axis=1)
    m64 = np.where(np.arange(64)[:, None] <= np.arange(64)[None, :], 0.0, NEG).astype(f32)
    g["mask_d"] = np.ascontiguousarray(np.tile(m64, (1, 8)))
    e2 = np.zeros((2, 128), ml_dtypes.bfloat16); e2[0, :64] = 1; e2[1, 64:] = 1
    g["e2a"] = e2
    g["indj"] = ((np.arange(32)[:, None] >> 1) == np.arange(16)[None, :]).astype(f32)
    g["eps_tab"] = np.broadcast_to(
        np.asarray([EPS5, EPS6, np.pi / 2], f32)[None, :], (128, 3)).copy()

    cores = []
    L_total = x.shape[0]
    t_all = np.arange(L_total, dtype=f32)
    inv_freq = (1.0 / 10000.0 ** (np.arange(0, 16, 2, dtype=f32) / 16.0)).astype(f32)
    for cid in range(ncores):
        d = dict(g)
        sl = slice(cid * T_core, (cid + 1) * T_core)
        d["x_in"] = np.ascontiguousarray(x[sl])
        ang = t_all[sl][None, :] * inv_freq[:, None]
        d["rope_cos"] = np.cos(ang).astype(f32)
        d["rope_sin"] = np.sin(ang).astype(f32)
        lt = np.zeros((32, 8), f32); lt[:, :cid] = 1.0
        d["core_lt"] = lt
        oh = np.zeros((1, 8), f32)
        if cid > 0:
            oh[0, cid - 1] = 1.0
        d["prev_oh"] = oh
        d["prevsel"] = np.broadcast_to(oh, (128, 8)).copy()
        ltc = np.zeros((8, 1), f32); ltc[:cid] = 1.0
        d["lt_col"] = ltc
        cores.append(d)
    return cores


def kernel(**inputs):
    T_core = 16384 // NCORES
    nc, meta = build_program(T_core)
    in_maps = prepare_inputs(inputs, T_core)
    res = run_bass_kernel_spmd(nc, in_maps, core_ids=list(range(NCORES)))
    out = np.concatenate([np.asarray(r["out"]) for r in res.results], axis=0)
    return out.reshape(1, 16384, D)


# version 03
# speedup vs baseline: 1.0220x; 1.0220x over previous
import numpy as np
import ml_dtypes
import concourse.bass as bass
import concourse.mybir as mybir
from concourse.bass import AP
from concourse.bass_utils import run_bass_kernel_spmd

F32 = mybir.dt.float32
BF16 = mybir.dt.bfloat16
I32 = mybir.dt.int32
AF = mybir.ActivationFunctionType
ALU = mybir.AluOpType

NCORES = 8
D = 1024
DC = D // 128
CH = 64
NEG = -1e30
EPS5 = 1e-5
EPS6 = 1e-6
RT127 = 1.0 / 127.0

ENGS = ("te", "ve", "se", "ge", "sy")
ESEM = {"te": "sTE", "ve": "sV", "se": "sS", "ge": "sG", "sy": "sSY"}


class Res:
    __slots__ = ("name", "last_w", "readers")

    def __init__(self, name):
        self.name = name
        self.last_w = None
        self.readers = []


class Op:
    __slots__ = ("eng", "fn", "deps", "dma_sem", "seq", "pub", "waits", "tag")

    def __init__(self, eng, fn, deps, dma_sem, tag):
        self.eng = eng
        self.fn = fn
        self.deps = deps
        self.dma_sem = dma_sem
        self.seq = None
        self.pub = None
        self.waits = []
        self.tag = tag


class Sched:

    def __init__(self, nc):
        self.nc = nc
        self.ops = []
        self.per_eng = {e: [] for e in ENGS}
        self.dma_seq = {}
        self.pub_cnt = {e: 0 for e in ENGS}

    def res(self, name):
        return Res(name)

    def add(self, eng, fn, reads=(), writes=(), dma=None, tag=""):
        deps = []
        for r in reads:
            if r.last_w is not None:
                deps.append(r.last_w)
        for w in writes:
            if w.last_w is not None:
                deps.append(w.last_w)
            deps.extend(w.readers)
        deps = list({id(d): d for d in deps}.values())
        o = Op(eng, fn, deps, dma, tag)
        o.seq = len(self.per_eng[eng])
        self.per_eng[eng].append(o)
        self.ops.append(o)
        for r in reads:
            r.readers.append(o)
        for w in writes:
            w.last_w = o
            w.readers = []
        return o

    DMA_POOL = {"sW": 2, "sWb": 2, "sIO": 4, "sSp": 4, "sGd": 4}

    def finalize(self):
        rrobin = {}
        dma_counts = {}
        prev_on_sem = {}
        for o in self.ops:
            if o.dma_sem is not None:
                ch = o.dma_sem
                n = self.DMA_POOL[ch]
                idx = rrobin.get(ch, 0)
                rrobin[ch] = (idx + 1) % n
                ps = f"{ch}{idx}"
                dma_counts[ps] = dma_counts.get(ps, 0) + 1
                o.pub = (ps, dma_counts[ps] * 16)
                if ps in prev_on_sem:
                    o.deps.append(prev_on_sem[ps])
                prev_on_sem[ps] = o
        need = set()
        for o in self.ops:
            for d in o.deps:
                if d.dma_sem is None:
                    need.add(id(d))
        pub_idx = {e: 0 for e in ENGS}
        for o in self.ops:
            if o.dma_sem is None and id(o) in need:
                pub_idx[o.eng] += 1
                o.pub = (ESEM[o.eng], pub_idx[o.eng])
        hw = {e: {} for e in ENGS}
        for o in self.ops:
            want = {}
            for d in o.deps:
                if d.dma_sem is not None:
                    s, v = d.pub
                    want[s] = max(want.get(s, 0), v)
                elif d.eng != o.eng:
                    if d.pub is None:
                        raise RuntimeError(f"dep of {o.tag} not published: {d.tag}")
                    s, v = d.pub
                    want[s] = max(want.get(s, 0), v)
                elif d.pub is not None:
                    s, v = d.pub
                    want[s] = max(want.get(s, 0), v)
            o.waits = []
            for s, v in want.items():
                if hw[o.eng].get(s, 0) < v:
                    o.waits.append((s, v))
                    hw[o.eng][s] = v

    def emit(self, c):
        self.finalize()
        nc = self.nc
        per_eng = self.per_eng

        def body(eng_name):
            def run(h):
                for o in per_eng[eng_name]:
                    for s, v in o.waits:
                        h.wait_ge(c.sem[s], v)
                    ins = o.fn(h)
                    if o.pub is not None:
                        if ins is None:
                            raise RuntimeError(f"op {o.tag} published but emitted None")
                        ins.then_inc(c.sem[o.pub[0]], 16 if o.dma_sem else 1)
            return run

        with nc.Block() as blk:
            blk.tensor(body("te"))
            blk.vector(body("ve"))
            blk.scalar(body("se"))
            blk.gpsimd(body("ge"))
            blk.sync(body("sy"))


class Ctx:
    def __init__(self, nc):
        self.nc = nc
        self.sem = {}

    def new_sem(self, name):
        self.sem[name] = self.nc.alloc_semaphore(name=name)



def build_program(T, stages=(0, 1, 2, 3, 4, 5), debug_taps=()):
    TT = min(512, T)
    NTT = T // TT
    ETT = min(1024, T)
    NCH = T // CH

    nc = bass.Bass("TRN2", target_bir_lowering=False)
    c = Ctx(nc)
    semlist = ["sTE", "sV", "sS", "sG", "sSY", "sCC"]
    for base, n in [("sW", 2), ("sWb", 2), ("sIO", 4), ("sSp", 4), ("sGd", 4)]:
        for i in range(n):
            semlist.append(f"{base}{i}")
    for s in semlist:
        c.new_sem(s)

    din = {}

    def inp(name, shape, dtype=F32):
        din[name] = nc.dram_tensor(name, list(shape), dtype, kind="ExternalInput")
        return din[name]

    inp("x_in", [T, D])
    out_t = nc.dram_tensor("out", [T, D], F32, kind="ExternalOutput")

    inp("wg_t", [D, 3072], I32)
    inp("wu_t", [D, 3072], I32)
    inp("wd_t", [3072, D], I32)

    inp("l0_ln_w", [128, DC])
    inp("l0_ln_b", [128, DC])
    inp("l0_W_in", [D, 4192])
    inp("l0_dt_bias", [32, 1])
    inp("l0_A", [32, 1])
    inp("l0_D", [128, 16])
    inp("l0_rms_w", [128, 16])
    inp("l0_W_out", [2048, D])
    inp("rope_cos", [8, T])
    inp("rope_sin", [8, T])

    inp("l1_W_in", [D, 4096])
    inp("l1_conv_w", [128, 64])
    inp("l1_conv_b", [128, 16])
    inp("l1_W_xproj", [2048, 33])
    inp("l1_D", [128, 16])
    inp("l1_W_out", [2048, D])

    inp("e_W1", [9, D, 4096])
    inp("e_W2", [9, 4096, D])
    inp("e_b1", [9, 128, 32])
    inp("e_g1", [9, 128, 32])
    inp("e_be1", [9, 128, 32])
    inp("e_b2", [9, 128, 8])
    inp("e_g2", [9, 128, 8])
    inp("e_be2", [9, 128, 8])
    inp("e_pb", [9, 128, 8])
    inp("e_ctx", [9, 128, 8])
    inp("e_th1", [9, 4])
    inp("e_th2", [9, 4])
    inp("e_wpath", [128, 8])
    inp("e_wteam", [128, 8])
    inp("e_bp", [9, 1])
    inp("e_ovp", [9, 1])
    inp("e_bt", [3, 1])
    inp("e_ovt", [3, 1])

    inp("ones_col", [128, 1])
    inp("ones_col_bf", [128, 1], BF16)
    inp("ones_row", [1, 128])
    inp("ones_row_bf", [1, 128], BF16)
    inp("ident_bf", [128, 128], BF16)
    inp("ident_f", [128, 128])
    inp("mask_d", [64, 512])
    inp("e2a", [2, 128], BF16)
    inp("e2x", [4, 8])
    inp("altsign", [8, 1])
    inp("pswap", [8, 128])
    inp("ipad", [128, 128], BF16)
    inp("ones_T", [1, 4096])
    inp("negid", [32, 2048])
    inp("tril8", [8, 8])
    inp("lt_col", [8, 1])
    inp("a2b_bf", [32, 128], BF16)
    inp("indj2", [32, 16])
    inp("inds_c", [32, 512])
    inp("ones32", [32, 4096])
    inp("zeros32", [32, 2048])
    inp("a2b_f", [32, 128])
    inp("mask_d2", [64, 512])
    inp("ident64f", [64, 128])
    inp("ident64b", [64, 64], BF16)
    inp("prevsel", [128, 8])
    inp("indj", [32, 16])
    inp("core_lt", [32, 8])
    inp("prev_oh", [1, 8])
    inp("eps_tab", [128, 3])

    dram = {}
    for nm, shape, dt in [
        ("sp_h0", [DC * 128, T], BF16),
        ("sp_yl", [16 * 128, T], BF16),
        ("sp_lnx4", [DC * 128, T], BF16),
        ("sp_xconv", [16 * 128, T], BF16),
        ("sp_tf", [3, DC * 128, TT], BF16),
        ("agA_in", [1, 2048 * 32 + 32], F32),
        ("agA_out", [NCORES, 2048 * 32 + 32], F32),
        ("agB_in", [1, 2048 * 3], F32),
        ("agB_out", [NCORES, 2048 * 3], F32),
        ("agC_in", [1, 2048 * 16], F32),
        ("agC_out", [NCORES, 2048 * 16], F32),
    ]:
        dram[nm] = nc.dram_tensor(nm, shape, dt)

    taps = {}
    for k in debug_taps:
        taps[k] = nc.dram_tensor(f"tap{k}", [T, D], F32, kind="ExternalOutput")

    t_ = nc.alloc_sbuf_tensor
    X = [t_(f"x{i}", [128, T], F32) for i in range(DC)]
    WS = t_("WS", [128, 8192], BF16)
    HB = t_("HB", [128, 32768], BF16)
    FA = t_("FA", [128, 4096], F32)
    FB = t_("FB", [128, 2048], F32)
    SM = t_("SM", [128, 2048], F32)
    SMb = t_("SMb", [128, 6656], BF16)

    CTb = t_("CTb", [128, 1024], BF16)
    CTf = t_("CTf", [128, 2176], F32)
    CT2 = t_("CT2", [128, 1536], F32)
    PA = nc.alloc_psum_tensor("pA", [128, 1024], F32)
    PB = nc.alloc_psum_tensor("pB", [128, 1024], F32)
    P47 = [nc.alloc_psum_tensor(f"pb{i}", [128, 512], F32) for i in range(4)]
    P = [PA[:, 0:512], PA[:, 512:1024], PB[:, 0:512], PB[:, 512:1024],
         P47[0][:, :], P47[1][:, :], P47[2][:, :], P47[3][:, :]]

    nc.const_aps.register(F32, EPS5, SM[:, 2040:2041].opt())
    nc.const_aps.register(F32, EPS6, SM[:, 2041:2042].opt())
    nc.const_aps.register(F32, float(np.pi / 2), SM[:, 2042:2043].opt())

    meta = dict(din=din, dram=dram, X=X, WS=WS, HB=HB, FA=FA, FB=FB, SM=SM,
                SMb=SMb, P=P, PA=PA, PB=PB, CTb=CTb, CTf=CTf, CT2=CT2, taps=taps, T=T, TT=TT, NTT=NTT, ETT=ETT,
                NCH=NCH, out=out_t, stages=stages)

    cst = {
        "ident": CTb[:, 0:128],
        "e2a": CTb[0:2, 128:256],
        "ipad": CTb[:, 512:640],
        "ones_col_bf": CTb[:, 256:257],
        "ones_row_bf": CTb[0:1, 384:512],
        "mask_d": CTf[0:64, 0:512],
        "ones_col": CTf[:, 512:513],
        "ones_row": CTf[0:1, 513 + 127:513 + 255][:, 0:128] if False else CTf[0:1, 640:768],
        "indj": CTf[0:32, 768:784],
        "core_lt": CTf[0:32, 784:792],
        "prev_oh": CTf[0:1, 792:800],
        "l0_ln_w": CTf[:, 800:808],
        "l0_ln_b": CTf[:, 808:816],

        "l0_D": CTf[:, 832:848],
        "l0_rms_w": CTf[:, 848:864],
        "l1_conv_w": CTf[:, 864:928],
        "l1_conv_b": CTf[:, 928:944],
        "l1_D": CTf[:, 944:960],
        "l1_W_xproj_sb": None,
        "e_b1": CT2[:, 0:288], "e_g1": CT2[:, 288:576], "e_be1": CT2[:, 576:864],
        "e_b2": CT2[:, 864:936], "e_g2": CT2[:, 936:1008], "e_be2": CT2[:, 1008:1080],
        "e_pb": CT2[:, 1080:1152], "e_ctx": CT2[:, 1152:1224],
        "e_wpath": CT2[:, 1224:1232], "e_wteam": CT2[:, 1232:1240],
        "e_bp": CT2[0:9, 1240:1241], "e_ovp": CT2[0:9, 1241:1242],
        "e_bt": CT2[0:3, 1242:1243], "e_ovt": CT2[0:3, 1243:1244],
        "e_th1": CT2[0:9, 1248:1252], "e_th2": CT2[0:9, 1252:1256],
        "e_cs1": CT2[0:9, 1256:1264],
        "e_cs2": CT2[0:9, 1264:1272],
        "ident_f": CTf[:, 1024:1152],
        "e2x": CTf[0:4, 1152:1160],
        "altsign": CTf[0:8, 1160:1161],
        "pswap": CTf[0:8, 1280:1408],
        "tril8": CTf[0:8, 960:968],
        "lt_col": CTf[0:8, 968:969],
        "indj64": CTf[64:96, 969:985],
        "dtb64": CTf[64:96, 816:817],
        "A64": CTf[64:96, 817:818],
        "a2b0": CTb[0:32, 640:768],
        "a2b64": CTb[64:96, 768:896],
        "inds": CTf[0:32, 1664:2176],
        "mask_d64": CTf[64:128, 0:512],
        "a2bf64": CTf[64:96, 1408:1536],
        "ident_f64": CTf[64:128, 1536:1664],
        "ident_b64": CTb[64:128, 896:960],
        "prevsel": CTf[:, 985:993],
    }
    meta_cst = cst
    sch = Sched(nc)
    meta["cst"] = cst
    rs = build_resources(sch, meta)
    emit_init(sch, rs, meta)
    for si, st in enumerate(stages):
        if si > 0:
            stage_fence(rs)
        if st in (0, 2):
            emit_ffn0(sch, rs, meta, st)
        elif st == 1:
            emit_l0_attn(sch, rs, meta)
        elif st == 3 or st == 5:
            emit_eidos(sch, rs, meta, st)
        elif st == 4:
            emit_l1_attn(sch, rs, meta)
        if st in meta["taps"]:
            emit_tap(sch, rs, meta, st)
    emit_out(sch, rs, meta)
    sch.emit(c)
    return nc, meta


def stage_fence(rs):
    groups = (("WS0", "WS1", "WS2a", "WS2b"),
              ("HBa", "HBb", "HBc") + tuple(f"HR{k}" for k in range(32)),
              ("SM", "RV0", "RV1"), ("SMb", "SQ1"))
    for grp in groups:
        pool = []
        for n in grp:
            r = rs[n]
            pool.extend(r.readers)
            if r.last_w is not None:
                pool.append(r.last_w)
        for n in grp:
            r = rs[n]
            seen = {id(o) for o in r.readers}
            for o in pool:
                if id(o) not in seen:
                    r.readers.append(o)
                    seen.add(id(o))


def build_resources(sch, meta):
    rs = {}
    for i in range(DC):
        rs[f"X{i}"] = sch.res(f"X{i}")
    for i in range(8):
        rs[f"P{i}"] = sch.res(f"P{i}")
    rs["Pt"] = sch.res("Pt")
    for nm in ["WS0", "WS1", "WS2a", "WS2b", "HBa", "HBb", "HBc", "FA", "FB",
               "SM", "SMb", "dram", "state", "G", "BC", "consts",
               "RV0", "RV1", "SQ1"]:
        rs[nm] = sch.res(nm)
    for k in range(32):
        rs[f"HR{k}"] = sch.res(f"HR{k}")
    rs["HB"] = rs["HBb"]
    rs["rows"] = rs["FB"]
    return rs



def emit_init(sch, rs, meta):
    din, X, P, FA, T = meta["din"], meta["X"], meta["P"], meta["FA"], meta["T"]
    nt = T // 128
    SM = meta["SM"]
    cst = meta["cst"]
    sch.add("sy", lambda h: h.dma_start(out=SM[:, 2040:2043], in_=din["eps_tab"].ap()),
            reads=(), writes=(rs["consts"],), dma="sIO", tag="ldeps")
    cload = [("ident", "ident_bf"), ("e2a", "e2a"), ("ones_col_bf", "ones_col_bf"),
             ("ones_row_bf", "ones_row_bf"), ("mask_d", "mask_d"),
             ("ones_col", "ones_col"), ("ones_row", "ones_row"), ("indj", "indj"),
             ("core_lt", "core_lt"), ("prev_oh", "prev_oh"),
             ("l0_ln_w", "l0_ln_w"), ("l0_ln_b", "l0_ln_b"),
             ("dtb64", "l0_dt_bias"), ("A64", "l0_A"), ("l0_D", "l0_D"),
             ("l0_rms_w", "l0_rms_w"), ("l1_conv_w", "l1_conv_w"),
             ("l1_conv_b", "l1_conv_b"), ("l1_D", "l1_D"),
             ("e_b1", "e_b1"), ("e_g1", "e_g1"), ("e_be1", "e_be1"),
             ("e_b2", "e_b2"), ("e_g2", "e_g2"), ("e_be2", "e_be2"),
             ("e_pb", "e_pb"), ("e_ctx", "e_ctx"),
             ("e_wpath", "e_wpath"), ("e_wteam", "e_wteam"),
             ("e_bp", "e_bp"), ("e_ovp", "e_ovp"), ("e_bt", "e_bt"),
             ("e_ovt", "e_ovt"), ("e_th1", "e_th1"), ("e_th2", "e_th2"),
             ("ident_f", "ident_f"), ("e2x", "e2x"), ("altsign", "altsign"),
             ("pswap", "pswap"), ("ipad", "ipad"), ("tril8", "tril8"),
             ("lt_col", "lt_col"), ("indj64", "indj2"), ("a2b0", "a2b_bf"),
             ("a2b64", "a2b_bf"), ("inds", "inds_c"), ("mask_d64", "mask_d2"),
             ("a2bf64", "a2b_f"), ("ident_f64", "ident64f"),
             ("ident_b64", "ident64b"), ("prevsel", "prevsel")]
    for cname, iname in cload:
        dst = cst[cname]
        ish = din[iname].shape
        if len(ish) == 3:
            sch.add("sy", lambda h, dst=dst, iname=iname, ish=ish: h.dma_start(
                out=dst.rearrange("p (a b) -> p a b", a=ish[0]),
                in_=din[iname].ap().rearrange("a p b -> p a b")),
                reads=(), writes=(rs["consts"],), dma="sIO", tag=f"ldc_{cname}")
        else:
            sch.add("sy", lambda h, dst=dst, iname=iname: h.dma_start(
                out=dst, in_=din[iname].ap()),
                reads=(), writes=(rs["consts"],), dma="sIO", tag=f"ldc_{cname}")
    for e in ENGS:
        sch.add(e, lambda h: h.nop(), reads=(rs["consts"],), writes=(),
                tag=f"cfence_{e}")
    stage = rs["FA"]
    for it in range(nt):
        sl = (it % 4) * 1024
        ld = sch.add("sy", lambda h, it=it, sl=sl: h.dma_start(
            out=FA[:, sl:sl + 1024], in_=din["x_in"][it * 128:(it + 1) * 128, :]),
            reads=(), writes=(stage,), dma="sIO", tag=f"ldx{it}")
        for d in range(DC):
            pb = rs[f"P{d % 2}"]
            tr = sch.add("te", lambda h, it=it, sl=sl, d=d: h.transpose(
                P[d % 2][:, (it % 2) * 128:(it % 2) * 128 + 128].bitcast(F32),
                FA[:, sl + d * 128:sl + (d + 1) * 128],
                meta["cst"]["ident_f"]),
                reads=(stage,), writes=(pb,), tag=f"trx{it}_{d}")
            sch.add("ve", lambda h, it=it, d=d: h.tensor_copy(
                X[d][:, it * 128:(it + 1) * 128],
                P[d % 2][:, (it % 2) * 128:(it % 2) * 128 + 128].bitcast(F32)),
                reads=(pb,), writes=(rs[f"X{d}"],), tag=f"evx{it}_{d}")


def emit_tap(sch, rs, meta, st):
    din, X, FA, T = meta["din"], meta["X"], meta["FA"], meta["T"]
    P, tap = meta["P"], meta["taps"][st]
    nt = T // 128
    for it in range(nt):
        sl = (it % 4) * 1024
        stg = rs["FA"]
        for d in range(DC):
            pb = rs[f"P{d % 2}"]
            sch.add("te", lambda h, it=it, d=d: h.transpose(
                P[d % 2][:, (it % 2) * 128:(it % 2) * 128 + 128].bitcast(F32),
                X[d][:, it * 128:(it + 1) * 128], meta["cst"]["ident_f"]),
                reads=(rs[f"X{d}"],), writes=(pb,), tag=f"ttr{st}_{it}_{d}")
            sch.add("ve", lambda h, it=it, sl=sl, d=d: h.tensor_copy(
                FA[:, sl + d * 128:sl + (d + 1) * 128],
                P[d % 2][:, (it % 2) * 128:(it % 2) * 128 + 128].bitcast(F32)),
                reads=(pb,), writes=(stg,), tag=f"tev{st}_{it}_{d}")
        sch.add("sy", lambda h, it=it, sl=sl: h.dma_start(
            out=tap[it * 128:(it + 1) * 128, :], in_=FA[:, sl:sl + 1024]),
            reads=(stg,), writes=(), dma="sIO", tag=f"tdm{st}_{it}")


def emit_out(sch, rs, meta):
    din, X, FA, T, P = meta["din"], meta["X"], meta["FA"], meta["T"], meta["P"]
    nt = T // 128
    last = None
    for it in range(nt):
        sl = (it % 4) * 1024
        stg = rs["FA"]
        for d in range(DC):
            pb = rs[f"P{d % 2}"]
            sch.add("te", lambda h, it=it, d=d: h.transpose(
                P[d % 2][:, (it % 2) * 128:(it % 2) * 128 + 128].bitcast(F32),
                X[d][:, it * 128:(it + 1) * 128], meta["cst"]["ident_f"]),
                reads=(rs[f"X{d}"],), writes=(pb,), tag=f"otr{it}_{d}")
            sch.add("ve", lambda h, it=it, sl=sl, d=d: h.tensor_copy(
                FA[:, sl + d * 128:sl + (d + 1) * 128],
                P[d % 2][:, (it % 2) * 128:(it % 2) * 128 + 128].bitcast(F32)),
                reads=(pb,), writes=(stg,), tag=f"oev{it}_{d}")
        last = sch.add("sy", lambda h, it=it, sl=sl: h.dma_start(
            out=meta["out"][it * 128:(it + 1) * 128, :], in_=FA[:, sl:sl + 1024]),
            reads=(stg,), writes=(), dma="sIO", tag=f"odm{it}")
    fin = sch.res("fin")
    sch.add("sy", lambda h: h.nop(), reads=(), writes=(fin,), tag="outwait")
    sch.per_eng["sy"][-1].deps.append(last)




def hap(handle, offset, dims):
    return AP(handle, offset, [list(d) for d in dims])


def emit_ln(sch, rs, meta, t0, TT, dst, eps, tag="ln",
            sq_off=24576, sq_res="HBc", dst_res="HBa", f32_scr=0,
            resid_x=False, src=None, src_res=None, affine=None,
            sq_res_fn=None):
    X, P, FB, HB, FA, din = (meta["X"], meta["P"], meta["FB"], meta["HB"],
                             meta["FA"], meta["din"])
    cst = meta["cst"]
    p6, p7, rrows = rs["P6"], rs["P7"], rs["rows"]
    if sq_res_fn is None:
        sq_res_fn = lambda d: rs[sq_res]
    dstr = rs[dst_res]
    if src is None:
        xa = lambda d: X[d][:, t0:t0 + TT]
        srcres = lambda d: rs[f"X{d}"]
    else:
        xa = src
        srcres = lambda d: rs[src_res]

    def sq(d):
        return HB[:, sq_off + d * TT: sq_off + (d + 1) * TT]

    for d in range(DC):
        sch.add("ve", lambda h, d=d: h.tensor_tensor(
            sq(d), xa(d), xa(d), ALU.mult),
            reads=(srcres(d),), writes=(sq_res_fn(d),), tag=f"{tag}sq{d}")
    onesv = cst["ones_col"] if src is None else cst["ones_col_bf"]
    for d in range(DC):
        sch.add("te", lambda h, d=d: h.matmul(
            P[6][0:1, 0:TT], onesv, xa(d),
            start=(d == 0), stop=(d == DC - 1)),
            reads=(srcres(d), rs["consts"]), writes=(p6,), tag=f"{tag}mmx{d}")
    for d in range(DC):
        sch.add("te", lambda h, d=d: h.matmul(
            P[7][0:1, 0:TT], cst["ones_col_bf"], sq(d),
            start=(d == 0), stop=(d == DC - 1)),
            reads=(sq_res_fn(d), rs["consts"]), writes=(p7,), tag=f"{tag}mmq{d}")
    mu = FB[0:1, 0:TT]
    m2 = FB[0:1, 512:512 + TT]
    var = FB[0:1, 1024:1024 + TT]
    nmr = FB[0:1, 1536:1536 + TT]
    sch.add("ve", lambda h: h.tensor_scalar(mu, P[6][0:1, 0:TT], 1.0 / D, None, ALU.mult),
            reads=(p6,), writes=(rrows,), tag=f"{tag}mu")
    sch.add("ve", lambda h: h.tensor_scalar(m2, P[7][0:1, 0:TT], 1.0 / D, None, ALU.mult),
            reads=(p7,), writes=(rrows,), tag=f"{tag}m2")
    sch.add("ve", lambda h: h.tensor_tensor(var, mu, mu, ALU.mult),
            reads=(rrows,), writes=(rrows,), tag=f"{tag}musq")
    sch.add("ve", lambda h: h.tensor_sub(m2, m2, var),
            reads=(rrows,), writes=(rrows,), tag=f"{tag}var")
    sch.add("se", lambda h: h.activation(var, m2, AF.Ln, bias=eps, scale=1.0),
            reads=(rrows, rs["consts"]), writes=(rrows,), tag=f"{tag}lnv")
    sch.add("se", lambda h: h.activation(m2, var, AF.Exp, bias=0.0, scale=-0.5),
            reads=(rrows,), writes=(rrows,), tag=f"{tag}rinv")
    rinv = m2
    sch.add("ve", lambda h: h.scalar_tensor_tensor(nmr, mu, -1.0, rinv,
                                                   ALU.mult, ALU.mult),
            reads=(rrows,), writes=(rrows,), tag=f"{tag}nmr")
    sch.add("te", lambda h: h.matmul(P[6][:, 0:TT], cst["ones_row"], rinv,
                                     start=True, stop=True),
            reads=(rrows, rs["consts"]), writes=(p6,), tag=f"{tag}bc1")
    sch.add("te", lambda h: h.matmul(P[7][:, 0:TT], cst["ones_row"], nmr,
                                     start=True, stop=True),
            reads=(rrows, rs["consts"]), writes=(p7,), tag=f"{tag}bc2")
    for d in range(DC):
        tsl = FA[:, f32_scr + (d % 2) * TT:f32_scr + (d % 2) * TT + TT]
        lsl = FA[:, f32_scr + (2 + d % 2) * TT:f32_scr + (2 + d % 2) * TT + TT] \
            if (resid_x or affine is None) and False else None
        sch.add("ve", lambda h, d=d, tsl=tsl: h.scalar_tensor_tensor(
            tsl, xa(d), 1.0, P[6][:, 0:TT], ALU.mult, ALU.mult),
            reads=(srcres(d), p6), writes=(rs["FA"],), tag=f"{tag}ap1_{d}")
        if affine is None and not resid_x:
            sch.add("ve", lambda h, d=d, tsl=tsl: h.tensor_tensor(
                dst(d), tsl, P[7][:, 0:TT], ALU.add),
                reads=(rs["FA"], p7), writes=(dstr,), tag=f"{tag}ap2_{d}")
        else:
            osl = FA[:, f32_scr + 2 * TT + (d % 2) * TT:
                     f32_scr + 2 * TT + (d % 2) * TT + TT]
            sch.add("ve", lambda h, d=d, tsl=tsl, osl=osl: h.tensor_tensor(
                osl, tsl, P[7][:, 0:TT], ALU.add),
                reads=(rs["FA"], p7), writes=(rs["FA"],), tag=f"{tag}ap2_{d}")
            if resid_x:
                sch.add("ve", lambda h, d=d, osl=osl: h.tensor_add(
                    X[d][:, t0:t0 + TT], X[d][:, t0:t0 + TT], osl),
                    reads=(rs["FA"],), writes=(rs[f"X{d}"],), tag=f"{tag}rx{d}")
            if affine is not None:
                ga, be = affine
                sch.add("se", lambda h, d=d, osl=osl, ga=ga, be=be: h.activation(
                    dst(d), osl, AF.Identity, bias=be[:, d:d + 1],
                    scale=ga[:, d:d + 1]),
                    reads=(rs["FA"], rs["consts"]), writes=(dstr,),
                    tag=f"{tag}af{d}")
            else:
                sch.add("ve", lambda h, d=d, osl=osl: h.tensor_copy(dst(d), osl),
                        reads=(rs["FA"],), writes=(dstr,), tag=f"{tag}cb{d}")



def emit_ffn0(sch, rs, meta, st):
    nc = sch.nc
    X, P, HB, WS, SMb, din = (meta["X"], meta["P"], meta["HB"], meta["WS"],
                              meta["SMb"], meta["din"])
    T, TT, NTT = meta["T"], meta["TT"], meta["NTT"]
    H3 = 3072

    def lnb(d):
        return HB[:, d * TT:(d + 1) * TT]

    def hb(k):
        return HB[:, 4096 + k * TT:4096 + (k + 1) * TT]

    wsq = [rs["WS0"], rs["WS1"]]
    hbr = rs["HBb"]
    smr = rs["SMb"]

    for tt in range(NTT):
        t0 = tt * TT
        emit_ln(sch, rs, meta, t0, TT, lnb, EPS5, tag=f"f{st}ln{tt}")
        lnr = rs["HBa"]
        for hbk in range(12):
            par = hbk % 2
            c0 = hbk * 256
            wof = par * 4096
            ld_g = sch.add("ge", lambda h, c0=c0, wof=wof: h.dma_start(
                out=WS[:, wof:wof + 2048].rearrange("p (k j) -> p k j", k=8),
                in_=hap(din["wg_t"], c0, [[H3, 128], [128 * H3, 8], [1, 256]])),
                reads=(), writes=(wsq[par],), dma=("sW" if par == 0 else "sWb"),
                tag=f"f{st}wg{tt}_{hbk}")
            ld_u = sch.add("ge", lambda h, c0=c0, wof=wof: h.dma_start(
                out=WS[:, wof + 2048:wof + 4096].rearrange("p (k j) -> p k j", k=8),
                in_=hap(din["wu_t"], c0, [[H3, 128], [128 * H3, 8], [1, 256]])),
                reads=(), writes=(wsq[par],), dma=("sW" if par == 0 else "sWb"),
                tag=f"f{st}wu{tt}_{hbk}")
            for hc in range(2):
                pg, pu = rs[f"P{2 * par}"], rs[f"P{2 * par + 1}"]
                for k in range(DC):
                    sch.add("te", lambda h, wof=wof, k=k, hc=hc, par=par: h.matmul(
                        P[2 * par][:, 0:TT],
                        WS[:, wof + k * 256 + hc * 128: wof + k * 256 + hc * 128 + 128],
                        lnb(k), start=(k == 0), stop=(k == DC - 1)),
                        reads=(wsq[par], lnr), writes=(pg,), tag=f"f{st}g{tt}_{hbk}_{hc}_{k}")
                for k in range(DC):
                    sch.add("te", lambda h, wof=wof, k=k, hc=hc, par=par: h.matmul(
                        P[2 * par + 1][:, 0:TT],
                        WS[:, wof + 2048 + k * 256 + hc * 128: wof + 2048 + k * 256 + hc * 128 + 128],
                        lnb(k), start=(k == 0), stop=(k == DC - 1)),
                        reads=(wsq[par], lnr), writes=(pu,), tag=f"f{st}u{tt}_{hbk}_{hc}_{k}")
                hidx = hbk * 2 + hc
                sch.add("se", lambda h, par=par: h.activation(
                    SMb[:, 0:TT], P[2 * par][:, 0:TT], AF.Silu, bias=0.0, scale=RT127),
                    reads=(pg,), writes=(smr,), tag=f"f{st}sil{tt}_{hidx}")
                sch.add("ve", lambda h, par=par, hidx=hidx: h.scalar_tensor_tensor(
                    hb(hidx), P[2 * par + 1][:, 0:TT], RT127, SMb[:, 0:TT],
                    ALU.mult, ALU.mult),
                    reads=(pu, smr), writes=(hbr,), tag=f"f{st}h{tt}_{hidx}")
        for k in range(24):
            par = k % 2
            wof = par * 1024
            sch.add("ge", lambda h, k=k, wof=wof: h.dma_start(
                out=WS[:, wof:wof + 1024],
                in_=din["wd_t"][k * 128:(k + 1) * 128, :]),
                reads=(), writes=(wsq[0], wsq[1]),
                dma=("sW" if par == 0 else "sWb"),
                tag=f"f{st}wd{tt}_{k}")
            for do in range(8):
                sch.add("te", lambda h, k=k, do=do, wof=wof: h.matmul(
                    P[do][:, 0:TT], WS[:, wof + do * 128:wof + (do + 1) * 128],
                    hb(k), start=(k == 0), stop=(k == 23)),
                    reads=(wsq[0], wsq[1], hbr), writes=(rs[f"P{do}"],),
                    tag=f"f{st}d{tt}_{k}_{do}")
        for do in range(8):
            sch.add("ve", lambda h, do=do, t0=t0: h.scalar_tensor_tensor(
                X[do][:, t0:t0 + TT], P[do][:, 0:TT], 0.5 * RT127,
                X[do][:, t0:t0 + TT], ALU.mult, ALU.add),
                reads=(rs[f"P{do}"],), writes=(rs[f"X{do}"],),
                tag=f"f{st}res{tt}_{do}")


def emit_l0_attn(sch, rs, meta):
    nc = sch.nc
    din, X, P, HB, WS, SM, SMb, FA, FB, cst = (
        meta["din"], meta["X"], meta["P"], meta["HB"], meta["WS"], meta["SM"],
        meta["SMb"], meta["FA"], meta["FB"], meta["cst"])
    PA, PB = meta["PA"], meta["PB"]
    T, TT, NTT = meta["T"], meta["TT"], meta["NTT"]
    CPT = TT // CH
    NCH = meta["NCH"]
    dram = meta["dram"]
    csts = rs["consts"]
    wsq = [rs["WS0"], rs["WS1"]]
    hba, hbb, hbc = rs["HBa"], rs["HBb"], rs["HBc"]
    far, fbr, smr, smbr = rs["FA"], rs["FB"], rs["SM"], rs["SMb"]
    star, gr = rs["state"], rs["G"]
    pan, pbn = sch.res("l0pa"), sch.res("l0pb")
    FApitch = 4096
    FBpitch = 2048
    SMbpitch = 6656

    def hT(d):
        return HB[:, d * TT:(d + 1) * TT]

    def xsT(j):
        return HB[:, 4096 + j * TT:4096 + (j + 1) * TT]

    def xcT(i):
        return HB[:, 12288 + i * 2048:12288 + (i + 1) * 2048]

    def ylT(j):
        return HB[:, 20480 + j * TT:20480 + (j + 1) * TT]

    MTreg = lambda c0, n: HB[:, 28672 + c0:28672 + c0 + n]
    SPT = HB[:, 30720:32768]
    Cbf = lambda c0, n: SMb[64:96, 2176 + c0:2176 + c0 + n]
    Bbf = lambda c0, n: SMb[64:96, 2176 + T + c0:2176 + T + c0 + n]
    dtrow = SM[64:96, 0:TT]
    stbuf = [SM[:, 512:1024], SM[:, 1082:1594]]
    dtTs = SM[:, 1024:1056]
    dtotc = SM[:, 1056:1072]
    carry = SM[64:96, 1080:1081]
    Acol = SM[64:96, 1081:1082]
    gprev = SM[64:96, 1596:1597]
    dttc = SM[64:96, 1597:1598]
    arow = FA[64:96, 2048:2048 + TT]
    eAr = SM[64:96, 1598:1662]
    dcyr = SM[64:96, 1662:1726]

    def Gsl(c0, n):
        return FA[64:96, c0:c0 + n]

    sch.add("sy", lambda h: h.dma_start(out=FA[96:128, 0:T],
                                        in_=din["ones32"].ap()[:, 0:T]),
            reads=(), writes=(far,), dma="sIO", tag="l0ones")
    sch.add("sy", lambda h: h.dma_start(out=FB[64:96, 0:2048],
                                        in_=din["negid"].ap()),
            reads=(), writes=(fbr,), dma="sIO", tag="l0negid")
    sch.add("sy", lambda h: h.dma_start(out=FB[96:128, 0:2048],
                                        in_=din["zeros32"].ap()),
            reads=(), writes=(fbr,), dma="sIO", tag="l0z32")
    sch.add("se", lambda h: h.activation(Acol, cst["A64"], AF.Exp,
                                         bias=0.0, scale=1.0),
            reads=(csts,), writes=(smr,), tag="l0A1")
    sch.add("ve", lambda h: h.tensor_scalar(Acol, Acol, -1.0, None, ALU.mult),
            reads=(smr,), writes=(smr,), tag="l0A2")
    sch.add("ve", lambda h: h.memset(stbuf[0], 0.0), reads=(), writes=(star,),
            tag="l0st0")
    sch.add("ve", lambda h: h.memset(carry, 0.0), reads=(), writes=(smr,),
            tag="l0cr0")
    sch.add("ve", lambda h: h.memset(gprev, 0.0), reads=(), writes=(smr,),
            tag="l0gp0")

    gc = 0
    for tt in range(NTT):
        t0 = tt * TT
        sch.add("sy", lambda h, t0=t0: h.dma_start(
            out=FB[0:8, 0:TT], in_=din["rope_cos"].ap()[:, t0:t0 + TT]),
            reads=(), writes=(fbr,), dma="sIO", tag=f"l0cos{tt}")
        sch.add("sy", lambda h, t0=t0: h.dma_start(
            out=FB[0:8, TT:2 * TT], in_=din["rope_sin"].ap()[:, t0:t0 + TT]),
            reads=(), writes=(fbr,), dma="sIO", tag=f"l0sin{tt}")
        emit_ln(sch, rs, meta, t0, TT, hT, EPS5, tag=f"l0A_{tt}",
                sq_off=24576, sq_res="HBc", dst_res="HBa", f32_scr=2048,
                resid_x=True)
        emit_ln(sch, rs, meta, t0, TT, hT, EPS6, tag=f"l0B_{tt}",
                sq_off=24576, sq_res="HBc", dst_res="HBa", f32_scr=2048,
                src=hT, src_res="HBa",
                affine=(cst["l0_ln_w"], cst["l0_ln_b"]))
        sch.add("sy", lambda h, t0=t0: h.dma_start(
            out=dram["sp_h0"][:, t0:t0 + TT].rearrange("(d q) t -> q d t", d=DC),
            in_=HB[:, 0:DC * TT].rearrange("q (d t) -> q d t", d=DC)),
            reads=(hba,), writes=(rs["dram"],), dma="sSp", tag=f"l0hsp{tt}")
        for xh in range(2):
            for k in range(DC):
                par = k % 2
                wof = par * 1024
                sch.add("ge", lambda h, k=k, xh=xh, wof=wof: h.dma_start(
                    out=WS[:, wof:wof + 1024],
                    in_=hap(din["l0_W_in"], k * 128 * 4192 + 2048 + xh * 1024,
                            [[4192, 128], [1, 1024]])),
                    reads=(), writes=(wsq[par],),
                    dma=("sW" if par == 0 else "sWb"), tag=f"l0wx{tt}_{xh}_{k}")
                for f in range(8):
                    sch.add("te", lambda h, k=k, f=f, wof=wof, par=par: h.matmul(
                        P[f][:, 0:TT], WS[:, wof + f * 128:wof + (f + 1) * 128],
                        hT(k), start=(k == 0), stop=(k == DC - 1)),
                        reads=(wsq[par], hba), writes=(rs[f"P{f}"],),
                        tag=f"l0mx{tt}_{xh}_{k}_{f}")
            for f in range(8):
                j = xh * 8 + f
                sch.add("se", lambda h, f=f, j=j: h.activation(
                    xsT(j), P[f][:, 0:TT], AF.Silu, bias=0.0, scale=1.0),
                    reads=(rs[f"P{f}"],), writes=(hbb,), tag=f"l0xs{tt}_{j}")
        grps = [(0, 8, 0, 0), (8, 8, 1, 0), (16, 16, 2, 32),
                (32, 8, 3, 0), (40, 8, 5, 0), (48, 16, 6, 32),
                (64, 32, 4, 64)]
        for k in range(DC):
            par = k % 2
            wof = 2048 + par * 96
            sch.add("ge", lambda h, k=k, wof=wof: h.dma_start(
                out=WS[:, wof:wof + 96],
                in_=hap(din["l0_W_in"], k * 128 * 4192 + 4096,
                        [[4192, 128], [1, 96]])),
                reads=(), writes=(wsq[par],),
                dma=("sW" if par == 0 else "sWb"), tag=f"l0wb{tt}_{k}")
            for (c0, nn, pi, pof) in grps:
                sch.add("te", lambda h, k=k, wof=wof, c0=c0, nn=nn, pi=pi,
                        pof=pof, par=par: h.matmul(
                    P[pi][pof:pof + nn, 0:TT], WS[:, wof + c0:wof + c0 + nn],
                    hT(k), start=(k == 0), stop=(k == DC - 1)),
                    reads=(wsq[par], hba), writes=(rs[f"P{pi}"],),
                    tag=f"l0mb{tt}_{k}_{c0}")
        sBv1 = FA[0:8, 2048:2048 + TT]
        sBv2 = FA[0:8, 2560:2560 + TT]
        sCv1 = FA[0:8, 3072:3072 + TT]
        sCv2 = FA[0:8, 3584:3584 + TT]
        sBr = FA[32:48, 2048:2048 + TT]
        sCr = FA[32:48, 2560:2560 + TT]
        for (tg, pi, pof, dst, nn) in [("bv1", 0, 0, sBv1, 8),
                                       ("bv2", 1, 0, sBv2, 8),
                                       ("br", 2, 32, sBr, 16),
                                       ("cv1", 3, 0, sCv1, 8),
                                       ("cv2", 5, 0, sCv2, 8),
                                       ("cr", 6, 32, sCr, 16)]:
            sch.add("se", lambda h, pi=pi, pof=pof, dst=dst, nn=nn: h.activation(
                dst, P[pi][pof:pof + nn, 0:TT], AF.Silu, bias=0.0, scale=1.0),
                reads=(rs[f"P{pi}"],), writes=(far,), tag=f"l0{tg}_{tt}")
        sch.add("se", lambda h: h.activation(
            dtrow, P[4][64:96, 0:TT], AF.Exp, bias=cst["dtb64"], scale=1.0),
            reads=(rs["P4"], csts), writes=(smr,), tag=f"l0dte{tt}")
        sch.add("ve", lambda h: h.tensor_scalar(
            arow, dtrow, 1.0, None, ALU.add), reads=(smr,), writes=(far,),
            tag=f"l0dt1{tt}")
        sch.add("se", lambda h: h.activation(
            dtrow, arow, AF.Ln, bias=0.0, scale=1.0),
            reads=(far,), writes=(smr,), tag=f"l0dtl{tt}")
        cosr = FB[0:8, 0:TT]
        sinr = FB[0:8, TT:2 * TT]
        scC = FB[0:8, 1024:1024 + TT]
        scD = FB[0:8, 1536:1536 + TT]
        scE = SM[0:8, 0:TT]
        for (va, vb, tg) in [(sBv1, sBv2, "B"), (sCv1, sCv2, "C")]:
            sch.add("ve", lambda h, va=va: h.tensor_mul(scC, va, cosr),
                    reads=(far, fbr), writes=(fbr,), tag=f"l0rp1{tg}{tt}")
            sch.add("ve", lambda h, va=va: h.tensor_mul(scD, va, sinr),
                    reads=(far, fbr), writes=(fbr,), tag=f"l0rp2{tg}{tt}")
            sch.add("ve", lambda h, vb=vb: h.tensor_mul(scE, vb, sinr),
                    reads=(far, fbr), writes=(smr,), tag=f"l0rp3{tg}{tt}")
            sch.add("ve", lambda h, va=va, vb=vb: h.tensor_mul(va, vb, cosr),
                    reads=(far, fbr), writes=(far,), tag=f"l0rp4{tg}{tt}")
            sch.add("ve", lambda h, vb=vb: h.tensor_sub(vb, scC, scE),
                    reads=(fbr, smr), writes=(far,), tag=f"l0rp5{tg}{tt}")
            sch.add("ve", lambda h, va=va: h.tensor_add(scC, scD, va),
                    reads=(fbr, far), writes=(fbr,), tag=f"l0rp6{tg}{tt}")
            sch.add("ve", lambda h, va=va: h.tensor_copy(va, scC),
                    reads=(fbr,), writes=(far,), tag=f"l0rp7{tg}{tt}")
        asm = [(sBv2, Bbf(t0, TT), 0, 8), (sBv1, Bbf(t0, TT), 8, 8),
               (sBr, Bbf(t0, TT), 16, 16),
               (sCv2, Cbf(t0, TT), 0, 8), (sCv1, Cbf(t0, TT), 8, 8),
               (sCr, Cbf(t0, TT), 16, 16)]
        for ai, (sap, dreg, drow, nn) in enumerate(asm):
            sch.add("ge", lambda h, sap=sap, dreg=dreg, drow=drow, nn=nn:
                    h.dma_start(out=dreg[drow:drow + nn, :], in_=sap),
                    reads=(far,), writes=(smbr,), dma="sGd",
                    tag=f"l0asm{tt}_{ai}")
        sch.add("ve", lambda h: h.tensor_scalar(
            arow, dtrow, Acol, None, ALU.mult),
            reads=(smr,), writes=(far,), tag=f"l0a{tt}")
        sch.add("ve", lambda h, t0=t0: h.tensor_tensor_scan(
            Gsl(t0, TT), arow, arow, carry, ALU.add, ALU.bypass),
            reads=(far, smr), writes=(gr,), tag=f"l0G{tt}")
        sch.add("ve", lambda h, t0=t0: h.tensor_copy(
            carry, Gsl(t0 + TT - 1, 1)),
            reads=(gr,), writes=(smr,), tag=f"l0Gc{tt}")
        for i in range(TT // 128):
            sch.add("te", lambda h, i=i: h.transpose(
                P[3][:, 0:32], dtrow[:, i * 128:(i + 1) * 128],
                cst["ident_f64"][0:32, 0:32]),
                reads=(smr, csts), writes=(rs["P3"],), tag=f"l0dT{tt}_{i}")
            sch.add("ve", lambda h: h.tensor_copy(dtTs, P[3][:, 0:32]),
                    reads=(rs["P3"],), writes=(smr,), tag=f"l0dTe{tt}_{i}")
            for j in range(16):
                sch.add("te", lambda h, i=i, j=j: h.transpose(
                    P[3][:, 128 + (j % 2) * 64:128 + (j % 2) * 64 + 64].bitcast(BF16),
                    xsT(j)[:, i * 128:(i + 1) * 128], cst["ident"]),
                    reads=(hbb, csts), writes=(rs["P3"],), tag=f"l0xT{tt}_{i}_{j}")
                dtap = hap(meta["SM"], 1024 + 2 * j, [[2048, 128], [1, 2], [0, 64]])
                sch.add("ve", lambda h, i=i, j=j, dtap=dtap: h.tensor_tensor(
                    xcT(i)[:, j * 128:(j + 1) * 128].rearrange(
                        "q (a b) -> q a b", a=2),
                    P[3][:, 128 + (j % 2) * 64:128 + (j % 2) * 64 + 64].bitcast(
                        BF16).rearrange("q (a b) -> q a b", a=2),
                    dtap, ALU.mult),
                    reads=(rs["P3"], smr), writes=(hbb,), tag=f"l0xcE{tt}_{i}_{j}")
        for ci in range(CPT):
            b = (ci % 2) * 64
            cs0 = t0 + ci * CH
            i128 = (ci * CH) // 128
            sch.add("ge", lambda h, cs0=cs0: h.dma_start(
                out=hap(meta["FB"], 96 * 2048, [[2048, 1], [1, 2048]]),
                in_=hap(meta["FA"], 64 * 4096 + cs0, [[4096, 32], [1, 64]])),
                reads=(gr,), writes=(fbr,), dma="sGd", tag=f"l0dg{tt}_{ci}")
            sch.add("te", lambda h, cs0=cs0, b=b: h.matmul(
                P[6][b:b + 64, 0:64], Bbf(cs0, 64), Cbf(cs0, 64),
                start=True, stop=True),
                reads=(smbr,), writes=(rs["P6"],), tag=f"l0sc{tt}_{ci}")
            sch.add("ve", lambda h, b=b: h.tensor_copy(
                SMb[b:b + 64, 6272:6336], P[6][b:b + 64, 0:64]),
                reads=(rs["P6"],), writes=(smbr,), tag=f"l0sce{tt}_{ci}")
            for hb in range(4):
                sch.add("te", lambda h, cs0=cs0, b=b, hb=hb: h.matmul(
                    P[5][b:b + 64, 0:512], FA[64:128, cs0:cs0 + 64],
                    FB[64:128, hb * 512:(hb + 1) * 512],
                    start=True, stop=True),
                    reads=(gr, far, fbr), writes=(rs["P5"],),
                    tag=f"l0dm{tt}_{ci}_{hb}")
                msk = cst["mask_d"] if b == 0 else cst["mask_d64"]
                sch.add("ve", lambda h, b=b, msk=msk: h.scalar_tensor_tensor(
                    P[6][b:b + 64, 0:512], P[5][b:b + 64, 0:512], 1.0,
                    msk, ALU.mult, ALU.add),
                    reads=(rs["P5"], csts), writes=(rs["P6"],),
                    tag=f"l0dk{tt}_{ci}_{hb}")
                sch.add("se", lambda h, b=b: h.activation(
                    P[5][b:b + 64, 0:512], P[6][b:b + 64, 0:512], AF.Exp,
                    bias=0.0, scale=1.0),
                    reads=(rs["P6"],), writes=(rs["P5"],),
                    tag=f"l0de{tt}_{ci}_{hb}")
                scap = hap(meta["SMb"], b * 6656 + 6272,
                           [[6656, 64], [0, 8], [1, 64]])
                sch.add("ve", lambda h, b=b, hb=hb, scap=scap: h.tensor_tensor(
                    MTreg(hb * 512, 512)[b:b + 64, :].rearrange(
                        "q (a l) -> q a l", a=8),
                    P[5][b:b + 64, 0:512].rearrange("q (a l) -> q a l", a=8),
                    scap, ALU.mult),
                    reads=(rs["P5"], smbr), writes=(hbc,),
                    tag=f"l0mt{tt}_{ci}_{hb}")
            sch.add("ve", lambda h, cs0=cs0: h.tensor_scalar(
                dcyr, Gsl(cs0, CH), Gsl(cs0 + CH - 1, 1), None, ALU.subtract),
                reads=(gr,), writes=(smr,), tag=f"l0dc1{tt}_{ci}")
            sch.add("se", lambda h: h.activation(
                dcyr, dcyr, AF.Exp, bias=0.0, scale=-1.0),
                reads=(smr,), writes=(smr,), tag=f"l0dc2{tt}_{ci}")
            sch.add("ve", lambda h, cs0=cs0: h.tensor_scalar(
                eAr, Gsl(cs0, CH), gprev, None, ALU.subtract),
                reads=(gr, smr), writes=(smr,), tag=f"l0ea1{tt}_{ci}")
            sch.add("se", lambda h: h.activation(
                eAr, eAr, AF.Exp, bias=0.0, scale=1.0),
                reads=(smr,), writes=(smr,), tag=f"l0ea2{tt}_{ci}")
            sch.add("ve", lambda h, cs0=cs0: h.tensor_sub(
                dttc, Gsl(cs0 + CH - 1, 1), gprev),
                reads=(gr, smr), writes=(smr,), tag=f"l0dt2{tt}_{ci}")
            sch.add("se", lambda h: h.activation(
                dttc, dttc, AF.Exp, bias=0.0, scale=1.0),
                reads=(smr,), writes=(smr,), tag=f"l0dt3{tt}_{ci}")
            sch.add("ve", lambda h, cs0=cs0: h.tensor_copy(
                gprev, Gsl(cs0 + CH - 1, 1)),
                reads=(gr,), writes=(smr,), tag=f"l0gp{tt}_{ci}")
            sch.add("ve", lambda h: h.tensor_scalar(
                SM[64:96, 1727:1743], cst["indj64"], dttc, None, ALU.mult),
                reads=(smr, csts), writes=(smr,), tag=f"l0dti{tt}_{ci}")
            sch.add("te", lambda h: h.matmul(
                P[6][:, 64:80], cst["a2bf64"], SM[64:96, 1727:1743],
                start=True, stop=True),
                reads=(smr, csts), writes=(rs["P6"],), tag=f"l0dtm{tt}_{ci}")
            sch.add("ve", lambda h: h.tensor_copy(dtotc, P[6][:, 64:80]),
                    reads=(rs["P6"],), writes=(smr,), tag=f"l0dte2{tt}_{ci}")
            sch.add("ve", lambda h, cs0=cs0: h.tensor_tensor(
                SMb[64:96, 0:64], Bbf(cs0, 64), dcyr, ALU.mult),
                reads=(smbr, smr), writes=(smbr,), tag=f"l0bd{tt}_{ci}")
            sch.add("te", lambda h: h.transpose(
                P[7][0:64, 96:128].bitcast(BF16)[:, 0:32], SMb[64:96, 0:64],
                cst["ident_b64"][0:32, 0:32]),
                reads=(smbr, csts), writes=(rs["P7"],), tag=f"l0bdt{tt}_{ci}")
            sch.add("ve", lambda h, b=b: h.tensor_copy(
                SMb[b:b + 64, 6336:6368], P[7][0:64, 96:128].bitcast(BF16)[:, 0:32]),
                reads=(rs["P7"],), writes=(smbr,), tag=f"l0bde{tt}_{ci}")
            for j in range(16):
                sch.add("te", lambda h, i128=i128, ci=ci, j=j, b=b: h.matmul(
                    P[4][:, j * 32:(j + 1) * 32],
                    xcT(i128)[b:b + 64, j * 128:(j + 1) * 128],
                    SMb[b:b + 64, 6336:6368], start=True, stop=True),
                    reads=(hbb, smbr), writes=(rs["P4"],),
                    tag=f"l0stm{tt}_{ci}_{j}")
            sb = stbuf[gc % 2]
            nb = stbuf[(gc + 1) % 2]
            for jq in range(4):
                for j4 in range(4):
                    j = jq * 4 + j4
                    sch.add("te", lambda h, j=j, j4=j4, sb=sb: h.transpose(
                        P[7][0:32, j4 * 128:(j4 + 1) * 128],
                        sb[:, j * 32:(j + 1) * 32],
                        cst["ident_f"]),
                        reads=(star,), writes=(rs["P7"],),
                        tag=f"l0spt{tt}_{ci}_{j}")
                sch.add("ve", lambda h, jq=jq: h.tensor_copy(
                    SPT[64:96, jq * 512:(jq + 1) * 512],
                    P[7][0:32, 0:512]),
                    reads=(rs["P7"],), writes=(hbc,), tag=f"l0spe{tt}_{ci}_{jq}")
            for j in range(16):
                sch.add("ve", lambda h, j=j, sb=sb, nb=nb: h.scalar_tensor_tensor(
                    nb[:, j * 32:(j + 1) * 32], sb[:, j * 32:(j + 1) * 32],
                    dtotc[:, j:j + 1], P[4][:, j * 32:(j + 1) * 32],
                    ALU.mult, ALU.add),
                    reads=(star, smr, rs["P4"]), writes=(star,),
                    tag=f"l0stu{tt}_{ci}_{j}")
            for j in range(16):
                jh = j % 8
                for hh in range(2):
                    sch.add("te", lambda h, i128=i128, ci=ci, j=j, hh=hh, b=b:
                            h.matmul(
                        PA[hh * 64:(hh + 1) * 64, j * 64:(j + 1) * 64],
                        xcT(i128)[b:b + 64, j * 128 + hh * 64:
                                  j * 128 + hh * 64 + 64],
                        MTreg((2 * j + hh) * 64, 64)[b:b + 64, :],
                        start=True, stop=True),
                        reads=(hbb, hbc), writes=(rs["P0"], rs["P1"]),
                        tag=f"l0yd{tt}_{ci}_{j}_{hh}")
                sch.add("te", lambda h, cs0=cs0, j=j: h.matmul(
                    PB[:, j * 64:(j + 1) * 64], SPT[64:96, j * 128:(j + 1) * 128],
                    Cbf(cs0, 64), start=True, stop=True),
                    reads=(hbc, smbr), writes=(rs["P2"], rs["P3"]),
                    tag=f"l0yo{tt}_{ci}_{j}")
                sch.add("ve", lambda h, j=j: h.tensor_scalar(
                    SMb[64:96, 96:160], eAr, cst["indj64"][:, j:j + 1],
                    None, ALU.mult),
                    reads=(smr, csts), writes=(smbr,), tag=f"l0eg1{tt}_{ci}_{j}")
                sch.add("te", lambda h, j=j, jh=jh: h.matmul(
                    P[6][:, jh * 64:(jh + 1) * 64], cst["a2b64"],
                    SMb[64:96, 96:160], start=True, stop=True),
                    reads=(smbr, csts), writes=(rs["P6"],),
                    tag=f"l0eg2{tt}_{ci}_{j}")
                sch.add("ve", lambda h, jh=jh: h.tensor_copy(
                    SMb[:, 6336 + 0:6336 + 64] if False else
                    SMb[:, 6400:6464], P[6][:, jh * 64:(jh + 1) * 64]),
                    reads=(rs["P6"],), writes=(smbr,), tag=f"l0eg3{tt}_{ci}_{j}")
                sch.add("ve", lambda h, j=j: h.tensor_tensor(
                    FA[:, 2560:2624], PB[:, j * 64:(j + 1) * 64],
                    SMb[:, 6400:6464], ALU.mult),
                    reads=(rs["P2"], rs["P3"], smbr), writes=(far,),
                    tag=f"l0ym1{tt}_{ci}_{j}")
                sch.add("ve", lambda h, j=j: h.tensor_tensor(
                    FA[:, 2624:2688], FA[:, 2560:2624],
                    PA[:, j * 64:(j + 1) * 64], ALU.add),
                    reads=(far, rs["P0"], rs["P1"]), writes=(far,),
                    tag=f"l0ym2{tt}_{ci}_{j}")
                sch.add("ve", lambda h, j=j, ci=ci: h.scalar_tensor_tensor(
                    ylT(j)[:, ci * CH:(ci + 1) * CH],
                    xsT(j)[:, ci * CH:(ci + 1) * CH], cst["l0_D"][:, j:j + 1],
                    FA[:, 2624:2688], ALU.mult, ALU.add),
                    reads=(hbb, csts, far), writes=(hbc,),
                    tag=f"l0ym3{tt}_{ci}_{j}")
            gc += 1
        sch.add("sy", lambda h, t0=t0: h.dma_start(
            out=dram["sp_yl"][:, t0:t0 + TT].rearrange("(j q) t -> q j t", j=16),
            in_=HB[:, 20480:20480 + 16 * TT].rearrange("q (j t) -> q j t", j=16)),
            reads=(hbc,), writes=(rs["dram"],), dma="sSp", tag=f"l0ysp{tt}")

    fstate = stbuf[NCH % 2]
    sch.add("sy", lambda h, fstate=fstate: h.dma_start(
        out=hap(dram["agA_in"], 0, [[512, 128], [1, 512]]), in_=fstate),
        reads=(star,), writes=(rs["dram"],), dma="sSp", tag="l0agw1")
    sch.add("sy", lambda h: h.dma_start(
        out=hap(dram["agA_in"], 65536, [[1, 32], [1, 1]]), in_=carry),
        reads=(smr,), writes=(rs["dram"],), dma="sSp", tag="l0agw2")
    cc = sch.add("ge", lambda h: h.collective_compute(
        "AllGather", ALU.bypass, replica_groups=[list(range(NCORES))],
        ins=[dram["agA_in"].ap().opt()], outs=[dram["agA_out"].ap().opt()]),
        reads=(rs["dram"],), writes=(rs["dram"],), tag="l0agA")
    sch.add("sy", lambda h: h.dma_start(
        out=SM[0:8, 1024:1056], in_=hap(dram["agA_out"], 65536,
                                        [[65568, 8], [1, 32]])),
        reads=(rs["dram"],), writes=(smr,), dma="sSp", tag="l0suml")
    sch.add("te", lambda h: h.matmul(
        P[2][0:32, 0:8], SM[0:8, 1024:1056], cst["tril8"],
        start=True, stop=True),
        reads=(smr, csts), writes=(rs["P2"],), tag="l0Q")
    sch.add("te", lambda h: h.matmul(
        P[3][0:32, 0:1], SM[0:8, 1024:1056], cst["lt_col"],
        start=True, stop=True),
        reads=(smr, csts), writes=(rs["P3"],), tag="l0P")
    sch.add("ve", lambda h: h.tensor_copy(SM[0:32, 1659:1660], P[3][0:32, 0:1]),
            reads=(rs["P3"],), writes=(smr,), tag="l0Pe")
    sch.add("ve", lambda h: h.tensor_scalar(
        SM[0:32, 1660:1668], P[2][0:32, 0:8], -1.0, SM[0:32, 1659:1660],
        ALU.mult, ALU.add),
        reads=(rs["P2"], smr), writes=(smr,), tag="l0cf1")
    sch.add("ve", lambda h: h.tensor_scalar(
        SM[0:32, 1660:1668], SM[0:32, 1660:1668], 0.0, None, ALU.min),
        reads=(smr,), writes=(smr,), tag="l0cfc")
    sch.add("se", lambda h: h.activation(
        SM[0:32, 1668:1676], SM[0:32, 1660:1668], AF.Exp, bias=0.0, scale=1.0),
        reads=(smr,), writes=(smr,), tag="l0cf2")
    sch.add("ve", lambda h: h.tensor_tensor(
        SM[0:32, 1660:1668], SM[0:32, 1668:1676], cst["core_lt"], ALU.mult),
        reads=(smr, csts), writes=(smr,), tag="l0cf3")
    for j in range(NCORES):
        sch.add("sy", lambda h, j=j: h.dma_start(
            out=HB[:, 4096 + j * 1024:4096 + (j + 1) * 1024].bitcast(F32),
            in_=hap(dram["agA_out"], j * 65568, [[512, 128], [1, 512]])),
            reads=(rs["dram"],), writes=(hbb,), dma="sSp", tag=f"l0stl{j}")
        sch.add("ve", lambda h, j=j: h.tensor_scalar(
            SMb[0:32, 0:512].bitcast(F32)[:, 0:0] if False else
            SMb[0:32, 0:512], cst["inds"], SM[0:32, 1660 + j:1661 + j],
            None, ALU.mult),
            reads=(csts, smr), writes=(smbr,), tag=f"l0ej1{j}")
        sch.add("te", lambda h: h.matmul(
            P[4][:, 0:512], cst["a2b0"], SMb[0:32, 0:512],
            start=True, stop=True),
            reads=(smbr, csts), writes=(rs["P4"],), tag=f"l0ej2{j}")
        dstb = stbuf[j % 2]
        if j == 0:
            sch.add("ve", lambda h, j=j: h.tensor_tensor(
                stbuf[0], P[4][:, 0:512],
                HB[:, 4096 + j * 1024:4096 + (j + 1) * 1024].bitcast(F32),
                ALU.mult),
                reads=(rs["P4"], hbb), writes=(star,), tag=f"l0si{j}")
        else:
            sch.add("ve", lambda h, j=j: h.tensor_tensor(
                FA[:, 2048:2560], P[4][:, 0:512],
                HB[:, 4096 + j * 1024:4096 + (j + 1) * 1024].bitcast(F32),
                ALU.mult),
                reads=(rs["P4"], hbb), writes=(far,), tag=f"l0si{j}")
            sch.add("ve", lambda h, j=j: h.tensor_add(
                stbuf[j % 2], stbuf[(j + 1) % 2], FA[:, 2048:2560]),
                reads=(far, star), writes=(star,), tag=f"l0sa{j}")
    sin_f = stbuf[(NCORES - 1) % 2]
    for jq in range(4):
        for j4 in range(4):
            j = jq * 4 + j4
            sch.add("te", lambda h, j=j, j4=j4, sin_f=sin_f: h.transpose(
                P[3][0:32, j4 * 128:(j4 + 1) * 128],
                sin_f[:, j * 32:(j + 1) * 32], cst["ident_f"]),
                reads=(star,), writes=(rs["P3"],), tag=f"l0sit{j}")
        sch.add("ve", lambda h, jq=jq: h.tensor_copy(
            SPT[64:96, jq * 512:(jq + 1) * 512], P[3][0:32, 0:512]),
            reads=(rs["P3"],), writes=(hbc,), tag=f"l0site{jq}")

    for tt in range(NTT):
        t0 = tt * TT
        sch.add("sy", lambda h, t0=t0: h.dma_start(
            out=HB[:, 0:DC * TT].rearrange("q (d t) -> q d t", d=DC),
            in_=dram["sp_h0"][:, t0:t0 + TT].rearrange("(d q) t -> q d t", d=DC)),
            reads=(rs["dram"],), writes=(hba,), dma="sSp", tag=f"l0hrl{tt}")
        sch.add("sy", lambda h, t0=t0: h.dma_start(
            out=HB[:, 20480:20480 + 16 * TT].rearrange("q (j t) -> q j t", j=16),
            in_=dram["sp_yl"][:, t0:t0 + TT].rearrange("(j q) t -> q j t", j=16)),
            reads=(rs["dram"],), writes=(hbc,), dma="sSp", tag=f"l0yrl{tt}")
        for zh in range(2):
            for k in range(DC):
                par = k % 2
                wof = par * 1024
                sch.add("ge", lambda h, k=k, zh=zh, wof=wof: h.dma_start(
                    out=WS[:, wof:wof + 1024],
                    in_=hap(din["l0_W_in"], k * 128 * 4192 + zh * 1024,
                            [[4192, 128], [1, 1024]])),
                    reads=(), writes=(wsq[par],),
                    dma=("sW" if par == 0 else "sWb"), tag=f"l0wz{tt}_{zh}_{k}")
                for f in range(8):
                    sch.add("te", lambda h, k=k, f=f, wof=wof, par=par: h.matmul(
                        P[f][:, 0:TT], WS[:, wof + f * 128:wof + (f + 1) * 128],
                        hT(k), start=(k == 0), stop=(k == DC - 1)),
                        reads=(wsq[par], hba), writes=(rs[f"P{f}"],),
                        tag=f"l0mz{tt}_{zh}_{k}_{f}")
            for f in range(8):
                j = zh * 8 + f
                sch.add("se", lambda h, f=f, j=j: h.activation(
                    xsT(j), P[f][:, 0:TT], AF.Silu, bias=0.0, scale=1.0),
                    reads=(rs[f"P{f}"],), writes=(hbb,), tag=f"l0zs{tt}_{j}")
        sch.add("se", lambda h, t0=t0: h.activation(
            FA[64:96, 2048:2048 + TT], Gsl(t0, TT), AF.Exp, bias=0.0, scale=1.0),
            reads=(gr,), writes=(far,), tag=f"l0xg{tt}")
        for j in range(16):
            sch.add("te", lambda h, j=j, t0=t0: h.matmul(
                P[2][:, 0:TT], SPT[64:96, j * 128:(j + 1) * 128],
                Cbf(t0, TT), start=True, stop=True),
                reads=(hbc, smbr), writes=(rs["P2"],), tag=f"l0cor{tt}_{j}")
            sch.add("ve", lambda h, j=j: h.tensor_scalar(
                SMb[64:96, 96:96 + TT], FA[64:96, 2048:2048 + TT],
                cst["indj64"][:, j:j + 1], None, ALU.mult),
                reads=(far, csts), writes=(smbr,), tag=f"l0gg1{tt}_{j}")
            sch.add("te", lambda h, j=j: h.matmul(
                P[3][:, 0:TT], cst["a2b64"], SMb[64:96, 96:96 + TT],
                start=True, stop=True),
                reads=(smbr, csts), writes=(rs["P3"],), tag=f"l0gg2{tt}_{j}")
            sch.add("ve", lambda h, j=j: h.tensor_copy(
                MTreg(0, TT), P[3][:, 0:TT]),
                reads=(rs["P3"],), writes=(hbc,), tag=f"l0gg3{tt}_{j}")
            sch.add("ve", lambda h, j=j: h.tensor_tensor(
                FA[:, 2560:2560 + TT], P[2][:, 0:TT], MTreg(0, TT), ALU.mult),
                reads=(rs["P2"], hbc), writes=(far,), tag=f"l0mg1{tt}_{j}")
            sch.add("ve", lambda h, j=j: h.tensor_tensor(
                FA[:, 3072:3072 + TT], FA[:, 2560:2560 + TT], ylT(j), ALU.add),
                reads=(far, hbc), writes=(far,), tag=f"l0mg2{tt}_{j}")
            sch.add("ve", lambda h, j=j: h.scalar_tensor_tensor(
                ylT(j), xsT(j), 1.0, FA[:, 3072:3072 + TT],
                ALU.mult, ALU.mult),
                reads=(hbb, far), writes=(hbc,), tag=f"l0mg3{tt}_{j}")
        for j in range(16):
            sch.add("ve", lambda h, j=j: h.tensor_tensor(
                MTreg(512, TT), ylT(j), ylT(j), ALU.mult),
                reads=(hbc,), writes=(hbc,), tag=f"l0rs{tt}_{j}")
            sch.add("te", lambda h, j=j: h.matmul(
                P[6][0:1, 0:TT], cst["ones_col_bf"], MTreg(512, TT),
                start=(j == 0), stop=(j == 15)),
                reads=(hbc, csts), writes=(rs["P6"],), tag=f"l0rm{tt}_{j}")
        mrow = FB[0:1, 0:TT]
        sch.add("ve", lambda h: h.tensor_scalar(
            mrow, P[6][0:1, 0:TT], 1.0 / 2048.0, None, ALU.mult),
            reads=(rs["P6"],), writes=(rs["rows"],), tag=f"l0rr1{tt}")
        sch.add("se", lambda h: h.activation(mrow, mrow, AF.Ln, bias=EPS6, scale=1.0),
                reads=(rs["rows"], csts), writes=(rs["rows"],), tag=f"l0rr2{tt}")
        sch.add("se", lambda h: h.activation(mrow, mrow, AF.Exp, bias=0.0, scale=-0.5),
                reads=(rs["rows"],), writes=(rs["rows"],), tag=f"l0rr3{tt}")
        sch.add("te", lambda h: h.matmul(
            P[6][:, 0:TT], cst["ones_row"], mrow, start=True, stop=True),
            reads=(rs["rows"], csts), writes=(rs["P6"],), tag=f"l0rb{tt}")
        for j in range(16):
            sch.add("ve", lambda h, j=j: h.scalar_tensor_tensor(
                xsT(j), ylT(j), cst["l0_rms_w"][:, j:j + 1], P[6][:, 0:TT],
                ALU.mult, ALU.mult),
                reads=(hbc, csts, rs["P6"]), writes=(hbb,), tag=f"l0yb{tt}_{j}")
        for k in range(16):
            par = k % 2
            wof = par * 1024
            sch.add("ge", lambda h, k=k, wof=wof: h.dma_start(
                out=WS[:, wof:wof + 1024],
                in_=din["l0_W_out"][k * 128:(k + 1) * 128, :]),
                reads=(), writes=(wsq[par],),
                dma=("sW" if par == 0 else "sWb"), tag=f"l0wo{tt}_{k}")
            for do in range(8):
                sch.add("te", lambda h, k=k, do=do, wof=wof, par=par: h.matmul(
                    P[do][:, 0:TT], WS[:, wof + do * 128:wof + (do + 1) * 128],
                    xsT(k), start=(k == 0), stop=(k == 15)),
                    reads=(wsq[par], hbb), writes=(rs[f"P{do}"],),
                    tag=f"l0wm{tt}_{k}_{do}")
        for do in range(8):
            sch.add("ve", lambda h, do=do, t0=t0: h.tensor_tensor(
                X[do][:, t0:t0 + TT], X[do][:, t0:t0 + TT], P[do][:, 0:TT],
                ALU.add),
                reads=(rs[f"P{do}"],), writes=(rs[f"X{do}"],),
                tag=f"l0res{tt}_{do}")


def eidos_init(sch, rs, meta):
    nc = sch.nc
    SM, SMb, FB, P, cst = meta["SM"], meta["SMb"], meta["FB"], meta["P"], meta["cst"]
    csts = rs["consts"]
    p5 = rs["P5"]
    for nm, dst in [("e_th1", "e_cs1"), ("e_th2", "e_cs2")]:
        th = cst[nm]
        cs = cst[dst]
        sch.add("se", lambda h, th=th, cs=cs: h.activation(
            cs[:, 0:4], th, AF.Sin, bias=float(np.pi / 2), scale=1.0),
            reads=(csts,), writes=(csts,), tag=f"ei_cos_{nm}")
        sch.add("se", lambda h, th=th, cs=cs: h.activation(
            cs[:, 4:8], th, AF.Sin, bias=0.0, scale=1.0),
            reads=(csts,), writes=(csts,), tag=f"ei_sin_{nm}")
    for j, coff, soff, csname in [(0, 0, 16, "e_cs1"), (1, 32, 48, "e_cs2")]:
        sch.add("te", lambda h, csname=csname: h.transpose(
            P[5][0:4, 0:9].bitcast(F32), cst[csname][:, 0:4],
            cst["ident_f"][0:9, 0:9]),
            reads=(csts,), writes=(p5,), tag=f"ei_cTc{j}")
        sch.add("ve", lambda h, coff=coff: h.tensor_copy(
            FB[0:4, coff:coff + 9], P[5][0:4, 0:9].bitcast(F32)),
            reads=(p5,), writes=(rs["FB"],), tag=f"ei_cTce{j}")
        sch.add("te", lambda h, csname=csname: h.transpose(
            P[5][0:4, 32:41].bitcast(F32), cst[csname][:, 4:8],
            cst["ident_f"][0:9, 0:9]),
            reads=(csts,), writes=(p5,), tag=f"ei_cTs{j}")
        sch.add("ve", lambda h, soff=soff: h.tensor_copy(
            FB[0:4, soff:soff + 9], P[5][0:4, 32:41].bitcast(F32)),
            reads=(p5,), writes=(rs["FB"],), tag=f"ei_cTse{j}")
    for j, dst0, coff, soff in [(0, 0, 0, 16), (1, 72, 32, 48)]:
        for p in range(9):
            sch.add("te", lambda h, p=p, coff=coff: h.transpose(
                P[5][0:1, 0:4].bitcast(F32), FB[0:4, coff + p:coff + p + 1],
                cst["ident_f"][0:4, 0:4]),
                reads=(rs["FB"], csts), writes=(p5,), tag=f"ei_cr{j}_{p}")
            sch.add("te", lambda h, p=p, soff=soff: h.transpose(
                P[5][0:1, 4:8].bitcast(F32), FB[0:4, soff + p:soff + p + 1],
                cst["ident_f"][0:4, 0:4]),
                reads=(rs["FB"], csts), writes=(p5,), tag=f"ei_sr{j}_{p}")
            sch.add("ve", lambda h: h.tensor_copy(FB[0:1, 64:72],
                                                  P[5][0:1, 0:8].bitcast(F32)),
                    reads=(p5,), writes=(rs["FB"],), tag=f"ei_cse{j}_{p}")
            sch.add("te", lambda h: h.matmul(
                P[5][:, 0:8], cst["ones_row"], FB[0:1, 64:72],
                start=True, stop=True),
                reads=(rs["FB"], csts), writes=(p5,), tag=f"ei_csb{j}_{p}")
            sch.add("ve", lambda h, p=p, dst0=dst0: h.tensor_copy(
                SM[:, dst0 + p * 8:dst0 + p * 8 + 8], P[5][:, 0:8]),
                reads=(p5,), writes=(csts,), tag=f"ei_csbe{j}_{p}")
    for j, r0, coff, soff in [(0, 4096, 0, 16), (1, 5248, 32, 48)]:
        for p in range(9):
            R = SMb[:, r0 + p * 128:r0 + (p + 1) * 128]
            sch.add("ve", lambda h, R=R: h.tensor_copy(R, cst["ipad"]),
                    reads=(csts,), writes=(csts,), tag=f"ei_Rc{j}_{p}")
            sch.add("te", lambda h, p=p, coff=coff: h.matmul(
                P[5][0:8, 16:17].bitcast(F32), cst["e2x"],
                FB[0:4, coff + p:coff + p + 1], start=True, stop=True),
                reads=(rs["FB"], csts), writes=(p5,), tag=f"ei_Rcc{j}_{p}")
            sch.add("ve", lambda h: h.tensor_copy(FB[0:8, 80:81],
                                                  P[5][0:8, 16:17].bitcast(F32)),
                    reads=(p5,), writes=(rs["FB"],), tag=f"ei_Rcce{j}_{p}")
            sch.add("te", lambda h, p=p, soff=soff: h.matmul(
                P[5][0:8, 16:17].bitcast(F32), cst["e2x"],
                FB[0:4, soff + p:soff + p + 1], start=True, stop=True),
                reads=(rs["FB"], csts), writes=(p5,), tag=f"ei_Rsc{j}_{p}")
            sch.add("ve", lambda h: h.tensor_mul(
                FB[0:8, 81:82], P[5][0:8, 16:17].bitcast(F32), cst["altsign"]),
                reads=(p5, csts), writes=(rs["FB"],), tag=f"ei_Rsse{j}_{p}")
            sch.add("ve", lambda h: h.tensor_mul(
                FB[0:8, 96:104], FB[0:8, 80:81].broadcast_to([8, 8]),
                cst["ident_f"][0:8, 0:8]),
                reads=(rs["FB"], csts), writes=(rs["FB"],), tag=f"ei_Rcd{j}_{p}")
            sch.add("ve", lambda h: h.tensor_mul(
                FB[0:8, 104:112], FB[0:8, 81:82].broadcast_to([8, 8]),
                cst["ident_f"][0:8, 0:8]),
                reads=(rs["FB"], csts), writes=(rs["FB"],), tag=f"ei_Rsd{j}_{p}")
            sch.add("te", lambda h: h.matmul(
                P[5][0:8, 0:128], FB[0:8, 96:104], cst["ident_f"][0:8, 0:128],
                start=True, stop=False),
                reads=(rs["FB"], csts), writes=(p5,), tag=f"ei_Rm1{j}_{p}")
            sch.add("te", lambda h: h.matmul(
                P[5][0:8, 0:128], FB[0:8, 104:112], cst["pswap"],
                start=False, stop=True),
                reads=(rs["FB"], csts), writes=(p5,), tag=f"ei_Rm2{j}_{p}")
            sch.add("ve", lambda h, R=R: h.tensor_copy(R[0:8, :], P[5][0:8, 0:128]),
                    reads=(p5,), writes=(csts,), tag=f"ei_Re{j}_{p}")
    sch.add("ve", lambda h: h.tensor_copy(SMb[:, 6416:6424], cst["e_wpath"]),
            reads=(csts,), writes=(csts,), tag="ei_wpb")
    sch.add("ve", lambda h: h.tensor_copy(SMb[:, 6424:6432], cst["e_wteam"]),
            reads=(csts,), writes=(csts,), tag="ei_wtb")
    sch.add("ve", lambda h: h.scalar_tensor_tensor(
        SM[:, 144:216], cst["e_pb"] if False else cst["e_be2"], 1.0,
        cst["e_be2"] if False else cst["e_pb"], ALU.mult, ALU.add),
        reads=(csts,), writes=(csts,), tag="ei_b2p")
    sch.add("ve", lambda h: h.tensor_scalar(
        SM[:, 216:288], cst["e_ctx"], 0.1, None, ALU.mult),
        reads=(csts,), writes=(csts,), tag="ei_ctx01")
    sch.add("ve", lambda h: h.tensor_scalar(
        SM[:, 288:360], cst["e_g2"], 7.0, None, ALU.mult),
        reads=(csts,), writes=(csts,), tag="ei_g27")
    sch.add("ve", lambda h: h.tensor_scalar(
        SM[:, 360:648], cst["e_g1"], 7.0, None, ALU.mult),
        reads=(csts,), writes=(csts,), tag="ei_g17")
    sch.add("ve", lambda h: h.scalar_tensor_tensor(
        SM[0:9, 2044:2045], cst["e_ovp"], -1.0, cst["e_bp"], ALU.mult, ALU.add),
        reads=(csts,), writes=(csts,), tag="ei_bmp")
    sch.add("ve", lambda h: h.scalar_tensor_tensor(
        SM[0:3, 2045:2046], cst["e_ovt"], -1.0, cst["e_bt"], ALU.mult, ALU.add),
        reads=(csts,), writes=(csts,), tag="ei_bmt")
    sch.add("te", lambda h: h.transpose(
        P[5][0:1, 0:9].bitcast(F32), SM[0:9, 2044:2045],
        cst["ident_f"][0:9, 0:9]),
        reads=(csts,), writes=(p5,), tag="ei_bmpT")
    sch.add("ve", lambda h: h.tensor_copy(SM[0:1, 2020:2029],
                                          P[5][0:1, 0:9].bitcast(F32)),
            reads=(p5,), writes=(csts,), tag="ei_bmpE")
    sch.add("te", lambda h: h.transpose(
        P[5][0:1, 16:19].bitcast(F32), SM[0:3, 2045:2046],
        cst["ident_f"][0:3, 0:3]),
        reads=(csts,), writes=(p5,), tag="ei_bmtT")
    sch.add("ve", lambda h: h.tensor_copy(SM[0:1, 2036:2039],
                                          P[5][0:1, 16:19].bitcast(F32)),
            reads=(p5,), writes=(csts,), tag="ei_bmtE")
    sch.add("te", lambda h: h.matmul(
        P[5][:, 32:35], cst["ones_row"], SM[0:1, 2036:2039],
        start=True, stop=True),
        reads=(csts,), writes=(p5,), tag="ei_bmtB")
    sch.add("ve", lambda h: h.tensor_copy(SM[:, 2030:2033], P[5][:, 32:35]),
            reads=(p5,), writes=(csts,), tag="ei_bmtBE")


def emit_eidos(sch, rs, meta, st):
    HB, WS, SM, SMb, FA, FB, X, P = (meta["HB"], meta["WS"], meta["SM"],
                                     meta["SMb"], meta["FA"], meta["FB"],
                                     meta["X"], meta["P"])
    TT, NTT, T = meta["TT"], meta["NTT"], meta["T"]
    cst = meta["cst"]
    din = meta["din"]
    dram = meta["dram"]
    if not meta.get("_eidos_init_done"):
        eidos_init(sch, rs, meta)
        meta["_eidos_init_done"] = True

    csts = rs["consts"]
    wsq = [rs["WS0"], rs["WS1"]]
    w2q = [rs["WS2a"], rs["WS2b"]]
    hba, hbc, smbr = rs["HBa"], rs["HBc"], rs["SMb"]
    smr, far, fbr = rs["SM"], rs["FA"], rs["FB"]
    HR = [rs[f"HR{k}"] for k in range(32)]
    RV = [rs["RV0"], rs["RV1"]]
    sq1r = rs["SQ1"]
    rinv1s = [SM[:, 800:800 + TT], SM[:, 1312:1312 + TT]]
    rinv2s = FA[:, 1024:1024 + TT]
    sq1scr = SMb[:, 2176:2176 + TT]

    def lnxt(k):
        return HB[:, k * TT:(k + 1) * TT]

    def hraw(k):
        return HB[:, 4096 + k * TT:4096 + (k + 1) * TT]

    def feat(s, dc):
        o = 20480 + s * 8 * TT + dc * TT
        return HB[:, o:o + TT]

    def o2(dc):
        return SMb[:, dc * TT:(dc + 1) * TT]

    bcol = lambda tab, idx: tab[:, idx:idx + 1]

    def w1dma_p(p, fq):
        par = fq % 2
        wof = par * 2048
        sch.add("ge", lambda h, p=p, fq=fq, wof=wof: h.dma_start(
            out=WS[:, wof:wof + 2048].rearrange("q (k j) -> q k j", k=8),
            in_=hap(din["e_W1"], p * D * 4096 + fq * 256,
                    [[4096, 128], [128 * 4096, 8], [1, 256]])),
            reads=(), writes=(wsq[par],),
            dma=("sW" if par == 0 else "sWb"), tag=f"e{st}W1_{p}_{fq}")

    def pf_a(p):
        w1dma_p(p, 0)
        w1dma_p(p, 1)

    def a_thunks(p):
        rv = rinv1s[p % 2]
        rvr = RV[p % 2]

        def w1dma(fq):
            w1dma_p(p, fq)

        def aux(fc):
            bank = fc % 4
            pb = rs[f"P{bank}"]
            sch.add("se", lambda h, fc=fc, bank=bank, p=p: h.activation(
                hraw(fc), P[bank][:, 0:TT], AF.Identity,
                bias=bcol(cst["e_b1"], p * 32 + fc), scale=1.0),
                reads=(pb, csts), writes=(HR[fc],), tag=f"e{st}ev1_{p}_{fc}")
            if fc == 0:
                sch.add("te", lambda h, p=p: h.matmul(
                    P[5][:, 0:TT], SMb[:, 4096 + p * 128:4096 + (p + 1) * 128],
                    hraw(0), start=True, stop=True),
                    reads=(HR[0], csts), writes=(rs["P5"],),
                    tag=f"e{st}rot1_{p}")
                sch.add("se", lambda h: h.activation(
                    hraw(0), P[5][:, 0:TT], AF.Identity, bias=0.0, scale=1.0),
                    reads=(rs["P5"],), writes=(HR[0],), tag=f"e{st}rev1_{p}")
            sch.add("ve", lambda h, fc=fc: h.tensor_tensor(
                sq1scr, hraw(fc), hraw(fc), ALU.mult),
                reads=(HR[fc],), writes=(sq1r,), tag=f"e{st}sq1_{p}_{fc}")
            sch.add("te", lambda h, fc=fc: h.matmul(
                P[6][0:1, 0:TT], cst["ones_col_bf"], sq1scr,
                start=(fc == 0), stop=(fc == 31)),
                reads=(sq1r, csts), writes=(rs["P6"],), tag=f"e{st}st1_{p}_{fc}")

        ths = []
        for fc in range(32):
            def grp(fc=fc):
                if fc % 2 == 0 and fc // 2 + 2 < 16:
                    w1dma(fc // 2 + 2)
                par = (fc // 2) % 2
                wof = par * 2048
                f2 = fc % 2
                pb = rs[f"P{fc % 4}"]
                for k in range(DC):
                    sch.add("te", lambda h, wof=wof, k=k, f2=f2, fc=fc: h.matmul(
                        P[fc % 4][:, 0:TT],
                        WS[:, wof + k * 256 + f2 * 128:
                           wof + k * 256 + (f2 + 1) * 128],
                        lnxt(k), start=(k == 0), stop=(k == DC - 1)),
                        reads=(wsq[par], hba), writes=(pb,),
                        tag=f"e{st}p1_{p}_{fc}_{k}")
                if fc >= 2:
                    aux(fc - 2)
            ths.append(grp)

        def tail():
            aux(30)
            aux(31)
            sch.add("ve", lambda h: h.tensor_scalar(
                rv[0:1, :], P[6][0:1, 0:TT], 1.0 / 4096.0, None, ALU.mult),
                reads=(rs["P6"],), writes=(rvr,), tag=f"e{st}m1_{p}")
            sch.add("se", lambda h: h.activation(
                rv[0:1, :], rv[0:1, :], AF.Ln, bias=EPS6, scale=1.0),
                reads=(rvr, csts), writes=(rvr,), tag=f"e{st}l1_{p}")
            sch.add("se", lambda h: h.activation(
                rv[0:1, :], rv[0:1, :], AF.Exp, bias=0.0, scale=-0.5),
                reads=(rvr,), writes=(rvr,), tag=f"e{st}r1_{p}")
            sch.add("te", lambda h: h.matmul(
                P[6][:, 0:TT], cst["ones_row"], rv[0:1, :],
                start=True, stop=True),
                reads=(rvr, csts), writes=(rs["P6"],), tag=f"e{st}bc1_{p}")
            sch.add("ve", lambda h: h.tensor_copy(rv, P[6][:, 0:TT]),
                    reads=(rs["P6"],), writes=(rvr,), tag=f"e{st}rvc_{p}")
        ths.append(tail)
        return ths

    def kloop(p):
        rv = rinv1s[p % 2]
        rvr = RV[p % 2]
        for kx in range(34):
            if kx < 32:
                par = kx % 2
                w2of = 4096 + par * 1024
                sch.add("ge", lambda h, p=p, kx=kx, w2of=w2of: h.dma_start(
                    out=WS[:, w2of:w2of + 1024],
                    in_=hap(din["e_W2"], p * 4096 * D + kx * 128 * D,
                            [[D, 128], [1, 1024]])),
                    reads=(), writes=(w2q[par],),
                    dma=("sW" if par == 0 else "sWb"), tag=f"e{st}W2_{p}_{kx}")
                g1c = bcol(cst["e_g1"], p * 32 + kx)
                g17b = SM[:, 360 + p * 32 + kx:
                          361 + p * 32 + kx].broadcast_to([128, TT])
                be1c = bcol(cst["e_be1"], p * 32 + kx)
                sch.add("ve", lambda h, kx=kx, g17b=g17b: h.scalar_tensor_tensor(
                    FB[:, 1536:1536 + TT], hraw(kx), 0.0, g17b,
                    ALU.is_lt, ALU.mult),
                    reads=(HR[kx], csts, smr), writes=(fbr,),
                    tag=f"e{st}t1_{p}_{kx}")
                sch.add("ve", lambda h, kx=kx, g1c=g1c, rv=rv:
                        h.scalar_tensor_tensor(
                    FA[:, 0:TT], hraw(kx), g1c, rv, ALU.mult, ALU.mult),
                    reads=(HR[kx], csts, rvr), writes=(far,),
                    tag=f"e{st}u_{p}_{kx}")
                sch.add("ve", lambda h: h.tensor_add(
                    FA[:, 512:512 + TT], FA[:, 0:TT], FB[:, 1536:1536 + TT]),
                    reads=(far, fbr), writes=(far,), tag=f"e{st}v_{p}_{kx}")
                sch.add("se", lambda h, kx=kx, be1c=be1c: h.activation(
                    hraw(kx), FA[:, 512:512 + TT], AF.Identity,
                    bias=be1c, scale=1.0),
                    reads=(far, csts), writes=(HR[kx],),
                    tag=f"e{st}mp1_{p}_{kx}")
            if kx >= 2:
                k = kx - 2
                par = k % 2
                w2of = 4096 + par * 1024
                for do in range(8):
                    pb = rs[f"P{do}"]
                    sch.add("te", lambda h, k=k, do=do, w2of=w2of: h.matmul(
                        P[do][:, 0:TT],
                        WS[:, w2of + do * 128:w2of + (do + 1) * 128],
                        hraw(k), start=(k == 0), stop=(k == 31)),
                        reads=(w2q[par], HR[k]), writes=(pb,),
                        tag=f"e{st}p2_{p}_{k}_{do}")

    def evac(p):
        s = p % 3
        for dc in range(8):
            pb = rs[f"P{dc}"]
            sch.add("se", lambda h, dc=dc, p=p: h.activation(
                o2(dc), P[dc][:, 0:TT], AF.Identity,
                bias=bcol(cst["e_b2"], p * 8 + dc), scale=1.0),
                reads=(pb, csts), writes=(smbr,), tag=f"e{st}ev2_{p}_{dc}")
            if dc == 0:
                sch.add("te", lambda h, p=p: h.matmul(
                    P[5][:, 0:TT], SMb[:, 5248 + p * 128:5248 + (p + 1) * 128],
                    o2(0), start=True, stop=True),
                    reads=(smbr, csts), writes=(rs["P5"],),
                    tag=f"e{st}rot2_{p}")
                sch.add("se", lambda h: h.activation(
                    o2(0), P[5][:, 0:TT], AF.Identity, bias=0.0, scale=1.0),
                    reads=(rs["P5"],), writes=(smbr,), tag=f"e{st}rev2_{p}")
        for dc in range(8):
            sch.add("se", lambda h, dc=dc, s=s: h.activation(
                feat(s, dc), o2(dc), AF.Square, bias=0.0, scale=1.0),
                reads=(smbr,), writes=(hbc,), tag=f"e{st}sq2_{p}_{dc}")
            sch.add("te", lambda h, dc=dc, s=s: h.matmul(
                P[7][0:1, 0:TT], cst["ones_col_bf"], feat(s, dc),
                start=(dc == 0), stop=(dc == 7)),
                reads=(hbc, csts), writes=(rs["P7"],),
                tag=f"e{st}st2_{p}_{dc}")
        m2row = FB[0:1, 512:512 + TT]
        sch.add("ve", lambda h: h.tensor_scalar(
            m2row, P[7][0:1, 0:TT], 1.0 / 1024.0, None, ALU.mult),
            reads=(rs["P7"],), writes=(fbr,), tag=f"e{st}m2_{p}")
        sch.add("se", lambda h: h.activation(
            m2row, m2row, AF.Ln, bias=EPS6, scale=1.0),
            reads=(fbr, csts), writes=(fbr,), tag=f"e{st}l2_{p}")
        sch.add("se", lambda h: h.activation(
            m2row, m2row, AF.Exp, bias=0.0, scale=-0.5),
            reads=(fbr,), writes=(fbr,), tag=f"e{st}r2_{p}")
        sch.add("te", lambda h: h.matmul(
            P[7][:, 0:TT], cst["ones_row"], m2row, start=True, stop=True),
            reads=(fbr, csts), writes=(rs["P7"],), tag=f"e{st}bc2_{p}")
        sch.add("ve", lambda h: h.tensor_copy(rinv2s, P[7][:, 0:TT]),
                reads=(rs["P7"],), writes=(far,), tag=f"e{st}rv2c_{p}")
        if p < 8:
            pf_a(p + 1)

    def c_thunks(p, tt):
        s = p % 3
        team = p // 3
        ths = []
        tes = []
        for dc in range(8):
            def mp2(dc=dc):
                g2c = bcol(cst["e_g2"], p * 8 + dc)
                g27b = SM[:, 288 + p * 8 + dc:
                          289 + p * 8 + dc].broadcast_to([128, TT])
                b2pc = SM[:, 144 + p * 8 + dc:145 + p * 8 + dc]
                ctxc = SM[:, 216 + p * 8 + dc:217 + p * 8 + dc]
                sch.add("ve", lambda h, dc=dc, g27b=g27b: h.scalar_tensor_tensor(
                    FB[:, 1536:1536 + TT], o2(dc), 0.0, g27b,
                    ALU.is_lt, ALU.mult),
                    reads=(smbr, smr), writes=(fbr,),
                    tag=f"e{st}t2_{p}_{dc}")
                sch.add("ve", lambda h, dc=dc, g2c=g2c: h.scalar_tensor_tensor(
                    FA[:, 0:TT], o2(dc), g2c, rinv2s, ALU.mult, ALU.mult),
                    reads=(smbr, csts, far), writes=(far,),
                    tag=f"e{st}u2_{p}_{dc}")
                sch.add("ve", lambda h: h.tensor_add(
                    FA[:, 512:512 + TT], FA[:, 0:TT], FB[:, 1536:1536 + TT]),
                    reads=(far, fbr), writes=(far,), tag=f"e{st}v2_{p}_{dc}")
                sch.add("se", lambda h, b2pc=b2pc: h.activation(
                    FA[:, 1536:1536 + TT], FA[:, 512:512 + TT], AF.Identity,
                    bias=b2pc, scale=1.0),
                    reads=(far, smr), writes=(far,), tag=f"e{st}w2_{p}_{dc}")
                sch.add("se", lambda h: h.activation(
                    FB[:, 1536:1536 + TT], FA[:, 1536:1536 + TT], AF.Tanh,
                    bias=0.0, scale=1.0),
                    reads=(far,), writes=(fbr,), tag=f"e{st}th_{p}_{dc}")
                sch.add("ve", lambda h, s=s, dc=dc, ctxc=ctxc:
                        h.scalar_tensor_tensor(
                    feat(s, dc), FB[:, 1536:1536 + TT], ctxc,
                    FA[:, 1536:1536 + TT], ALU.mult, ALU.add),
                    reads=(fbr, far, smr), writes=(hbc,),
                    tag=f"e{st}ft_{p}_{dc}")
            ths.append(mp2)

            def mp2te(dc=dc):
                wp = hap(meta["SMb"], 6416 + dc, [[6656, 128], [1, 1]])
                wt = hap(meta["SMb"], 6424 + dc, [[6656, 128], [1, 1]])
                sch.add("te", lambda h, s=s, dc=dc, wp=wp: h.matmul(
                    P[7][0:1, 0:TT], wp, feat(s, dc),
                    start=(dc == 0), stop=(dc == 7)),
                    reads=(hbc, csts), writes=(rs["P7"],),
                    tag=f"e{st}pq_{p}_{dc}")
                sch.add("te", lambda h, s=s, dc=dc, wt=wt: h.matmul(
                    P[4][0:1, 0:TT], wt, feat(s, dc),
                    start=(dc == 0), stop=(dc == 7)),
                    reads=(hbc, csts), writes=(rs["P4"],),
                    tag=f"e{st}qq_{p}_{dc}")
            tes.append(mp2te)
            if dc >= 2:
                ths.append(tes.pop(0))
        ths.extend(tes)
        tes = []

        def rows():
            bmv = SM[0:1, 2020 + p:2021 + p]
            sch.add("ve", lambda h, bmv=bmv: h.tensor_scalar(
                FB[0:1, 1536:1536 + TT], P[7][0:1, 0:TT], bmv, None, ALU.add),
                reads=(rs["P7"], csts), writes=(fbr,), tag=f"e{st}psr_{p}")
            sch.add("ve", lambda h: h.tensor_copy(
                FB[0:1, 1024:1024 + TT], P[4][0:1, 0:TT]),
                reads=(rs["P4"],), writes=(fbr,), tag=f"e{st}qr_{p}")
        ths.append(rows)

        for i in range(TT // 128):
            def ptr(i=i):
                sch.add("te", lambda h, i=i: h.transpose(
                    P[4][:, 64 + i:65 + i].bitcast(F32),
                    FB[0:1, 1536 + i * 128:1536 + (i + 1) * 128],
                    cst["ident_f"][0:1, 0:1]),
                    reads=(fbr, csts), writes=(rs["P4"],),
                    tag=f"e{st}psT{p}_{i}")
                sch.add("ve", lambda h, i=i, p=p: h.tensor_copy(
                    SM[:, 648 + i * 32 + p:649 + i * 32 + p],
                    P[4][:, 64 + i:65 + i].bitcast(F32)),
                    reads=(rs["P4"],), writes=(smr,), tag=f"e{st}psE{p}_{i}")
                sch.add("te", lambda h, i=i: h.transpose(
                    P[4][:, 72 + i:73 + i].bitcast(F32),
                    FB[0:1, 1024 + i * 128:1024 + (i + 1) * 128],
                    cst["ident_f"][0:1, 0:1]),
                    reads=(fbr, csts), writes=(rs["P4"],),
                    tag=f"e{st}qT{p}_{i}")
                sch.add("ve", lambda h, i=i, p=p: h.tensor_copy(
                    SM[:, 648 + i * 32 + 16 + p:649 + i * 32 + 16 + p],
                    P[4][:, 72 + i:73 + i].bitcast(F32)),
                    reads=(rs["P4"],), writes=(smr,), tag=f"e{st}qE{p}_{i}")
            ths.append(ptr)

        if s == 2:
            ths.append(lambda: _eidos_team(sch, rs, meta, st, tt, team))
        return ths

    def ln_tile(tt2):
        emit_ln(sch, rs, meta, tt2 * TT, TT, lnxt, EPS5, tag=f"e{st}ln{tt2}",
                sq_off=4096, dst_res="HBa",
                sq_res_fn=lambda d: HR[d])

    for tt in range(NTT):
        t0 = tt * TT
        if tt == 0:
            pf_a(0)
            ln_tile(0)
            for th in a_thunks(0):
                th()
        for p in range(9):
            kloop(p)
            evac(p)
            cth = c_thunks(p, tt)
            if p < 8:
                ath = a_thunks(p + 1)
                j = 0
                ncp = len(cth)
                na = len(ath)
                for i, a in enumerate(ath):
                    a()
                    while j < ncp and j / ncp <= (i + 1) / na:
                        cth[j]()
                        j += 1
                while j < ncp:
                    cth[j]()
                    j += 1
            else:
                for c in cth:
                    c()
                if tt < NTT - 1:
                    pf_a(0)
                    ln_tile(tt + 1)
                _eidos_final(sch, rs, meta, st, tt, t0)
        if tt < NTT - 1:
            for th in a_thunks(0):
                th()



def _softmax3_cols(sch, rs, meta, col_fn, out_rows_off, tag):
    SM, FB, P, cst = meta["SM"], meta["FB"], meta["P"], meta["cst"]
    TT = meta["TT"]
    p5, fbr, smr, csts = rs["P5"], rs["FB"], rs["SM"], rs["consts"]
    nch = TT // 128
    for i in range(nch):
        cols = col_fn(i)
        sch.add("ve", lambda h, cols=cols: h.reduce_max(
            SM[:, 795:796], cols, mybir.AxisListType.X),
            reads=(smr,), writes=(smr,), tag=f"{tag}mx{i}")
        sch.add("ve", lambda h: h.tensor_scalar(
            SM[:, 796:797], SM[:, 795:796], -1.0, None, ALU.mult),
            reads=(smr,), writes=(smr,), tag=f"{tag}ng{i}")
        sch.add("se", lambda h, cols=cols: h.activation(
            cols, cols, AF.Exp, bias=SM[:, 796:797], scale=1.0),
            reads=(smr,), writes=(smr,), tag=f"{tag}ex{i}")
        sch.add("ve", lambda h, cols=cols: h.reduce_sum(
            SM[:, 797:798], cols, mybir.AxisListType.X),
            reads=(smr,), writes=(smr,), tag=f"{tag}sm{i}")
        sch.add("ve", lambda h: h.reciprocal(
            SM[:, 798:799], SM[:, 797:798]),
            reads=(smr,), writes=(smr,), tag=f"{tag}rc{i}")
        sch.add("ve", lambda h, cols=cols: h.tensor_scalar(
            cols, cols, SM[:, 798:799], None, ALU.mult),
            reads=(smr,), writes=(smr,), tag=f"{tag}wn{i}")
        for s in range(3):
            sch.add("te", lambda h, cols=cols, s=s: h.transpose(
                P[5][0:1, s * 128:(s + 1) * 128], cols[:, s:s + 1],
                cst["ident_f"]),
                reads=(smr, csts), writes=(p5,), tag=f"{tag}tr{i}_{s}")
            sch.add("ve", lambda h, i=i, s=s: h.tensor_copy(
                FB[0:1, out_rows_off + s * TT + i * 128:
                   out_rows_off + s * TT + (i + 1) * 128],
                P[5][0:1, s * 128:(s + 1) * 128]),
                reads=(p5,), writes=(fbr,), tag=f"{tag}te{i}_{s}")


def _eidos_team(sch, rs, meta, st, tt, team):
    SM, FB, FA, HB, P, cst = (meta["SM"], meta["FB"], meta["FA"], meta["HB"],
                              meta["P"], meta["cst"])
    TT = meta["TT"]
    dram = meta["dram"]
    p5, fbr, smr, far, hbc, csts = (rs["P5"], rs["FB"], rs["SM"], rs["FA"],
                                    rs["HBc"], rs["consts"])
    nch = TT // 128

    def feat(s, dc):
        o = 20480 + s * 8 * TT + dc * TT
        return HB[:, o:o + TT]

    _softmax3_cols(sch, rs, meta,
                   lambda i: SM[:, 648 + i * 32 + 3 * team:648 + i * 32 + 3 * team + 3],
                   0, f"e{st}wi{tt}_{team}_")
    for i in range(nch):
        wi = SM[:, 648 + i * 32 + 3 * team:648 + i * 32 + 3 * team + 3]
        qq = SM[:, 648 + i * 32 + 16 + 3 * team:648 + i * 32 + 16 + 3 * team + 3]
        sch.add("ve", lambda h, wi=wi, qq=qq: h.tensor_mul(
            SM[:, 792:795], wi, qq), reads=(smr,), writes=(smr,),
            tag=f"e{st}tsm{tt}_{team}_{i}")
        sch.add("ve", lambda h: h.reduce_sum(
            SM[:, 795:796], SM[:, 792:795], mybir.AxisListType.X),
            reads=(smr,), writes=(smr,), tag=f"e{st}tss{tt}_{team}_{i}")
        sch.add("ve", lambda h, i=i, team=team: h.tensor_scalar(
            SM[:, 776 + i * 4 + team:777 + i * 4 + team], SM[:, 795:796],
            SM[:, 2030 + team:2031 + team], None, ALU.add),
            reads=(smr, csts), writes=(smr,), tag=f"e{st}tsb{tt}_{team}_{i}")
    for s3 in range(3):
        sch.add("te", lambda h, s3=s3: h.matmul(
            P[5][:, 0:TT], cst["ones_row"], FB[0:1, s3 * TT:(s3 + 1) * TT],
            start=True, stop=True),
            reads=(fbr, csts), writes=(p5,), tag=f"e{st}wib_{tt}_{team}_{s3}")
        for dc in range(8):
            if s3 == 0:
                sch.add("ve", lambda h, s3=s3, dc=dc: h.scalar_tensor_tensor(
                    FA[:, dc * TT:(dc + 1) * TT], feat(s3, dc), 1.0,
                    P[5][:, 0:TT], ALU.mult, ALU.mult),
                    reads=(hbc, p5), writes=(far,), tag=f"e{st}tf0_{tt}_{team}_{dc}")
            else:
                sch.add("ve", lambda h, s3=s3, dc=dc: h.scalar_tensor_tensor(
                    FB[:, 1536:1536 + TT], feat(s3, dc), 1.0, P[5][:, 0:TT],
                    ALU.mult, ALU.mult),
                    reads=(hbc, p5), writes=(fbr,), tag=f"e{st}tfm{s3}_{tt}_{team}_{dc}")
                sch.add("ve", lambda h, dc=dc: h.tensor_add(
                    FA[:, dc * TT:(dc + 1) * TT], FA[:, dc * TT:(dc + 1) * TT],
                    FB[:, 1536:1536 + TT]),
                    reads=(fbr, far), writes=(far,), tag=f"e{st}tfa{s3}_{tt}_{team}_{dc}")
    sch.add("ge", lambda h, team=team: h.dma_start(
        out=dram["sp_tf"][team].rearrange("(dc q) t -> q dc t", dc=8),
        in_=FA[:, 0:8 * TT].rearrange("q (dc t) -> q dc t", dc=8)),
        reads=(far,), writes=(rs["dram"],), dma="sGd", tag=f"e{st}tfsp_{tt}_{team}")


def _eidos_final(sch, rs, meta, st, tt, t0):
    SM, FB, FA, HB, P, X, cst = (meta["SM"], meta["FB"], meta["FA"], meta["HB"],
                                 meta["P"], meta["X"], meta["cst"])
    TT = meta["TT"]
    dram = meta["dram"]
    p5, fbr, smr, far, hbc, csts = (rs["P5"], rs["FB"], rs["SM"], rs["FA"],
                                    rs["HBc"], rs["consts"])
    _softmax3_cols(sch, rs, meta,
                   lambda i: SM[:, 776 + i * 4:776 + i * 4 + 3],
                   0, f"e{st}wt{tt}_")
    for t3 in range(3):
        sch.add("ge", lambda h, t3=t3: h.dma_start(
            out=HB[:, 20480 + t3 * 8 * TT:20480 + (t3 + 1) * 8 * TT].rearrange(
                "q (dc t) -> q dc t", dc=8),
            in_=dram["sp_tf"][t3].rearrange("(dc q) t -> q dc t", dc=8)),
            reads=(rs["dram"],), writes=(rs["HBc"],), dma="sGd",
            tag=f"e{st}tfrl_{tt}_{t3}")
        sch.add("te", lambda h, t3=t3: h.matmul(
            P[5][:, 0:TT], cst["ones_row"], FB[0:1, t3 * TT:(t3 + 1) * TT],
            start=True, stop=True),
            reads=(fbr, csts), writes=(p5,), tag=f"e{st}wtb_{tt}_{t3}")
        for dc in range(8):
            tfb = HB[:, 20480 + t3 * 8 * TT + dc * TT:20480 + t3 * 8 * TT + (dc + 1) * TT]
            if t3 == 0:
                sch.add("ve", lambda h, tfb=tfb, dc=dc: h.scalar_tensor_tensor(
                    FA[:, dc * TT:(dc + 1) * TT], tfb, 1.0, P[5][:, 0:TT],
                    ALU.mult, ALU.mult),
                    reads=(rs["HBc"], p5), writes=(far,), tag=f"e{st}o0_{tt}_{dc}")
            else:
                sch.add("ve", lambda h, tfb=tfb, dc=dc: h.scalar_tensor_tensor(
                    FB[:, 1536:1536 + TT], tfb, 1.0, P[5][:, 0:TT],
                    ALU.mult, ALU.mult),
                    reads=(rs["HBc"], p5), writes=(fbr,), tag=f"e{st}om{t3}_{tt}_{dc}")
                sch.add("ve", lambda h, dc=dc: h.tensor_add(
                    FA[:, dc * TT:(dc + 1) * TT], FA[:, dc * TT:(dc + 1) * TT],
                    FB[:, 1536:1536 + TT]),
                    reads=(fbr, far), writes=(far,), tag=f"e{st}oa{t3}_{tt}_{dc}")
    for dc in range(8):
        sch.add("ve", lambda h, dc=dc, t0=t0: h.scalar_tensor_tensor(
            X[dc][:, t0:t0 + TT], FA[:, dc * TT:(dc + 1) * TT], 0.5,
            X[dc][:, t0:t0 + TT], ALU.mult, ALU.add),
            reads=(far,), writes=(rs[f"X{dc}"],), tag=f"e{st}res_{tt}_{dc}")


def emit_l1_attn(sch, rs, meta):
    nc = sch.nc
    din, X, P, HB, WS, SM, SMb, FA, FB, cst = (
        meta["din"], meta["X"], meta["P"], meta["HB"], meta["WS"], meta["SM"],
        meta["SMb"], meta["FA"], meta["FB"], meta["cst"])
    T, TT, NTT = meta["T"], meta["TT"], meta["NTT"]
    dram = meta["dram"]
    csts = rs["consts"]
    wsq = [rs["WS0"], rs["WS1"]]
    hba, hbb, hbc = rs["HBa"], rs["HBb"], rs["HBc"]
    far, fbr, smr, smbr = rs["FA"], rs["FB"], rs["SM"], rs["SMb"]

    def lnxT(d):
        return HB[:, d * TT:(d + 1) * TT]

    def xcv(ch):
        return HB[:, 4096 + ch * TT:4096 + (ch + 1) * TT]

    def zsT(ch):
        return HB[:, 12288 + ch * TT:12288 + (ch + 1) * TT]

    def ygT(ch):
        return HB[:, 20480 + ch * TT:20480 + (ch + 1) * TT]

    Cmb = lambda c0, n: SMb[64:80, c0:c0 + n]
    hmT = lambda c0, n: SMb[64:80, 2048 + c0:2048 + c0 + n]
    carries = SM[:, 512:560]
    hpbuf = [SM[:, 560:816], SM[:, 816:1072]]
    last3 = SM[:, 1072:1120]
    dBr = FB[:, 640:640 + TT]
    dltr = FB[0:1, 1152:1152 + TT]

    sch.add("ge", lambda h: h.dma_start(
        out=WS[:, 4096:4624].rearrange("q (k c) -> q k c", k=16),
        in_=din["l1_W_xproj"].ap().rearrange("(k q) c -> q k c", k=16)),
        reads=(), writes=(rs["WS2a"],), dma="sW", tag="l1wxp")

    for tt in range(NTT):
        t0 = tt * TT
        emit_ln(sch, rs, meta, t0, TT, lnxT, EPS5, tag=f"l1ln{tt}",
                sq_off=24576, sq_res="HBc", dst_res="HBa", f32_scr=2048)
        sch.add("sy", lambda h, t0=t0: h.dma_start(
            out=dram["sp_lnx4"][:, t0:t0 + TT].rearrange("(d q) t -> q d t", d=DC),
            in_=HB[:, 0:DC * TT].rearrange("q (d t) -> q d t", d=DC)),
            reads=(hba,), writes=(rs["dram"],), dma="sSp", tag=f"l1lsp{tt}")
        for xh in range(2):
            for k in range(DC):
                par = k % 2
                wof = par * 1024
                sch.add("ge", lambda h, k=k, xh=xh, wof=wof: h.dma_start(
                    out=WS[:, wof:wof + 1024],
                    in_=hap(din["l1_W_in"], k * 128 * 4096 + xh * 1024,
                            [[4096, 128], [1, 1024]])),
                    reads=(), writes=(wsq[par],),
                    dma=("sW" if par == 0 else "sWb"), tag=f"l1wp{tt}_{xh}_{k}")
                for f in range(8):
                    sch.add("te", lambda h, k=k, f=f, wof=wof, par=par: h.matmul(
                        P[f][:, 0:TT], WS[:, wof + f * 128:wof + (f + 1) * 128],
                        lnxT(k), start=(k == 0), stop=(k == DC - 1)),
                        reads=(wsq[par], hba), writes=(rs[f"P{f}"],),
                        tag=f"l1mp{tt}_{xh}_{k}_{f}")
            for f in range(8):
                ch = xh * 8 + f
                sch.add("se", lambda h, f=f, ch=ch: h.activation(
                    xcv(ch), P[f][:, 0:TT], AF.Identity, bias=0.0, scale=1.0),
                    reads=(rs[f"P{f}"],), writes=(hbb,), tag=f"l1xpe{tt}_{ch}")
                if tt == NTT - 1:
                    sch.add("ve", lambda h, ch=ch: h.tensor_copy(
                        last3[:, ch * 3:(ch + 1) * 3], xcv(ch)[:, TT - 3:TT]),
                        reads=(hbb,), writes=(smr,), tag=f"l1l3{tt}_{ch}")
        sch.add("sy", lambda h, t0=t0: h.dma_start(
            out=dram["sp_xconv"][:, t0:t0 + TT].rearrange(
                "(c q) t -> q c t", c=16),
            in_=HB[:, 4096:4096 + 16 * TT].rearrange("q (c t) -> q c t", c=16)),
            reads=(hbb,), writes=(rs["dram"],), dma="sSp", tag=f"l1psp{tt}")
    sch.add("sy", lambda h: h.dma_start(
        out=hap(dram["agB_in"], 0, [[48, 128], [1, 48]]), in_=last3),
        reads=(smr,), writes=(rs["dram"],), dma="sSp", tag="l1agBw")
    sch.add("ge", lambda h: h.collective_compute(
        "AllGather", ALU.bypass, replica_groups=[list(range(NCORES))],
        ins=[dram["agB_in"].ap().opt()], outs=[dram["agB_out"].ap().opt()]),
        reads=(rs["dram"],), writes=(rs["dram"],), tag="l1agB")
    for j in range(NCORES):
        sch.add("sy", lambda h, j=j: h.dma_start(
            out=FA[:, 2048 + (j % 2) * 48:2048 + (j % 2) * 48 + 48],
            in_=hap(dram["agB_out"], j * 6144, [[48, 128], [1, 48]])),
            reads=(rs["dram"],), writes=(far,), dma="sSp", tag=f"l1hl{j}")
        if j == 0:
            sch.add("ve", lambda h, j=j: h.tensor_scalar(
                carries, FA[:, 2048:2096], cst["prevsel"][:, j:j + 1], None,
                ALU.mult),
                reads=(far, csts), writes=(smr,), tag=f"l1hm{j}")
        else:
            sch.add("ve", lambda h, j=j: h.scalar_tensor_tensor(
                carries, FA[:, 2048 + (j % 2) * 48:2048 + (j % 2) * 48 + 48],
                cst["prevsel"][:, j:j + 1], carries, ALU.mult, ALU.add),
                reads=(far, csts, smr), writes=(smr,), tag=f"l1hm{j}")
    for tt in range(NTT):
        t0 = tt * TT
        sch.add("sy", lambda h, t0=t0: h.dma_start(
            out=HB[:, 4096:4096 + 16 * TT].rearrange("q (c t) -> q c t", c=16),
            in_=dram["sp_xconv"][:, t0:t0 + TT].rearrange(
                "(c q) t -> q c t", c=16)),
            reads=(rs["dram"],), writes=(hbb,), dma="sSp", tag=f"l1prl{tt}")
        for ch in range(16):
            ext = FB[:, 0:TT + 3]
            sch.add("ve", lambda h, ch=ch: h.tensor_copy(
                ext[:, 0:3], carries[:, ch * 3:(ch + 1) * 3]),
                reads=(smr,), writes=(fbr,), tag=f"l1cx{tt}_{ch}")
            sch.add("ve", lambda h, ch=ch: h.tensor_copy(
                ext[:, 3:TT + 3], xcv(ch)),
                reads=(hbb,), writes=(fbr,), tag=f"l1cy{tt}_{ch}")
            sch.add("ve", lambda h, ch=ch: h.tensor_copy(
                carries[:, ch * 3:(ch + 1) * 3], ext[:, TT:TT + 3]),
                reads=(fbr,), writes=(smr,), tag=f"l1cc{tt}_{ch}")
            w = lambda tap, ch=ch: cst["l1_conv_w"][:, ch * 4 + tap:ch * 4 + tap + 1]
            a1 = FB[:, 640:640 + TT]
            a2 = FB[:, 1152:1152 + TT]
            sch.add("ve", lambda h, ch=ch, w=w: h.tensor_scalar(
                a1, ext[:, 0:TT], w(0), None, ALU.mult),
                reads=(fbr, csts), writes=(fbr,), tag=f"l1c0{tt}_{ch}")
            sch.add("ve", lambda h, ch=ch, w=w: h.scalar_tensor_tensor(
                a2, ext[:, 1:TT + 1], w(1), a1, ALU.mult, ALU.add),
                reads=(fbr, csts), writes=(fbr,), tag=f"l1c1{tt}_{ch}")
            sch.add("ve", lambda h, ch=ch, w=w: h.scalar_tensor_tensor(
                a1, ext[:, 2:TT + 2], w(2), a2, ALU.mult, ALU.add),
                reads=(fbr, csts), writes=(fbr,), tag=f"l1c2{tt}_{ch}")
            sch.add("ve", lambda h, ch=ch, w=w: h.scalar_tensor_tensor(
                a2, ext[:, 3:TT + 3], w(3), a1, ALU.mult, ALU.add),
                reads=(fbr, csts), writes=(fbr,), tag=f"l1c3{tt}_{ch}")
            sch.add("se", lambda h, ch=ch: h.activation(
                xcv(ch), a2, AF.Silu, bias=cst["l1_conv_b"][:, ch:ch + 1],
                scale=1.0),
                reads=(fbr, csts), writes=(hbb,), tag=f"l1cs{tt}_{ch}")
        for k in range(16):
            sch.add("te", lambda h, k=k: h.matmul(
                P[7][0:1, 0:TT], WS[:, 4096 + k * 33:4096 + k * 33 + 1],
                xcv(k), start=(k == 0), stop=(k == 15)),
                reads=(rs["WS2a"], hbb), writes=(rs["P7"],), tag=f"l1sd{tt}_{k}")
            sch.add("te", lambda h, k=k: h.matmul(
                P[5][0:16, 0:TT], WS[:, 4096 + k * 33 + 1:4096 + k * 33 + 17],
                xcv(k), start=(k == 0), stop=(k == 15)),
                reads=(rs["WS2a"], hbb), writes=(rs["P5"],), tag=f"l1sb{tt}_{k}")
            sch.add("te", lambda h, k=k: h.matmul(
                P[3][64:80, 0:TT], WS[:, 4096 + k * 33 + 17:4096 + k * 33 + 33],
                xcv(k), start=(k == 0), stop=(k == 15)),
                reads=(rs["WS2a"], hbb), writes=(rs["P3"],), tag=f"l1sc{tt}_{k}")
        sch.add("se", lambda h: h.activation(
            dltr, P[7][0:1, 0:TT], AF.Exp, bias=0.0, scale=1.0),
            reads=(rs["P7"],), writes=(fbr,), tag=f"l1de{tt}")
        sch.add("ve", lambda h: h.tensor_scalar(
            dltr, dltr, 1.0, None, ALU.add), reads=(fbr,), writes=(fbr,),
            tag=f"l1d1{tt}")
        sch.add("se", lambda h: h.activation(
            dltr, dltr, AF.Ln, bias=0.0, scale=1.0),
            reads=(fbr,), writes=(fbr,), tag=f"l1d2{tt}")
        sch.add("ve", lambda h, t0=t0: h.tensor_copy(
            Cmb(t0, TT), P[3][64:80, 0:TT]),
            reads=(rs["P3"],), writes=(smbr,), tag=f"l1cme{tt}")
        sch.add("te", lambda h: h.matmul(
            P[4][0:16, 0:TT], cst["ones_row"][:, 0:16], dltr,
            start=True, stop=True),
            reads=(fbr, csts), writes=(rs["P4"],), tag=f"l1db{tt}")
        sch.add("ve", lambda h: h.tensor_tensor(
            dBr[0:16, :], P[5][0:16, 0:TT], P[4][0:16, 0:TT], ALU.mult)
            if False else h.tensor_copy(FB[0:16, 1664:1664 + 256]
                                        if False else dBr[0:16, :],
                                        P[5][0:16, 0:TT]),
            reads=(rs["P5"],), writes=(fbr,), tag=f"l1bme{tt}")
        sch.add("ve", lambda h: h.tensor_tensor(
            dBr[0:16, :], dBr[0:16, :], P[4][0:16, 0:TT], ALU.mult),
            reads=(fbr, rs["P4"]), writes=(fbr,), tag=f"l1dB{tt}")
        for i in range(TT // 128):
            sch.add("te", lambda h, i=i: h.transpose(
                P[2][:, 0:16], dBr[0:16, i * 128:(i + 1) * 128],
                cst["ident_f"][0:16, 0:16]),
                reads=(fbr, csts), writes=(rs["P2"],), tag=f"l1dT{tt}_{i}")
            sch.add("ve", lambda h, i=i: h.tensor_copy(
                FB[:, 1664:1680], P[2][:, 0:16]),
                reads=(rs["P2"],), writes=(fbr,), tag=f"l1dTe{tt}_{i}")
            sch.add("ge", lambda h: h.dma_start(
                out=FB[:, 1680:1696].bitcast(BF16)[:, 0:16],
                in_=FB[:, 1664:1680]),
                reads=(fbr,), writes=(fbr,), dma="sGd", tag=f"l1dTb{tt}_{i}")
            for ch in range(16):
                sch.add("te", lambda h, i=i, ch=ch: h.transpose(
                    P[2][:, 128 + (ch % 2) * 64:128 + (ch % 2) * 64 + 64].bitcast(BF16),
                    xcv(ch)[:, i * 128:(i + 1) * 128], cst["ident"]),
                    reads=(hbb, csts), writes=(rs["P2"],), tag=f"l1xT{tt}_{i}_{ch}")
                sch.add("ve", lambda h, i=i, ch=ch: h.tensor_copy(
                    SMb[:, 4224 + (ch % 2) * 128:4224 + (ch % 2) * 128 + 128],
                    P[2][:, 128 + (ch % 2) * 64:128 + (ch % 2) * 64 + 64].bitcast(BF16)),
                    reads=(rs["P2"],), writes=(smbr,), tag=f"l1xTe{tt}_{i}_{ch}")
                sch.add("te", lambda h, i=i, ch=ch, tt=tt: h.matmul(
                    P[6][:, ch * 16:(ch + 1) * 16],
                    SMb[:, 4224 + (ch % 2) * 128:4224 + (ch % 2) * 128 + 128],
                    FB[:, 1680:1696].bitcast(BF16)[:, 0:16],
                    start=True, stop=True),
                    reads=(smbr, fbr), writes=(rs["P6"],),
                    tag=f"l1hp{tt}_{i}_{ch}")
            if tt == 0 and i == 0:
                sch.add("ve", lambda h: h.tensor_copy(hpbuf[0], P[6][:, 0:256]),
                        reads=(rs["P6"],), writes=(smr,), tag=f"l1hpa{tt}_{i}")
            else:
                sch.add("ve", lambda h: h.tensor_add(
                    hpbuf[0], hpbuf[0], P[6][:, 0:256]),
                    reads=(rs["P6"], smr), writes=(smr,), tag=f"l1hpa{tt}_{i}")
    sch.add("sy", lambda h: h.dma_start(
        out=hap(dram["agC_in"], 0, [[256, 128], [1, 256]]), in_=hpbuf[0]),
        reads=(smr,), writes=(rs["dram"],), dma="sSp", tag="l1agCw")
    sch.add("ge", lambda h: h.collective_compute(
        "AllGather", ALU.bypass, replica_groups=[list(range(NCORES))],
        ins=[dram["agC_in"].ap().opt()], outs=[dram["agC_out"].ap().opt()]),
        reads=(rs["dram"],), writes=(rs["dram"],), tag="l1agC")
    for j in range(NCORES):
        sch.add("sy", lambda h, j=j: h.dma_start(
            out=FA[:, 2048 + (j % 2) * 256:2048 + (j % 2) * 256 + 256],
            in_=hap(dram["agC_out"], j * 32768, [[256, 128], [1, 256]])),
            reads=(rs["dram"],), writes=(far,), dma="sSp", tag=f"l1hc{j}")
        if j == 0:
            sch.add("ve", lambda h: h.tensor_copy(hpbuf[0], FA[:, 2048:2304]),
                    reads=(far,), writes=(smr,), tag="l1hs0")
        else:
            sch.add("ve", lambda h, j=j: h.tensor_add(
                hpbuf[j % 2], hpbuf[(j + 1) % 2],
                FA[:, 2048 + (j % 2) * 256:2048 + (j % 2) * 256 + 256]),
                reads=(far, smr), writes=(smr,), tag=f"l1hs{j}")
    hm = hpbuf[(NCORES - 1) % 2]
    sch.add("ve", lambda h: h.tensor_scalar(
        hm, hm, 1.0 / (NCORES * T), None, ALU.mult),
        reads=(smr,), writes=(smr,), tag="l1hdiv")
    for ch in range(16):
        sch.add("te", lambda h, ch=ch: h.transpose(
            P[2][0:16, (ch % 4) * 128:(ch % 4) * 128 + 128],
            hm[:, ch * 16:(ch + 1) * 16], cst["ident_f"]),
            reads=(smr, csts), writes=(rs["P2"],), tag=f"l1hT{ch}")
        sch.add("ve", lambda h, ch=ch: h.tensor_copy(
            hmT(ch * 128, 128), P[2][0:16, (ch % 4) * 128:(ch % 4) * 128 + 128]),
            reads=(rs["P2"],), writes=(smbr,), tag=f"l1hTe{ch}")
    for tt in range(NTT):
        t0 = tt * TT
        sch.add("sy", lambda h, t0=t0: h.dma_start(
            out=HB[:, 0:DC * TT].rearrange("q (d t) -> q d t", d=DC),
            in_=dram["sp_lnx4"][:, t0:t0 + TT].rearrange("(d q) t -> q d t", d=DC)),
            reads=(rs["dram"],), writes=(hba,), dma="sSp", tag=f"l1lrl{tt}")
        sch.add("sy", lambda h, t0=t0: h.dma_start(
            out=HB[:, 4096:4096 + 16 * TT].rearrange("q (c t) -> q c t", c=16),
            in_=dram["sp_xconv"][:, t0:t0 + TT].rearrange(
                "(c q) t -> q c t", c=16)),
            reads=(rs["dram"],), writes=(hbb,), dma="sSp", tag=f"l1prl2{tt}")
        for zh in range(2):
            for k in range(DC):
                par = k % 2
                wof = par * 1024
                sch.add("ge", lambda h, k=k, zh=zh, wof=wof: h.dma_start(
                    out=WS[:, wof:wof + 1024],
                    in_=hap(din["l1_W_in"], k * 128 * 4096 + 2048 + zh * 1024,
                            [[4096, 128], [1, 1024]])),
                    reads=(), writes=(wsq[par],),
                    dma=("sW" if par == 0 else "sWb"), tag=f"l1wz{tt}_{zh}_{k}")
                for f in range(8):
                    sch.add("te", lambda h, k=k, f=f, wof=wof, par=par: h.matmul(
                        P[f][:, 0:TT], WS[:, wof + f * 128:wof + (f + 1) * 128],
                        lnxT(k), start=(k == 0), stop=(k == DC - 1)),
                        reads=(wsq[par], hba), writes=(rs[f"P{f}"],),
                        tag=f"l1mz{tt}_{zh}_{k}_{f}")
            for f in range(8):
                ch = zh * 8 + f
                sch.add("se", lambda h, f=f, ch=ch: h.activation(
                    zsT(ch), P[f][:, 0:TT], AF.Silu, bias=0.0, scale=1.0),
                    reads=(rs[f"P{f}"],), writes=(hbb,), tag=f"l1zs{tt}_{ch}")
        for ch in range(16):
            pi = ch % 4
            sch.add("te", lambda h, ch=ch, t0=t0, pi=pi: h.matmul(
                P[pi][:, 0:TT], hmT(ch * 128, 128), Cmb(t0, TT),
                start=True, stop=True),
                reads=(smbr,), writes=(rs[f"P{pi}"],), tag=f"l1ym{tt}_{ch}")
            sch.add("ve", lambda h, ch=ch, pi=pi: h.scalar_tensor_tensor(
                FA[:, 2048:2048 + TT], xcv(ch), cst["l1_D"][:, ch:ch + 1],
                P[pi][:, 0:TT], ALU.mult, ALU.add),
                reads=(hbb, csts, rs[f"P{pi}"]), writes=(far,),
                tag=f"l1yd{tt}_{ch}")
            sch.add("ve", lambda h, ch=ch: h.tensor_tensor(
                ygT(ch), FA[:, 2048:2048 + TT], zsT(ch), ALU.mult),
                reads=(far, hbb), writes=(hbc,), tag=f"l1yg{tt}_{ch}")
        for k in range(16):
            par = k % 2
            wof = par * 1024
            sch.add("ge", lambda h, k=k, wof=wof: h.dma_start(
                out=WS[:, wof:wof + 1024],
                in_=din["l1_W_out"][k * 128:(k + 1) * 128, :]),
                reads=(), writes=(wsq[par],),
                dma=("sW" if par == 0 else "sWb"), tag=f"l1wo{tt}_{k}")
            for do in range(8):
                sch.add("te", lambda h, k=k, do=do, wof=wof, par=par: h.matmul(
                    P[do][:, 0:TT], WS[:, wof + do * 128:wof + (do + 1) * 128],
                    ygT(k), start=(k == 0), stop=(k == 15)),
                    reads=(wsq[par], hbc), writes=(rs[f"P{do}"],),
                    tag=f"l1wm{tt}_{k}_{do}")
        for do in range(8):
            sch.add("ve", lambda h, do=do, t0=t0: h.tensor_tensor(
                X[do][:, t0:t0 + TT], X[do][:, t0:t0 + TT], P[do][:, 0:TT],
                ALU.add),
                reads=(rs[f"P{do}"],), writes=(rs[f"X{do}"],),
                tag=f"l1res{tt}_{do}")



def _bcdt_perm():
    b = np.arange(32)
    bp = np.concatenate([b[0:16:2], b[1:16:2], b[16:]])
    return np.concatenate([bp, 32 + bp, 64 + np.arange(32)])


def prepare_inputs(inputs, T_core, ncores=NCORES):
    f32 = np.float32
    x = np.asarray(inputs["L0_attn_x"], f32).reshape(-1, D)

    g = {}
    g["wg_t"] = np.ascontiguousarray(np.asarray(inputs["L0_ffn_w_gate"], np.int32).T)
    g["wu_t"] = np.ascontiguousarray(np.asarray(inputs["L0_ffn_w_up"], np.int32).T)
    g["wd_t"] = np.ascontiguousarray(np.asarray(inputs["L0_ffn_w_down"], np.int32).T)

    g["l0_ln_w"] = np.ascontiguousarray(np.asarray(inputs["L0_attn_ln_w"], f32).reshape(DC, 128).T)
    g["l0_ln_b"] = np.ascontiguousarray(np.asarray(inputs["L0_attn_ln_b"], f32).reshape(DC, 128).T)
    W_in = np.asarray(inputs["L0_attn_W_in"], f32)
    perm = np.concatenate([np.arange(4096), 4096 + _bcdt_perm()])
    g["l0_W_in"] = np.ascontiguousarray(W_in[:, perm])
    g["l0_dt_bias"] = np.asarray(inputs["L0_attn_dt_bias"], f32).reshape(32, 1)
    g["l0_A"] = np.asarray(inputs["L0_attn_A_log"], f32).reshape(32, 1)
    g["l0_D"] = np.ascontiguousarray(
        np.repeat(np.asarray(inputs["L0_attn_D"], f32), 64).reshape(16, 128).T)
    g["l0_rms_w"] = np.ascontiguousarray(
        np.asarray(inputs["L0_attn_rms_w"], f32).reshape(16, 128).T)
    g["l0_W_out"] = np.ascontiguousarray(np.asarray(inputs["L0_attn_W_out"], f32))

    g["l1_W_in"] = np.ascontiguousarray(np.asarray(inputs["L1_attn_W_in"], f32))
    cw = np.asarray(inputs["L1_attn_conv_w"], f32).reshape(2048, 4)
    g["l1_conv_w"] = np.ascontiguousarray(
        cw.reshape(16, 128, 4).transpose(1, 0, 2).reshape(128, 64))
    g["l1_conv_b"] = np.ascontiguousarray(
        np.asarray(inputs["L1_attn_conv_b"], f32).reshape(16, 128).T)
    g["l1_W_xproj"] = np.ascontiguousarray(np.asarray(inputs["L1_attn_W_xproj"], f32))
    g["l1_D"] = np.ascontiguousarray(np.asarray(inputs["L1_attn_D"], f32).reshape(16, 128).T)
    g["l1_W_out"] = np.ascontiguousarray(np.asarray(inputs["L1_attn_W_out"], f32))

    g["e_W1"] = np.ascontiguousarray(np.asarray(inputs["L1_ffn_W1"], f32))
    g["e_W2"] = np.ascontiguousarray(np.asarray(inputs["L1_ffn_W2"], f32))
    for nm, src, nch in [("e_b1", "L1_ffn_b1", 32), ("e_g1", "L1_ffn_g1", 32),
                         ("e_be1", "L1_ffn_be1", 32), ("e_b2", "L1_ffn_b2", 8),
                         ("e_g2", "L1_ffn_g2", 8), ("e_be2", "L1_ffn_be2", 8),
                         ("e_pb", "L1_ffn_path_bias", 8), ("e_ctx", "L1_ffn_context", 8)]:
        a = np.asarray(inputs[src], f32)
        g[nm] = np.ascontiguousarray(a.reshape(9, nch, 128).transpose(0, 2, 1))
    g["e_th1"] = np.asarray(inputs["L1_ffn_th1"], f32).copy()
    g["e_th2"] = np.asarray(inputs["L1_ffn_th2"], f32).copy()
    g["e_wpath"] = np.ascontiguousarray(np.asarray(inputs["L1_ffn_w_path"], f32).reshape(8, 128).T)
    g["e_wteam"] = np.ascontiguousarray(np.asarray(inputs["L1_ffn_w_team"], f32).reshape(8, 128).T)
    g["e_bp"] = np.broadcast_to(np.asarray(inputs["L1_ffn_b_path"], f32).reshape(1, 1), (9, 1)).copy()
    g["e_ovp"] = np.asarray(inputs["L1_ffn_ov_path"], f32).reshape(9, 1).copy()
    g["e_bt"] = np.broadcast_to(np.asarray(inputs["L1_ffn_b_team"], f32).reshape(1, 1), (3, 1)).copy()
    g["e_ovt"] = np.asarray(inputs["L1_ffn_ov_team"], f32).reshape(3, 1).copy()

    g["ones_col"] = np.ones((128, 1), f32)
    g["ones_col_bf"] = np.ones((128, 1), ml_dtypes.bfloat16)
    g["ones_row"] = np.ones((1, 128), f32)
    g["ones_row_bf"] = np.ones((1, 128), ml_dtypes.bfloat16)
    g["ident_bf"] = np.eye(128, dtype=ml_dtypes.bfloat16)
    g["ident_f"] = np.eye(128, dtype=f32)
    g["e2x"] = ((np.arange(8)[None, :] >> 1) == np.arange(4)[:, None]).astype(f32)
    g["altsign"] = (1.0 - 2.0 * (np.arange(8) % 2)).astype(f32).reshape(8, 1)
    g["pswap"] = (np.arange(128)[None, :] == (np.arange(8)[:, None] ^ 1)).astype(f32)
    ip = np.eye(128, dtype=np.float32); ip[:8, :8] = 0
    g["ipad"] = ip.astype(ml_dtypes.bfloat16)
    g["ones_T"] = np.ones((1, 4096), f32)
    nid = np.zeros((32, 2048), f32)
    for hh in range(32):
        nid[hh, hh * 64:(hh + 1) * 64] = -1.0
    g["negid"] = nid
    g["tril8"] = np.tril(np.ones((8, 8), f32))
    g["a2b_bf"] = (np.arange(32)[:, None] % 2 == (np.arange(128)[None, :] >> 6)
                   ).astype(ml_dtypes.bfloat16)
    g["indj2"] = ((np.arange(32)[:, None] >> 1) == np.arange(16)[None, :]).astype(f32)
    g["ones32"] = np.ones((32, 4096), f32)
    g["zeros32"] = np.zeros((32, 2048), f32)
    g["a2b_f"] = (np.arange(32)[:, None] % 2 == (np.arange(128)[None, :] >> 6)).astype(f32)
    m64b = np.where(np.arange(64)[:, None] <= np.arange(64)[None, :], 0.0, NEG).astype(f32)
    g["mask_d2"] = np.ascontiguousarray(np.tile(m64b, (1, 8)))
    i64 = np.zeros((64, 128), f32); i64[:, :64] = np.eye(64)
    g["ident64f"] = i64
    g["ident64b"] = np.eye(64, dtype=ml_dtypes.bfloat16)
    g["inds_c"] = np.repeat(
        ((np.arange(32)[:, None] >> 1) == np.arange(16)[None, :]).astype(f32),
        32, axis=1)
    m64 = np.where(np.arange(64)[:, None] <= np.arange(64)[None, :], 0.0, NEG).astype(f32)
    g["mask_d"] = np.ascontiguousarray(np.tile(m64, (1, 8)))
    e2 = np.zeros((2, 128), ml_dtypes.bfloat16); e2[0, :64] = 1; e2[1, 64:] = 1
    g["e2a"] = e2
    g["indj"] = ((np.arange(32)[:, None] >> 1) == np.arange(16)[None, :]).astype(f32)
    g["eps_tab"] = np.broadcast_to(
        np.asarray([EPS5, EPS6, np.pi / 2], f32)[None, :], (128, 3)).copy()

    cores = []
    L_total = x.shape[0]
    t_all = np.arange(L_total, dtype=f32)
    inv_freq = (1.0 / 10000.0 ** (np.arange(0, 16, 2, dtype=f32) / 16.0)).astype(f32)
    for cid in range(ncores):
        d = dict(g)
        sl = slice(cid * T_core, (cid + 1) * T_core)
        d["x_in"] = np.ascontiguousarray(x[sl])
        ang = t_all[sl][None, :] * inv_freq[:, None]
        d["rope_cos"] = np.cos(ang).astype(f32)
        d["rope_sin"] = np.sin(ang).astype(f32)
        lt = np.zeros((32, 8), f32); lt[:, :cid] = 1.0
        d["core_lt"] = lt
        oh = np.zeros((1, 8), f32)
        if cid > 0:
            oh[0, cid - 1] = 1.0
        d["prev_oh"] = oh
        d["prevsel"] = np.broadcast_to(oh, (128, 8)).copy()
        ltc = np.zeros((8, 1), f32); ltc[:cid] = 1.0
        d["lt_col"] = ltc
        cores.append(d)
    return cores


def kernel(**inputs):
    T_core = 16384 // NCORES
    nc, meta = build_program(T_core)
    in_maps = prepare_inputs(inputs, T_core)
    res = run_bass_kernel_spmd(nc, in_maps, core_ids=list(range(NCORES)))
    out = np.concatenate([np.asarray(r["out"]) for r in res.results], axis=0)
    return out.reshape(1, 16384, D)


# version 05
# speedup vs baseline: 1.0329x; 1.0106x over previous
import numpy as np
import ml_dtypes
import concourse.bass as bass
import concourse.mybir as mybir
from concourse.bass import AP
from concourse.bass_utils import run_bass_kernel_spmd

F32 = mybir.dt.float32
BF16 = mybir.dt.bfloat16
I32 = mybir.dt.int32
AF = mybir.ActivationFunctionType
ALU = mybir.AluOpType

NCORES = 8
D = 1024
DC = D // 128
CH = 64
NEG = -1e30
EPS5 = 1e-5
EPS6 = 1e-6
RT127 = 1.0 / 127.0

ENGS = ("te", "ve", "se", "ge", "sy")
ESEM = {"te": "sTE", "ve": "sV", "se": "sS", "ge": "sG", "sy": "sSY"}


class Res:
    __slots__ = ("name", "last_w", "readers")

    def __init__(self, name):
        self.name = name
        self.last_w = None
        self.readers = []


class Op:
    __slots__ = ("eng", "fn", "deps", "dma_sem", "seq", "pub", "waits", "tag")

    def __init__(self, eng, fn, deps, dma_sem, tag):
        self.eng = eng
        self.fn = fn
        self.deps = deps
        self.dma_sem = dma_sem
        self.seq = None
        self.pub = None
        self.waits = []
        self.tag = tag


class Sched:

    def __init__(self, nc):
        self.nc = nc
        self.ops = []
        self.per_eng = {e: [] for e in ENGS}
        self.dma_seq = {}
        self.pub_cnt = {e: 0 for e in ENGS}

    def res(self, name):
        return Res(name)

    def add(self, eng, fn, reads=(), writes=(), dma=None, tag=""):
        deps = []
        for r in reads:
            if r.last_w is not None:
                deps.append(r.last_w)
        for w in writes:
            if w.last_w is not None:
                deps.append(w.last_w)
            deps.extend(w.readers)
        deps = list({id(d): d for d in deps}.values())
        o = Op(eng, fn, deps, dma, tag)
        o.seq = len(self.per_eng[eng])
        self.per_eng[eng].append(o)
        self.ops.append(o)
        for r in reads:
            r.readers.append(o)
        for w in writes:
            w.last_w = o
            w.readers = []
        return o

    DMA_POOL = {"sW": 2, "sWb": 2, "sIO": 4, "sSp": 4, "sGd": 4}

    def finalize(self):
        rrobin = {}
        dma_counts = {}
        prev_on_sem = {}
        for o in self.ops:
            if o.dma_sem is not None:
                ch = o.dma_sem
                n = self.DMA_POOL[ch]
                idx = rrobin.get(ch, 0)
                rrobin[ch] = (idx + 1) % n
                ps = f"{ch}{idx}"
                dma_counts[ps] = dma_counts.get(ps, 0) + 1
                o.pub = (ps, dma_counts[ps] * 16)
                if ps in prev_on_sem:
                    o.deps.append(prev_on_sem[ps])
                prev_on_sem[ps] = o
        need = set()
        for o in self.ops:
            for d in o.deps:
                if d.dma_sem is None:
                    need.add(id(d))
        pub_idx = {e: 0 for e in ENGS}
        for o in self.ops:
            if o.dma_sem is None and id(o) in need:
                pub_idx[o.eng] += 1
                o.pub = (ESEM[o.eng], pub_idx[o.eng])
        hw = {e: {} for e in ENGS}
        for o in self.ops:
            want = {}
            for d in o.deps:
                if d.dma_sem is not None:
                    s, v = d.pub
                    want[s] = max(want.get(s, 0), v)
                elif d.eng != o.eng:
                    if d.pub is None:
                        raise RuntimeError(f"dep of {o.tag} not published: {d.tag}")
                    s, v = d.pub
                    want[s] = max(want.get(s, 0), v)
                elif d.pub is not None:
                    s, v = d.pub
                    want[s] = max(want.get(s, 0), v)
            o.waits = []
            for s, v in want.items():
                if hw[o.eng].get(s, 0) < v:
                    o.waits.append((s, v))
                    hw[o.eng][s] = v

    def emit(self, c):
        self.finalize()
        nc = self.nc
        per_eng = self.per_eng

        def body(eng_name):
            def run(h):
                for o in per_eng[eng_name]:
                    for s, v in o.waits:
                        h.wait_ge(c.sem[s], v)
                    ins = o.fn(h)
                    if o.pub is not None:
                        if ins is None:
                            raise RuntimeError(f"op {o.tag} published but emitted None")
                        ins.then_inc(c.sem[o.pub[0]], 16 if o.dma_sem else 1)
            return run

        with nc.Block() as blk:
            blk.tensor(body("te"))
            blk.vector(body("ve"))
            blk.scalar(body("se"))
            blk.gpsimd(body("ge"))
            blk.sync(body("sy"))


class Ctx:
    def __init__(self, nc):
        self.nc = nc
        self.sem = {}

    def new_sem(self, name):
        self.sem[name] = self.nc.alloc_semaphore(name=name)



def build_program(T, stages=(0, 1, 2, 3, 4, 5), debug_taps=()):
    TT = min(512, T)
    NTT = T // TT
    ETT = min(1024, T)
    NCH = T // CH

    nc = bass.Bass("TRN2", target_bir_lowering=False)
    c = Ctx(nc)
    semlist = ["sTE", "sV", "sS", "sG", "sSY", "sCC"]
    for base, n in [("sW", 2), ("sWb", 2), ("sIO", 4), ("sSp", 4), ("sGd", 4)]:
        for i in range(n):
            semlist.append(f"{base}{i}")
    for s in semlist:
        c.new_sem(s)

    din = {}

    def inp(name, shape, dtype=F32):
        din[name] = nc.dram_tensor(name, list(shape), dtype, kind="ExternalInput")
        return din[name]

    inp("x_in", [T, D])
    out_t = nc.dram_tensor("out", [T, D], F32, kind="ExternalOutput")

    inp("wg_t", [D, 3072], I32)
    inp("wu_t", [D, 3072], I32)
    inp("wd_t", [3072, D], I32)

    inp("l0_ln_w", [128, DC])
    inp("l0_ln_b", [128, DC])
    inp("l0_W_in", [D, 4192])
    inp("l0_dt_bias", [32, 1])
    inp("l0_A", [32, 1])
    inp("l0_D", [128, 16])
    inp("l0_rms_w", [128, 16])
    inp("l0_W_out", [2048, D])
    inp("rope_cos", [8, T])
    inp("rope_sin", [8, T])

    inp("l1_W_in", [D, 4096])
    inp("l1_conv_w", [128, 64])
    inp("l1_conv_b", [128, 16])
    inp("l1_W_xproj", [2048, 33])
    inp("l1_D", [128, 16])
    inp("l1_W_out", [2048, D])

    inp("e_W1", [9, D, 4096])
    inp("e_W2", [9, 4096, D])
    inp("e_b1", [9, 128, 32])
    inp("e_g1", [9, 128, 32])
    inp("e_be1", [9, 128, 32])
    inp("e_b2", [9, 128, 8])
    inp("e_g2", [9, 128, 8])
    inp("e_be2", [9, 128, 8])
    inp("e_pb", [9, 128, 8])
    inp("e_ctx", [9, 128, 8])
    inp("e_th1", [9, 4])
    inp("e_th2", [9, 4])
    inp("e_wpath", [128, 8])
    inp("e_wteam", [128, 8])
    inp("e_bp", [9, 1])
    inp("e_ovp", [9, 1])
    inp("e_bt", [3, 1])
    inp("e_ovt", [3, 1])

    inp("ones_col", [128, 1])
    inp("ones_col_bf", [128, 1], BF16)
    inp("ones_row", [1, 128])
    inp("ones_row_bf", [1, 128], BF16)
    inp("ident_bf", [128, 128], BF16)
    inp("ident_f", [128, 128])
    inp("mask_d", [64, 512])
    inp("e2a", [2, 128], BF16)
    inp("e2x", [4, 8])
    inp("altsign", [8, 1])
    inp("pswap", [8, 128])
    inp("ipad", [128, 128], BF16)
    inp("ones_T", [1, 4096])
    inp("negid", [32, 2048])
    inp("tril8", [8, 8])
    inp("lt_col", [8, 1])
    inp("a2b_bf", [32, 128], BF16)
    inp("indj2", [32, 16])
    inp("inds_c", [32, 512])
    inp("ones32", [32, 4096])
    inp("zeros32", [32, 2048])
    inp("a2b_f", [32, 128])
    inp("mask_d2", [64, 512])
    inp("ident64f", [64, 128])
    inp("ident64b", [64, 64], BF16)
    inp("prevsel", [128, 8])
    inp("indj", [32, 16])
    inp("core_lt", [32, 8])
    inp("prev_oh", [1, 8])
    inp("eps_tab", [128, 3])

    dram = {}
    for nm, shape, dt in [
        ("sp_h0", [DC * 128, T], BF16),
        ("sp_yl", [16 * 128, T], BF16),
        ("sp_lnx4", [DC * 128, T], BF16),
        ("sp_xconv", [16 * 128, T], BF16),
        ("sp_tf", [3, DC * 128, TT], BF16),
        ("agA_in", [1, 2048 * 32 + 32], F32),
        ("agA_out", [NCORES, 2048 * 32 + 32], F32),
        ("agB_in", [1, 2048 * 3], F32),
        ("agB_out", [NCORES, 2048 * 3], F32),
        ("agC_in", [1, 2048 * 16], F32),
        ("agC_out", [NCORES, 2048 * 16], F32),
    ]:
        dram[nm] = nc.dram_tensor(nm, shape, dt)

    taps = {}
    for k in debug_taps:
        taps[k] = nc.dram_tensor(f"tap{k}", [T, D], F32, kind="ExternalOutput")

    t_ = nc.alloc_sbuf_tensor
    X = [t_(f"x{i}", [128, T], F32) for i in range(DC)]
    WS = t_("WS", [128, 8192], BF16)
    HB = t_("HB", [128, 32768], BF16)
    FA = t_("FA", [128, 4096], F32)
    FB = t_("FB", [128, 2048], F32)
    SM = t_("SM", [128, 2048], F32)
    SMb = t_("SMb", [128, 6656], BF16)

    CTb = t_("CTb", [128, 1024], BF16)
    CTf = t_("CTf", [128, 2176], F32)
    CT2 = t_("CT2", [128, 1536], F32)
    PA = nc.alloc_psum_tensor("pA", [128, 1024], F32)
    PB = nc.alloc_psum_tensor("pB", [128, 1024], F32)
    P47 = [nc.alloc_psum_tensor(f"pb{i}", [128, 512], F32) for i in range(4)]
    P = [PA[:, 0:512], PA[:, 512:1024], PB[:, 0:512], PB[:, 512:1024],
         P47[0][:, :], P47[1][:, :], P47[2][:, :], P47[3][:, :]]

    nc.const_aps.register(F32, EPS5, SM[:, 2040:2041].opt())
    nc.const_aps.register(F32, EPS6, SM[:, 2041:2042].opt())
    nc.const_aps.register(F32, float(np.pi / 2), SM[:, 2042:2043].opt())

    meta = dict(din=din, dram=dram, X=X, WS=WS, HB=HB, FA=FA, FB=FB, SM=SM,
                SMb=SMb, P=P, PA=PA, PB=PB, CTb=CTb, CTf=CTf, CT2=CT2, taps=taps, T=T, TT=TT, NTT=NTT, ETT=ETT,
                NCH=NCH, out=out_t, stages=stages)

    cst = {
        "ident": CTb[:, 0:128],
        "e2a": CTb[0:2, 128:256],
        "ipad": CTb[:, 512:640],
        "ones_col_bf": CTb[:, 256:257],
        "ones_row_bf": CTb[0:1, 384:512],
        "mask_d": CTf[0:64, 0:512],
        "ones_col": CTf[:, 512:513],
        "ones_row": CTf[0:1, 513 + 127:513 + 255][:, 0:128] if False else CTf[0:1, 640:768],
        "indj": CTf[0:32, 768:784],
        "core_lt": CTf[0:32, 784:792],
        "prev_oh": CTf[0:1, 792:800],
        "l0_ln_w": CTf[:, 800:808],
        "l0_ln_b": CTf[:, 808:816],

        "l0_D": CTf[:, 832:848],
        "l0_rms_w": CTf[:, 848:864],
        "l1_conv_w": CTf[:, 864:928],
        "l1_conv_b": CTf[:, 928:944],
        "l1_D": CTf[:, 944:960],
        "l1_W_xproj_sb": None,
        "e_b1": CT2[:, 0:288], "e_g1": CT2[:, 288:576], "e_be1": CT2[:, 576:864],
        "e_b2": CT2[:, 864:936], "e_g2": CT2[:, 936:1008], "e_be2": CT2[:, 1008:1080],
        "e_pb": CT2[:, 1080:1152], "e_ctx": CT2[:, 1152:1224],
        "e_wpath": CT2[:, 1224:1232], "e_wteam": CT2[:, 1232:1240],
        "e_bp": CT2[0:9, 1240:1241], "e_ovp": CT2[0:9, 1241:1242],
        "e_bt": CT2[0:3, 1242:1243], "e_ovt": CT2[0:3, 1243:1244],
        "e_th1": CT2[0:9, 1248:1252], "e_th2": CT2[0:9, 1252:1256],
        "e_cs1": CT2[0:9, 1256:1264],
        "e_cs2": CT2[0:9, 1264:1272],
        "ident_f": CTf[:, 1024:1152],
        "e2x": CTf[0:4, 1152:1160],
        "altsign": CTf[0:8, 1160:1161],
        "pswap": CTf[0:8, 1280:1408],
        "tril8": CTf[0:8, 960:968],
        "lt_col": CTf[0:8, 968:969],
        "indj64": CTf[64:96, 969:985],
        "dtb64": CTf[64:96, 816:817],
        "A64": CTf[64:96, 817:818],
        "a2b0": CTb[0:32, 640:768],
        "a2b64": CTb[64:96, 768:896],
        "inds": CTf[0:32, 1664:2176],
        "mask_d64": CTf[64:128, 0:512],
        "a2bf64": CTf[64:96, 1408:1536],
        "ident_f64": CTf[64:128, 1536:1664],
        "ident_b64": CTb[64:128, 896:960],
        "prevsel": CTf[:, 985:993],
    }
    meta_cst = cst
    sch = Sched(nc)
    meta["cst"] = cst
    rs = build_resources(sch, meta)
    emit_init(sch, rs, meta)
    for si, st in enumerate(stages):
        if si > 0:
            stage_fence(rs)
        if st in (0, 2):
            emit_ffn0(sch, rs, meta, st)
        elif st == 1:
            emit_l0_attn(sch, rs, meta)
        elif st == 3 or st == 5:
            emit_eidos(sch, rs, meta, st)
        elif st == 4:
            emit_l1_attn(sch, rs, meta)
        if st in meta["taps"]:
            emit_tap(sch, rs, meta, st)
    emit_out(sch, rs, meta)
    sch.emit(c)
    return nc, meta


def stage_fence(rs):
    groups = (("WS0", "WS1", "WS2a", "WS2b", "WS2c", "WS2d"),
              ("HBa", "HBb", "HBc") + tuple(f"HR{k}" for k in range(32)),
              ("SM", "RV0", "RV1"), ("SMb", "SQ1", "SQ1b"))
    for grp in groups:
        pool = []
        for n in grp:
            r = rs[n]
            pool.extend(r.readers)
            if r.last_w is not None:
                pool.append(r.last_w)
        for n in grp:
            r = rs[n]
            seen = {id(o) for o in r.readers}
            for o in pool:
                if id(o) not in seen:
                    r.readers.append(o)
                    seen.add(id(o))


def build_resources(sch, meta):
    rs = {}
    for i in range(DC):
        rs[f"X{i}"] = sch.res(f"X{i}")
    for i in range(8):
        rs[f"P{i}"] = sch.res(f"P{i}")
    rs["Pt"] = sch.res("Pt")
    for nm in ["WS0", "WS1", "WS2a", "WS2b", "HBa", "HBb", "HBc", "FA", "FB",
               "SM", "SMb", "dram", "state", "G", "BC", "consts",
               "RV0", "RV1", "SQ1", "SQ1b", "WS2c", "WS2d"]:
        rs[nm] = sch.res(nm)
    for k in range(32):
        rs[f"HR{k}"] = sch.res(f"HR{k}")
    rs["HB"] = rs["HBb"]
    rs["rows"] = rs["FB"]
    return rs



def emit_init(sch, rs, meta):
    din, X, P, FA, T = meta["din"], meta["X"], meta["P"], meta["FA"], meta["T"]
    nt = T // 128
    SM = meta["SM"]
    cst = meta["cst"]
    sch.add("sy", lambda h: h.dma_start(out=SM[:, 2040:2043], in_=din["eps_tab"].ap()),
            reads=(), writes=(rs["consts"],), dma="sIO", tag="ldeps")
    cload = [("ident", "ident_bf"), ("e2a", "e2a"), ("ones_col_bf", "ones_col_bf"),
             ("ones_row_bf", "ones_row_bf"), ("mask_d", "mask_d"),
             ("ones_col", "ones_col"), ("ones_row", "ones_row"), ("indj", "indj"),
             ("core_lt", "core_lt"), ("prev_oh", "prev_oh"),
             ("l0_ln_w", "l0_ln_w"), ("l0_ln_b", "l0_ln_b"),
             ("dtb64", "l0_dt_bias"), ("A64", "l0_A"), ("l0_D", "l0_D"),
             ("l0_rms_w", "l0_rms_w"), ("l1_conv_w", "l1_conv_w"),
             ("l1_conv_b", "l1_conv_b"), ("l1_D", "l1_D"),
             ("e_b1", "e_b1"), ("e_g1", "e_g1"), ("e_be1", "e_be1"),
             ("e_b2", "e_b2"), ("e_g2", "e_g2"), ("e_be2", "e_be2"),
             ("e_pb", "e_pb"), ("e_ctx", "e_ctx"),
             ("e_wpath", "e_wpath"), ("e_wteam", "e_wteam"),
             ("e_bp", "e_bp"), ("e_ovp", "e_ovp"), ("e_bt", "e_bt"),
             ("e_ovt", "e_ovt"), ("e_th1", "e_th1"), ("e_th2", "e_th2"),
             ("ident_f", "ident_f"), ("e2x", "e2x"), ("altsign", "altsign"),
             ("pswap", "pswap"), ("ipad", "ipad"), ("tril8", "tril8"),
             ("lt_col", "lt_col"), ("indj64", "indj2"), ("a2b0", "a2b_bf"),
             ("a2b64", "a2b_bf"), ("inds", "inds_c"), ("mask_d64", "mask_d2"),
             ("a2bf64", "a2b_f"), ("ident_f64", "ident64f"),
             ("ident_b64", "ident64b"), ("prevsel", "prevsel")]
    for cname, iname in cload:
        dst = cst[cname]
        ish = din[iname].shape
        if len(ish) == 3:
            sch.add("sy", lambda h, dst=dst, iname=iname, ish=ish: h.dma_start(
                out=dst.rearrange("p (a b) -> p a b", a=ish[0]),
                in_=din[iname].ap().rearrange("a p b -> p a b")),
                reads=(), writes=(rs["consts"],), dma="sIO", tag=f"ldc_{cname}")
        else:
            sch.add("sy", lambda h, dst=dst, iname=iname: h.dma_start(
                out=dst, in_=din[iname].ap()),
                reads=(), writes=(rs["consts"],), dma="sIO", tag=f"ldc_{cname}")
    for e in ENGS:
        sch.add(e, lambda h: h.nop(), reads=(rs["consts"],), writes=(),
                tag=f"cfence_{e}")
    stage = rs["FA"]
    for it in range(nt):
        sl = (it % 4) * 1024
        ld = sch.add("sy", lambda h, it=it, sl=sl: h.dma_start(
            out=FA[:, sl:sl + 1024], in_=din["x_in"][it * 128:(it + 1) * 128, :]),
            reads=(), writes=(stage,), dma="sIO", tag=f"ldx{it}")
        for d in range(DC):
            pb = rs[f"P{d % 2}"]
            tr = sch.add("te", lambda h, it=it, sl=sl, d=d: h.transpose(
                P[d % 2][:, (it % 2) * 128:(it % 2) * 128 + 128].bitcast(F32),
                FA[:, sl + d * 128:sl + (d + 1) * 128],
                meta["cst"]["ident_f"]),
                reads=(stage,), writes=(pb,), tag=f"trx{it}_{d}")
            sch.add("ve", lambda h, it=it, d=d: h.tensor_copy(
                X[d][:, it * 128:(it + 1) * 128],
                P[d % 2][:, (it % 2) * 128:(it % 2) * 128 + 128].bitcast(F32)),
                reads=(pb,), writes=(rs[f"X{d}"],), tag=f"evx{it}_{d}")


def emit_tap(sch, rs, meta, st):
    din, X, FA, T = meta["din"], meta["X"], meta["FA"], meta["T"]
    P, tap = meta["P"], meta["taps"][st]
    nt = T // 128
    for it in range(nt):
        sl = (it % 4) * 1024
        stg = rs["FA"]
        for d in range(DC):
            pb = rs[f"P{d % 2}"]
            sch.add("te", lambda h, it=it, d=d: h.transpose(
                P[d % 2][:, (it % 2) * 128:(it % 2) * 128 + 128].bitcast(F32),
                X[d][:, it * 128:(it + 1) * 128], meta["cst"]["ident_f"]),
                reads=(rs[f"X{d}"],), writes=(pb,), tag=f"ttr{st}_{it}_{d}")
            sch.add("ve", lambda h, it=it, sl=sl, d=d: h.tensor_copy(
                FA[:, sl + d * 128:sl + (d + 1) * 128],
                P[d % 2][:, (it % 2) * 128:(it % 2) * 128 + 128].bitcast(F32)),
                reads=(pb,), writes=(stg,), tag=f"tev{st}_{it}_{d}")
        sch.add("sy", lambda h, it=it, sl=sl: h.dma_start(
            out=tap[it * 128:(it + 1) * 128, :], in_=FA[:, sl:sl + 1024]),
            reads=(stg,), writes=(), dma="sIO", tag=f"tdm{st}_{it}")


def emit_out(sch, rs, meta):
    din, X, FA, T, P = meta["din"], meta["X"], meta["FA"], meta["T"], meta["P"]
    nt = T // 128
    last = None
    for it in range(nt):
        sl = (it % 4) * 1024
        stg = rs["FA"]
        for d in range(DC):
            pb = rs[f"P{d % 2}"]
            sch.add("te", lambda h, it=it, d=d: h.transpose(
                P[d % 2][:, (it % 2) * 128:(it % 2) * 128 + 128].bitcast(F32),
                X[d][:, it * 128:(it + 1) * 128], meta["cst"]["ident_f"]),
                reads=(rs[f"X{d}"],), writes=(pb,), tag=f"otr{it}_{d}")
            sch.add("ve", lambda h, it=it, sl=sl, d=d: h.tensor_copy(
                FA[:, sl + d * 128:sl + (d + 1) * 128],
                P[d % 2][:, (it % 2) * 128:(it % 2) * 128 + 128].bitcast(F32)),
                reads=(pb,), writes=(stg,), tag=f"oev{it}_{d}")
        last = sch.add("sy", lambda h, it=it, sl=sl: h.dma_start(
            out=meta["out"][it * 128:(it + 1) * 128, :], in_=FA[:, sl:sl + 1024]),
            reads=(stg,), writes=(), dma="sIO", tag=f"odm{it}")
    fin = sch.res("fin")
    sch.add("sy", lambda h: h.nop(), reads=(), writes=(fin,), tag="outwait")
    sch.per_eng["sy"][-1].deps.append(last)




def hap(handle, offset, dims):
    return AP(handle, offset, [list(d) for d in dims])


def emit_ln(sch, rs, meta, t0, TT, dst, eps, tag="ln",
            sq_off=24576, sq_res="HBc", dst_res="HBa", f32_scr=0,
            resid_x=False, src=None, src_res=None, affine=None,
            sq_res_fn=None):
    X, P, FB, HB, FA, din = (meta["X"], meta["P"], meta["FB"], meta["HB"],
                             meta["FA"], meta["din"])
    cst = meta["cst"]
    p6, p7, rrows = rs["P6"], rs["P7"], rs["rows"]
    if sq_res_fn is None:
        sq_res_fn = lambda d: rs[sq_res]
    dstr = rs[dst_res]
    if src is None:
        xa = lambda d: X[d][:, t0:t0 + TT]
        srcres = lambda d: rs[f"X{d}"]
    else:
        xa = src
        srcres = lambda d: rs[src_res]

    def sq(d):
        return HB[:, sq_off + d * TT: sq_off + (d + 1) * TT]

    for d in range(DC):
        sch.add("ve", lambda h, d=d: h.tensor_tensor(
            sq(d), xa(d), xa(d), ALU.mult),
            reads=(srcres(d),), writes=(sq_res_fn(d),), tag=f"{tag}sq{d}")
    onesv = cst["ones_col"] if src is None else cst["ones_col_bf"]
    for d in range(DC):
        sch.add("te", lambda h, d=d: h.matmul(
            P[6][0:1, 0:TT], onesv, xa(d),
            start=(d == 0), stop=(d == DC - 1)),
            reads=(srcres(d), rs["consts"]), writes=(p6,), tag=f"{tag}mmx{d}")
    for d in range(DC):
        sch.add("te", lambda h, d=d: h.matmul(
            P[7][0:1, 0:TT], cst["ones_col_bf"], sq(d),
            start=(d == 0), stop=(d == DC - 1)),
            reads=(sq_res_fn(d), rs["consts"]), writes=(p7,), tag=f"{tag}mmq{d}")
    mu = FB[0:1, 0:TT]
    m2 = FB[0:1, 512:512 + TT]
    var = FB[0:1, 1024:1024 + TT]
    nmr = FB[0:1, 1536:1536 + TT]
    sch.add("ve", lambda h: h.tensor_scalar(mu, P[6][0:1, 0:TT], 1.0 / D, None, ALU.mult),
            reads=(p6,), writes=(rrows,), tag=f"{tag}mu")
    sch.add("ve", lambda h: h.tensor_scalar(m2, P[7][0:1, 0:TT], 1.0 / D, None, ALU.mult),
            reads=(p7,), writes=(rrows,), tag=f"{tag}m2")
    sch.add("ve", lambda h: h.tensor_tensor(var, mu, mu, ALU.mult),
            reads=(rrows,), writes=(rrows,), tag=f"{tag}musq")
    sch.add("ve", lambda h: h.tensor_sub(m2, m2, var),
            reads=(rrows,), writes=(rrows,), tag=f"{tag}var")
    sch.add("se", lambda h: h.activation(var, m2, AF.Ln, bias=eps, scale=1.0),
            reads=(rrows, rs["consts"]), writes=(rrows,), tag=f"{tag}lnv")
    sch.add("se", lambda h: h.activation(m2, var, AF.Exp, bias=0.0, scale=-0.5),
            reads=(rrows,), writes=(rrows,), tag=f"{tag}rinv")
    rinv = m2
    sch.add("ve", lambda h: h.scalar_tensor_tensor(nmr, mu, -1.0, rinv,
                                                   ALU.mult, ALU.mult),
            reads=(rrows,), writes=(rrows,), tag=f"{tag}nmr")
    sch.add("te", lambda h: h.matmul(P[6][:, 0:TT], cst["ones_row"], rinv,
                                     start=True, stop=True),
            reads=(rrows, rs["consts"]), writes=(p6,), tag=f"{tag}bc1")
    sch.add("te", lambda h: h.matmul(P[7][:, 0:TT], cst["ones_row"], nmr,
                                     start=True, stop=True),
            reads=(rrows, rs["consts"]), writes=(p7,), tag=f"{tag}bc2")
    for d in range(DC):
        tsl = FA[:, f32_scr + (d % 2) * TT:f32_scr + (d % 2) * TT + TT]
        lsl = FA[:, f32_scr + (2 + d % 2) * TT:f32_scr + (2 + d % 2) * TT + TT] \
            if (resid_x or affine is None) and False else None
        sch.add("ve", lambda h, d=d, tsl=tsl: h.scalar_tensor_tensor(
            tsl, xa(d), 1.0, P[6][:, 0:TT], ALU.mult, ALU.mult),
            reads=(srcres(d), p6), writes=(rs["FA"],), tag=f"{tag}ap1_{d}")
        if affine is None and not resid_x:
            sch.add("ve", lambda h, d=d, tsl=tsl: h.tensor_tensor(
                dst(d), tsl, P[7][:, 0:TT], ALU.add),
                reads=(rs["FA"], p7), writes=(dstr,), tag=f"{tag}ap2_{d}")
        else:
            osl = FA[:, f32_scr + 2 * TT + (d % 2) * TT:
                     f32_scr + 2 * TT + (d % 2) * TT + TT]
            sch.add("ve", lambda h, d=d, tsl=tsl, osl=osl: h.tensor_tensor(
                osl, tsl, P[7][:, 0:TT], ALU.add),
                reads=(rs["FA"], p7), writes=(rs["FA"],), tag=f"{tag}ap2_{d}")
            if resid_x:
                sch.add("ve", lambda h, d=d, osl=osl: h.tensor_add(
                    X[d][:, t0:t0 + TT], X[d][:, t0:t0 + TT], osl),
                    reads=(rs["FA"],), writes=(rs[f"X{d}"],), tag=f"{tag}rx{d}")
            if affine is not None:
                ga, be = affine
                sch.add("se", lambda h, d=d, osl=osl, ga=ga, be=be: h.activation(
                    dst(d), osl, AF.Identity, bias=be[:, d:d + 1],
                    scale=ga[:, d:d + 1]),
                    reads=(rs["FA"], rs["consts"]), writes=(dstr,),
                    tag=f"{tag}af{d}")
            else:
                sch.add("ve", lambda h, d=d, osl=osl: h.tensor_copy(dst(d), osl),
                        reads=(rs["FA"],), writes=(dstr,), tag=f"{tag}cb{d}")



def emit_ffn0(sch, rs, meta, st):
    nc = sch.nc
    X, P, HB, WS, SMb, din = (meta["X"], meta["P"], meta["HB"], meta["WS"],
                              meta["SMb"], meta["din"])
    T, TT, NTT = meta["T"], meta["TT"], meta["NTT"]
    H3 = 3072

    def lnb(d):
        return HB[:, d * TT:(d + 1) * TT]

    def hb(k):
        return HB[:, 4096 + k * TT:4096 + (k + 1) * TT]

    wsq = [rs["WS0"], rs["WS1"]]
    hbr = rs["HBb"]
    smr = rs["SMb"]

    for tt in range(NTT):
        t0 = tt * TT
        emit_ln(sch, rs, meta, t0, TT, lnb, EPS5, tag=f"f{st}ln{tt}")
        lnr = rs["HBa"]
        for hbk in range(12):
            par = hbk % 2
            c0 = hbk * 256
            wof = par * 4096
            ld_g = sch.add("ge", lambda h, c0=c0, wof=wof: h.dma_start(
                out=WS[:, wof:wof + 2048].rearrange("p (k j) -> p k j", k=8),
                in_=hap(din["wg_t"], c0, [[H3, 128], [128 * H3, 8], [1, 256]])),
                reads=(), writes=(wsq[par],), dma=("sW" if par == 0 else "sWb"),
                tag=f"f{st}wg{tt}_{hbk}")
            ld_u = sch.add("ge", lambda h, c0=c0, wof=wof: h.dma_start(
                out=WS[:, wof + 2048:wof + 4096].rearrange("p (k j) -> p k j", k=8),
                in_=hap(din["wu_t"], c0, [[H3, 128], [128 * H3, 8], [1, 256]])),
                reads=(), writes=(wsq[par],), dma=("sW" if par == 0 else "sWb"),
                tag=f"f{st}wu{tt}_{hbk}")
            for hc in range(2):
                pg, pu = rs[f"P{2 * par}"], rs[f"P{2 * par + 1}"]
                for k in range(DC):
                    sch.add("te", lambda h, wof=wof, k=k, hc=hc, par=par: h.matmul(
                        P[2 * par][:, 0:TT],
                        WS[:, wof + k * 256 + hc * 128: wof + k * 256 + hc * 128 + 128],
                        lnb(k), start=(k == 0), stop=(k == DC - 1)),
                        reads=(wsq[par], lnr), writes=(pg,), tag=f"f{st}g{tt}_{hbk}_{hc}_{k}")
                for k in range(DC):
                    sch.add("te", lambda h, wof=wof, k=k, hc=hc, par=par: h.matmul(
                        P[2 * par + 1][:, 0:TT],
                        WS[:, wof + 2048 + k * 256 + hc * 128: wof + 2048 + k * 256 + hc * 128 + 128],
                        lnb(k), start=(k == 0), stop=(k == DC - 1)),
                        reads=(wsq[par], lnr), writes=(pu,), tag=f"f{st}u{tt}_{hbk}_{hc}_{k}")
                hidx = hbk * 2 + hc
                sch.add("se", lambda h, par=par: h.activation(
                    SMb[:, 0:TT], P[2 * par][:, 0:TT], AF.Silu, bias=0.0, scale=RT127),
                    reads=(pg,), writes=(smr,), tag=f"f{st}sil{tt}_{hidx}")
                sch.add("ve", lambda h, par=par, hidx=hidx: h.scalar_tensor_tensor(
                    hb(hidx), P[2 * par + 1][:, 0:TT], RT127, SMb[:, 0:TT],
                    ALU.mult, ALU.mult),
                    reads=(pu, smr), writes=(hbr,), tag=f"f{st}h{tt}_{hidx}")
        for k in range(24):
            par = k % 2
            wof = par * 1024
            sch.add("ge", lambda h, k=k, wof=wof: h.dma_start(
                out=WS[:, wof:wof + 1024],
                in_=din["wd_t"][k * 128:(k + 1) * 128, :]),
                reads=(), writes=(wsq[0], wsq[1]),
                dma=("sW" if par == 0 else "sWb"),
                tag=f"f{st}wd{tt}_{k}")
            for do in range(8):
                sch.add("te", lambda h, k=k, do=do, wof=wof: h.matmul(
                    P[do][:, 0:TT], WS[:, wof + do * 128:wof + (do + 1) * 128],
                    hb(k), start=(k == 0), stop=(k == 23)),
                    reads=(wsq[0], wsq[1], hbr), writes=(rs[f"P{do}"],),
                    tag=f"f{st}d{tt}_{k}_{do}")
        for do in range(8):
            sch.add("ve", lambda h, do=do, t0=t0: h.scalar_tensor_tensor(
                X[do][:, t0:t0 + TT], P[do][:, 0:TT], 0.5 * RT127,
                X[do][:, t0:t0 + TT], ALU.mult, ALU.add),
                reads=(rs[f"P{do}"],), writes=(rs[f"X{do}"],),
                tag=f"f{st}res{tt}_{do}")


def emit_l0_attn(sch, rs, meta):
    nc = sch.nc
    din, X, P, HB, WS, SM, SMb, FA, FB, cst = (
        meta["din"], meta["X"], meta["P"], meta["HB"], meta["WS"], meta["SM"],
        meta["SMb"], meta["FA"], meta["FB"], meta["cst"])
    PA, PB = meta["PA"], meta["PB"]
    T, TT, NTT = meta["T"], meta["TT"], meta["NTT"]
    CPT = TT // CH
    NCH = meta["NCH"]
    dram = meta["dram"]
    csts = rs["consts"]
    wsq = [rs["WS0"], rs["WS1"]]
    hba, hbb, hbc = rs["HBa"], rs["HBb"], rs["HBc"]
    far, fbr, smr, smbr = rs["FA"], rs["FB"], rs["SM"], rs["SMb"]
    star, gr = rs["state"], rs["G"]
    pan, pbn = sch.res("l0pa"), sch.res("l0pb")
    FApitch = 4096
    FBpitch = 2048
    SMbpitch = 6656

    def hT(d):
        return HB[:, d * TT:(d + 1) * TT]

    def xsT(j):
        return HB[:, 4096 + j * TT:4096 + (j + 1) * TT]

    def xcT(i):
        return HB[:, 12288 + i * 2048:12288 + (i + 1) * 2048]

    def ylT(j):
        return HB[:, 20480 + j * TT:20480 + (j + 1) * TT]

    MTreg = lambda c0, n: HB[:, 28672 + c0:28672 + c0 + n]
    SPT = HB[:, 30720:32768]
    Cbf = lambda c0, n: SMb[64:96, 2176 + c0:2176 + c0 + n]
    Bbf = lambda c0, n: SMb[64:96, 2176 + T + c0:2176 + T + c0 + n]
    dtrow = SM[64:96, 0:TT]
    stbuf = [SM[:, 512:1024], SM[:, 1082:1594]]
    dtTs = SM[:, 1024:1056]
    dtotc = SM[:, 1056:1072]
    carry = SM[64:96, 1080:1081]
    Acol = SM[64:96, 1081:1082]
    gprev = SM[64:96, 1596:1597]
    dttc = SM[64:96, 1597:1598]
    arow = FA[64:96, 2048:2048 + TT]
    eAr = SM[64:96, 1598:1662]
    dcyr = SM[64:96, 1662:1726]

    def Gsl(c0, n):
        return FA[64:96, c0:c0 + n]

    sch.add("sy", lambda h: h.dma_start(out=FA[96:128, 0:T],
                                        in_=din["ones32"].ap()[:, 0:T]),
            reads=(), writes=(far,), dma="sIO", tag="l0ones")
    sch.add("sy", lambda h: h.dma_start(out=FB[64:96, 0:2048],
                                        in_=din["negid"].ap()),
            reads=(), writes=(fbr,), dma="sIO", tag="l0negid")
    sch.add("sy", lambda h: h.dma_start(out=FB[96:128, 0:2048],
                                        in_=din["zeros32"].ap()),
            reads=(), writes=(fbr,), dma="sIO", tag="l0z32")
    sch.add("se", lambda h: h.activation(Acol, cst["A64"], AF.Exp,
                                         bias=0.0, scale=1.0),
            reads=(csts,), writes=(smr,), tag="l0A1")
    sch.add("ve", lambda h: h.tensor_scalar(Acol, Acol, -1.0, None, ALU.mult),
            reads=(smr,), writes=(smr,), tag="l0A2")
    sch.add("ve", lambda h: h.memset(stbuf[0], 0.0), reads=(), writes=(star,),
            tag="l0st0")
    sch.add("ve", lambda h: h.memset(carry, 0.0), reads=(), writes=(smr,),
            tag="l0cr0")
    sch.add("ve", lambda h: h.memset(gprev, 0.0), reads=(), writes=(smr,),
            tag="l0gp0")

    gc = 0
    for tt in range(NTT):
        t0 = tt * TT
        sch.add("sy", lambda h, t0=t0: h.dma_start(
            out=FB[0:8, 0:TT], in_=din["rope_cos"].ap()[:, t0:t0 + TT]),
            reads=(), writes=(fbr,), dma="sIO", tag=f"l0cos{tt}")
        sch.add("sy", lambda h, t0=t0: h.dma_start(
            out=FB[0:8, TT:2 * TT], in_=din["rope_sin"].ap()[:, t0:t0 + TT]),
            reads=(), writes=(fbr,), dma="sIO", tag=f"l0sin{tt}")
        emit_ln(sch, rs, meta, t0, TT, hT, EPS5, tag=f"l0A_{tt}",
                sq_off=24576, sq_res="HBc", dst_res="HBa", f32_scr=2048,
                resid_x=True)
        emit_ln(sch, rs, meta, t0, TT, hT, EPS6, tag=f"l0B_{tt}",
                sq_off=24576, sq_res="HBc", dst_res="HBa", f32_scr=2048,
                src=hT, src_res="HBa",
                affine=(cst["l0_ln_w"], cst["l0_ln_b"]))
        sch.add("sy", lambda h, t0=t0: h.dma_start(
            out=dram["sp_h0"][:, t0:t0 + TT].rearrange("(d q) t -> q d t", d=DC),
            in_=HB[:, 0:DC * TT].rearrange("q (d t) -> q d t", d=DC)),
            reads=(hba,), writes=(rs["dram"],), dma="sSp", tag=f"l0hsp{tt}")
        for xh in range(2):
            for k in range(DC):
                par = k % 2
                wof = par * 1024
                sch.add("ge", lambda h, k=k, xh=xh, wof=wof: h.dma_start(
                    out=WS[:, wof:wof + 1024],
                    in_=hap(din["l0_W_in"], k * 128 * 4192 + 2048 + xh * 1024,
                            [[4192, 128], [1, 1024]])),
                    reads=(), writes=(wsq[par],),
                    dma=("sW" if par == 0 else "sWb"), tag=f"l0wx{tt}_{xh}_{k}")
                for f in range(8):
                    sch.add("te", lambda h, k=k, f=f, wof=wof, par=par: h.matmul(
                        P[f][:, 0:TT], WS[:, wof + f * 128:wof + (f + 1) * 128],
                        hT(k), start=(k == 0), stop=(k == DC - 1)),
                        reads=(wsq[par], hba), writes=(rs[f"P{f}"],),
                        tag=f"l0mx{tt}_{xh}_{k}_{f}")
            for f in range(8):
                j = xh * 8 + f
                sch.add("se", lambda h, f=f, j=j: h.activation(
                    xsT(j), P[f][:, 0:TT], AF.Silu, bias=0.0, scale=1.0),
                    reads=(rs[f"P{f}"],), writes=(hbb,), tag=f"l0xs{tt}_{j}")
        grps = [(0, 8, 0, 0), (8, 8, 1, 0), (16, 16, 2, 32),
                (32, 8, 3, 0), (40, 8, 5, 0), (48, 16, 6, 32),
                (64, 32, 4, 64)]
        for k in range(DC):
            par = k % 2
            wof = 2048 + par * 96
            sch.add("ge", lambda h, k=k, wof=wof: h.dma_start(
                out=WS[:, wof:wof + 96],
                in_=hap(din["l0_W_in"], k * 128 * 4192 + 4096,
                        [[4192, 128], [1, 96]])),
                reads=(), writes=(wsq[par],),
                dma=("sW" if par == 0 else "sWb"), tag=f"l0wb{tt}_{k}")
            for (c0, nn, pi, pof) in grps:
                sch.add("te", lambda h, k=k, wof=wof, c0=c0, nn=nn, pi=pi,
                        pof=pof, par=par: h.matmul(
                    P[pi][pof:pof + nn, 0:TT], WS[:, wof + c0:wof + c0 + nn],
                    hT(k), start=(k == 0), stop=(k == DC - 1)),
                    reads=(wsq[par], hba), writes=(rs[f"P{pi}"],),
                    tag=f"l0mb{tt}_{k}_{c0}")
        sBv1 = FA[0:8, 2048:2048 + TT]
        sBv2 = FA[0:8, 2560:2560 + TT]
        sCv1 = FA[0:8, 3072:3072 + TT]
        sCv2 = FA[0:8, 3584:3584 + TT]
        sBr = FA[32:48, 2048:2048 + TT]
        sCr = FA[32:48, 2560:2560 + TT]
        for (tg, pi, pof, dst, nn) in [("bv1", 0, 0, sBv1, 8),
                                       ("bv2", 1, 0, sBv2, 8),
                                       ("br", 2, 32, sBr, 16),
                                       ("cv1", 3, 0, sCv1, 8),
                                       ("cv2", 5, 0, sCv2, 8),
                                       ("cr", 6, 32, sCr, 16)]:
            sch.add("se", lambda h, pi=pi, pof=pof, dst=dst, nn=nn: h.activation(
                dst, P[pi][pof:pof + nn, 0:TT], AF.Silu, bias=0.0, scale=1.0),
                reads=(rs[f"P{pi}"],), writes=(far,), tag=f"l0{tg}_{tt}")
        sch.add("se", lambda h: h.activation(
            dtrow, P[4][64:96, 0:TT], AF.Exp, bias=cst["dtb64"], scale=1.0),
            reads=(rs["P4"], csts), writes=(smr,), tag=f"l0dte{tt}")
        sch.add("ve", lambda h: h.tensor_scalar(
            arow, dtrow, 1.0, None, ALU.add), reads=(smr,), writes=(far,),
            tag=f"l0dt1{tt}")
        sch.add("se", lambda h: h.activation(
            dtrow, arow, AF.Ln, bias=0.0, scale=1.0),
            reads=(far,), writes=(smr,), tag=f"l0dtl{tt}")
        cosr = FB[0:8, 0:TT]
        sinr = FB[0:8, TT:2 * TT]
        scC = FB[0:8, 1024:1024 + TT]
        scD = FB[0:8, 1536:1536 + TT]
        scE = SM[0:8, 0:TT]
        for (va, vb, tg) in [(sBv1, sBv2, "B"), (sCv1, sCv2, "C")]:
            sch.add("ve", lambda h, va=va: h.tensor_mul(scC, va, cosr),
                    reads=(far, fbr), writes=(fbr,), tag=f"l0rp1{tg}{tt}")
            sch.add("ve", lambda h, va=va: h.tensor_mul(scD, va, sinr),
                    reads=(far, fbr), writes=(fbr,), tag=f"l0rp2{tg}{tt}")
            sch.add("ve", lambda h, vb=vb: h.tensor_mul(scE, vb, sinr),
                    reads=(far, fbr), writes=(smr,), tag=f"l0rp3{tg}{tt}")
            sch.add("ve", lambda h, va=va, vb=vb: h.tensor_mul(va, vb, cosr),
                    reads=(far, fbr), writes=(far,), tag=f"l0rp4{tg}{tt}")
            sch.add("ve", lambda h, vb=vb: h.tensor_sub(vb, scC, scE),
                    reads=(fbr, smr), writes=(far,), tag=f"l0rp5{tg}{tt}")
            sch.add("ve", lambda h, va=va: h.tensor_add(scC, scD, va),
                    reads=(fbr, far), writes=(fbr,), tag=f"l0rp6{tg}{tt}")
            sch.add("ve", lambda h, va=va: h.tensor_copy(va, scC),
                    reads=(fbr,), writes=(far,), tag=f"l0rp7{tg}{tt}")
        asm = [(sBv2, Bbf(t0, TT), 0, 8), (sBv1, Bbf(t0, TT), 8, 8),
               (sBr, Bbf(t0, TT), 16, 16),
               (sCv2, Cbf(t0, TT), 0, 8), (sCv1, Cbf(t0, TT), 8, 8),
               (sCr, Cbf(t0, TT), 16, 16)]
        for ai, (sap, dreg, drow, nn) in enumerate(asm):
            sch.add("ge", lambda h, sap=sap, dreg=dreg, drow=drow, nn=nn:
                    h.dma_start(out=dreg[drow:drow + nn, :], in_=sap),
                    reads=(far,), writes=(smbr,), dma="sGd",
                    tag=f"l0asm{tt}_{ai}")
        sch.add("ve", lambda h: h.tensor_scalar(
            arow, dtrow, Acol, None, ALU.mult),
            reads=(smr,), writes=(far,), tag=f"l0a{tt}")
        sch.add("ve", lambda h, t0=t0: h.tensor_tensor_scan(
            Gsl(t0, TT), arow, arow, carry, ALU.add, ALU.bypass),
            reads=(far, smr), writes=(gr,), tag=f"l0G{tt}")
        sch.add("ve", lambda h, t0=t0: h.tensor_copy(
            carry, Gsl(t0 + TT - 1, 1)),
            reads=(gr,), writes=(smr,), tag=f"l0Gc{tt}")
        for i in range(TT // 128):
            sch.add("te", lambda h, i=i: h.transpose(
                P[3][:, 0:32], dtrow[:, i * 128:(i + 1) * 128],
                cst["ident_f64"][0:32, 0:32]),
                reads=(smr, csts), writes=(rs["P3"],), tag=f"l0dT{tt}_{i}")
            sch.add("ve", lambda h: h.tensor_copy(dtTs, P[3][:, 0:32]),
                    reads=(rs["P3"],), writes=(smr,), tag=f"l0dTe{tt}_{i}")
            for j in range(16):
                sch.add("te", lambda h, i=i, j=j: h.transpose(
                    P[3][:, 128 + (j % 2) * 64:128 + (j % 2) * 64 + 64].bitcast(BF16),
                    xsT(j)[:, i * 128:(i + 1) * 128], cst["ident"]),
                    reads=(hbb, csts), writes=(rs["P3"],), tag=f"l0xT{tt}_{i}_{j}")
                dtap = hap(meta["SM"], 1024 + 2 * j, [[2048, 128], [1, 2], [0, 64]])
                sch.add("ve", lambda h, i=i, j=j, dtap=dtap: h.tensor_tensor(
                    xcT(i)[:, j * 128:(j + 1) * 128].rearrange(
                        "q (a b) -> q a b", a=2),
                    P[3][:, 128 + (j % 2) * 64:128 + (j % 2) * 64 + 64].bitcast(
                        BF16).rearrange("q (a b) -> q a b", a=2),
                    dtap, ALU.mult),
                    reads=(rs["P3"], smr), writes=(hbb,), tag=f"l0xcE{tt}_{i}_{j}")
        for ci in range(CPT):
            b = (ci % 2) * 64
            cs0 = t0 + ci * CH
            i128 = (ci * CH) // 128
            sch.add("ge", lambda h, cs0=cs0: h.dma_start(
                out=hap(meta["FB"], 96 * 2048, [[2048, 1], [1, 2048]]),
                in_=hap(meta["FA"], 64 * 4096 + cs0, [[4096, 32], [1, 64]])),
                reads=(gr,), writes=(fbr,), dma="sGd", tag=f"l0dg{tt}_{ci}")
            sch.add("te", lambda h, cs0=cs0, b=b: h.matmul(
                P[6][b:b + 64, 0:64], Bbf(cs0, 64), Cbf(cs0, 64),
                start=True, stop=True),
                reads=(smbr,), writes=(rs["P6"],), tag=f"l0sc{tt}_{ci}")
            sch.add("ve", lambda h, b=b: h.tensor_copy(
                SMb[b:b + 64, 6272:6336], P[6][b:b + 64, 0:64]),
                reads=(rs["P6"],), writes=(smbr,), tag=f"l0sce{tt}_{ci}")
            for hb in range(4):
                sch.add("te", lambda h, cs0=cs0, b=b, hb=hb: h.matmul(
                    P[5][b:b + 64, 0:512], FA[64:128, cs0:cs0 + 64],
                    FB[64:128, hb * 512:(hb + 1) * 512],
                    start=True, stop=True),
                    reads=(gr, far, fbr), writes=(rs["P5"],),
                    tag=f"l0dm{tt}_{ci}_{hb}")
                msk = cst["mask_d"] if b == 0 else cst["mask_d64"]
                sch.add("ve", lambda h, b=b, msk=msk: h.scalar_tensor_tensor(
                    P[6][b:b + 64, 0:512], P[5][b:b + 64, 0:512], 1.0,
                    msk, ALU.mult, ALU.add),
                    reads=(rs["P5"], csts), writes=(rs["P6"],),
                    tag=f"l0dk{tt}_{ci}_{hb}")
                sch.add("se", lambda h, b=b: h.activation(
                    P[5][b:b + 64, 0:512], P[6][b:b + 64, 0:512], AF.Exp,
                    bias=0.0, scale=1.0),
                    reads=(rs["P6"],), writes=(rs["P5"],),
                    tag=f"l0de{tt}_{ci}_{hb}")
                scap = hap(meta["SMb"], b * 6656 + 6272,
                           [[6656, 64], [0, 8], [1, 64]])
                sch.add("ve", lambda h, b=b, hb=hb, scap=scap: h.tensor_tensor(
                    MTreg(hb * 512, 512)[b:b + 64, :].rearrange(
                        "q (a l) -> q a l", a=8),
                    P[5][b:b + 64, 0:512].rearrange("q (a l) -> q a l", a=8),
                    scap, ALU.mult),
                    reads=(rs["P5"], smbr), writes=(hbc,),
                    tag=f"l0mt{tt}_{ci}_{hb}")
            sch.add("ve", lambda h, cs0=cs0: h.tensor_scalar(
                dcyr, Gsl(cs0, CH), Gsl(cs0 + CH - 1, 1), None, ALU.subtract),
                reads=(gr,), writes=(smr,), tag=f"l0dc1{tt}_{ci}")
            sch.add("se", lambda h: h.activation(
                dcyr, dcyr, AF.Exp, bias=0.0, scale=-1.0),
                reads=(smr,), writes=(smr,), tag=f"l0dc2{tt}_{ci}")
            sch.add("ve", lambda h, cs0=cs0: h.tensor_scalar(
                eAr, Gsl(cs0, CH), gprev, None, ALU.subtract),
                reads=(gr, smr), writes=(smr,), tag=f"l0ea1{tt}_{ci}")
            sch.add("se", lambda h: h.activation(
                eAr, eAr, AF.Exp, bias=0.0, scale=1.0),
                reads=(smr,), writes=(smr,), tag=f"l0ea2{tt}_{ci}")
            sch.add("ve", lambda h, cs0=cs0: h.tensor_sub(
                dttc, Gsl(cs0 + CH - 1, 1), gprev),
                reads=(gr, smr), writes=(smr,), tag=f"l0dt2{tt}_{ci}")
            sch.add("se", lambda h: h.activation(
                dttc, dttc, AF.Exp, bias=0.0, scale=1.0),
                reads=(smr,), writes=(smr,), tag=f"l0dt3{tt}_{ci}")
            sch.add("ve", lambda h, cs0=cs0: h.tensor_copy(
                gprev, Gsl(cs0 + CH - 1, 1)),
                reads=(gr,), writes=(smr,), tag=f"l0gp{tt}_{ci}")
            sch.add("ve", lambda h: h.tensor_scalar(
                SM[64:96, 1727:1743], cst["indj64"], dttc, None, ALU.mult),
                reads=(smr, csts), writes=(smr,), tag=f"l0dti{tt}_{ci}")
            sch.add("te", lambda h: h.matmul(
                P[6][:, 64:80], cst["a2bf64"], SM[64:96, 1727:1743],
                start=True, stop=True),
                reads=(smr, csts), writes=(rs["P6"],), tag=f"l0dtm{tt}_{ci}")
            sch.add("ve", lambda h: h.tensor_copy(dtotc, P[6][:, 64:80]),
                    reads=(rs["P6"],), writes=(smr,), tag=f"l0dte2{tt}_{ci}")
            sch.add("ve", lambda h, cs0=cs0: h.tensor_tensor(
                SMb[64:96, 0:64], Bbf(cs0, 64), dcyr, ALU.mult),
                reads=(smbr, smr), writes=(smbr,), tag=f"l0bd{tt}_{ci}")
            sch.add("te", lambda h: h.transpose(
                P[7][0:64, 96:128].bitcast(BF16)[:, 0:32], SMb[64:96, 0:64],
                cst["ident_b64"][0:32, 0:32]),
                reads=(smbr, csts), writes=(rs["P7"],), tag=f"l0bdt{tt}_{ci}")
            sch.add("ve", lambda h, b=b: h.tensor_copy(
                SMb[b:b + 64, 6336:6368], P[7][0:64, 96:128].bitcast(BF16)[:, 0:32]),
                reads=(rs["P7"],), writes=(smbr,), tag=f"l0bde{tt}_{ci}")
            for j in range(16):
                sch.add("te", lambda h, i128=i128, ci=ci, j=j, b=b: h.matmul(
                    P[4][:, j * 32:(j + 1) * 32],
                    xcT(i128)[b:b + 64, j * 128:(j + 1) * 128],
                    SMb[b:b + 64, 6336:6368], start=True, stop=True),
                    reads=(hbb, smbr), writes=(rs["P4"],),
                    tag=f"l0stm{tt}_{ci}_{j}")
            sb = stbuf[gc % 2]
            nb = stbuf[(gc + 1) % 2]
            for jq in range(4):
                for j4 in range(4):
                    j = jq * 4 + j4
                    sch.add("te", lambda h, j=j, j4=j4, sb=sb: h.transpose(
                        P[7][0:32, j4 * 128:(j4 + 1) * 128],
                        sb[:, j * 32:(j + 1) * 32],
                        cst["ident_f"]),
                        reads=(star,), writes=(rs["P7"],),
                        tag=f"l0spt{tt}_{ci}_{j}")
                sch.add("ve", lambda h, jq=jq: h.tensor_copy(
                    SPT[64:96, jq * 512:(jq + 1) * 512],
                    P[7][0:32, 0:512]),
                    reads=(rs["P7"],), writes=(hbc,), tag=f"l0spe{tt}_{ci}_{jq}")
            for j in range(16):
                sch.add("ve", lambda h, j=j, sb=sb, nb=nb: h.scalar_tensor_tensor(
                    nb[:, j * 32:(j + 1) * 32], sb[:, j * 32:(j + 1) * 32],
                    dtotc[:, j:j + 1], P[4][:, j * 32:(j + 1) * 32],
                    ALU.mult, ALU.add),
                    reads=(star, smr, rs["P4"]), writes=(star,),
                    tag=f"l0stu{tt}_{ci}_{j}")
            for j in range(16):
                jh = j % 8
                for hh in range(2):
                    sch.add("te", lambda h, i128=i128, ci=ci, j=j, hh=hh, b=b:
                            h.matmul(
                        PA[hh * 64:(hh + 1) * 64, j * 64:(j + 1) * 64],
                        xcT(i128)[b:b + 64, j * 128 + hh * 64:
                                  j * 128 + hh * 64 + 64],
                        MTreg((2 * j + hh) * 64, 64)[b:b + 64, :],
                        start=True, stop=True),
                        reads=(hbb, hbc), writes=(rs["P0"], rs["P1"]),
                        tag=f"l0yd{tt}_{ci}_{j}_{hh}")
                sch.add("te", lambda h, cs0=cs0, j=j: h.matmul(
                    PB[:, j * 64:(j + 1) * 64], SPT[64:96, j * 128:(j + 1) * 128],
                    Cbf(cs0, 64), start=True, stop=True),
                    reads=(hbc, smbr), writes=(rs["P2"], rs["P3"]),
                    tag=f"l0yo{tt}_{ci}_{j}")
                sch.add("ve", lambda h, j=j: h.tensor_scalar(
                    SMb[64:96, 96:160], eAr, cst["indj64"][:, j:j + 1],
                    None, ALU.mult),
                    reads=(smr, csts), writes=(smbr,), tag=f"l0eg1{tt}_{ci}_{j}")
                sch.add("te", lambda h, j=j, jh=jh: h.matmul(
                    P[6][:, jh * 64:(jh + 1) * 64], cst["a2b64"],
                    SMb[64:96, 96:160], start=True, stop=True),
                    reads=(smbr, csts), writes=(rs["P6"],),
                    tag=f"l0eg2{tt}_{ci}_{j}")
                sch.add("ve", lambda h, jh=jh: h.tensor_copy(
                    SMb[:, 6336 + 0:6336 + 64] if False else
                    SMb[:, 6400:6464], P[6][:, jh * 64:(jh + 1) * 64]),
                    reads=(rs["P6"],), writes=(smbr,), tag=f"l0eg3{tt}_{ci}_{j}")
                sch.add("ve", lambda h, j=j: h.tensor_tensor(
                    FA[:, 2560:2624], PB[:, j * 64:(j + 1) * 64],
                    SMb[:, 6400:6464], ALU.mult),
                    reads=(rs["P2"], rs["P3"], smbr), writes=(far,),
                    tag=f"l0ym1{tt}_{ci}_{j}")
                sch.add("ve", lambda h, j=j: h.tensor_tensor(
                    FA[:, 2624:2688], FA[:, 2560:2624],
                    PA[:, j * 64:(j + 1) * 64], ALU.add),
                    reads=(far, rs["P0"], rs["P1"]), writes=(far,),
                    tag=f"l0ym2{tt}_{ci}_{j}")
                sch.add("ve", lambda h, j=j, ci=ci: h.scalar_tensor_tensor(
                    ylT(j)[:, ci * CH:(ci + 1) * CH],
                    xsT(j)[:, ci * CH:(ci + 1) * CH], cst["l0_D"][:, j:j + 1],
                    FA[:, 2624:2688], ALU.mult, ALU.add),
                    reads=(hbb, csts, far), writes=(hbc,),
                    tag=f"l0ym3{tt}_{ci}_{j}")
            gc += 1
        sch.add("sy", lambda h, t0=t0: h.dma_start(
            out=dram["sp_yl"][:, t0:t0 + TT].rearrange("(j q) t -> q j t", j=16),
            in_=HB[:, 20480:20480 + 16 * TT].rearrange("q (j t) -> q j t", j=16)),
            reads=(hbc,), writes=(rs["dram"],), dma="sSp", tag=f"l0ysp{tt}")

    fstate = stbuf[NCH % 2]
    sch.add("sy", lambda h, fstate=fstate: h.dma_start(
        out=hap(dram["agA_in"], 0, [[512, 128], [1, 512]]), in_=fstate),
        reads=(star,), writes=(rs["dram"],), dma="sSp", tag="l0agw1")
    sch.add("sy", lambda h: h.dma_start(
        out=hap(dram["agA_in"], 65536, [[1, 32], [1, 1]]), in_=carry),
        reads=(smr,), writes=(rs["dram"],), dma="sSp", tag="l0agw2")
    cc = sch.add("ge", lambda h: h.collective_compute(
        "AllGather", ALU.bypass, replica_groups=[list(range(NCORES))],
        ins=[dram["agA_in"].ap().opt()], outs=[dram["agA_out"].ap().opt()]),
        reads=(rs["dram"],), writes=(rs["dram"],), tag="l0agA")
    sch.add("sy", lambda h: h.dma_start(
        out=SM[0:8, 1024:1056], in_=hap(dram["agA_out"], 65536,
                                        [[65568, 8], [1, 32]])),
        reads=(rs["dram"],), writes=(smr,), dma="sSp", tag="l0suml")
    sch.add("te", lambda h: h.matmul(
        P[2][0:32, 0:8], SM[0:8, 1024:1056], cst["tril8"],
        start=True, stop=True),
        reads=(smr, csts), writes=(rs["P2"],), tag="l0Q")
    sch.add("te", lambda h: h.matmul(
        P[3][0:32, 0:1], SM[0:8, 1024:1056], cst["lt_col"],
        start=True, stop=True),
        reads=(smr, csts), writes=(rs["P3"],), tag="l0P")
    sch.add("ve", lambda h: h.tensor_copy(SM[0:32, 1659:1660], P[3][0:32, 0:1]),
            reads=(rs["P3"],), writes=(smr,), tag="l0Pe")
    sch.add("ve", lambda h: h.tensor_scalar(
        SM[0:32, 1660:1668], P[2][0:32, 0:8], -1.0, SM[0:32, 1659:1660],
        ALU.mult, ALU.add),
        reads=(rs["P2"], smr), writes=(smr,), tag="l0cf1")
    sch.add("ve", lambda h: h.tensor_scalar(
        SM[0:32, 1660:1668], SM[0:32, 1660:1668], 0.0, None, ALU.min),
        reads=(smr,), writes=(smr,), tag="l0cfc")
    sch.add("se", lambda h: h.activation(
        SM[0:32, 1668:1676], SM[0:32, 1660:1668], AF.Exp, bias=0.0, scale=1.0),
        reads=(smr,), writes=(smr,), tag="l0cf2")
    sch.add("ve", lambda h: h.tensor_tensor(
        SM[0:32, 1660:1668], SM[0:32, 1668:1676], cst["core_lt"], ALU.mult),
        reads=(smr, csts), writes=(smr,), tag="l0cf3")
    for j in range(NCORES):
        sch.add("sy", lambda h, j=j: h.dma_start(
            out=HB[:, 4096 + j * 1024:4096 + (j + 1) * 1024].bitcast(F32),
            in_=hap(dram["agA_out"], j * 65568, [[512, 128], [1, 512]])),
            reads=(rs["dram"],), writes=(hbb,), dma="sSp", tag=f"l0stl{j}")
        sch.add("ve", lambda h, j=j: h.tensor_scalar(
            SMb[0:32, 0:512].bitcast(F32)[:, 0:0] if False else
            SMb[0:32, 0:512], cst["inds"], SM[0:32, 1660 + j:1661 + j],
            None, ALU.mult),
            reads=(csts, smr), writes=(smbr,), tag=f"l0ej1{j}")
        sch.add("te", lambda h: h.matmul(
            P[4][:, 0:512], cst["a2b0"], SMb[0:32, 0:512],
            start=True, stop=True),
            reads=(smbr, csts), writes=(rs["P4"],), tag=f"l0ej2{j}")
        dstb = stbuf[j % 2]
        if j == 0:
            sch.add("ve", lambda h, j=j: h.tensor_tensor(
                stbuf[0], P[4][:, 0:512],
                HB[:, 4096 + j * 1024:4096 + (j + 1) * 1024].bitcast(F32),
                ALU.mult),
                reads=(rs["P4"], hbb), writes=(star,), tag=f"l0si{j}")
        else:
            sch.add("ve", lambda h, j=j: h.tensor_tensor(
                FA[:, 2048:2560], P[4][:, 0:512],
                HB[:, 4096 + j * 1024:4096 + (j + 1) * 1024].bitcast(F32),
                ALU.mult),
                reads=(rs["P4"], hbb), writes=(far,), tag=f"l0si{j}")
            sch.add("ve", lambda h, j=j: h.tensor_add(
                stbuf[j % 2], stbuf[(j + 1) % 2], FA[:, 2048:2560]),
                reads=(far, star), writes=(star,), tag=f"l0sa{j}")
    sin_f = stbuf[(NCORES - 1) % 2]
    for jq in range(4):
        for j4 in range(4):
            j = jq * 4 + j4
            sch.add("te", lambda h, j=j, j4=j4, sin_f=sin_f: h.transpose(
                P[3][0:32, j4 * 128:(j4 + 1) * 128],
                sin_f[:, j * 32:(j + 1) * 32], cst["ident_f"]),
                reads=(star,), writes=(rs["P3"],), tag=f"l0sit{j}")
        sch.add("ve", lambda h, jq=jq: h.tensor_copy(
            SPT[64:96, jq * 512:(jq + 1) * 512], P[3][0:32, 0:512]),
            reads=(rs["P3"],), writes=(hbc,), tag=f"l0site{jq}")

    for tt in range(NTT):
        t0 = tt * TT
        sch.add("sy", lambda h, t0=t0: h.dma_start(
            out=HB[:, 0:DC * TT].rearrange("q (d t) -> q d t", d=DC),
            in_=dram["sp_h0"][:, t0:t0 + TT].rearrange("(d q) t -> q d t", d=DC)),
            reads=(rs["dram"],), writes=(hba,), dma="sSp", tag=f"l0hrl{tt}")
        sch.add("sy", lambda h, t0=t0: h.dma_start(
            out=HB[:, 20480:20480 + 16 * TT].rearrange("q (j t) -> q j t", j=16),
            in_=dram["sp_yl"][:, t0:t0 + TT].rearrange("(j q) t -> q j t", j=16)),
            reads=(rs["dram"],), writes=(hbc,), dma="sSp", tag=f"l0yrl{tt}")
        for zh in range(2):
            for k in range(DC):
                par = k % 2
                wof = par * 1024
                sch.add("ge", lambda h, k=k, zh=zh, wof=wof: h.dma_start(
                    out=WS[:, wof:wof + 1024],
                    in_=hap(din["l0_W_in"], k * 128 * 4192 + zh * 1024,
                            [[4192, 128], [1, 1024]])),
                    reads=(), writes=(wsq[par],),
                    dma=("sW" if par == 0 else "sWb"), tag=f"l0wz{tt}_{zh}_{k}")
                for f in range(8):
                    sch.add("te", lambda h, k=k, f=f, wof=wof, par=par: h.matmul(
                        P[f][:, 0:TT], WS[:, wof + f * 128:wof + (f + 1) * 128],
                        hT(k), start=(k == 0), stop=(k == DC - 1)),
                        reads=(wsq[par], hba), writes=(rs[f"P{f}"],),
                        tag=f"l0mz{tt}_{zh}_{k}_{f}")
            for f in range(8):
                j = zh * 8 + f
                sch.add("se", lambda h, f=f, j=j: h.activation(
                    xsT(j), P[f][:, 0:TT], AF.Silu, bias=0.0, scale=1.0),
                    reads=(rs[f"P{f}"],), writes=(hbb,), tag=f"l0zs{tt}_{j}")
        sch.add("se", lambda h, t0=t0: h.activation(
            FA[64:96, 2048:2048 + TT], Gsl(t0, TT), AF.Exp, bias=0.0, scale=1.0),
            reads=(gr,), writes=(far,), tag=f"l0xg{tt}")
        for j in range(16):
            sch.add("te", lambda h, j=j, t0=t0: h.matmul(
                P[2][:, 0:TT], SPT[64:96, j * 128:(j + 1) * 128],
                Cbf(t0, TT), start=True, stop=True),
                reads=(hbc, smbr), writes=(rs["P2"],), tag=f"l0cor{tt}_{j}")
            sch.add("ve", lambda h, j=j: h.tensor_scalar(
                SMb[64:96, 96:96 + TT], FA[64:96, 2048:2048 + TT],
                cst["indj64"][:, j:j + 1], None, ALU.mult),
                reads=(far, csts), writes=(smbr,), tag=f"l0gg1{tt}_{j}")
            sch.add("te", lambda h, j=j: h.matmul(
                P[3][:, 0:TT], cst["a2b64"], SMb[64:96, 96:96 + TT],
                start=True, stop=True),
                reads=(smbr, csts), writes=(rs["P3"],), tag=f"l0gg2{tt}_{j}")
            sch.add("ve", lambda h, j=j: h.tensor_copy(
                MTreg(0, TT), P[3][:, 0:TT]),
                reads=(rs["P3"],), writes=(hbc,), tag=f"l0gg3{tt}_{j}")
            sch.add("ve", lambda h, j=j: h.tensor_tensor(
                FA[:, 2560:2560 + TT], P[2][:, 0:TT], MTreg(0, TT), ALU.mult),
                reads=(rs["P2"], hbc), writes=(far,), tag=f"l0mg1{tt}_{j}")
            sch.add("ve", lambda h, j=j: h.tensor_tensor(
                FA[:, 3072:3072 + TT], FA[:, 2560:2560 + TT], ylT(j), ALU.add),
                reads=(far, hbc), writes=(far,), tag=f"l0mg2{tt}_{j}")
            sch.add("ve", lambda h, j=j: h.scalar_tensor_tensor(
                ylT(j), xsT(j), 1.0, FA[:, 3072:3072 + TT],
                ALU.mult, ALU.mult),
                reads=(hbb, far), writes=(hbc,), tag=f"l0mg3{tt}_{j}")
        for j in range(16):
            sch.add("ve", lambda h, j=j: h.tensor_tensor(
                MTreg(512, TT), ylT(j), ylT(j), ALU.mult),
                reads=(hbc,), writes=(hbc,), tag=f"l0rs{tt}_{j}")
            sch.add("te", lambda h, j=j: h.matmul(
                P[6][0:1, 0:TT], cst["ones_col_bf"], MTreg(512, TT),
                start=(j == 0), stop=(j == 15)),
                reads=(hbc, csts), writes=(rs["P6"],), tag=f"l0rm{tt}_{j}")
        mrow = FB[0:1, 0:TT]
        sch.add("ve", lambda h: h.tensor_scalar(
            mrow, P[6][0:1, 0:TT], 1.0 / 2048.0, None, ALU.mult),
            reads=(rs["P6"],), writes=(rs["rows"],), tag=f"l0rr1{tt}")
        sch.add("se", lambda h: h.activation(mrow, mrow, AF.Ln, bias=EPS6, scale=1.0),
                reads=(rs["rows"], csts), writes=(rs["rows"],), tag=f"l0rr2{tt}")
        sch.add("se", lambda h: h.activation(mrow, mrow, AF.Exp, bias=0.0, scale=-0.5),
                reads=(rs["rows"],), writes=(rs["rows"],), tag=f"l0rr3{tt}")
        sch.add("te", lambda h: h.matmul(
            P[6][:, 0:TT], cst["ones_row"], mrow, start=True, stop=True),
            reads=(rs["rows"], csts), writes=(rs["P6"],), tag=f"l0rb{tt}")
        for j in range(16):
            sch.add("ve", lambda h, j=j: h.scalar_tensor_tensor(
                xsT(j), ylT(j), cst["l0_rms_w"][:, j:j + 1], P[6][:, 0:TT],
                ALU.mult, ALU.mult),
                reads=(hbc, csts, rs["P6"]), writes=(hbb,), tag=f"l0yb{tt}_{j}")
        for k in range(16):
            par = k % 2
            wof = par * 1024
            sch.add("ge", lambda h, k=k, wof=wof: h.dma_start(
                out=WS[:, wof:wof + 1024],
                in_=din["l0_W_out"][k * 128:(k + 1) * 128, :]),
                reads=(), writes=(wsq[par],),
                dma=("sW" if par == 0 else "sWb"), tag=f"l0wo{tt}_{k}")
            for do in range(8):
                sch.add("te", lambda h, k=k, do=do, wof=wof, par=par: h.matmul(
                    P[do][:, 0:TT], WS[:, wof + do * 128:wof + (do + 1) * 128],
                    xsT(k), start=(k == 0), stop=(k == 15)),
                    reads=(wsq[par], hbb), writes=(rs[f"P{do}"],),
                    tag=f"l0wm{tt}_{k}_{do}")
        for do in range(8):
            sch.add("ve", lambda h, do=do, t0=t0: h.tensor_tensor(
                X[do][:, t0:t0 + TT], X[do][:, t0:t0 + TT], P[do][:, 0:TT],
                ALU.add),
                reads=(rs[f"P{do}"],), writes=(rs[f"X{do}"],),
                tag=f"l0res{tt}_{do}")


def eidos_init(sch, rs, meta):
    nc = sch.nc
    SM, SMb, FB, P, cst = meta["SM"], meta["SMb"], meta["FB"], meta["P"], meta["cst"]
    csts = rs["consts"]
    p5 = rs["P5"]
    for nm, dst in [("e_th1", "e_cs1"), ("e_th2", "e_cs2")]:
        th = cst[nm]
        cs = cst[dst]
        sch.add("se", lambda h, th=th, cs=cs: h.activation(
            cs[:, 0:4], th, AF.Sin, bias=float(np.pi / 2), scale=1.0),
            reads=(csts,), writes=(csts,), tag=f"ei_cos_{nm}")
        sch.add("se", lambda h, th=th, cs=cs: h.activation(
            cs[:, 4:8], th, AF.Sin, bias=0.0, scale=1.0),
            reads=(csts,), writes=(csts,), tag=f"ei_sin_{nm}")
    for j, coff, soff, csname in [(0, 0, 16, "e_cs1"), (1, 32, 48, "e_cs2")]:
        sch.add("te", lambda h, csname=csname: h.transpose(
            P[5][0:4, 0:9].bitcast(F32), cst[csname][:, 0:4],
            cst["ident_f"][0:9, 0:9]),
            reads=(csts,), writes=(p5,), tag=f"ei_cTc{j}")
        sch.add("ve", lambda h, coff=coff: h.tensor_copy(
            FB[0:4, coff:coff + 9], P[5][0:4, 0:9].bitcast(F32)),
            reads=(p5,), writes=(rs["FB"],), tag=f"ei_cTce{j}")
        sch.add("te", lambda h, csname=csname: h.transpose(
            P[5][0:4, 32:41].bitcast(F32), cst[csname][:, 4:8],
            cst["ident_f"][0:9, 0:9]),
            reads=(csts,), writes=(p5,), tag=f"ei_cTs{j}")
        sch.add("ve", lambda h, soff=soff: h.tensor_copy(
            FB[0:4, soff:soff + 9], P[5][0:4, 32:41].bitcast(F32)),
            reads=(p5,), writes=(rs["FB"],), tag=f"ei_cTse{j}")
    for j, dst0, coff, soff in [(0, 0, 0, 16), (1, 72, 32, 48)]:
        for p in range(9):
            sch.add("te", lambda h, p=p, coff=coff: h.transpose(
                P[5][0:1, 0:4].bitcast(F32), FB[0:4, coff + p:coff + p + 1],
                cst["ident_f"][0:4, 0:4]),
                reads=(rs["FB"], csts), writes=(p5,), tag=f"ei_cr{j}_{p}")
            sch.add("te", lambda h, p=p, soff=soff: h.transpose(
                P[5][0:1, 4:8].bitcast(F32), FB[0:4, soff + p:soff + p + 1],
                cst["ident_f"][0:4, 0:4]),
                reads=(rs["FB"], csts), writes=(p5,), tag=f"ei_sr{j}_{p}")
            sch.add("ve", lambda h: h.tensor_copy(FB[0:1, 64:72],
                                                  P[5][0:1, 0:8].bitcast(F32)),
                    reads=(p5,), writes=(rs["FB"],), tag=f"ei_cse{j}_{p}")
            sch.add("te", lambda h: h.matmul(
                P[5][:, 0:8], cst["ones_row"], FB[0:1, 64:72],
                start=True, stop=True),
                reads=(rs["FB"], csts), writes=(p5,), tag=f"ei_csb{j}_{p}")
            sch.add("ve", lambda h, p=p, dst0=dst0: h.tensor_copy(
                SM[:, dst0 + p * 8:dst0 + p * 8 + 8], P[5][:, 0:8]),
                reads=(p5,), writes=(csts,), tag=f"ei_csbe{j}_{p}")
    for j, r0, coff, soff in [(0, 4096, 0, 16), (1, 5248, 32, 48)]:
        for p in range(9):
            R = SMb[:, r0 + p * 128:r0 + (p + 1) * 128]
            sch.add("ve", lambda h, R=R: h.tensor_copy(R, cst["ipad"]),
                    reads=(csts,), writes=(csts,), tag=f"ei_Rc{j}_{p}")
            sch.add("te", lambda h, p=p, coff=coff: h.matmul(
                P[5][0:8, 16:17].bitcast(F32), cst["e2x"],
                FB[0:4, coff + p:coff + p + 1], start=True, stop=True),
                reads=(rs["FB"], csts), writes=(p5,), tag=f"ei_Rcc{j}_{p}")
            sch.add("ve", lambda h: h.tensor_copy(FB[0:8, 80:81],
                                                  P[5][0:8, 16:17].bitcast(F32)),
                    reads=(p5,), writes=(rs["FB"],), tag=f"ei_Rcce{j}_{p}")
            sch.add("te", lambda h, p=p, soff=soff: h.matmul(
                P[5][0:8, 16:17].bitcast(F32), cst["e2x"],
                FB[0:4, soff + p:soff + p + 1], start=True, stop=True),
                reads=(rs["FB"], csts), writes=(p5,), tag=f"ei_Rsc{j}_{p}")
            sch.add("ve", lambda h: h.tensor_mul(
                FB[0:8, 81:82], P[5][0:8, 16:17].bitcast(F32), cst["altsign"]),
                reads=(p5, csts), writes=(rs["FB"],), tag=f"ei_Rsse{j}_{p}")
            sch.add("ve", lambda h: h.tensor_mul(
                FB[0:8, 96:104], FB[0:8, 80:81].broadcast_to([8, 8]),
                cst["ident_f"][0:8, 0:8]),
                reads=(rs["FB"], csts), writes=(rs["FB"],), tag=f"ei_Rcd{j}_{p}")
            sch.add("ve", lambda h: h.tensor_mul(
                FB[0:8, 104:112], FB[0:8, 81:82].broadcast_to([8, 8]),
                cst["ident_f"][0:8, 0:8]),
                reads=(rs["FB"], csts), writes=(rs["FB"],), tag=f"ei_Rsd{j}_{p}")
            sch.add("te", lambda h: h.matmul(
                P[5][0:8, 0:128], FB[0:8, 96:104], cst["ident_f"][0:8, 0:128],
                start=True, stop=False),
                reads=(rs["FB"], csts), writes=(p5,), tag=f"ei_Rm1{j}_{p}")
            sch.add("te", lambda h: h.matmul(
                P[5][0:8, 0:128], FB[0:8, 104:112], cst["pswap"],
                start=False, stop=True),
                reads=(rs["FB"], csts), writes=(p5,), tag=f"ei_Rm2{j}_{p}")
            sch.add("ve", lambda h, R=R: h.tensor_copy(R[0:8, :], P[5][0:8, 0:128]),
                    reads=(p5,), writes=(csts,), tag=f"ei_Re{j}_{p}")
    sch.add("ve", lambda h: h.tensor_copy(SMb[:, 6416:6424], cst["e_wpath"]),
            reads=(csts,), writes=(csts,), tag="ei_wpb")
    sch.add("ve", lambda h: h.tensor_copy(SMb[:, 6424:6432], cst["e_wteam"]),
            reads=(csts,), writes=(csts,), tag="ei_wtb")
    sch.add("ve", lambda h: h.scalar_tensor_tensor(
        SM[:, 144:216], cst["e_pb"] if False else cst["e_be2"], 1.0,
        cst["e_be2"] if False else cst["e_pb"], ALU.mult, ALU.add),
        reads=(csts,), writes=(csts,), tag="ei_b2p")
    sch.add("ve", lambda h: h.tensor_scalar(
        SM[:, 216:288], cst["e_ctx"], 0.1, None, ALU.mult),
        reads=(csts,), writes=(csts,), tag="ei_ctx01")
    sch.add("ve", lambda h: h.tensor_scalar(
        SM[:, 288:360], cst["e_g2"], 7.0, None, ALU.mult),
        reads=(csts,), writes=(csts,), tag="ei_g27")
    sch.add("ve", lambda h: h.tensor_scalar(
        SM[:, 360:648], cst["e_g1"], 7.0, None, ALU.mult),
        reads=(csts,), writes=(csts,), tag="ei_g17")
    sch.add("ve", lambda h: h.scalar_tensor_tensor(
        SM[0:9, 2044:2045], cst["e_ovp"], -1.0, cst["e_bp"], ALU.mult, ALU.add),
        reads=(csts,), writes=(csts,), tag="ei_bmp")
    sch.add("ve", lambda h: h.scalar_tensor_tensor(
        SM[0:3, 2045:2046], cst["e_ovt"], -1.0, cst["e_bt"], ALU.mult, ALU.add),
        reads=(csts,), writes=(csts,), tag="ei_bmt")
    sch.add("te", lambda h: h.transpose(
        P[5][0:1, 0:9].bitcast(F32), SM[0:9, 2044:2045],
        cst["ident_f"][0:9, 0:9]),
        reads=(csts,), writes=(p5,), tag="ei_bmpT")
    sch.add("ve", lambda h: h.tensor_copy(SM[0:1, 2020:2029],
                                          P[5][0:1, 0:9].bitcast(F32)),
            reads=(p5,), writes=(csts,), tag="ei_bmpE")
    sch.add("te", lambda h: h.transpose(
        P[5][0:1, 16:19].bitcast(F32), SM[0:3, 2045:2046],
        cst["ident_f"][0:3, 0:3]),
        reads=(csts,), writes=(p5,), tag="ei_bmtT")
    sch.add("ve", lambda h: h.tensor_copy(SM[0:1, 2036:2039],
                                          P[5][0:1, 16:19].bitcast(F32)),
            reads=(p5,), writes=(csts,), tag="ei_bmtE")
    sch.add("te", lambda h: h.matmul(
        P[5][:, 32:35], cst["ones_row"], SM[0:1, 2036:2039],
        start=True, stop=True),
        reads=(csts,), writes=(p5,), tag="ei_bmtB")
    sch.add("ve", lambda h: h.tensor_copy(SM[:, 2030:2033], P[5][:, 32:35]),
            reads=(p5,), writes=(csts,), tag="ei_bmtBE")


def emit_eidos(sch, rs, meta, st):
    HB, WS, SM, SMb, FA, FB, X, P = (meta["HB"], meta["WS"], meta["SM"],
                                     meta["SMb"], meta["FA"], meta["FB"],
                                     meta["X"], meta["P"])
    TT, NTT, T = meta["TT"], meta["NTT"], meta["T"]
    cst = meta["cst"]
    din = meta["din"]
    dram = meta["dram"]
    if not meta.get("_eidos_init_done"):
        eidos_init(sch, rs, meta)
        meta["_eidos_init_done"] = True

    csts = rs["consts"]
    wsq = [rs["WS0"], rs["WS1"]]
    w2q = [rs["WS2a"], rs["WS2b"], rs["WS2c"], rs["WS2d"]]
    hba, hbc, smbr = rs["HBa"], rs["HBc"], rs["SMb"]
    smr, far, fbr = rs["SM"], rs["FA"], rs["FB"]
    HR = [rs[f"HR{k}"] for k in range(32)]
    RV = [rs["RV0"], rs["RV1"]]
    sq1r = [rs["SQ1"], rs["SQ1b"]]
    rinv1s = [SM[:, 800:800 + TT], SM[:, 1312:1312 + TT]]
    rinv2s = FA[:, 1024:1024 + TT]
    sq1scr = [SMb[:, 2176:2176 + TT], SMb[:, 2688:2688 + TT]]

    def lnxt(k):
        return HB[:, k * TT:(k + 1) * TT]

    def hraw(k):
        return HB[:, 4096 + k * TT:4096 + (k + 1) * TT]

    def feat(s, dc):
        o = 20480 + s * 8 * TT + dc * TT
        return HB[:, o:o + TT]

    def o2(dc):
        return SMb[:, dc * TT:(dc + 1) * TT]

    bcol = lambda tab, idx: tab[:, idx:idx + 1]

    def w1dma_p(p, fq):
        par = fq % 2
        wof = par * 2048
        sch.add("ge", lambda h, p=p, fq=fq, wof=wof: h.dma_start(
            out=WS[:, wof:wof + 2048].rearrange("q (k j) -> q k j", k=8),
            in_=hap(din["e_W1"], p * D * 4096 + fq * 256,
                    [[4096, 128], [128 * 4096, 8], [1, 256]])),
            reads=(), writes=(wsq[par],),
            dma=("sW" if par == 0 else "sWb"), tag=f"e{st}W1_{p}_{fq}")

    def pf_a(p):
        w1dma_p(p, 0)
        w1dma_p(p, 1)

    def a_thunks(p):
        rv = rinv1s[p % 2]
        rvr = RV[p % 2]

        def w1dma(fq):
            w1dma_p(p, fq)

        def aux(fc):
            bank = fc % 4
            pb = rs[f"P{bank}"]
            sch.add("se", lambda h, fc=fc, bank=bank, p=p: h.activation(
                hraw(fc), P[bank][:, 0:TT], AF.Identity,
                bias=bcol(cst["e_b1"], p * 32 + fc), scale=1.0),
                reads=(pb, csts), writes=(HR[fc],), tag=f"e{st}ev1_{p}_{fc}")
            if fc == 0:
                sch.add("te", lambda h, p=p: h.matmul(
                    P[5][:, 0:TT], SMb[:, 4096 + p * 128:4096 + (p + 1) * 128],
                    hraw(0), start=True, stop=True),
                    reads=(HR[0], csts), writes=(rs["P5"],),
                    tag=f"e{st}rot1_{p}")
                sch.add("se", lambda h: h.activation(
                    hraw(0), P[5][:, 0:TT], AF.Identity, bias=0.0, scale=1.0),
                    reads=(rs["P5"],), writes=(HR[0],), tag=f"e{st}rev1_{p}")
            sch.add("ve", lambda h, fc=fc: h.tensor_tensor(
                sq1scr[fc % 2], hraw(fc), hraw(fc), ALU.mult),
                reads=(HR[fc],), writes=(sq1r[fc % 2],), tag=f"e{st}sq1_{p}_{fc}")

        def aux_st(fc):
            sch.add("te", lambda h, fc=fc: h.matmul(
                P[6][0:1, 0:TT], cst["ones_col_bf"], sq1scr[fc % 2],
                start=(fc == 0), stop=(fc == 31)),
                reads=(sq1r[fc % 2], csts), writes=(rs["P6"],),
                tag=f"e{st}st1_{p}_{fc}")

        ths = []
        for fc in range(32):
            def grp(fc=fc):
                if fc % 2 == 0 and fc // 2 + 2 < 16:
                    w1dma(fc // 2 + 2)
                par = (fc // 2) % 2
                wof = par * 2048
                f2 = fc % 2
                pb = rs[f"P{fc % 4}"]
                for k in range(DC):
                    sch.add("te", lambda h, wof=wof, k=k, f2=f2, fc=fc: h.matmul(
                        P[fc % 4][:, 0:TT],
                        WS[:, wof + k * 256 + f2 * 128:
                           wof + k * 256 + (f2 + 1) * 128],
                        lnxt(k), start=(k == 0), stop=(k == DC - 1)),
                        reads=(wsq[par], hba), writes=(pb,),
                        tag=f"e{st}p1_{p}_{fc}_{k}")
                if fc >= 2:
                    aux(fc - 2)
                if fc >= 3:
                    aux_st(fc - 3)
            ths.append(grp)

        def tail():
            aux(30)
            aux_st(29)
            aux(31)
            aux_st(30)
            aux_st(31)
            sch.add("ve", lambda h: h.tensor_scalar(
                rv[0:1, :], P[6][0:1, 0:TT], 1.0 / 4096.0, None, ALU.mult),
                reads=(rs["P6"],), writes=(rvr,), tag=f"e{st}m1_{p}")
            sch.add("se", lambda h: h.activation(
                rv[0:1, :], rv[0:1, :], AF.Ln, bias=EPS6, scale=1.0),
                reads=(rvr, csts), writes=(rvr,), tag=f"e{st}l1_{p}")
            sch.add("se", lambda h: h.activation(
                rv[0:1, :], rv[0:1, :], AF.Exp, bias=0.0, scale=-0.5),
                reads=(rvr,), writes=(rvr,), tag=f"e{st}r1_{p}")
            sch.add("te", lambda h: h.matmul(
                P[6][:, 0:TT], cst["ones_row"], rv[0:1, :],
                start=True, stop=True),
                reads=(rvr, csts), writes=(rs["P6"],), tag=f"e{st}bc1_{p}")
            sch.add("ve", lambda h: h.tensor_copy(rv, P[6][:, 0:TT]),
                    reads=(rs["P6"],), writes=(rvr,), tag=f"e{st}rvc_{p}")
        ths.append(tail)
        return ths

    def kloop(p):
        rv = rinv1s[p % 2]
        rvr = RV[p % 2]
        for kx in range(35):
            if kx < 32:
                sl = kx % 4
                w2of = 4096 + sl * 1024
                sch.add("ge", lambda h, p=p, kx=kx, w2of=w2of: h.dma_start(
                    out=WS[:, w2of:w2of + 1024],
                    in_=hap(din["e_W2"], p * 4096 * D + kx * 128 * D,
                            [[D, 128], [1, 1024]])),
                    reads=(), writes=(w2q[sl],),
                    dma=("sW" if kx % 2 == 0 else "sWb"),
                    tag=f"e{st}W2_{p}_{kx}")
                g1c = bcol(cst["e_g1"], p * 32 + kx)
                g17b = SM[:, 360 + p * 32 + kx:
                          361 + p * 32 + kx].broadcast_to([128, TT])
                be1c = bcol(cst["e_be1"], p * 32 + kx)
                sch.add("ve", lambda h, kx=kx, g17b=g17b: h.scalar_tensor_tensor(
                    FB[:, 1536:1536 + TT], hraw(kx), 0.0, g17b,
                    ALU.is_lt, ALU.mult),
                    reads=(HR[kx], csts, smr), writes=(fbr,),
                    tag=f"e{st}t1_{p}_{kx}")
                sch.add("ve", lambda h, kx=kx, g1c=g1c, rv=rv:
                        h.scalar_tensor_tensor(
                    FA[:, 0:TT], hraw(kx), g1c, rv, ALU.mult, ALU.mult),
                    reads=(HR[kx], csts, rvr), writes=(far,),
                    tag=f"e{st}u_{p}_{kx}")
                sch.add("ge", lambda h: h.tensor_add(
                    FA[:, 512:512 + TT], FA[:, 0:TT], FB[:, 1536:1536 + TT]),
                    reads=(far, fbr), writes=(far,), tag=f"e{st}v_{p}_{kx}")
                sch.add("se", lambda h, kx=kx, be1c=be1c: h.activation(
                    hraw(kx), FA[:, 512:512 + TT], AF.Identity,
                    bias=be1c, scale=1.0),
                    reads=(far, csts), writes=(HR[kx],),
                    tag=f"e{st}mp1_{p}_{kx}")
            if kx >= 3:
                k = kx - 3
                sl = k % 4
                w2of = 4096 + sl * 1024
                for do in range(8):
                    pb = rs[f"P{do}"]
                    sch.add("te", lambda h, k=k, do=do, w2of=w2of: h.matmul(
                        P[do][:, 0:TT],
                        WS[:, w2of + do * 128:w2of + (do + 1) * 128],
                        hraw(k), start=(k == 0), stop=(k == 31)),
                        reads=(w2q[sl], HR[k]), writes=(pb,),
                        tag=f"e{st}p2_{p}_{k}_{do}")

    def ev2(p, dc):
        pb = rs[f"P{dc}"]
        sch.add("se", lambda h, dc=dc, p=p: h.activation(
            o2(dc), P[dc][:, 0:TT], AF.Identity,
            bias=bcol(cst["e_b2"], p * 8 + dc), scale=1.0),
            reads=(pb, csts), writes=(smbr,), tag=f"e{st}ev2_{p}_{dc}")

    def evac(p):
        s = p % 3
        for dc in range(4):
            ev2(p, dc)
            if dc == 0:
                sch.add("te", lambda h, p=p: h.matmul(
                    P[5][:, 0:TT], SMb[:, 5248 + p * 128:5248 + (p + 1) * 128],
                    o2(0), start=True, stop=True),
                    reads=(smbr, csts), writes=(rs["P5"],),
                    tag=f"e{st}rot2_{p}")
                sch.add("se", lambda h: h.activation(
                    o2(0), P[5][:, 0:TT], AF.Identity, bias=0.0, scale=1.0),
                    reads=(rs["P5"],), writes=(smbr,), tag=f"e{st}rev2_{p}")
        if p < 8:
            pf_a(p + 1)
        ths = [lambda: (ev2(p, 4), ev2(p, 5)), lambda: (ev2(p, 6), ev2(p, 7))]
        for dc in range(8):
            def sqst(dc=dc):
                sch.add("se", lambda h, dc=dc, s=s: h.activation(
                    feat(s, dc), o2(dc), AF.Square, bias=0.0, scale=1.0),
                    reads=(smbr,), writes=(hbc,), tag=f"e{st}sq2_{p}_{dc}")
                sch.add("te", lambda h, dc=dc, s=s: h.matmul(
                    P[7][0:1, 0:TT], cst["ones_col_bf"], feat(s, dc),
                    start=(dc == 0), stop=(dc == 7)),
                    reads=(hbc, csts), writes=(rs["P7"],),
                    tag=f"e{st}st2_{p}_{dc}")
            ths.append(sqst)

        def rinv2(p=p):
            m2row = FB[0:1, 512:512 + TT]
            sch.add("ve", lambda h: h.tensor_scalar(
                m2row, P[7][0:1, 0:TT], 1.0 / 1024.0, None, ALU.mult),
                reads=(rs["P7"],), writes=(fbr,), tag=f"e{st}m2_{p}")
            sch.add("se", lambda h: h.activation(
                m2row, m2row, AF.Ln, bias=EPS6, scale=1.0),
                reads=(fbr, csts), writes=(fbr,), tag=f"e{st}l2_{p}")
            sch.add("se", lambda h: h.activation(
                m2row, m2row, AF.Exp, bias=0.0, scale=-0.5),
                reads=(fbr,), writes=(fbr,), tag=f"e{st}r2_{p}")
            sch.add("te", lambda h: h.matmul(
                P[7][:, 0:TT], cst["ones_row"], m2row, start=True, stop=True),
                reads=(fbr, csts), writes=(rs["P7"],), tag=f"e{st}bc2_{p}")
            sch.add("ve", lambda h: h.tensor_copy(rinv2s, P[7][:, 0:TT]),
                    reads=(rs["P7"],), writes=(far,), tag=f"e{st}rv2c_{p}")
        ths.append(rinv2)
        return ths

    def c_thunks(p, tt):
        s = p % 3
        team = p // 3
        ths = []
        tes = []
        for dc in range(8):
            def mp2(dc=dc):
                g2c = bcol(cst["e_g2"], p * 8 + dc)
                g27b = SM[:, 288 + p * 8 + dc:
                          289 + p * 8 + dc].broadcast_to([128, TT])
                b2pc = SM[:, 144 + p * 8 + dc:145 + p * 8 + dc]
                ctxc = SM[:, 216 + p * 8 + dc:217 + p * 8 + dc]
                sch.add("ve", lambda h, dc=dc, g27b=g27b: h.scalar_tensor_tensor(
                    FB[:, 1536:1536 + TT], o2(dc), 0.0, g27b,
                    ALU.is_lt, ALU.mult),
                    reads=(smbr, smr), writes=(fbr,),
                    tag=f"e{st}t2_{p}_{dc}")
                sch.add("ve", lambda h, dc=dc, g2c=g2c: h.scalar_tensor_tensor(
                    FA[:, 0:TT], o2(dc), g2c, rinv2s, ALU.mult, ALU.mult),
                    reads=(smbr, csts, far), writes=(far,),
                    tag=f"e{st}u2_{p}_{dc}")
                sch.add("ve", lambda h: h.tensor_add(
                    FA[:, 512:512 + TT], FA[:, 0:TT], FB[:, 1536:1536 + TT]),
                    reads=(far, fbr), writes=(far,), tag=f"e{st}v2_{p}_{dc}")
                sch.add("se", lambda h, b2pc=b2pc: h.activation(
                    FA[:, 1536:1536 + TT], FA[:, 512:512 + TT], AF.Identity,
                    bias=b2pc, scale=1.0),
                    reads=(far, smr), writes=(far,), tag=f"e{st}w2_{p}_{dc}")
                sch.add("se", lambda h: h.activation(
                    FB[:, 1536:1536 + TT], FA[:, 1536:1536 + TT], AF.Tanh,
                    bias=0.0, scale=1.0),
                    reads=(far,), writes=(fbr,), tag=f"e{st}th_{p}_{dc}")
                sch.add("ve", lambda h, s=s, dc=dc, ctxc=ctxc:
                        h.scalar_tensor_tensor(
                    feat(s, dc), FB[:, 1536:1536 + TT], ctxc,
                    FA[:, 1536:1536 + TT], ALU.mult, ALU.add),
                    reads=(fbr, far, smr), writes=(hbc,),
                    tag=f"e{st}ft_{p}_{dc}")
            ths.append(mp2)

            def mp2te(dc=dc):
                wp = hap(meta["SMb"], 6416 + dc, [[6656, 128], [1, 1]])
                wt = hap(meta["SMb"], 6424 + dc, [[6656, 128], [1, 1]])
                sch.add("te", lambda h, s=s, dc=dc, wp=wp: h.matmul(
                    P[7][0:1, 0:TT], wp, feat(s, dc),
                    start=(dc == 0), stop=(dc == 7)),
                    reads=(hbc, csts), writes=(rs["P7"],),
                    tag=f"e{st}pq_{p}_{dc}")
                sch.add("te", lambda h, s=s, dc=dc, wt=wt: h.matmul(
                    P[4][0:1, 0:TT], wt, feat(s, dc),
                    start=(dc == 0), stop=(dc == 7)),
                    reads=(hbc, csts), writes=(rs["P4"],),
                    tag=f"e{st}qq_{p}_{dc}")
            tes.append(mp2te)
            if dc >= 2:
                ths.append(tes.pop(0))
        ths.extend(tes)
        tes = []

        def rows():
            bmv = SM[0:1, 2020 + p:2021 + p]
            sch.add("ve", lambda h, bmv=bmv: h.tensor_scalar(
                FB[0:1, 1536:1536 + TT], P[7][0:1, 0:TT], bmv, None, ALU.add),
                reads=(rs["P7"], csts), writes=(fbr,), tag=f"e{st}psr_{p}")
            sch.add("ve", lambda h: h.tensor_copy(
                FB[0:1, 1024:1024 + TT], P[4][0:1, 0:TT]),
                reads=(rs["P4"],), writes=(fbr,), tag=f"e{st}qr_{p}")
        ths.append(rows)

        for i in range(TT // 128):
            def ptr(i=i):
                sch.add("te", lambda h, i=i: h.transpose(
                    P[4][:, 64 + i:65 + i].bitcast(F32),
                    FB[0:1, 1536 + i * 128:1536 + (i + 1) * 128],
                    cst["ident_f"][0:1, 0:1]),
                    reads=(fbr, csts), writes=(rs["P4"],),
                    tag=f"e{st}psT{p}_{i}")
                sch.add("ve", lambda h, i=i, p=p: h.tensor_copy(
                    SM[:, 648 + i * 32 + p:649 + i * 32 + p],
                    P[4][:, 64 + i:65 + i].bitcast(F32)),
                    reads=(rs["P4"],), writes=(smr,), tag=f"e{st}psE{p}_{i}")
                sch.add("te", lambda h, i=i: h.transpose(
                    P[4][:, 72 + i:73 + i].bitcast(F32),
                    FB[0:1, 1024 + i * 128:1024 + (i + 1) * 128],
                    cst["ident_f"][0:1, 0:1]),
                    reads=(fbr, csts), writes=(rs["P4"],),
                    tag=f"e{st}qT{p}_{i}")
                sch.add("ve", lambda h, i=i, p=p: h.tensor_copy(
                    SM[:, 648 + i * 32 + 16 + p:649 + i * 32 + 16 + p],
                    P[4][:, 72 + i:73 + i].bitcast(F32)),
                    reads=(rs["P4"],), writes=(smr,), tag=f"e{st}qE{p}_{i}")
            ths.append(ptr)

        if s == 2:
            ths.append(lambda: _eidos_team(sch, rs, meta, st, tt, team))
        return ths

    def ln_tile(tt2):
        emit_ln(sch, rs, meta, tt2 * TT, TT, lnxt, EPS5, tag=f"e{st}ln{tt2}",
                sq_off=4096, dst_res="HBa",
                sq_res_fn=lambda d: HR[d])

    for tt in range(NTT):
        t0 = tt * TT
        if tt == 0:
            pf_a(0)
            ln_tile(0)
            for th in a_thunks(0):
                th()
        for p in range(9):
            kloop(p)
            cth = evac(p) + c_thunks(p, tt)
            if p < 8:
                ath = a_thunks(p + 1)
                j = 0
                ncp = len(cth)
                na = len(ath)
                for i, a in enumerate(ath):
                    a()
                    while j < ncp and j / ncp <= (i + 1) / na:
                        cth[j]()
                        j += 1
                while j < ncp:
                    cth[j]()
                    j += 1
            else:
                for c in cth:
                    c()
                if tt < NTT - 1:
                    pf_a(0)
                    ln_tile(tt + 1)
                _eidos_final(sch, rs, meta, st, tt, t0)
        if tt < NTT - 1:
            for th in a_thunks(0):
                th()



def _softmax3_cols(sch, rs, meta, col_fn, out_rows_off, tag):
    SM, FB, P, cst = meta["SM"], meta["FB"], meta["P"], meta["cst"]
    TT = meta["TT"]
    p5, fbr, smr, csts = rs["P5"], rs["FB"], rs["SM"], rs["consts"]
    nch = TT // 128
    for i in range(nch):
        cols = col_fn(i)
        sch.add("ve", lambda h, cols=cols: h.reduce_max(
            SM[:, 795:796], cols, mybir.AxisListType.X),
            reads=(smr,), writes=(smr,), tag=f"{tag}mx{i}")
        sch.add("ve", lambda h: h.tensor_scalar(
            SM[:, 796:797], SM[:, 795:796], -1.0, None, ALU.mult),
            reads=(smr,), writes=(smr,), tag=f"{tag}ng{i}")
        sch.add("se", lambda h, cols=cols: h.activation(
            cols, cols, AF.Exp, bias=SM[:, 796:797], scale=1.0),
            reads=(smr,), writes=(smr,), tag=f"{tag}ex{i}")
        sch.add("ve", lambda h, cols=cols: h.reduce_sum(
            SM[:, 797:798], cols, mybir.AxisListType.X),
            reads=(smr,), writes=(smr,), tag=f"{tag}sm{i}")
        sch.add("ve", lambda h: h.reciprocal(
            SM[:, 798:799], SM[:, 797:798]),
            reads=(smr,), writes=(smr,), tag=f"{tag}rc{i}")
        sch.add("ve", lambda h, cols=cols: h.tensor_scalar(
            cols, cols, SM[:, 798:799], None, ALU.mult),
            reads=(smr,), writes=(smr,), tag=f"{tag}wn{i}")
        for s in range(3):
            sch.add("te", lambda h, cols=cols, s=s: h.transpose(
                P[5][0:1, s * 128:(s + 1) * 128], cols[:, s:s + 1],
                cst["ident_f"]),
                reads=(smr, csts), writes=(p5,), tag=f"{tag}tr{i}_{s}")
            sch.add("ve", lambda h, i=i, s=s: h.tensor_copy(
                FB[0:1, out_rows_off + s * TT + i * 128:
                   out_rows_off + s * TT + (i + 1) * 128],
                P[5][0:1, s * 128:(s + 1) * 128]),
                reads=(p5,), writes=(fbr,), tag=f"{tag}te{i}_{s}")


def _eidos_team(sch, rs, meta, st, tt, team):
    SM, FB, FA, HB, P, cst = (meta["SM"], meta["FB"], meta["FA"], meta["HB"],
                              meta["P"], meta["cst"])
    TT = meta["TT"]
    dram = meta["dram"]
    p5, fbr, smr, far, hbc, csts = (rs["P5"], rs["FB"], rs["SM"], rs["FA"],
                                    rs["HBc"], rs["consts"])
    nch = TT // 128

    def feat(s, dc):
        o = 20480 + s * 8 * TT + dc * TT
        return HB[:, o:o + TT]

    _softmax3_cols(sch, rs, meta,
                   lambda i: SM[:, 648 + i * 32 + 3 * team:648 + i * 32 + 3 * team + 3],
                   0, f"e{st}wi{tt}_{team}_")
    for i in range(nch):
        wi = SM[:, 648 + i * 32 + 3 * team:648 + i * 32 + 3 * team + 3]
        qq = SM[:, 648 + i * 32 + 16 + 3 * team:648 + i * 32 + 16 + 3 * team + 3]
        sch.add("ve", lambda h, wi=wi, qq=qq: h.tensor_mul(
            SM[:, 792:795], wi, qq), reads=(smr,), writes=(smr,),
            tag=f"e{st}tsm{tt}_{team}_{i}")
        sch.add("ve", lambda h: h.reduce_sum(
            SM[:, 795:796], SM[:, 792:795], mybir.AxisListType.X),
            reads=(smr,), writes=(smr,), tag=f"e{st}tss{tt}_{team}_{i}")
        sch.add("ve", lambda h, i=i, team=team: h.tensor_scalar(
            SM[:, 776 + i * 4 + team:777 + i * 4 + team], SM[:, 795:796],
            SM[:, 2030 + team:2031 + team], None, ALU.add),
            reads=(smr, csts), writes=(smr,), tag=f"e{st}tsb{tt}_{team}_{i}")
    for s3 in range(3):
        sch.add("te", lambda h, s3=s3: h.matmul(
            P[5][:, 0:TT], cst["ones_row"], FB[0:1, s3 * TT:(s3 + 1) * TT],
            start=True, stop=True),
            reads=(fbr, csts), writes=(p5,), tag=f"e{st}wib_{tt}_{team}_{s3}")
        for dc in range(8):
            if s3 == 0:
                sch.add("ve", lambda h, s3=s3, dc=dc: h.scalar_tensor_tensor(
                    FA[:, dc * TT:(dc + 1) * TT], feat(s3, dc), 1.0,
                    P[5][:, 0:TT], ALU.mult, ALU.mult),
                    reads=(hbc, p5), writes=(far,), tag=f"e{st}tf0_{tt}_{team}_{dc}")
            else:
                sch.add("ve", lambda h, s3=s3, dc=dc: h.scalar_tensor_tensor(
                    FB[:, 1536:1536 + TT], feat(s3, dc), 1.0, P[5][:, 0:TT],
                    ALU.mult, ALU.mult),
                    reads=(hbc, p5), writes=(fbr,), tag=f"e{st}tfm{s3}_{tt}_{team}_{dc}")
                sch.add("ve", lambda h, dc=dc: h.tensor_add(
                    FA[:, dc * TT:(dc + 1) * TT], FA[:, dc * TT:(dc + 1) * TT],
                    FB[:, 1536:1536 + TT]),
                    reads=(fbr, far), writes=(far,), tag=f"e{st}tfa{s3}_{tt}_{team}_{dc}")
    sch.add("ge", lambda h, team=team: h.dma_start(
        out=dram["sp_tf"][team].rearrange("(dc q) t -> q dc t", dc=8),
        in_=FA[:, 0:8 * TT].rearrange("q (dc t) -> q dc t", dc=8)),
        reads=(far,), writes=(rs["dram"],), dma="sGd", tag=f"e{st}tfsp_{tt}_{team}")


def _eidos_final(sch, rs, meta, st, tt, t0):
    SM, FB, FA, HB, P, X, cst = (meta["SM"], meta["FB"], meta["FA"], meta["HB"],
                                 meta["P"], meta["X"], meta["cst"])
    TT = meta["TT"]
    dram = meta["dram"]
    p5, fbr, smr, far, hbc, csts = (rs["P5"], rs["FB"], rs["SM"], rs["FA"],
                                    rs["HBc"], rs["consts"])
    _softmax3_cols(sch, rs, meta,
                   lambda i: SM[:, 776 + i * 4:776 + i * 4 + 3],
                   0, f"e{st}wt{tt}_")
    for t3 in range(3):
        sch.add("ge", lambda h, t3=t3: h.dma_start(
            out=HB[:, 20480 + t3 * 8 * TT:20480 + (t3 + 1) * 8 * TT].rearrange(
                "q (dc t) -> q dc t", dc=8),
            in_=dram["sp_tf"][t3].rearrange("(dc q) t -> q dc t", dc=8)),
            reads=(rs["dram"],), writes=(rs["HBc"],), dma="sGd",
            tag=f"e{st}tfrl_{tt}_{t3}")
        sch.add("te", lambda h, t3=t3: h.matmul(
            P[5][:, 0:TT], cst["ones_row"], FB[0:1, t3 * TT:(t3 + 1) * TT],
            start=True, stop=True),
            reads=(fbr, csts), writes=(p5,), tag=f"e{st}wtb_{tt}_{t3}")
        for dc in range(8):
            tfb = HB[:, 20480 + t3 * 8 * TT + dc * TT:20480 + t3 * 8 * TT + (dc + 1) * TT]
            if t3 == 0:
                sch.add("ve", lambda h, tfb=tfb, dc=dc: h.scalar_tensor_tensor(
                    FA[:, dc * TT:(dc + 1) * TT], tfb, 1.0, P[5][:, 0:TT],
                    ALU.mult, ALU.mult),
                    reads=(rs["HBc"], p5), writes=(far,), tag=f"e{st}o0_{tt}_{dc}")
            else:
                sch.add("ve", lambda h, tfb=tfb, dc=dc: h.scalar_tensor_tensor(
                    FB[:, 1536:1536 + TT], tfb, 1.0, P[5][:, 0:TT],
                    ALU.mult, ALU.mult),
                    reads=(rs["HBc"], p5), writes=(fbr,), tag=f"e{st}om{t3}_{tt}_{dc}")
                sch.add("ve", lambda h, dc=dc: h.tensor_add(
                    FA[:, dc * TT:(dc + 1) * TT], FA[:, dc * TT:(dc + 1) * TT],
                    FB[:, 1536:1536 + TT]),
                    reads=(fbr, far), writes=(far,), tag=f"e{st}oa{t3}_{tt}_{dc}")
    for dc in range(8):
        sch.add("ve", lambda h, dc=dc, t0=t0: h.scalar_tensor_tensor(
            X[dc][:, t0:t0 + TT], FA[:, dc * TT:(dc + 1) * TT], 0.5,
            X[dc][:, t0:t0 + TT], ALU.mult, ALU.add),
            reads=(far,), writes=(rs[f"X{dc}"],), tag=f"e{st}res_{tt}_{dc}")


def emit_l1_attn(sch, rs, meta):
    nc = sch.nc
    din, X, P, HB, WS, SM, SMb, FA, FB, cst = (
        meta["din"], meta["X"], meta["P"], meta["HB"], meta["WS"], meta["SM"],
        meta["SMb"], meta["FA"], meta["FB"], meta["cst"])
    T, TT, NTT = meta["T"], meta["TT"], meta["NTT"]
    dram = meta["dram"]
    csts = rs["consts"]
    wsq = [rs["WS0"], rs["WS1"]]
    hba, hbb, hbc = rs["HBa"], rs["HBb"], rs["HBc"]
    far, fbr, smr, smbr = rs["FA"], rs["FB"], rs["SM"], rs["SMb"]

    def lnxT(d):
        return HB[:, d * TT:(d + 1) * TT]

    def xcv(ch):
        return HB[:, 4096 + ch * TT:4096 + (ch + 1) * TT]

    def zsT(ch):
        return HB[:, 12288 + ch * TT:12288 + (ch + 1) * TT]

    def ygT(ch):
        return HB[:, 20480 + ch * TT:20480 + (ch + 1) * TT]

    Cmb = lambda c0, n: SMb[64:80, c0:c0 + n]
    hmT = lambda c0, n: SMb[64:80, 2048 + c0:2048 + c0 + n]
    carries = SM[:, 512:560]
    hpbuf = [SM[:, 560:816], SM[:, 816:1072]]
    last3 = SM[:, 1072:1120]
    dBr = FB[:, 640:640 + TT]
    dltr = FB[0:1, 1152:1152 + TT]

    sch.add("ge", lambda h: h.dma_start(
        out=WS[:, 4096:4624].rearrange("q (k c) -> q k c", k=16),
        in_=din["l1_W_xproj"].ap().rearrange("(k q) c -> q k c", k=16)),
        reads=(), writes=(rs["WS2a"],), dma="sW", tag="l1wxp")

    for tt in range(NTT):
        t0 = tt * TT
        emit_ln(sch, rs, meta, t0, TT, lnxT, EPS5, tag=f"l1ln{tt}",
                sq_off=24576, sq_res="HBc", dst_res="HBa", f32_scr=2048)
        sch.add("sy", lambda h, t0=t0: h.dma_start(
            out=dram["sp_lnx4"][:, t0:t0 + TT].rearrange("(d q) t -> q d t", d=DC),
            in_=HB[:, 0:DC * TT].rearrange("q (d t) -> q d t", d=DC)),
            reads=(hba,), writes=(rs["dram"],), dma="sSp", tag=f"l1lsp{tt}")
        for xh in range(2):
            for k in range(DC):
                par = k % 2
                wof = par * 1024
                sch.add("ge", lambda h, k=k, xh=xh, wof=wof: h.dma_start(
                    out=WS[:, wof:wof + 1024],
                    in_=hap(din["l1_W_in"], k * 128 * 4096 + xh * 1024,
                            [[4096, 128], [1, 1024]])),
                    reads=(), writes=(wsq[par],),
                    dma=("sW" if par == 0 else "sWb"), tag=f"l1wp{tt}_{xh}_{k}")
                for f in range(8):
                    sch.add("te", lambda h, k=k, f=f, wof=wof, par=par: h.matmul(
                        P[f][:, 0:TT], WS[:, wof + f * 128:wof + (f + 1) * 128],
                        lnxT(k), start=(k == 0), stop=(k == DC - 1)),
                        reads=(wsq[par], hba), writes=(rs[f"P{f}"],),
                        tag=f"l1mp{tt}_{xh}_{k}_{f}")
            for f in range(8):
                ch = xh * 8 + f
                sch.add("se", lambda h, f=f, ch=ch: h.activation(
                    xcv(ch), P[f][:, 0:TT], AF.Identity, bias=0.0, scale=1.0),
                    reads=(rs[f"P{f}"],), writes=(hbb,), tag=f"l1xpe{tt}_{ch}")
                if tt == NTT - 1:
                    sch.add("ve", lambda h, ch=ch: h.tensor_copy(
                        last3[:, ch * 3:(ch + 1) * 3], xcv(ch)[:, TT - 3:TT]),
                        reads=(hbb,), writes=(smr,), tag=f"l1l3{tt}_{ch}")
        sch.add("sy", lambda h, t0=t0: h.dma_start(
            out=dram["sp_xconv"][:, t0:t0 + TT].rearrange(
                "(c q) t -> q c t", c=16),
            in_=HB[:, 4096:4096 + 16 * TT].rearrange("q (c t) -> q c t", c=16)),
            reads=(hbb,), writes=(rs["dram"],), dma="sSp", tag=f"l1psp{tt}")
    sch.add("sy", lambda h: h.dma_start(
        out=hap(dram["agB_in"], 0, [[48, 128], [1, 48]]), in_=last3),
        reads=(smr,), writes=(rs["dram"],), dma="sSp", tag="l1agBw")
    sch.add("ge", lambda h: h.collective_compute(
        "AllGather", ALU.bypass, replica_groups=[list(range(NCORES))],
        ins=[dram["agB_in"].ap().opt()], outs=[dram["agB_out"].ap().opt()]),
        reads=(rs["dram"],), writes=(rs["dram"],), tag="l1agB")
    for j in range(NCORES):
        sch.add("sy", lambda h, j=j: h.dma_start(
            out=FA[:, 2048 + (j % 2) * 48:2048 + (j % 2) * 48 + 48],
            in_=hap(dram["agB_out"], j * 6144, [[48, 128], [1, 48]])),
            reads=(rs["dram"],), writes=(far,), dma="sSp", tag=f"l1hl{j}")
        if j == 0:
            sch.add("ve", lambda h, j=j: h.tensor_scalar(
                carries, FA[:, 2048:2096], cst["prevsel"][:, j:j + 1], None,
                ALU.mult),
                reads=(far, csts), writes=(smr,), tag=f"l1hm{j}")
        else:
            sch.add("ve", lambda h, j=j: h.scalar_tensor_tensor(
                carries, FA[:, 2048 + (j % 2) * 48:2048 + (j % 2) * 48 + 48],
                cst["prevsel"][:, j:j + 1], carries, ALU.mult, ALU.add),
                reads=(far, csts, smr), writes=(smr,), tag=f"l1hm{j}")
    for tt in range(NTT):
        t0 = tt * TT
        sch.add("sy", lambda h, t0=t0: h.dma_start(
            out=HB[:, 4096:4096 + 16 * TT].rearrange("q (c t) -> q c t", c=16),
            in_=dram["sp_xconv"][:, t0:t0 + TT].rearrange(
                "(c q) t -> q c t", c=16)),
            reads=(rs["dram"],), writes=(hbb,), dma="sSp", tag=f"l1prl{tt}")
        for ch in range(16):
            ext = FB[:, 0:TT + 3]
            sch.add("ve", lambda h, ch=ch: h.tensor_copy(
                ext[:, 0:3], carries[:, ch * 3:(ch + 1) * 3]),
                reads=(smr,), writes=(fbr,), tag=f"l1cx{tt}_{ch}")
            sch.add("ve", lambda h, ch=ch: h.tensor_copy(
                ext[:, 3:TT + 3], xcv(ch)),
                reads=(hbb,), writes=(fbr,), tag=f"l1cy{tt}_{ch}")
            sch.add("ve", lambda h, ch=ch: h.tensor_copy(
                carries[:, ch * 3:(ch + 1) * 3], ext[:, TT:TT + 3]),
                reads=(fbr,), writes=(smr,), tag=f"l1cc{tt}_{ch}")
            w = lambda tap, ch=ch: cst["l1_conv_w"][:, ch * 4 + tap:ch * 4 + tap + 1]
            a1 = FB[:, 640:640 + TT]
            a2 = FB[:, 1152:1152 + TT]
            sch.add("ve", lambda h, ch=ch, w=w: h.tensor_scalar(
                a1, ext[:, 0:TT], w(0), None, ALU.mult),
                reads=(fbr, csts), writes=(fbr,), tag=f"l1c0{tt}_{ch}")
            sch.add("ve", lambda h, ch=ch, w=w: h.scalar_tensor_tensor(
                a2, ext[:, 1:TT + 1], w(1), a1, ALU.mult, ALU.add),
                reads=(fbr, csts), writes=(fbr,), tag=f"l1c1{tt}_{ch}")
            sch.add("ve", lambda h, ch=ch, w=w: h.scalar_tensor_tensor(
                a1, ext[:, 2:TT + 2], w(2), a2, ALU.mult, ALU.add),
                reads=(fbr, csts), writes=(fbr,), tag=f"l1c2{tt}_{ch}")
            sch.add("ve", lambda h, ch=ch, w=w: h.scalar_tensor_tensor(
                a2, ext[:, 3:TT + 3], w(3), a1, ALU.mult, ALU.add),
                reads=(fbr, csts), writes=(fbr,), tag=f"l1c3{tt}_{ch}")
            sch.add("se", lambda h, ch=ch: h.activation(
                xcv(ch), a2, AF.Silu, bias=cst["l1_conv_b"][:, ch:ch + 1],
                scale=1.0),
                reads=(fbr, csts), writes=(hbb,), tag=f"l1cs{tt}_{ch}")
        for k in range(16):
            sch.add("te", lambda h, k=k: h.matmul(
                P[7][0:1, 0:TT], WS[:, 4096 + k * 33:4096 + k * 33 + 1],
                xcv(k), start=(k == 0), stop=(k == 15)),
                reads=(rs["WS2a"], hbb), writes=(rs["P7"],), tag=f"l1sd{tt}_{k}")
            sch.add("te", lambda h, k=k: h.matmul(
                P[5][0:16, 0:TT], WS[:, 4096 + k * 33 + 1:4096 + k * 33 + 17],
                xcv(k), start=(k == 0), stop=(k == 15)),
                reads=(rs["WS2a"], hbb), writes=(rs["P5"],), tag=f"l1sb{tt}_{k}")
            sch.add("te", lambda h, k=k: h.matmul(
                P[3][64:80, 0:TT], WS[:, 4096 + k * 33 + 17:4096 + k * 33 + 33],
                xcv(k), start=(k == 0), stop=(k == 15)),
                reads=(rs["WS2a"], hbb), writes=(rs["P3"],), tag=f"l1sc{tt}_{k}")
        sch.add("se", lambda h: h.activation(
            dltr, P[7][0:1, 0:TT], AF.Exp, bias=0.0, scale=1.0),
            reads=(rs["P7"],), writes=(fbr,), tag=f"l1de{tt}")
        sch.add("ve", lambda h: h.tensor_scalar(
            dltr, dltr, 1.0, None, ALU.add), reads=(fbr,), writes=(fbr,),
            tag=f"l1d1{tt}")
        sch.add("se", lambda h: h.activation(
            dltr, dltr, AF.Ln, bias=0.0, scale=1.0),
            reads=(fbr,), writes=(fbr,), tag=f"l1d2{tt}")
        sch.add("ve", lambda h, t0=t0: h.tensor_copy(
            Cmb(t0, TT), P[3][64:80, 0:TT]),
            reads=(rs["P3"],), writes=(smbr,), tag=f"l1cme{tt}")
        sch.add("te", lambda h: h.matmul(
            P[4][0:16, 0:TT], cst["ones_row"][:, 0:16], dltr,
            start=True, stop=True),
            reads=(fbr, csts), writes=(rs["P4"],), tag=f"l1db{tt}")
        sch.add("ve", lambda h: h.tensor_tensor(
            dBr[0:16, :], P[5][0:16, 0:TT], P[4][0:16, 0:TT], ALU.mult)
            if False else h.tensor_copy(FB[0:16, 1664:1664 + 256]
                                        if False else dBr[0:16, :],
                                        P[5][0:16, 0:TT]),
            reads=(rs["P5"],), writes=(fbr,), tag=f"l1bme{tt}")
        sch.add("ve", lambda h: h.tensor_tensor(
            dBr[0:16, :], dBr[0:16, :], P[4][0:16, 0:TT], ALU.mult),
            reads=(fbr, rs["P4"]), writes=(fbr,), tag=f"l1dB{tt}")
        for i in range(TT // 128):
            sch.add("te", lambda h, i=i: h.transpose(
                P[2][:, 0:16], dBr[0:16, i * 128:(i + 1) * 128],
                cst["ident_f"][0:16, 0:16]),
                reads=(fbr, csts), writes=(rs["P2"],), tag=f"l1dT{tt}_{i}")
            sch.add("ve", lambda h, i=i: h.tensor_copy(
                FB[:, 1664:1680], P[2][:, 0:16]),
                reads=(rs["P2"],), writes=(fbr,), tag=f"l1dTe{tt}_{i}")
            sch.add("ge", lambda h: h.dma_start(
                out=FB[:, 1680:1696].bitcast(BF16)[:, 0:16],
                in_=FB[:, 1664:1680]),
                reads=(fbr,), writes=(fbr,), dma="sGd", tag=f"l1dTb{tt}_{i}")
            for ch in range(16):
                sch.add("te", lambda h, i=i, ch=ch: h.transpose(
                    P[2][:, 128 + (ch % 2) * 64:128 + (ch % 2) * 64 + 64].bitcast(BF16),
                    xcv(ch)[:, i * 128:(i + 1) * 128], cst["ident"]),
                    reads=(hbb, csts), writes=(rs["P2"],), tag=f"l1xT{tt}_{i}_{ch}")
                sch.add("ve", lambda h, i=i, ch=ch: h.tensor_copy(
                    SMb[:, 4224 + (ch % 2) * 128:4224 + (ch % 2) * 128 + 128],
                    P[2][:, 128 + (ch % 2) * 64:128 + (ch % 2) * 64 + 64].bitcast(BF16)),
                    reads=(rs["P2"],), writes=(smbr,), tag=f"l1xTe{tt}_{i}_{ch}")
                sch.add("te", lambda h, i=i, ch=ch, tt=tt: h.matmul(
                    P[6][:, ch * 16:(ch + 1) * 16],
                    SMb[:, 4224 + (ch % 2) * 128:4224 + (ch % 2) * 128 + 128],
                    FB[:, 1680:1696].bitcast(BF16)[:, 0:16],
                    start=True, stop=True),
                    reads=(smbr, fbr), writes=(rs["P6"],),
                    tag=f"l1hp{tt}_{i}_{ch}")
            if tt == 0 and i == 0:
                sch.add("ve", lambda h: h.tensor_copy(hpbuf[0], P[6][:, 0:256]),
                        reads=(rs["P6"],), writes=(smr,), tag=f"l1hpa{tt}_{i}")
            else:
                sch.add("ve", lambda h: h.tensor_add(
                    hpbuf[0], hpbuf[0], P[6][:, 0:256]),
                    reads=(rs["P6"], smr), writes=(smr,), tag=f"l1hpa{tt}_{i}")
    sch.add("sy", lambda h: h.dma_start(
        out=hap(dram["agC_in"], 0, [[256, 128], [1, 256]]), in_=hpbuf[0]),
        reads=(smr,), writes=(rs["dram"],), dma="sSp", tag="l1agCw")
    sch.add("ge", lambda h: h.collective_compute(
        "AllGather", ALU.bypass, replica_groups=[list(range(NCORES))],
        ins=[dram["agC_in"].ap().opt()], outs=[dram["agC_out"].ap().opt()]),
        reads=(rs["dram"],), writes=(rs["dram"],), tag="l1agC")
    for j in range(NCORES):
        sch.add("sy", lambda h, j=j: h.dma_start(
            out=FA[:, 2048 + (j % 2) * 256:2048 + (j % 2) * 256 + 256],
            in_=hap(dram["agC_out"], j * 32768, [[256, 128], [1, 256]])),
            reads=(rs["dram"],), writes=(far,), dma="sSp", tag=f"l1hc{j}")
        if j == 0:
            sch.add("ve", lambda h: h.tensor_copy(hpbuf[0], FA[:, 2048:2304]),
                    reads=(far,), writes=(smr,), tag="l1hs0")
        else:
            sch.add("ve", lambda h, j=j: h.tensor_add(
                hpbuf[j % 2], hpbuf[(j + 1) % 2],
                FA[:, 2048 + (j % 2) * 256:2048 + (j % 2) * 256 + 256]),
                reads=(far, smr), writes=(smr,), tag=f"l1hs{j}")
    hm = hpbuf[(NCORES - 1) % 2]
    sch.add("ve", lambda h: h.tensor_scalar(
        hm, hm, 1.0 / (NCORES * T), None, ALU.mult),
        reads=(smr,), writes=(smr,), tag="l1hdiv")
    for ch in range(16):
        sch.add("te", lambda h, ch=ch: h.transpose(
            P[2][0:16, (ch % 4) * 128:(ch % 4) * 128 + 128],
            hm[:, ch * 16:(ch + 1) * 16], cst["ident_f"]),
            reads=(smr, csts), writes=(rs["P2"],), tag=f"l1hT{ch}")
        sch.add("ve", lambda h, ch=ch: h.tensor_copy(
            hmT(ch * 128, 128), P[2][0:16, (ch % 4) * 128:(ch % 4) * 128 + 128]),
            reads=(rs["P2"],), writes=(smbr,), tag=f"l1hTe{ch}")
    for tt in range(NTT):
        t0 = tt * TT
        sch.add("sy", lambda h, t0=t0: h.dma_start(
            out=HB[:, 0:DC * TT].rearrange("q (d t) -> q d t", d=DC),
            in_=dram["sp_lnx4"][:, t0:t0 + TT].rearrange("(d q) t -> q d t", d=DC)),
            reads=(rs["dram"],), writes=(hba,), dma="sSp", tag=f"l1lrl{tt}")
        sch.add("sy", lambda h, t0=t0: h.dma_start(
            out=HB[:, 4096:4096 + 16 * TT].rearrange("q (c t) -> q c t", c=16),
            in_=dram["sp_xconv"][:, t0:t0 + TT].rearrange(
                "(c q) t -> q c t", c=16)),
            reads=(rs["dram"],), writes=(hbb,), dma="sSp", tag=f"l1prl2{tt}")
        for zh in range(2):
            for k in range(DC):
                par = k % 2
                wof = par * 1024
                sch.add("ge", lambda h, k=k, zh=zh, wof=wof: h.dma_start(
                    out=WS[:, wof:wof + 1024],
                    in_=hap(din["l1_W_in"], k * 128 * 4096 + 2048 + zh * 1024,
                            [[4096, 128], [1, 1024]])),
                    reads=(), writes=(wsq[par],),
                    dma=("sW" if par == 0 else "sWb"), tag=f"l1wz{tt}_{zh}_{k}")
                for f in range(8):
                    sch.add("te", lambda h, k=k, f=f, wof=wof, par=par: h.matmul(
                        P[f][:, 0:TT], WS[:, wof + f * 128:wof + (f + 1) * 128],
                        lnxT(k), start=(k == 0), stop=(k == DC - 1)),
                        reads=(wsq[par], hba), writes=(rs[f"P{f}"],),
                        tag=f"l1mz{tt}_{zh}_{k}_{f}")
            for f in range(8):
                ch = zh * 8 + f
                sch.add("se", lambda h, f=f, ch=ch: h.activation(
                    zsT(ch), P[f][:, 0:TT], AF.Silu, bias=0.0, scale=1.0),
                    reads=(rs[f"P{f}"],), writes=(hbb,), tag=f"l1zs{tt}_{ch}")
        for ch in range(16):
            pi = ch % 4
            sch.add("te", lambda h, ch=ch, t0=t0, pi=pi: h.matmul(
                P[pi][:, 0:TT], hmT(ch * 128, 128), Cmb(t0, TT),
                start=True, stop=True),
                reads=(smbr,), writes=(rs[f"P{pi}"],), tag=f"l1ym{tt}_{ch}")
            sch.add("ve", lambda h, ch=ch, pi=pi: h.scalar_tensor_tensor(
                FA[:, 2048:2048 + TT], xcv(ch), cst["l1_D"][:, ch:ch + 1],
                P[pi][:, 0:TT], ALU.mult, ALU.add),
                reads=(hbb, csts, rs[f"P{pi}"]), writes=(far,),
                tag=f"l1yd{tt}_{ch}")
            sch.add("ve", lambda h, ch=ch: h.tensor_tensor(
                ygT(ch), FA[:, 2048:2048 + TT], zsT(ch), ALU.mult),
                reads=(far, hbb), writes=(hbc,), tag=f"l1yg{tt}_{ch}")
        for k in range(16):
            par = k % 2
            wof = par * 1024
            sch.add("ge", lambda h, k=k, wof=wof: h.dma_start(
                out=WS[:, wof:wof + 1024],
                in_=din["l1_W_out"][k * 128:(k + 1) * 128, :]),
                reads=(), writes=(wsq[par],),
                dma=("sW" if par == 0 else "sWb"), tag=f"l1wo{tt}_{k}")
            for do in range(8):
                sch.add("te", lambda h, k=k, do=do, wof=wof, par=par: h.matmul(
                    P[do][:, 0:TT], WS[:, wof + do * 128:wof + (do + 1) * 128],
                    ygT(k), start=(k == 0), stop=(k == 15)),
                    reads=(wsq[par], hbc), writes=(rs[f"P{do}"],),
                    tag=f"l1wm{tt}_{k}_{do}")
        for do in range(8):
            sch.add("ve", lambda h, do=do, t0=t0: h.tensor_tensor(
                X[do][:, t0:t0 + TT], X[do][:, t0:t0 + TT], P[do][:, 0:TT],
                ALU.add),
                reads=(rs[f"P{do}"],), writes=(rs[f"X{do}"],),
                tag=f"l1res{tt}_{do}")



def _bcdt_perm():
    b = np.arange(32)
    bp = np.concatenate([b[0:16:2], b[1:16:2], b[16:]])
    return np.concatenate([bp, 32 + bp, 64 + np.arange(32)])


def prepare_inputs(inputs, T_core, ncores=NCORES):
    f32 = np.float32
    x = np.asarray(inputs["L0_attn_x"], f32).reshape(-1, D)

    g = {}
    g["wg_t"] = np.ascontiguousarray(np.asarray(inputs["L0_ffn_w_gate"], np.int32).T)
    g["wu_t"] = np.ascontiguousarray(np.asarray(inputs["L0_ffn_w_up"], np.int32).T)
    g["wd_t"] = np.ascontiguousarray(np.asarray(inputs["L0_ffn_w_down"], np.int32).T)

    g["l0_ln_w"] = np.ascontiguousarray(np.asarray(inputs["L0_attn_ln_w"], f32).reshape(DC, 128).T)
    g["l0_ln_b"] = np.ascontiguousarray(np.asarray(inputs["L0_attn_ln_b"], f32).reshape(DC, 128).T)
    W_in = np.asarray(inputs["L0_attn_W_in"], f32)
    perm = np.concatenate([np.arange(4096), 4096 + _bcdt_perm()])
    g["l0_W_in"] = np.ascontiguousarray(W_in[:, perm])
    g["l0_dt_bias"] = np.asarray(inputs["L0_attn_dt_bias"], f32).reshape(32, 1)
    g["l0_A"] = np.asarray(inputs["L0_attn_A_log"], f32).reshape(32, 1)
    g["l0_D"] = np.ascontiguousarray(
        np.repeat(np.asarray(inputs["L0_attn_D"], f32), 64).reshape(16, 128).T)
    g["l0_rms_w"] = np.ascontiguousarray(
        np.asarray(inputs["L0_attn_rms_w"], f32).reshape(16, 128).T)
    g["l0_W_out"] = np.ascontiguousarray(np.asarray(inputs["L0_attn_W_out"], f32))

    g["l1_W_in"] = np.ascontiguousarray(np.asarray(inputs["L1_attn_W_in"], f32))
    cw = np.asarray(inputs["L1_attn_conv_w"], f32).reshape(2048, 4)
    g["l1_conv_w"] = np.ascontiguousarray(
        cw.reshape(16, 128, 4).transpose(1, 0, 2).reshape(128, 64))
    g["l1_conv_b"] = np.ascontiguousarray(
        np.asarray(inputs["L1_attn_conv_b"], f32).reshape(16, 128).T)
    g["l1_W_xproj"] = np.ascontiguousarray(np.asarray(inputs["L1_attn_W_xproj"], f32))
    g["l1_D"] = np.ascontiguousarray(np.asarray(inputs["L1_attn_D"], f32).reshape(16, 128).T)
    g["l1_W_out"] = np.ascontiguousarray(np.asarray(inputs["L1_attn_W_out"], f32))

    g["e_W1"] = np.ascontiguousarray(np.asarray(inputs["L1_ffn_W1"], f32))
    g["e_W2"] = np.ascontiguousarray(np.asarray(inputs["L1_ffn_W2"], f32))
    for nm, src, nch in [("e_b1", "L1_ffn_b1", 32), ("e_g1", "L1_ffn_g1", 32),
                         ("e_be1", "L1_ffn_be1", 32), ("e_b2", "L1_ffn_b2", 8),
                         ("e_g2", "L1_ffn_g2", 8), ("e_be2", "L1_ffn_be2", 8),
                         ("e_pb", "L1_ffn_path_bias", 8), ("e_ctx", "L1_ffn_context", 8)]:
        a = np.asarray(inputs[src], f32)
        g[nm] = np.ascontiguousarray(a.reshape(9, nch, 128).transpose(0, 2, 1))
    g["e_th1"] = np.asarray(inputs["L1_ffn_th1"], f32).copy()
    g["e_th2"] = np.asarray(inputs["L1_ffn_th2"], f32).copy()
    g["e_wpath"] = np.ascontiguousarray(np.asarray(inputs["L1_ffn_w_path"], f32).reshape(8, 128).T)
    g["e_wteam"] = np.ascontiguousarray(np.asarray(inputs["L1_ffn_w_team"], f32).reshape(8, 128).T)
    g["e_bp"] = np.broadcast_to(np.asarray(inputs["L1_ffn_b_path"], f32).reshape(1, 1), (9, 1)).copy()
    g["e_ovp"] = np.asarray(inputs["L1_ffn_ov_path"], f32).reshape(9, 1).copy()
    g["e_bt"] = np.broadcast_to(np.asarray(inputs["L1_ffn_b_team"], f32).reshape(1, 1), (3, 1)).copy()
    g["e_ovt"] = np.asarray(inputs["L1_ffn_ov_team"], f32).reshape(3, 1).copy()

    g["ones_col"] = np.ones((128, 1), f32)
    g["ones_col_bf"] = np.ones((128, 1), ml_dtypes.bfloat16)
    g["ones_row"] = np.ones((1, 128), f32)
    g["ones_row_bf"] = np.ones((1, 128), ml_dtypes.bfloat16)
    g["ident_bf"] = np.eye(128, dtype=ml_dtypes.bfloat16)
    g["ident_f"] = np.eye(128, dtype=f32)
    g["e2x"] = ((np.arange(8)[None, :] >> 1) == np.arange(4)[:, None]).astype(f32)
    g["altsign"] = (1.0 - 2.0 * (np.arange(8) % 2)).astype(f32).reshape(8, 1)
    g["pswap"] = (np.arange(128)[None, :] == (np.arange(8)[:, None] ^ 1)).astype(f32)
    ip = np.eye(128, dtype=np.float32); ip[:8, :8] = 0
    g["ipad"] = ip.astype(ml_dtypes.bfloat16)
    g["ones_T"] = np.ones((1, 4096), f32)
    nid = np.zeros((32, 2048), f32)
    for hh in range(32):
        nid[hh, hh * 64:(hh + 1) * 64] = -1.0
    g["negid"] = nid
    g["tril8"] = np.tril(np.ones((8, 8), f32))
    g["a2b_bf"] = (np.arange(32)[:, None] % 2 == (np.arange(128)[None, :] >> 6)
                   ).astype(ml_dtypes.bfloat16)
    g["indj2"] = ((np.arange(32)[:, None] >> 1) == np.arange(16)[None, :]).astype(f32)
    g["ones32"] = np.ones((32, 4096), f32)
    g["zeros32"] = np.zeros((32, 2048), f32)
    g["a2b_f"] = (np.arange(32)[:, None] % 2 == (np.arange(128)[None, :] >> 6)).astype(f32)
    m64b = np.where(np.arange(64)[:, None] <= np.arange(64)[None, :], 0.0, NEG).astype(f32)
    g["mask_d2"] = np.ascontiguousarray(np.tile(m64b, (1, 8)))
    i64 = np.zeros((64, 128), f32); i64[:, :64] = np.eye(64)
    g["ident64f"] = i64
    g["ident64b"] = np.eye(64, dtype=ml_dtypes.bfloat16)
    g["inds_c"] = np.repeat(
        ((np.arange(32)[:, None] >> 1) == np.arange(16)[None, :]).astype(f32),
        32, axis=1)
    m64 = np.where(np.arange(64)[:, None] <= np.arange(64)[None, :], 0.0, NEG).astype(f32)
    g["mask_d"] = np.ascontiguousarray(np.tile(m64, (1, 8)))
    e2 = np.zeros((2, 128), ml_dtypes.bfloat16); e2[0, :64] = 1; e2[1, 64:] = 1
    g["e2a"] = e2
    g["indj"] = ((np.arange(32)[:, None] >> 1) == np.arange(16)[None, :]).astype(f32)
    g["eps_tab"] = np.broadcast_to(
        np.asarray([EPS5, EPS6, np.pi / 2], f32)[None, :], (128, 3)).copy()

    cores = []
    L_total = x.shape[0]
    t_all = np.arange(L_total, dtype=f32)
    inv_freq = (1.0 / 10000.0 ** (np.arange(0, 16, 2, dtype=f32) / 16.0)).astype(f32)
    for cid in range(ncores):
        d = dict(g)
        sl = slice(cid * T_core, (cid + 1) * T_core)
        d["x_in"] = np.ascontiguousarray(x[sl])
        ang = t_all[sl][None, :] * inv_freq[:, None]
        d["rope_cos"] = np.cos(ang).astype(f32)
        d["rope_sin"] = np.sin(ang).astype(f32)
        lt = np.zeros((32, 8), f32); lt[:, :cid] = 1.0
        d["core_lt"] = lt
        oh = np.zeros((1, 8), f32)
        if cid > 0:
            oh[0, cid - 1] = 1.0
        d["prev_oh"] = oh
        d["prevsel"] = np.broadcast_to(oh, (128, 8)).copy()
        ltc = np.zeros((8, 1), f32); ltc[:cid] = 1.0
        d["lt_col"] = ltc
        cores.append(d)
    return cores


def kernel(**inputs):
    T_core = 16384 // NCORES
    nc, meta = build_program(T_core)
    in_maps = prepare_inputs(inputs, T_core)
    res = run_bass_kernel_spmd(nc, in_maps, core_ids=list(range(NCORES)))
    out = np.concatenate([np.asarray(r["out"]) for r in res.results], axis=0)
    return out.reshape(1, 16384, D)
